# version 04
# baseline (speedup 1.0000x reference)
import contextlib
import math
import numpy as np
import ml_dtypes
import concourse.bass as bass
import concourse.mybir as mybir
from concourse.bass_utils import run_bass_kernel_spmd

F32 = mybir.dt.float32
BF16 = mybir.dt.bfloat16
AF = mybir.ActivationFunctionType
ALU = mybir.AluOpType
AXX = mybir.AxisListType.X

NCORE = 8
DUMP = False
B1_STAGE = 5
B1_SUB = 9
B2_STAGE = 5
B2_SUB = 9
T = 2048
NC = 8
H = 16
HD = 64
TT = 512
NTT = T // TT

ENGINES = ("tensor", "vector", "scalar", "gpsimd", "sync")
DMA_POOL = {"sync": 18, "gpsimd": 18, "scalar": 12, "vector": 4, "tensor": 4}
CC_POOL = 6


class Reg:
    __slots__ = ("writer", "readers", "name")

    def __init__(self, name=""):
        self.writer = None
        self.readers = []
        self.name = name


class _Op:
    __slots__ = ("engine", "emit", "deps", "dma", "cc", "seq", "flagged",
                 "tag", "sem_slot", "sem_use", "eidx")

    def __init__(self, engine, emit, deps, dma, cc, seq, tag):
        self.engine = engine
        self.emit = emit
        self.deps = deps
        self.dma = dma
        self.cc = cc
        self.seq = seq
        self.flagged = False
        self.tag = tag
        self.sem_slot = None
        self.sem_use = None
        self.eidx = None


class Sched:
    def __init__(self, nc):
        self.nc = nc
        self.ops = []
        self.streams = {e: [] for e in ENGINES}
        self._dma_count = {e: 0 for e in ENGINES}
        self._cc_count = 0

    def op(self, engine, emit, deps=(), dma=False, cc=False, tag=""):
        assert engine in ENGINES, engine
        deps = [d for d in deps if d is not None]
        o = _Op(engine, emit, deps, dma, cc, len(self.ops), tag)
        if dma:
            n = self._dma_count[engine]
            pool = DMA_POOL[engine]
            o.sem_slot = (engine, n % pool)
            o.sem_use = n // pool + 1
            self._dma_count[engine] = n + 1
        elif cc:
            n = self._cc_count
            o.sem_slot = ("cc", n % CC_POOL)
            o.sem_use = n // CC_POOL + 1
            self._cc_count = n + 1
        self.streams[engine].append(o)
        self.ops.append(o)
        for d in deps:
            if not (d.engine == "tensor" and engine == "tensor"
                    and not d.dma and not d.cc and not dma and not cc):
                d.flagged = True
        return o

    def rw(self, engine, emit, reads=(), writes=(), deps=(), dma=False, cc=False,
           tag=""):
        dd = list(deps)
        for r in reads:
            if r.writer is not None:
                dd.append(r.writer)
        for w in writes:
            if w.writer is not None:
                dd.append(w.writer)
            dd.extend(w.readers)
        h = self.op(engine, emit, deps=dd, dma=dma, cc=cc, tag=tag)
        for r in reads:
            r.readers.append(h)
        for w in writes:
            w.writer = h
            w.readers = []
        return h

    def dma(self, queue, out, in_, reads=(), writes=(), deps=(), tag="", **kw):
        return self.rw(queue, lambda e: e.dma_start(out=out, in_=in_, **kw),
                       reads=reads, writes=writes, deps=deps, dma=True, tag=tag)

    def prepare(self, stack):
        nc = self.nc
        self._esem = {e: stack.enter_context(nc.semaphore(f"s_{e}"))
                      for e in ENGINES}
        self._dsem = {}
        for q, npool in DMA_POOL.items():
            used = self._dma_count[q]
            for slot in range(min(npool, used)):
                self._dsem[(q, slot)] = stack.enter_context(
                    nc.semaphore(f"d_{q}_{slot}"))
        for slot in range(min(CC_POOL, self._cc_count)):
            self._dsem[("cc", slot)] = stack.enter_context(
                nc.semaphore(f"d_cc_{slot}"))
        self._flag_count = {e: {} for e in ENGINES}
        for e in ENGINES:
            c = 0
            for o in self.streams[e]:
                if (not o.dma) and (not o.cc) and o.flagged:
                    c += 1
                o.eidx = c
                self._flag_count[e][o.seq] = c

    def emit_engine(self, engine, handle):
        water = {e: 0 for e in ENGINES}
        dwater = {}
        for o in self.streams[engine]:
            for d in o.deps:
                if d.dma or d.cc:
                    key = d.sem_slot
                    need = (16 if d.dma else 1) * d.sem_use
                    if dwater.get(key, 0) < need:
                        handle.wait_ge(self._dsem[key], need)
                        dwater[key] = need
                elif not (d.engine == "tensor" and engine == "tensor"
                          and not o.dma and not o.cc):
                    need = self._flag_count[d.engine][d.seq]
                    assert need > 0, (d.engine, d.tag, o.tag)
                    if water[d.engine] < need:
                        handle.wait_ge(self._esem[d.engine], need)
                        water[d.engine] = need
            if (o.dma or o.cc) and o.sem_use > 1:
                key = o.sem_slot
                need = (16 if o.dma else 1) * (o.sem_use - 1)
                if dwater.get(key, 0) < need:
                    handle.wait_ge(self._dsem[key], need)
                    dwater[key] = need
            inst = o.emit(handle)
            if o.dma:
                inst.then_inc(self._dsem[o.sem_slot], 16)
            elif o.cc:
                inst.then_inc(self._dsem[o.sem_slot])
            elif o.flagged:
                inst.then_inc(self._esem[engine], 1)

    def build(self, stack, block):
        self.prepare(stack)
        sched = self

        @block.sync
        def _(sync):
            sched.emit_engine("sync", sync)

        @block.vector
        def _(vector):
            sched.emit_engine("vector", vector)

        @block.scalar
        def _(scalar):
            sched.emit_engine("scalar", scalar)

        @block.tensor
        def _(tensor):
            sched.emit_engine("tensor", tensor)

        @block.gpsimd
        def _(gpsimd):
            sched.emit_engine("gpsimd", gpsimd)


class SB:
    def __init__(self, t, reg):
        self.t = t
        self.r = reg

    def __getitem__(self, idx):
        return self.t[idx]


def _mk_emit(fn_name, args, kw):
    def emit(e):
        return getattr(e, fn_name)(*args, **kw)
    return emit


class Ctx:
    def __init__(self, nc, stk, sched):
        self.nc = nc
        self.stk = stk
        self.s = sched
        self.psum_banks = []
        self.psum_rr = 0
        self.cm = {}

    def sb(self, name, shape, dtype, stk=None):
        t = (stk or self.stk).enter_context(
            self.nc.sbuf_tensor("s_" + name, list(shape), dtype))
        r = Reg(name)
        r.writer = getattr(self, "_fence", None)
        return SB(t, r)

    def fence(self, tag="fence"):
        lasts = [st[-1] for st in self.s.streams.values() if st]
        hs = []
        for e in ENGINES:
            hs.append(self.s.op(e, _mk_emit("nop", (), {}), deps=lasts,
                                tag=tag + e))
        f = self.s.op("gpsimd", _mk_emit("nop", (), {}), deps=hs, tag=tag)
        self._fence = f
        return f

    def psum_init(self, n_f32=6, n_bf16=2):
        for i in range(n_f32):
            t = self.stk.enter_context(self.nc.psum_tensor(f"pb{i}", [128, 512], F32))
            self.psum_banks.append(SB(t, Reg(f"pb{i}")))
        self.psum_bf = []
        for i in range(n_bf16):
            t = self.stk.enter_context(self.nc.psum_tensor(f"pbh{i}", [128, 1024], BF16))
            self.psum_bf.append(SB(t, Reg(f"pbh{i}")))
        self.psum_bf_rr = 0

    def psum(self):
        b = self.psum_banks[self.psum_rr % len(self.psum_banks)]
        self.psum_rr += 1
        return b

    def psum_at(self, i):
        return self.psum_banks[i]

    def psumb(self):
        b = self.psum_bf[self.psum_bf_rr % len(self.psum_bf)]
        self.psum_bf_rr += 1
        return b

    def e(self, engine, fn, *args, reads=(), writes=(), deps=(), tag="", **kw):
        return self.s.rw(engine, _mk_emit(fn, args, kw),
                         reads=reads, writes=writes, deps=deps, tag=tag)

    def mm(self, out_ap, lhsT_ap, rhs_ap, start, stop, reads=(), writes=(),
           deps=(), tag=""):
        return self.s.rw("tensor", _mk_emit(
            "matmul", (out_ap, lhsT_ap, rhs_ap),
            dict(start=start, stop=stop, skip_group_check=True)),
            reads=reads, writes=writes, deps=deps, tag=tag)

    def dma(self, queue, out, in_, reads=(), writes=(), deps=(), tag="", **kw):
        return self.s.dma(queue, out, in_, reads=reads, writes=writes,
                          deps=deps, tag=tag, **kw)


def ts(i, n):
    return slice(i * n, (i + 1) * n)


def alloc_common(ctx):
    cm = ctx.cm
    cm["xt"] = ctx.sb("cm_xt", [128, NC, TT], F32)
    cm["xb"] = ctx.sb("cm_xb", [128, NC, TT], BF16)
    cm["lnst"] = ctx.sb("cm_lnst", [1, 4 * TT], F32)
    cm["absb"] = ctx.sb("cm_absb", [128, TT], F32)
    cm["bbsb"] = ctx.sb("cm_bbsb", [128, TT], F32)
    cm["ones_b"] = ctx.sb("cm_onesb", [128, 1], BF16)
    cm["ones1f"] = ctx.sb("cm_ones1f", [1, 128], F32)
    cm["e1"] = ctx.sb("cm_e1", [128, TT], F32)
    cm["ei"] = ctx.sb("cm_ei", [128, TT], mybir.dt.int32)
    cm["eb1"] = ctx.sb("cm_eb1", [128, TT], BF16)
    cm["e2"] = ctx.sb("cm_e2", [128, TT], F32)
    cm["e3"] = ctx.sb("cm_e3", [128, TT], F32)
    cm["e4"] = ctx.sb("cm_e4", [128, TT], F32)
    cm["st16"] = ctx.sb("cm_st16", [16, TT], F32)
    ctx.e("vector", "memset", cm["ones_b"][:], 1.0, writes=[cm["ones_b"].r])
    ctx.e("vector", "memset", cm["ones1f"][:], 1.0, writes=[cm["ones1f"].r])


def emit_ln(ctx, xdram, xdram_reg, xn, Tc):
    cm = ctx.cm
    starts = [i * TT for i in range(Tc // TT)]
    if Tc % TT:
        starts.append(Tc - TT)
    for it, a0 in enumerate(starts):
        w = TT
        xt, xb, st = cm["xt"], cm["xb"], cm["lnst"]
        ctx.dma("sync", xt[:, :, :w],
                xdram.rearrange("(c p) t -> p c t", p=128)[:, :, a0:a0 + w],
                reads=[xdram_reg], writes=[xt.r], tag=f"lnld{it}",
                **({"allow_slow_non_contiguous": True} if w < 16 else {}))
        ctx.e("vector", "tensor_copy", xb[:, :, :w], xt[:, :, :w],
              reads=[xt.r], writes=[xb.r])
        psA = ctx.psum()
        psB = ctx.psum()
        for c in range(NC):
            ctx.mm(psA[0:1, :w], cm["ones_b"][:, 0:1], xb[:, c, :w],
                   start=(c == 0), stop=(c == NC - 1),
                   reads=[cm["ones_b"].r, xb.r], writes=[psA.r])
        ctx.e("scalar", "activation", xb[:, :, :w], xb[:, :, :w], AF.Square,
              reads=[xb.r], writes=[xb.r])
        for c in range(NC):
            ctx.mm(psB[0:1, :w], cm["ones_b"][:, 0:1], xb[:, c, :w],
                   start=(c == 0), stop=(c == NC - 1),
                   reads=[cm["ones_b"].r, xb.r], writes=[psB.r])
        M = slice(0, w)
        V = slice(TT, TT + w)
        R = slice(2 * TT, 2 * TT + w)
        Bb = slice(3 * TT, 3 * TT + w)
        ctx.e("vector", "tensor_scalar_mul", st[0:1, M], psA[0:1, :w], 1.0 / 1024.0,
              reads=[psA.r], writes=[st.r])
        ctx.e("vector", "tensor_scalar_mul", st[0:1, V], psB[0:1, :w], 1.0 / 1024.0,
              reads=[psB.r], writes=[st.r])
        ctx.e("vector", "tensor_mul", st[0:1, R], st[0:1, M], st[0:1, M],
              reads=[st.r], writes=[st.r])
        ctx.e("vector", "tensor_sub", st[0:1, V], st[0:1, V], st[0:1, R],
              reads=[st.r], writes=[st.r])
        ctx.e("vector", "tensor_scalar_add", st[0:1, V], st[0:1, V], 1e-5,
              reads=[st.r], writes=[st.r])
        ctx.e("scalar", "activation", st[0:1, R], st[0:1, V], AF.Sqrt,
              reads=[st.r], writes=[st.r])
        ctx.e("vector", "reciprocal", st[0:1, Bb], st[0:1, R],
              reads=[st.r], writes=[st.r])
        ctx.e("vector", "scalar_tensor_tensor", st[0:1, M], st[0:1, M], -1.0,
              st[0:1, Bb], op0=ALU.mult, op1=ALU.mult,
              reads=[st.r], writes=[st.r])
        psC = ctx.psum()
        psD = ctx.psum()
        ctx.mm(psC[:, :w], cm["ones1f"][:], st[0:1, Bb], start=True, stop=True,
               reads=[cm["ones1f"].r, st.r], writes=[psC.r])
        ctx.mm(psD[:, :w], cm["ones1f"][:], st[0:1, M], start=True, stop=True,
               reads=[cm["ones1f"].r, st.r], writes=[psD.r])
        ab, bb = cm["absb"], cm["bbsb"]
        ctx.e("scalar", "copy", ab[:, :w], psC[:, :w], reads=[psC.r], writes=[ab.r])
        ctx.e("scalar", "copy", bb[:, :w], psD[:, :w], reads=[psD.r], writes=[bb.r])
        for c in range(NC):
            eng = "vector" if c % 2 == 0 else "gpsimd"
            eb = cm["e1"] if c % 2 == 0 else cm["e2"]
            ctx.e(eng, "tensor_mul", eb[:, :w], xt[:, c, :w], ab[:, :w],
                  reads=[xt.r, ab.r], writes=[eb.r])
            ctx.e(eng, "tensor_add", xn[:, c, a0:a0 + w], eb[:, :w], bb[:, :w],
                  reads=[eb.r, bb.r], writes=[xn.r])


def wslices_ap(wdram, c):
    return wdram.rearrange("(kc p) o -> p kc o", p=128)[:, :, ts(c, 128)]


def xfm(xdram, p=128):
    return xdram.rearrange("(c p) t -> p c t", p=p)


def build_block0(ctx, iv, ir, xin_d, xin_r, xout_d, xout_r, gin, gout, gr):
    s, cm = ctx.s, ctx.cm
    TC = T + 1
    e1, e2, e3, e4 = cm["e1"], cm["e2"], cm["e3"], cm["e4"]
    st16 = cm["st16"]

    xn = ctx.sb("b0_xn", [128, NC, TC], BF16)
    att = ctx.sb("b0_att", [128, NC, T], BF16)

    with contextlib.ExitStack() as ms:
        zsh = ctx.sb("b0_zsh", [64, T], BF16, ms)
        zg = ctx.sb("b0_zg", [64, T], BF16, ms)
        wsb = ctx.sb("b0_w", [16, T], F32, ms)
        asb = ctx.sb("b0_a", [16, T], F32, ms)
        mixsb = ctx.sb("b0_mix", [128, 6, NC], F32, ms)
        w0sb = ctx.sb("b0_w0s", [16, 1], F32, ms)
        a0sb = ctx.sb("b0_a0s", [16, 1], F32, ms)
        kkv = ctx.sb("b0_kkvs", [128, NC], F32, ms)
        rkv = ctx.sb("b0_rkvs", [128, NC], F32, ms)
        lnw = ctx.sb("b0_lnws", [128, NC], F32, ms)
        lnb = ctx.sb("b0_lnbs", [128, NC], F32, ms)
        pmk = ctx.sb("b0_pmks", [128, NC], F32, ms)
        ebcf = ctx.sb("b0_ebcfs", [16, NC * 128], F32, ms)
        etf = ctx.sb("b0_etfs", [128, NC * 16], F32, ms)
        w1sb = ctx.sb("b0_w1s", [128, NC, 64], BF16, ms)
        a1sb = ctx.sb("b0_a1s", [128, NC, 64], BF16, ms)
        g1sb = ctx.sb("b0_g1s", [128, NC, 64], BF16, ms)
        w2sb = ctx.sb("b0_w2s", [64, 16], BF16, ms)
        a2sb = ctx.sb("b0_a2s", [64, 16], BF16, ms)
        g2sb = ctx.sb("b0_g2s", [64, 1024], BF16, ms)
        rsb = ctx.sb("b0_r", [128, T], BF16, ms)
        kkn = ctx.sb("b0_kkn", [128, T], BF16, ms)
        vsb = ctx.sb("b0_v", [128, T], BF16, ms)
        kvsb = ctx.sb("b0_kv", [128, T], F32, ms)
        wbc = ctx.sb("b0_wbc", [128, T], F32, ms)
        states = ctx.sb("b0_states", [128, T], F32, ms)
        mxa = ctx.sb("b0_mxa", [128, TT], BF16, ms)
        mxb = ctx.sb("b0_mxb", [128, TT], BF16, ms)
        mxc = ctx.sb("b0_mxc", [128, TT], BF16, ms)
        sxa = ctx.sb("b0_sxa", [128, TT], BF16, ms)
        wk3 = [ctx.sb(f"b0_wk3_{i}", [128, NC, 3 * 128], BF16, ms) for i in range(2)]
        stin = [ctx.sb(f"b0_stin{i}", [128, 1], F32, ms) for i in range(4)]
        jS = [ctx.sb(f"b0_jS{i}", [128, 1], F32, ms) for i in range(3)]
        jA = [ctx.sb(f"b0_jA{i}", [128, 1], F32, ms) for i in range(3)]
        jA2 = [ctx.sb(f"b0_jA2{i}", [128, 1], F32, ms) for i in range(3)]
        jS2 = [ctx.sb(f"b0_jS2{i}", [128, 1], F32, ms) for i in range(3)]
        acol = ctx.sb("b0_acol", [128, 1], F32, ms)
        lsum = ctx.sb("b0_lsum", [16, 8], F32, ms)
        nsb = ctx.sb("b0_nsb", [16, TT], F32, ms)
        bcoef = ctx.sb("b0_bcoef", [16, TT], F32, ms)

        for sb, nm in [(mixsb, "b0_mixv"), (w0sb, "b0_w0v"), (a0sb, "b0_a0v"),
                       (kkv, "b0_kk"), (rkv, "b0_rk"), (lnw, "b0_lnw"),
                       (lnb, "b0_lnb"), (pmk, "pmask"), (ebcf, "ebcf"),
                       (etf, "etf")]:
            ctx.dma("sync", sb[:], iv[nm], reads=[ir[nm]], writes=[sb.r], tag=nm)
        for sb, nm in [(w1sb, "b0_w1"), (a1sb, "b0_a1"), (g1sb, "b0_g1")]:
            ctx.dma("gpsimd", sb[:], iv[nm].rearrange("(kc p) o -> p kc o", p=128),
                    reads=[ir[nm]], writes=[sb.r], tag=nm)
        for sb, nm in [(w2sb, "b0_w2"), (a2sb, "b0_a2"), (g2sb, "b0_g2")]:
            ctx.dma("gpsimd", sb[:], iv[nm], reads=[ir[nm]], writes=[sb.r], tag=nm)

        emit_ln(ctx, xin_d, xin_r, xn, TC)

        def mix_tiles(mjs, dsts, kc, tt_i, w=TT):
            a0 = tt_i * TT
            sxs = sxa
            ctx.e("gpsimd", "tensor_sub", sxs[:, :w], xn[:, kc, 1 + a0:1 + a0 + w],
                  xn[:, kc, a0:a0 + w], reads=[xn.r], writes=[sxs.r])
            for mj, dst in zip(mjs, dsts):
                ctx.e("vector", "scalar_tensor_tensor", dst[:, :w],
                      sxs[:, :w], mixsb[:, mj, kc:kc + 1],
                      xn[:, kc, 1 + a0:1 + a0 + w], op0=ALU.mult, op1=ALU.add,
                      reads=[sxs.r, mixsb.r, xn.r], writes=[dst.r])

        mz = [mxa, mxb, mxc]
        for tt_i in range(NTT):
            psw, psa, psg = ctx.psum(), ctx.psum(), ctx.psum()
            for kc in range(NC):
                mix_tiles([1, 4, 5], mz, kc, tt_i)
                for ps, w1x, mx in [(psw, w1sb, mz[0]), (psa, a1sb, mz[1]),
                                    (psg, g1sb, mz[2])]:
                    ctx.mm(ps[0:64, :], w1x[:, kc, :], mx[:, :],
                           start=(kc == 0), stop=(kc == NC - 1),
                           reads=[w1x.r, mx.r], writes=[ps.r])
            ctx.e("scalar", "activation", zsh[:, ts(tt_i, TT)], psw[0:64, :],
                  AF.Tanh, reads=[psw.r], writes=[zsh.r])
            ctx.e("scalar", "activation", zg[:, ts(tt_i, TT)], psg[0:64, :],
                  AF.Tanh, reads=[psg.r], writes=[zg.r])
            ctx.e("scalar", "activation", mz[1][0:64, 0:TT], psa[0:64, :],
                  AF.Tanh, reads=[psa.r], writes=[mz[1].r], tag="zatile")
            ps2 = ctx.psum()
            ctx.mm(ps2[0:16, :], a2sb[:], mz[1][0:64, 0:TT], start=True,
                   stop=True, reads=[a2sb.r, mz[1].r], writes=[ps2.r])
            ctx.e("scalar", "activation", st16[0:16, :], ps2[0:16, :],
                  AF.Sigmoid, bias=a0sb[:, 0:1], scale=1.0,
                  reads=[ps2.r, a0sb.r], writes=[st16.r])
            ctx.e("vector", "tensor_copy", asb[:, ts(tt_i, TT)], st16[0:16, :],
                  reads=[st16.r], writes=[asb.r])
        for tt_i in range(NTT):
            ps2 = ctx.psum()
            ctx.mm(ps2[0:16, :], w2sb[:], zsh[:, ts(tt_i, TT)], start=True,
                   stop=True, reads=[w2sb.r, zsh.r], writes=[ps2.r])
            ctx.e("scalar", "activation", st16[0:16, :], ps2[0:16, :],
                  AF.Sigmoid, bias=w0sb[:, 0:1], scale=1.0,
                  reads=[ps2.r, w0sb.r], writes=[st16.r])
            ctx.e("vector", "tensor_scalar_mul", wsb[:, ts(tt_i, TT)],
                  st16[0:16, :], math.exp(-0.5), reads=[st16.r], writes=[wsb.r])
        for tt_i in range(NTT):
            ctx.e("scalar", "activation", st16[0:16, :], wsb[:, ts(tt_i, TT)],
                  AF.Ln, reads=[wsb.r], writes=[st16.r])
            ctx.e("vector", "reduce_sum", lsum[0:16, tt_i + 1:tt_i + 2],
                  st16[0:16, :], axis=AXX, reads=[st16.r], writes=[lsum.r])
        ctx.e("vector", "tensor_add", lsum[0:16, 1:2], lsum[0:16, 1:2],
              lsum[0:16, 2:3], reads=[lsum.r], writes=[lsum.r])
        ctx.e("vector", "tensor_add", lsum[0:16, 3:4], lsum[0:16, 3:4],
              lsum[0:16, 4:5], reads=[lsum.r], writes=[lsum.r])
        ctx.e("vector", "tensor_add", lsum[0:16, 1:2], lsum[0:16, 1:2],
              lsum[0:16, 3:4], reads=[lsum.r], writes=[lsum.r])
        ctx.e("scalar", "activation", lsum[0:16, 0:1], lsum[0:16, 1:2],
              AF.Exp, reads=[lsum.r], writes=[lsum.r])

        if DUMP:
            dxn = ctx.nc.dram_tensor("d_xn", [1024, T + 1], F32, kind="ExternalOutput").ap()
            dxnr = Reg("d_xn")
            for cc in range(NC):
                ctx.dma("gpsimd", dxn.rearrange("(c p) t -> p c t", p=128)[:, cc, :],
                        xn[:, cc, :], reads=[xn.r], writes=[dxnr], tag=f"dxn{cc}")
            dw = ctx.nc.dram_tensor("d_w", [16, T], F32, kind="ExternalOutput").ap()
            dwr = Reg("d_w")
            ctx.dma("gpsimd", dw, wsb[:], reads=[wsb.r], writes=[dwr])
            da = ctx.nc.dram_tensor("d_a", [16, T], F32, kind="ExternalOutput").ap()
            dar = Reg("d_a")
            ctx.dma("gpsimd", da, asb[:], reads=[asb.r], writes=[dar])
            dzg = ctx.nc.dram_tensor("d_zg", [64, T], F32, kind="ExternalOutput").ap()
            dzgr = Reg("d_zg")
            ctx.dma("gpsimd", dzg, zg[:], reads=[zg.r], writes=[dzgr])
            dst_d = ctx.nc.dram_tensor("d_states", [1024, T], F32, kind="ExternalOutput").ap()
            dstr = Reg("d_states")
            dkv_d = ctx.nc.dram_tensor("d_kv", [1024, T], F32, kind="ExternalOutput").ap()
            dkvr = Reg("d_kv")
            dr_d = ctx.nc.dram_tensor("d_r", [1024, T], F32, kind="ExternalOutput").ap()
            drr = Reg("d_r")
            dkk_d = ctx.nc.dram_tensor("d_kkn", [1024, T], F32, kind="ExternalOutput").ap()
            dkkr = Reg("d_kkn")
            dv_d = ctx.nc.dram_tensor("d_v", [1024, T], F32, kind="ExternalOutput").ap()
            dvr = Reg("d_v")
            dsi_d = ctx.nc.dram_tensor("d_stin", [1024, NC], F32, kind="ExternalOutput").ap()
            dsir = Reg("d_stin")
            dls_d = ctx.nc.dram_tensor("d_lsum", [16, NC], F32, kind="ExternalOutput").ap()
            dlsr = Reg("d_lsum")
            ctx.dma("gpsimd", dls_d, lsum[0:16, :], reads=[lsum.r], writes=[dlsr])
        for c in range(NC):
            wk = wk3[c % 2]
            for j, nm in enumerate(["b0_Wr", "b0_Wk", "b0_Wv"]):
                ctx.dma("gpsimd", wk[:, :, ts(j, 128)], wslices_ap(iv[nm], c),
                        reads=[ir[nm]], writes=[wk.r], tag=f"{nm}{c}")
            ebc_c = ebcf[:, ts(c, 128)]
            et_c = etf[:, ts(c, 16)]
            for tt_i in range(NTT):
                ta = ts(tt_i, TT)
                psR, psK, psV = ctx.psum(), ctx.psum(), ctx.psum()
                m3 = [mz[0], mz[1], mz[2]]
                for kc in range(NC):
                    mix_tiles([0, 2, 3], m3, kc, tt_i)
                    for ps, j, mx in [(psR, 0, m3[0]), (psK, 1, m3[1]),
                                      (psV, 2, m3[2])]:
                        ctx.mm(ps[:, :], wk[:, kc, ts(j, 128)], mx[:, :],
                               start=(kc == 0), stop=(kc == NC - 1),
                               reads=[wk.r, mx.r], writes=[ps.r])
                ctx.e("scalar", "copy", rsb[:, ta], psR[:, :],
                      reads=[psR.r], writes=[rsb.r])
                ctx.e("scalar", "copy", vsb[:, ta], psV[:, :],
                      reads=[psV.r], writes=[vsb.r])
                ctx.e("vector", "tensor_scalar_mul", kkn[:, ta], psK[:, :],
                      kkv[:, c:c + 1], reads=[psK.r, kkv.r], writes=[kkn.r])
                ctx.e("scalar", "activation", e1[:, :], kkn[:, ta], AF.Square,
                      reads=[kkn.r], writes=[e1.r])
                psN = ctx.psum()
                ctx.mm(psN[0:16, :], et_c, e1[:, :], start=True, stop=True,
                       reads=[etf.r, e1.r], writes=[psN.r])
                ctx.e("scalar", "activation", nsb[0:16, :], psN[0:16, :],
                      AF.Sqrt, reads=[psN.r], writes=[nsb.r])
                ctx.e("vector", "tensor_scalar_max", nsb[0:16, :],
                      nsb[0:16, :], 1e-12, reads=[nsb.r], writes=[nsb.r])
                ctx.e("vector", "reciprocal", nsb[0:16, :], nsb[0:16, :],
                      reads=[nsb.r], writes=[nsb.r])
                psI = ctx.psum()
                ctx.mm(psI[:, :], ebc_c, nsb[0:16, :], start=True, stop=True,
                       reads=[ebcf.r, nsb.r], writes=[psI.r])
                ctx.e("vector", "tensor_mul", kkn[:, ta], kkn[:, ta], psI[:, :],
                      reads=[kkn.r, psI.r], writes=[kkn.r])
                psA = ctx.psum()
                ctx.mm(psA[:, :], ebc_c, asb[:, ta], start=True, stop=True,
                       reads=[ebcf.r, asb.r], writes=[psA.r])
                ctx.e("gpsimd", "tensor_mul", e2[:, :], kkn[:, ta], vsb[:, ta],
                      reads=[kkn.r, vsb.r], writes=[e2.r])
                ctx.e("vector", "tensor_mul", kvsb[:, ta], e2[:, :], psA[:, :],
                      reads=[e2.r, psA.r], writes=[kvsb.r])
                psW = ctx.psum()
                ctx.mm(psW[:, :], ebc_c, wsb[:, ta], start=True, stop=True,
                       reads=[ebcf.r, wsb.r], writes=[psW.r])
                ctx.e("scalar", "copy", wbc[:, ta], psW[:, :],
                      reads=[psW.r], writes=[wbc.r])
            ctx.e("vector", "tensor_tensor_scan", states[:], wbc[:], kvsb[:],
                  0.0, op0=ALU.mult, op1=ALU.add, reads=[wbc.r, kvsb.r],
                  writes=[states.r])
            psq = ctx.psum()
            ctx.mm(psq[:, 0:1], ebc_c, lsum[0:16, 0:1], start=True, stop=True,
                   reads=[ebcf.r, lsum.r], writes=[psq.r])
            ctx.e("vector", "tensor_copy", acol[:], psq[:, 0:1],
                  reads=[psq.r], writes=[acol.r])
            gi, go = gin[c], gout[c]
            gir, gor = gr[2 * c], gr[2 * c + 1]
            ctx.dma("scalar", gi[0:1, :].rearrange("a p -> p a"),
                    states[:, T - 1:T], reads=[states.r], writes=[gir],
                    tag=f"gS{c}")
            ctx.dma("scalar", gi[1:2, :].rearrange("a p -> p a"), acol[:],
                    reads=[acol.r, gir], writes=[gir], tag=f"gA{c}")
            ctx.s.rw("gpsimd", _mk_emit("collective_compute",
                     ("AllGather", ALU.bypass), dict(
                         replica_groups=ctx.groups, ins=[gi], outs=[go])),
                     reads=[gir], writes=[gor], cc=True, tag=f"ag0_{c}")
            ctx.e("vector", "memset", stin[0][:], 0.0, writes=[stin[0].r])
            for j in range(3):
                ctx.dma("scalar", jS[j][:],
                        go[2 * j:2 * j + 1, :].rearrange("a p -> p a"),
                        reads=[gor], writes=[jS[j].r], tag=f"ldS{c}_{j}")
                ctx.dma("scalar", jA[j][:],
                        go[2 * j + 1:2 * j + 2, :].rearrange("a p -> p a"),
                        reads=[gor], writes=[jA[j].r], tag=f"ldA{c}_{j}")
                ctx.e("vector", "tensor_mul", jA2[j][:], stin[j][:], jA[j][:],
                      reads=[stin[j].r, jA[j].r], writes=[jA2[j].r])
                ctx.e("vector", "tensor_add", jS2[j][:], jA2[j][:], jS[j][:],
                      reads=[jA2[j].r, jS[j].r], writes=[jS2[j].r])
                ctx.e("vector", "tensor_mul", jA2[j][:], jS2[j][:],
                      pmk[:, j:j + 1], reads=[jS2[j].r, pmk.r],
                      writes=[jA2[j].r])
                ctx.e("vector", "tensor_mul", jS2[j][:], stin[j][:],
                      pmk[:, 4 + j:5 + j], reads=[stin[j].r, pmk.r],
                      writes=[jS2[j].r])
                ctx.e("vector", "tensor_add", stin[j + 1][:], jA2[j][:],
                      jS2[j][:], reads=[jA2[j].r, jS2[j].r],
                      writes=[stin[j + 1].r])
            if DUMP:
                ctx.dma("gpsimd", dsi_d.rearrange("(cc p) o -> p cc o", p=128)[:, c, c:c + 1],
                        stin[3][:], reads=[stin[3].r], writes=[dsir], tag=f"dsi{c}",
                        allow_slow_non_contiguous=True)
            ctx.e("vector", "tensor_tensor_scan", states[:], wbc[:], kvsb[:],
                  stin[3][:, 0:1], op0=ALU.mult, op1=ALU.add,
                  reads=[wbc.r, kvsb.r, stin[3].r], writes=[states.r])
            if DUMP:
                ctx.dma("gpsimd", dst_d.rearrange("(cc p) t -> p cc t", p=128)[:, c, :],
                        states[:], reads=[states.r], writes=[dstr], tag=f"dst{c}")
                ctx.dma("gpsimd", dkv_d.rearrange("(cc p) t -> p cc t", p=128)[:, c, :],
                        kvsb[:], reads=[kvsb.r], writes=[dkvr], tag=f"dkv{c}")
                ctx.dma("gpsimd", dr_d.rearrange("(cc p) t -> p cc t", p=128)[:, c, :],
                        rsb[:], reads=[rsb.r], writes=[drr], tag=f"dr{c}")
                ctx.dma("gpsimd", dkk_d.rearrange("(cc p) t -> p cc t", p=128)[:, c, :],
                        kkn[:], reads=[kkn.r], writes=[dkkr], tag=f"dkk{c}")
                ctx.dma("gpsimd", dv_d.rearrange("(cc p) t -> p cc t", p=128)[:, c, :],
                        vsb[:], reads=[vsb.r], writes=[dvr], tag=f"dv{c}")
            for tt_i in range(NTT):
                ta = ts(tt_i, TT)
                ctx.e("gpsimd", "tensor_mul", e1[:, :], rsb[:, ta], kkn[:, ta],
                      reads=[rsb.r, kkn.r], writes=[e1.r])
                ctx.e("vector", "tensor_scalar_mul", e1[:, :], e1[:, :],
                      rkv[:, c:c + 1], reads=[rkv.r], writes=[e1.r])
                psB16 = ctx.psum()
                ctx.mm(psB16[0:16, :], et_c, e1[:, :], start=True, stop=True,
                       reads=[etf.r, e1.r], writes=[psB16.r])
                ctx.e("scalar", "copy", bcoef[0:16, :], psB16[0:16, :],
                      reads=[psB16.r], writes=[bcoef.r])
                psBC = ctx.psum()
                ctx.mm(psBC[:, :], ebc_c, bcoef[0:16, :], start=True, stop=True,
                       reads=[ebcf.r, bcoef.r], writes=[psBC.r])
                ctx.e("gpsimd", "tensor_mul", e2[:, :], states[:, ta],
                      rsb[:, ta], reads=[states.r, rsb.r], writes=[e2.r])
                ctx.e("vector", "tensor_mul", e3[:, :], vsb[:, ta], psBC[:, :],
                      reads=[vsb.r, psBC.r], writes=[e3.r])
                ctx.e("vector", "tensor_add", e2[:, :], e2[:, :], e3[:, :],
                      reads=[e3.r], writes=[e2.r])
                ctx.e("scalar", "activation", e3[:, :], e2[:, :], AF.Square,
                      reads=[e2.r], writes=[e3.r])
                psS = ctx.psum()
                ctx.mm(psS[0:16, :], et_c, e2[:, :], start=True, stop=True,
                       reads=[etf.r, e2.r], writes=[psS.r])
                psQ = ctx.psum()
                ctx.mm(psQ[0:16, :], et_c, e3[:, :], start=True, stop=True,
                       reads=[etf.r, e3.r], writes=[psQ.r])
                ctx.e("vector", "tensor_scalar_mul", nsb[0:16, :],
                      psS[0:16, :], 1.0 / 64.0, reads=[psS.r], writes=[nsb.r])
                ctx.e("vector", "tensor_scalar_mul", bcoef[0:16, :],
                      psQ[0:16, :], 1.0 / 64.0, reads=[psQ.r], writes=[bcoef.r])
                ctx.e("vector", "tensor_mul", st16[0:16, :], nsb[0:16, :],
                      nsb[0:16, :], reads=[nsb.r], writes=[st16.r])
                ctx.e("vector", "tensor_sub", bcoef[0:16, :], bcoef[0:16, :],
                      st16[0:16, :], reads=[st16.r], writes=[bcoef.r])
                ctx.e("vector", "tensor_scalar_add", bcoef[0:16, :],
                      bcoef[0:16, :], 1e-5, reads=[bcoef.r], writes=[bcoef.r])
                ctx.e("scalar", "activation", bcoef[0:16, :], bcoef[0:16, :],
                      AF.Sqrt, reads=[bcoef.r], writes=[bcoef.r])
                ctx.e("vector", "reciprocal", bcoef[0:16, :], bcoef[0:16, :],
                      reads=[bcoef.r], writes=[bcoef.r])
                ctx.e("vector", "scalar_tensor_tensor", st16[0:16, :],
                      nsb[0:16, :], -1.0, bcoef[0:16, :], op0=ALU.mult,
                      op1=ALU.mult, reads=[nsb.r, bcoef.r], writes=[st16.r])
                psAA = ctx.psum()
                ctx.mm(psAA[:, :], ebc_c, bcoef[0:16, :], start=True,
                       stop=True, reads=[ebcf.r, bcoef.r], writes=[psAA.r])
                psBB = ctx.psum()
                ctx.mm(psBB[:, :], ebc_c, st16[0:16, :], start=True, stop=True,
                       reads=[ebcf.r, st16.r], writes=[psBB.r])
                ctx.e("vector", "tensor_mul", e4[:, :], e2[:, :], psAA[:, :],
                      reads=[e2.r, psAA.r], writes=[e4.r])
                ctx.e("vector", "tensor_add", e4[:, :], e4[:, :], psBB[:, :],
                      reads=[psBB.r], writes=[e4.r])
                psG = ctx.psum()
                ctx.mm(psG[:, :], g2sb[:, ts(c, 128)], zg[:, ta], start=True,
                       stop=True, reads=[g2sb.r, zg.r], writes=[psG.r])
                ctx.e("scalar", "activation", e3[:, :], psG[:, :], AF.Sigmoid,
                      reads=[psG.r], writes=[e3.r])
                ctx.e("vector", "tensor_scalar", e4[:, :], e4[:, :],
                      lnw[:, c:c + 1], lnb[:, c:c + 1], op0=ALU.mult,
                      op1=ALU.add, reads=[lnw.r, lnb.r], writes=[e4.r])
                ctx.e("vector", "tensor_mul", att[:, c, ta], e4[:, :],
                      e3[:, :], reads=[e4.r, e3.r], writes=[att.r])
        ctx.fence("b0mix")
    build_b0_ffn(ctx, iv, ir, xn, att, xin_d, xin_r, xout_d, xout_r)
    return xn, att


def build_b0_ffn(ctx, iv, ir, xn, att, xin_d, xin_r, xout_d, xout_r):
    s, cm = ctx.s, ctx.cm
    e1 = cm["e1"]
    with contextlib.ExitStack() as fs:
        b1sb = ctx.sb("b0f_b1s", [128, 32], F32, fs)
        b2sb = ctx.sb("b0f_b2s", [128, NC], F32, fs)
        al1 = ctx.sb("b0f_al1", [128, 32], F32, fs)
        al2 = ctx.sb("b0f_al2", [128, NC], F32, fs)
        w1t = iv["b0f_w1t"].rearrange("(kc p) o -> p kc o", p=128)
        w2t = iv["b0f_w2t"].rearrange("(kc p) o -> p kc o", p=128)
        ctx.dma("sync", b1sb[:], iv["b0f_b1"], reads=[ir["b0f_b1"]], writes=[b1sb.r])
        ctx.dma("sync", b2sb[:], iv["b0f_b2"], reads=[ir["b0f_b2"]], writes=[b2sb.r])
        with contextlib.ExitStack() as als:
            aw = ctx.sb("b0f_aw", [128, 4096], BF16, als)
            awa = ctx.sb("b0f_awa", [128, 4096], BF16, als)
            psL = ctx.psum()
            for kc in range(NC):
                ctx.dma("gpsimd", aw[:], w1t[:, kc, :], reads=[ir["b0f_w1t"]],
                        writes=[aw.r], tag=f"alw1_{kc}")
                ctx.e("scalar", "activation", awa[:], aw[:], AF.Abs,
                      reads=[aw.r], writes=[awa.r])
                for hc in range(32):
                    ctx.mm(psL[:, hc:hc + 1], awa[:, ts(hc, 128)],
                           cm["ones_b"][:, 0:1],
                           start=(kc == 0 and hc == 0),
                           stop=(kc == NC - 1 and hc == 31),
                           reads=[awa.r, cm["ones_b"].r], writes=[psL.r])
            ctx.e("vector", "tensor_scalar_mul", al1[:, 0:32], psL[:, 0:32],
                  1.0 / 1024.0, reads=[psL.r], writes=[al1.r])
            psL2 = ctx.psum()
            for kc in range(32):
                ctx.dma("gpsimd", aw[:, 0:1024], w2t[:, kc, :],
                        reads=[ir["b0f_w2t"]], writes=[aw.r], tag=f"alw2_{kc}")
                ctx.e("scalar", "activation", awa[:, 0:1024], aw[:, 0:1024],
                      AF.Abs, reads=[aw.r], writes=[awa.r])
                for oc in range(NC):
                    ctx.mm(psL2[:, oc:oc + 1], awa[:, ts(oc, 128)],
                           cm["ones_b"][:, 0:1],
                           start=(kc == 0 and oc == 0),
                           stop=(kc == 31 and oc == NC - 1),
                           reads=[awa.r, cm["ones_b"].r], writes=[psL2.r])
            ctx.e("vector", "tensor_scalar_mul", al2[:, 0:NC], psL2[:, 0:NC],
                  1.0 / 4096.0, reads=[psL2.r], writes=[al2.r])
            ctx.fence("b0alpha")
        w1q = [ctx.sb(f"b0f_w1q{i}", [128, NC, 1024], BF16, fs) for i in range(2)]
        w2q = [ctx.sb(f"b0f_w2q{i}", [128, NC, 1024], BF16, fs) for i in range(2)]
        wos = [ctx.sb(f"b0_wos{i}", [128, NC, 128], BF16, fs) for i in range(2)]
        hsb = ctx.sb("b0f_h", [128, NC, TT], BF16, fs)
        oacc = ctx.sb("b0f_oacc", [128, NC, TT], F32, fs)
        xo1 = cm["e3"]

        for tt_i in range(NTT):
            ta = ts(tt_i, TT)
            xcols = slice(1 + tt_i * TT, 1 + (tt_i + 1) * TT)
            for quar in range(4):
                w1b = w1q[quar % 2]
                w2b = w2q[quar % 2]
                ctx.dma("gpsimd", w1b[:], w1t[:, :, ts(quar, 1024)],
                        reads=[ir["b0f_w1t"]], writes=[w1b.r], tag=f"w1q{tt_i}{quar}")
                ctx.e("scalar", "activation", w1b[:, :, :], w1b[:, :, :],
                      AF.Sign, reads=[w1b.r], writes=[w1b.r])
                ctx.dma("gpsimd", w2b[:], w2t[:, ts(quar, NC), :],
                        reads=[ir["b0f_w2t"]], writes=[w2b.r], tag=f"w2q{tt_i}{quar}")
                ctx.e("scalar", "activation", w2b[:, :, :], w2b[:, :, :],
                      AF.Sign, reads=[w2b.r], writes=[w2b.r])
                for hcl in range(NC):
                    hc = quar * NC + hcl
                    ps = ctx.psum()
                    for kc in range(NC):
                        ctx.mm(ps[:, :], w1b[:, kc, ts(hcl, 128)],
                               xn[:, kc, xcols], start=(kc == 0),
                               stop=(kc == NC - 1),
                               reads=[w1b.r, xn.r], writes=[ps.r])
                    ctx.e("scalar", "activation", hsb[:, hcl, :], ps[:, :],
                          AF.Gelu, bias=b1sb[:, hc:hc + 1],
                          scale=al1[:, hc:hc + 1],
                          reads=[ps.r, b1sb.r, al1.r], writes=[hsb.r])
                for oc in range(NC):
                    ps = ctx.psum()
                    for hcl in range(NC):
                        ctx.mm(ps[:, :], w2b[:, hcl, ts(oc, 128)],
                               hsb[:, hcl, :], start=(hcl == 0),
                               stop=(hcl == NC - 1),
                               reads=[w2b.r, hsb.r], writes=[ps.r])
                    if quar == 0:
                        ctx.e("vector", "tensor_scalar_mul", oacc[:, oc, :],
                              ps[:, :], al2[:, oc:oc + 1],
                              reads=[ps.r, al2.r], writes=[oacc.r])
                    else:
                        ctx.e("vector", "scalar_tensor_tensor", oacc[:, oc, :],
                              ps[:, :], al2[:, oc:oc + 1], oacc[:, oc, :],
                              op0=ALU.mult, op1=ALU.add,
                              reads=[ps.r, al2.r], writes=[oacc.r])
            for oc in range(NC):
                wob = wos[oc % 2]
                ctx.dma("gpsimd", wob[:], wslices_ap(iv["b0_Wo"], oc),
                        reads=[ir["b0_Wo"]], writes=[wob.r], tag=f"wo{tt_i}{oc}")
                ctx.dma("sync", xo1[:], xfm(xin_d)[:, oc, xcols],
                        reads=[xin_r], writes=[xo1.r], tag=f"xold{tt_i}{oc}")
                ps = ctx.psum()
                for kc in range(NC):
                    ctx.mm(ps[:, :], wob[:, kc, :], att[:, kc, ta],
                           start=(kc == 0), stop=(kc == NC - 1),
                           reads=[wob.r, att.r], writes=[ps.r])
                ctx.e("vector", "tensor_scalar", e1[:, :], ps[:, :],
                      b2sb[:, oc:oc + 1], None, op0=ALU.add,
                      reads=[ps.r, b2sb.r], writes=[e1.r])
                ctx.e("vector", "tensor_add", e1[:, :], e1[:, :],
                      oacc[:, oc, :], reads=[oacc.r], writes=[e1.r])
                ctx.e("vector", "tensor_add", xo1[:, :], e1[:, :],
                      xo1[:, :], reads=[e1.r], writes=[xo1.r])
                ctx.dma("sync", xfm(xout_d)[:, oc, ta], xo1[:],
                        reads=[xo1.r], writes=[xout_r], tag=f"xo{tt_i}{oc}")
        ctx.fence("b0ffn")


def build_block1(ctx, iv, ir, xin_d, xin_r, xout_d, xout_r, g1i, g1o, g1r):
    s, cm = ctx.s, ctx.cm
    e1, e2, e3, e4 = cm["e1"], cm["e2"], cm["e3"], cm["e4"]

    xn = ctx.sb("b1_xn", [128, NC, T], BF16)
    att = ctx.sb("b1_att", [128, NC, T], BF16)
    normnd_d = ctx.nc.dram_tensor("b1_normnd", [16, T], F32).ap()
    normnd_r = Reg("b1_normnd")

    mixstk = contextlib.ExitStack()
    qp = ctx.sb("b1_qp", [128, NC, T], BF16, mixstk)
    ebcf = ctx.sb("b1_ebcf", [16, NC * 128], F32, mixstk)
    pmk = ctx.sb("b1_pmk", [128, NC], F32, mixstk)
    with contextlib.ExitStack() as ms:
        qb = ctx.sb("b1_qb", [128, NC], F32, ms)
        kb = ctx.sb("b1_kb", [128, NC], F32, ms)
        vb = ctx.sb("b1_vb", [128, NC], F32, ms)
        mskd = ctx.sb("b1_mskd", [128, 256], F32, ms)
        kp = ctx.sb("b1_kp", [128, T], BF16, ms)
        vsb = ctx.sb("b1_v", [128, T], BF16, ms)
        ktm = ctx.sb("b1_ktm", [128, 128], BF16, ms)
        vtm = ctx.sb("b1_vtm", [128, 130], BF16, ms)
        p01 = ctx.sb("b1_p01", [128, 256], BF16, ms)
        nf = ctx.sb("b1_nf", [1, 512], F32, ms)
        Sm = [ctx.sb(f"b1_Sm_{i}", [128, 65], F32, ms) for i in range(2)]
        Smb = ctx.sb("b1_Smb", [128, 65], BF16, ms)
        nrow = ctx.sb("b1_nrow", [1, 2, T], F32, ms)
        wk3 = [ctx.sb(f"b1_wk3_{i}", [128, NC, 3 * 128], BF16, ms)
               for i in range(2)]
        idb = ctx.sb("b1_idb", [128, 128], BF16, ms)

        for sb, nm in [(ebcf, "ebcf"), (pmk, "pmask"), (qb, "b1_qb"),
                       (kb, "b1_kb"), (vb, "b1_vb"), (mskd, "l1mask")]:
            ctx.dma("sync", sb[:], iv[nm], reads=[ir[nm]], writes=[sb.r], tag=nm)
        ctx.dma("gpsimd", idb[:], iv["idf"], reads=[ir["idf"]], writes=[idb.r])
        ctx.e("vector", "memset", vtm[:, 64:65], 1.0, writes=[vtm.r])
        ctx.e("vector", "memset", vtm[:, 129:130], 1.0, writes=[vtm.r])

        emit_ln(ctx, xin_d, xin_r, xn, T)

        qkvw = iv["b1_qkvw"]
        for c in (range(NC) if B1_STAGE >= 2 else []):
            wk = wk3[c % 2]
            for j, off in enumerate([c * 128, 1024 + c * 128, 2048 + c * 128]):
                ctx.dma("gpsimd", wk[:, :, ts(j, 128)],
                        qkvw.rearrange("(kc p) o -> p kc o", p=128)[:, :, off:off + 128],
                        reads=[ir["b1_qkvw"]], writes=[wk.r], tag=f"b1w{c}{j}")
            for tt_i in range(NTT):
                ta = ts(tt_i, TT)
                psQ, psK, psV = ctx.psum(), ctx.psum(), ctx.psum()
                for kc in range(NC):
                    for ps, j in [(psQ, 0), (psK, 1), (psV, 2)]:
                        ctx.mm(ps[:, :], wk[:, kc, ts(j, 128)], xn[:, kc, ta],
                               start=(kc == 0), stop=(kc == NC - 1),
                               reads=[wk.r, xn.r], writes=[ps.r])
                for ps, bcol, dst in [(psQ, qb, None), (psK, kb, None)]:
                    ctx.e("vector", "tensor_scalar", e1[:, :], ps[:, :],
                          bcol[:, c:c + 1], None, op0=ALU.add,
                          reads=[ps.r, bcol.r], writes=[e1.r])
                    ctx.e("vector", "tensor_scalar_min", e2[:, :], e1[:, :],
                          0.0, reads=[e1.r], writes=[e2.r])
                    ctx.e("scalar", "activation", e3[:, :], e2[:, :], AF.Exp,
                          reads=[e2.r], writes=[e3.r])
                    ctx.e("vector", "tensor_scalar_max", e4[:, :], e1[:, :],
                          0.0, reads=[e1.r], writes=[e4.r])
                    out_ap = qp[:, c, ta] if ps is psQ else kp[:, ta]
                    out_rg = qp.r if ps is psQ else kp.r
                    ctx.e("vector", "tensor_add", out_ap, e3[:, :], e4[:, :],
                          reads=[e3.r, e4.r], writes=[out_rg])
                ctx.e("vector", "tensor_scalar", vsb[:, ta], psV[:, :],
                      vb[:, c:c + 1], None, op0=ALU.add,
                      reads=[psV.r, vb.r], writes=[vsb.r])
            ctx.e("vector", "memset", Sm[0][:], 0.0, writes=[Sm[0].r])
            ctx.e("vector", "memset", Smb[:], 0.0, writes=[Smb.r])
            for tc in (range(T // 128) if B1_STAGE >= 3 else []):
                tb = ts(tc, 128)
                pst = ctx.psumb()
                ctx.e("tensor", "transpose", pst[:, 0:128], kp[:, tb], idb[:],
                      reads=[kp.r, idb.r], writes=[pst.r])
                ctx.e("tensor", "transpose", pst[:, 128:256], vsb[:, tb], idb[:],
                      reads=[vsb.r, idb.r], writes=[pst.r])
                ctx.e("scalar", "copy", ktm[:, :], pst[:, 0:128],
                      reads=[pst.r], writes=[ktm.r])
                ctx.e("scalar", "copy", vtm[:, 0:64], pst[:, 128:192],
                      reads=[pst.r], writes=[vtm.r])
                ctx.e("scalar", "copy", vtm[:, 65:129], pst[:, 192:256],
                      reads=[pst.r], writes=[vtm.r])
                if B1_SUB < 2:
                    continue
                psA0 = ctx.psum()
                psA1 = ctx.psum()
                for h, psA in [(0, psA0), (1, psA1)]:
                    ctx.mm(psA[:, 0:128], kp[ts(h, 64), tb],
                           qp[ts(h, 64), c, tb], start=True, stop=True,
                           reads=[kp.r, qp.r], writes=[psA.r])
                for h, psA in [(0, psA0), (1, psA1)]:
                    ctx.e("vector", "tensor_mul", p01[:, ts(h, 128)],
                          psA[:, 0:128], mskd[:, ts(h, 128)],
                          reads=[psA.r, mskd.r], writes=[p01.r])
                if B1_SUB < 3:
                    continue
                psC = [ctx.psum(), ctx.psum()]
                for h in range(2):
                    ctx.mm(psC[h][0:65, 0:128], vtm[:, 65 * h:65 * h + 65],
                           p01[:, ts(h, 128)], start=True, stop=False,
                           reads=[vtm.r, p01.r], writes=[psC[h].r])
                    ctx.mm(psC[h][0:65, 0:128], Smb[ts(h, 64), :],
                           qp[ts(h, 64), c, tb], start=False, stop=True,
                           reads=[Smb.r, qp.r], writes=[psC[h].r])
                for h in range(2):
                    ctx.e("vector", "tensor_copy", att[ts(h, 64), c, tb],
                          psC[h][0:64, 0:128], reads=[psC[h].r], writes=[att.r])
                    ctx.e("vector", "tensor_copy", nrow[0:1, h, tb],
                          psC[h][64:65, 0:128], reads=[psC[h].r],
                          writes=[nrow.r])
                if B1_SUB < 4:
                    continue
                psSa = ctx.psum()
                psSb = ctx.psum()
                for h, psS in [(0, psSa), (1, psSb)]:
                    ctx.mm(psS[ts(h, 64), 0:65], ktm[:, ts(h, 64)],
                           vtm[:, 65 * h:65 * h + 65], start=True,
                           stop=True, reads=[ktm.r, vtm.r],
                           writes=[psS.r])
                src = Sm[tc % 2]
                dst = Sm[(tc + 1) % 2]
                for h, psS in [(0, psSa), (1, psSb)]:
                    ctx.e("vector", "tensor_add", dst[ts(h, 64), :],
                          src[ts(h, 64), :], psS[ts(h, 64), 0:65],
                          reads=[src.r, psS.r], writes=[dst.r])
                ctx.e("vector", "tensor_copy", Smb[:], dst[:],
                      reads=[dst.r], writes=[Smb.r])
            fin = T // 128 % 2
            for h in range(2):
                ctx.dma("scalar",
                        g1i[2 * c + h:2 * c + h + 1, :].rearrange(
                            "a (p q) -> (a p) q", p=64),
                        Sm[fin][ts(h, 64), :], reads=[Sm[fin].r],
                        writes=[g1r[0]], tag=f"b1gi{c}{h}")
            ctx.dma("gpsimd", normnd_d[2 * c:2 * c + 2, :], nrow[0:1, :, :],
                    reads=[nrow.r], writes=[normnd_r], tag=f"b1nd{c}")
        if DUMP:
            dqp = ctx.nc.dram_tensor("d_qp", [1024, T], F32, kind="ExternalOutput").ap()
            dkp = ctx.nc.dram_tensor("d_kp", [1024, T], F32, kind="ExternalOutput").ap()
            dvv = ctx.nc.dram_tensor("d_v", [1024, T], F32, kind="ExternalOutput").ap()
            dcx = ctx.nc.dram_tensor("d_ctxnd", [1024, T], F32, kind="ExternalOutput").ap()
            dnn = ctx.nc.dram_tensor("d_nnd", [16, T], F32, kind="ExternalOutput").ap()
            rgs = [Reg(x) for x in "abcde"]
            for cc2 in range(NC):
                ctx.dma("gpsimd", dqp.rearrange("(c p) t -> p c t", p=128)[:, cc2, :],
                        qp[:, cc2, :], reads=[qp.r], writes=[rgs[0]], tag=f"dq{cc2}")
                ctx.dma("gpsimd", dcx.rearrange("(c p) t -> p c t", p=128)[:, cc2, :],
                        att[:, cc2, :], reads=[att.r], writes=[rgs[3]], tag=f"dc{cc2}")
            ctx.dma("gpsimd", dnn, normnd_d, reads=[normnd_r], writes=[rgs[4]], tag="dn")
        if B1_STAGE < 3:
            for h2 in range(16):
                ctx.e("vector", "memset", Sm[0][:], 0.0, writes=[Sm[0].r])
        ctx.s.rw("gpsimd", _mk_emit("collective_compute",
                 ("AllGather", ALU.bypass), dict(
                     replica_groups=ctx.groups, ins=[g1i], outs=[g1o])),
                 reads=[g1r[0]], writes=[g1r[1]], cc=True, tag="ag1")
        ctx.fence("b1tc")
    with contextlib.ExitStack() as ms:
        Sacc = [ctx.sb(f"b1_Sacc{i}", [64, 16, 65], F32, ms) for i in range(2)]
        Sg = ctx.sb("b1_Sg", [64, 16, 65], F32, ms)
        ctx.e("vector", "memset", Sacc[0][:], 0.0, writes=[Sacc[0].r])
        for j in range(3):
            ctx.dma("scalar", Sg[:],
                    g1o[16 * j:16 * (j + 1), :].rearrange(
                        "h (p q) -> p h q", p=64),
                    reads=[g1r[1]], writes=[Sg.r], tag=f"b1ld{j}")
            src, dst = Sacc[j % 2], Sacc[(j + 1) % 2]
            ctx.e("vector", "scalar_tensor_tensor", dst[:], Sg[:],
                  pmk[0:64, j:j + 1], src[:], op0=ALU.mult, op1=ALU.add,
                  reads=[Sg.r, pmk.r, src.r], writes=[dst.r])
        SaccF = Sacc[3 % 2]
        mca = ctx.sb("b1_mca", [128, NC, 16], BF16, ms)
        Scm = ctx.sb("b1_Scm", [128, 65], BF16, ms)
        normsb = ctx.sb("b1_normsb", [16, T], F32, ms)
        ninv = ctx.sb("b1_ninv", [16, TT], F32, ms)
        ctx.e("vector", "memset", mca[:], 0.0, writes=[mca.r])
        for h in range(16):
            ctx.e("vector", "tensor_copy", mca[ts(h % 2, 64), h // 2, h:h + 1],
                  SaccF[:, h, 64:65], reads=[SaccF.r], writes=[mca.r])
        ctx.dma("sync", normsb[:], normnd_d, reads=[normnd_r],
                writes=[normsb.r], tag="b1nl")
        for tt_i in (range(NTT) if B1_STAGE >= 4 else []):
            ta = ts(tt_i, TT)
            psM = ctx.psum()
            for c in range(NC):
                ctx.mm(psM[0:16, :], mca[:, c, :], qp[:, c, ta],
                       start=(c == 0), stop=(c == NC - 1),
                       reads=[mca.r, qp.r], writes=[psM.r])
            ctx.e("vector", "tensor_add", ninv[0:16, :], psM[0:16, :],
                  normsb[0:16, ta], reads=[psM.r, normsb.r], writes=[ninv.r])
            ctx.e("vector", "tensor_scalar_add", ninv[0:16, :], ninv[0:16, :],
                  1e-6, reads=[], writes=[ninv.r])
            ctx.e("vector", "reciprocal", ninv[0:16, :], ninv[0:16, :],
                  reads=[], writes=[ninv.r])
            for c in range(NC):
                for h in range(2):
                    ctx.e("vector", "tensor_copy", Scm[ts(h, 64), :],
                          SaccF[:, 2 * c + h, :], reads=[SaccF.r],
                          writes=[Scm.r])
                psI = ctx.psum()
                ctx.mm(psI[:, :], ebcf[:, ts(c, 128)], ninv[0:16, :],
                       start=True, stop=True, reads=[ebcf.r, ninv.r],
                       writes=[psI.r])
                ctx.e("scalar", "copy", e4[:, :], psI[:, :],
                      reads=[psI.r], writes=[e4.r])
                psK2a = ctx.psum()
                psK2b = ctx.psum()
                for h, psK2 in [(0, psK2a), (1, psK2b)]:
                    ctx.mm(psK2[0:64, :], Scm[ts(h, 64), 0:64],
                           qp[ts(h, 64), c, ta], start=True, stop=True,
                           reads=[Scm.r, qp.r], writes=[psK2.r])
                for h, psK2 in [(0, psK2a), (1, psK2b)]:
                    ctx.e("vector", "tensor_add", e1[ts(h, 64), :],
                          att[ts(h, 64), c, ta], psK2[0:64, :],
                          reads=[att.r, psK2.r], writes=[e1.r])
                ctx.e("vector", "tensor_mul", att[:, c, ta], e1[:, :],
                      e4[:, :], reads=[e1.r, e4.r], writes=[att.r])
        ctx.fence("b1mix")
    mixstk.close()
    build_b1_ffn(ctx, iv, ir, xn, att, xin_d, xin_r, xout_d, xout_r)
    return xn, att


def build_b1_ffn(ctx, iv, ir, xn, att, xin_d, xin_r, xout_d, xout_r):
    s, cm = ctx.s, ctx.cm
    e1, e2, e3 = cm["e1"], cm["e2"], cm["e3"]
    with contextlib.ExitStack() as fs:
        b1sb = ctx.sb("b1f_b1s", [128, 32], F32, fs)
        b2sb = ctx.sb("b1f_b2s", [128, NC], F32, fs)
        obsb = ctx.sb("b1f_obs", [128, NC], F32, fs)
        thsb = ctx.sb("b1f_th", [128, 32], F32, fs)
        wrsb = ctx.sb("b1f_wr", [128, 32], F32, fs)
        wisb = ctx.sb("b1f_wi", [128, 32], F32, fs)
        coef = ctx.sb("b1f_coef", [128, 32], F32, fs)
        w1q = [ctx.sb(f"b1f_w1q{i}", [128, NC, 1024], BF16, fs) for i in range(2)]
        w2q = [ctx.sb(f"b1f_w2q{i}", [128, NC, 1024], BF16, fs) for i in range(2)]
        wos = [ctx.sb(f"b1_wos{i}", [128, NC, 128], BF16, fs) for i in range(2)]
        hsb = ctx.sb("b1f_h", [128, NC, TT], BF16, fs)
        oacc = ctx.sb("b1f_oacc", [128, NC, TT], F32, fs)
        xo1 = cm["e3"]

        for sb, nm in [(b1sb, "b1f_b1"), (b2sb, "b1f_b2"), (obsb, "b1_ob"),
                       (thsb, "b1f_th"), (wrsb, "b1f_wr"), (wisb, "b1f_wi")]:
            ctx.dma("sync", sb[:], iv[nm], reads=[ir[nm]], writes=[sb.r], tag=nm)
        ei = cm["ei"]

        def rr_sin(dst, shift):
            ctx.e("vector", "tensor_scalar", e1[:, 0:32], thsb[:], shift,
                  1.0 / (2.0 * math.pi), op0=ALU.add, op1=ALU.mult,
                  reads=[thsb.r], writes=[e1.r])
            ctx.e("vector", "tensor_copy", ei[:, 0:32], e1[:, 0:32],
                  reads=[e1.r], writes=[ei.r])
            ctx.e("vector", "tensor_copy", e4[:, 0:32], ei[:, 0:32],
                  reads=[ei.r], writes=[e4.r])
            ctx.e("vector", "tensor_sub", e4[:, 0:32], e1[:, 0:32],
                  e4[:, 0:32], reads=[e1.r], writes=[e4.r])
            ctx.e("vector", "tensor_scalar_mul", e4[:, 0:32], e4[:, 0:32],
                  2.0 * math.pi, reads=[], writes=[e4.r])
            ctx.e("scalar", "activation", dst[:, 0:32], e4[:, 0:32], AF.Sin,
                  reads=[e4.r], writes=[dst.r])
        e4 = cm["e4"]
        rr_sin(e2, 0.0)
        rr_sin(e3, math.pi / 2.0)
        ctx.e("vector", "tensor_mul", e3[:, 0:32], e3[:, 0:32], wrsb[:],
              reads=[wrsb.r], writes=[e3.r])
        ctx.e("vector", "tensor_mul", e1[:, 0:32], e2[:, 0:32], wisb[:],
              reads=[e2.r, wisb.r], writes=[e1.r])
        ctx.e("vector", "tensor_add", e3[:, 0:32], e3[:, 0:32], e1[:, 0:32],
              reads=[e1.r], writes=[e3.r])
        ctx.e("vector", "tensor_mul", e3[:, 0:32], e3[:, 0:32], wisb[:],
              reads=[wisb.r], writes=[e3.r])
        ctx.e("vector", "tensor_scalar_mul", coef[:], e3[:, 0:32], 0.1,
              reads=[e3.r], writes=[coef.r])

        w1t = iv["b1f_w1"].rearrange("(kc p) o -> p kc o", p=128)
        w2t = iv["b1f_w2"].rearrange("(kc p) o -> p kc o", p=128)
        for tt_i in range(NTT):
            ta = ts(tt_i, TT)
            for quar in (range(4) if B1_STAGE >= 5 else [0]):
                w1b = w1q[quar % 2]
                w2b = w2q[quar % 2]
                ctx.dma("gpsimd", w1b[:], w1t[:, :, ts(quar, 1024)],
                        reads=[ir["b1f_w1"]], writes=[w1b.r], tag=f"b1w1{tt_i}{quar}")
                ctx.dma("gpsimd", w2b[:], w2t[:, ts(quar, NC), :],
                        reads=[ir["b1f_w2"]], writes=[w2b.r], tag=f"b1w2{tt_i}{quar}")
                for hcl in range(NC):
                    hc = quar * NC + hcl
                    ps = ctx.psum()
                    for kc in range(NC):
                        ctx.mm(ps[:, :], w1b[:, kc, ts(hcl, 128)],
                               xn[:, kc, ta], start=(kc == 0),
                               stop=(kc == NC - 1),
                               reads=[w1b.r, xn.r], writes=[ps.r])
                    ctx.e("vector", "tensor_scalar", e1[:, :], ps[:, :],
                          b1sb[:, hc:hc + 1], None, op0=ALU.add,
                          reads=[ps.r, b1sb.r], writes=[e1.r])
                    ctx.e("vector", "tensor_scalar", e2[:, :], e1[:, :],
                          0.1, thsb[:, hc:hc + 1], op0=ALU.mult, op1=ALU.add,
                          reads=[e1.r, thsb.r], writes=[e2.r])
                    ctx.e("vector", "tensor_scalar_mul", e3[:, :], e2[:, :],
                          1.0 / (2.0 * math.pi), reads=[e2.r], writes=[e3.r])
                    ctx.e("vector", "tensor_copy", cm["ei"][:, :], e3[:, :],
                          reads=[e3.r], writes=[cm["ei"].r])
                    ctx.e("vector", "tensor_copy", e2[:, :], cm["ei"][:, :],
                          reads=[cm["ei"].r], writes=[e2.r])
                    ctx.e("vector", "tensor_sub", e2[:, :], e3[:, :],
                          e2[:, :], reads=[e3.r], writes=[e2.r])
                    ctx.e("vector", "tensor_scalar_mul", e2[:, :], e2[:, :],
                          2.0 * math.pi, reads=[], writes=[e2.r])
                    ctx.e("scalar", "activation", e2[:, :], e2[:, :], AF.Sin,
                          reads=[], writes=[e2.r])
                    ctx.e("vector", "scalar_tensor_tensor", e3[:, :],
                          e2[:, :], coef[:, hc:hc + 1], e1[:, :],
                          op0=ALU.mult, op1=ALU.add,
                          reads=[e2.r, coef.r, e1.r], writes=[e3.r])
                    ctx.e("vector", "tensor_scalar_max", hsb[:, hcl, :],
                          e3[:, :], 0.0, reads=[e3.r], writes=[hsb.r])
                for oc in range(NC):
                    ps = ctx.psum()
                    for hcl in range(NC):
                        ctx.mm(ps[:, :], w2b[:, hcl, ts(oc, 128)],
                               hsb[:, hcl, :], start=(hcl == 0),
                               stop=(hcl == NC - 1),
                               reads=[w2b.r, hsb.r], writes=[ps.r])
                    if quar == 0:
                        ctx.e("vector", "tensor_scalar", oacc[:, oc, :],
                              ps[:, :], b2sb[:, oc:oc + 1], None, op0=ALU.add,
                              reads=[ps.r, b2sb.r], writes=[oacc.r])
                    else:
                        ctx.e("vector", "tensor_add", oacc[:, oc, :],
                              ps[:, :], oacc[:, oc, :],
                              reads=[ps.r], writes=[oacc.r])
            for oc in range(NC):
                wob = wos[oc % 2]
                ctx.dma("gpsimd", wob[:], wslices_ap(iv["b1_outw"], oc),
                        reads=[ir["b1_outw"]], writes=[wob.r], tag=f"b1wo{tt_i}{oc}")
                ctx.dma("sync", xo1[:], xfm(xin_d)[:, oc, ta],
                        reads=[xin_r], writes=[xo1.r], tag=f"b1xo{tt_i}{oc}")
                ps = ctx.psum()
                for kc in range(NC):
                    ctx.mm(ps[:, :], wob[:, kc, :], att[:, kc, ta],
                           start=(kc == 0), stop=(kc == NC - 1),
                           reads=[wob.r, att.r], writes=[ps.r])
                ctx.e("vector", "tensor_scalar", e1[:, :], ps[:, :],
                      obsb[:, oc:oc + 1], None, op0=ALU.add,
                      reads=[ps.r, obsb.r], writes=[e1.r])
                ctx.e("vector", "tensor_add", e1[:, :], e1[:, :],
                      oacc[:, oc, :], reads=[oacc.r], writes=[e1.r])
                ctx.e("vector", "tensor_add", xo1[:, :], e1[:, :],
                      xo1[:, :], reads=[e1.r], writes=[xo1.r])
                ctx.dma("sync", xfm(xout_d)[:, oc, ta], xo1[:],
                        reads=[xo1.r], writes=[xout_r], tag=f"b1xw{tt_i}{oc}")
        ctx.fence("b1ffn")


def build_block2(ctx, iv, ir, xin_d, xin_r, xout_d, xout_r,
                 gki, gko, gvi, gvo, gsi, gso, g2r):
    s, cm = ctx.s, ctx.cm
    e1, e2 = cm["e1"], cm["e2"]
    eb1 = cm["eb1"]

    xn = ctx.sb("b2_xn", [128, NC, T], BF16)
    att = ctx.sb("b2_att", [128, NC, T], BF16)
    bmax = ctx.sb("b2_bmax", [16, 4], F32)
    q_d = ctx.nc.dram_tensor("b2_qd", [1024, T], BF16).ap()
    q_dr = Reg("b2_qd")

    with contextlib.ExitStack() as ms:
        aqT = ctx.sb("b2_aqT", [128, NC, 48], BF16, ms)
        bqT = ctx.sb("b2_bqT", [16, 3 * 1024], BF16, ms)
        lra = ctx.sb("b2_lra", [16, 3 * T], BF16, ms)
        wk3 = [ctx.sb(f"b2_wk3_{i}", [128, NC, 3 * 128], BF16, ms)
               for i in range(2)]
        vsb = ctx.sb("b2_vs", [128, T], BF16, ms)
        idb = ctx.sb("b2_idb", [128, 128], BF16, ms)
        qst = ctx.sb("b2_qst", [16, 2], F32, ms)
        qmx = ctx.sb("b2_qmx", [16, 32], F32, ms)
        kmx = ctx.sb("b2_kmx", [16, 32], F32, ms)
        gmx = ctx.sb("b2_gmx", [16, 4], F32, ms)
        etf = ctx.sb("b2_etf", [128, NC * 16], F32, ms)

        for dst, nm, sl in [(aqT, "b2_AqT", ts(0, 16)), (aqT, "b2_AkT", ts(1, 16)),
                            (aqT, "b2_AvT", ts(2, 16))]:
            ctx.dma("gpsimd", dst[:, :, sl],
                    iv[nm].rearrange("(kc p) o -> p kc o", p=128),
                    reads=[ir[nm]], writes=[dst.r], tag=nm)
        for nm, sl in [("b2_BqT", ts(0, 1024)), ("b2_BkT", ts(1, 1024)),
                       ("b2_BvT", ts(2, 1024))]:
            ctx.dma("gpsimd", bqT[:, sl], iv[nm], reads=[ir[nm]],
                    writes=[bqT.r], tag=nm)
        ctx.dma("gpsimd", idb[:], iv["idf"], reads=[ir["idf"]], writes=[idb.r])
        ctx.dma("sync", etf[:], iv["etf"], reads=[ir["etf"]], writes=[etf.r])
        ctx.e("vector", "memset", bmax[:], 0.0, writes=[bmax.r])

        emit_ln(ctx, xin_d, xin_r, xn, T)

        for j in range(3):
            for tt_i in range(NTT):
                ps = ctx.psum()
                for kc in range(NC):
                    ctx.mm(ps[0:16, :], aqT[:, kc, ts(j, 16)],
                           xn[:, kc, ts(tt_i, TT)],
                           start=(kc == 0), stop=(kc == NC - 1),
                           reads=[aqT.r, xn.r], writes=[ps.r])
                ctx.e("vector", "tensor_scalar_mul",
                      lra[0:16, j * T + tt_i * TT: j * T + (tt_i + 1) * TT],
                      ps[0:16, :], 1.0 / 16.0, reads=[ps.r], writes=[lra.r])

        for c in (range(NC) if B2_SUB >= 2 else []):
            wk = wk3[c % 2]
            for j, nm in enumerate(["b2_WqT", "b2_WkT", "b2_WvT"]):
                ctx.dma("gpsimd", wk[:, :, ts(j, 128)], wslices_ap(iv[nm], c),
                        reads=[ir[nm]], writes=[wk.r], tag=f"{nm}{c}")
            for tt_i in range(NTT):
                ta = ts(tt_i, TT)
                pj = [ctx.psum(), ctx.psum(), ctx.psum()]
                for kc in range(NC):
                    for j in range(3):
                        ctx.mm(pj[j][:, :], wk[:, kc, ts(j, 128)], xn[:, kc, ta],
                               start=(kc == 0), stop=False,
                               reads=[wk.r, xn.r], writes=[pj[j].r])
                for j in range(3):
                    ctx.mm(pj[j][:, :],
                           bqT[:, j * 1024 + c * 128: j * 1024 + (c + 1) * 128],
                           lra[0:16, j * T + tt_i * TT: j * T + (tt_i + 1) * TT],
                           start=False, stop=True, reads=[bqT.r, lra.r],
                           writes=[pj[j].r])
                def stats(j, src, src_r):
                    ctx.e("scalar", "activation", e2[:, :], src, AF.Square,
                          reads=[src_r], writes=[e2.r])
                    psq = ctx.psum()
                    ctx.mm(psq[0:16, :], etf[:, ts(c, 16)], e2[:, :],
                           start=True, stop=True, reads=[etf.r, e2.r],
                           writes=[psq.r])
                    dst_m = qmx if j == 0 else kmx
                    ctx.e("vector", "reduce_max",
                          dst_m[0:16, c * 4 + tt_i:c * 4 + tt_i + 1],
                          psq[0:16, :], axis=AXX, reads=[psq.r],
                          writes=[dst_m.r])

                ctx.e("vector", "tensor_copy", eb1[:, :], pj[0][:, :],
                      reads=[pj[0].r], writes=[eb1.r])
                ctx.dma("sync", xfm(q_d)[:, c, ta], eb1[:, :],
                        reads=[eb1.r], writes=[q_dr], tag=f"q{c}{tt_i}")
                if B2_SUB >= 3:
                    stats(0, eb1[:, :], eb1.r)
                ctx.e("vector", "tensor_copy", eb1[:, :], pj[1][:, :],
                      reads=[pj[1].r], writes=[eb1.r])
                ctx.dma("sync", gki.rearrange("(cc p) t -> p cc t", p=128)[:, c, ta],
                        eb1[:, :], reads=[eb1.r], writes=[g2r[0]],
                        tag=f"k{c}{tt_i}")
                if B2_SUB >= 3:
                    stats(1, eb1[:, :], eb1.r)
                ctx.e("vector", "tensor_copy", vsb[:, ta], pj[2][:, :],
                      reads=[pj[2].r], writes=[vsb.r])
            for tc in (range(T // 128) if B2_SUB >= 4 else []):
                pst = ctx.psumb()
                ctx.e("tensor", "transpose", pst[:, 0:128], vsb[:, ts(tc, 128)],
                      idb[:], reads=[vsb.r, idb.r], writes=[pst.r])
                ctx.e("vector", "tensor_copy", eb1[:, 0:128], pst[:, 0:128],
                      reads=[pst.r], writes=[eb1.r])
                ctx.dma("sync", gvi[ts(tc, 128), ts(c, 128)], eb1[:, 0:128],
                        reads=[eb1.r], writes=[g2r[2]], tag=f"v{c}{tc}")
        ctx.e("vector", "reduce_max", bmax[0:16, 0:1], qmx[0:16, :],
              axis=AXX, reads=[qmx.r], writes=[bmax.r])
        ctx.e("vector", "reduce_max", bmax[0:16, 1:2], kmx[0:16, :],
              axis=AXX, reads=[kmx.r], writes=[bmax.r])
        ctx.e("scalar", "activation", bmax[0:16, 0:2], bmax[0:16, 0:2],
              AF.Sqrt, reads=[], writes=[bmax.r])
        ctx.dma("scalar", gsi[0:1, :].rearrange("a p -> p a"), bmax[0:16, 1:2],
                reads=[bmax.r], writes=[g2r[4]], tag="kstat")
        for gi_, go_, ra_, rb_, tg in ([(gki, gko, g2r[0], g2r[1], "agK"),
                                       (gvi, gvo, g2r[2], g2r[3], "agV"),
                                       (gsi, gso, g2r[4], g2r[5], "agS")]
                                       if B2_SUB >= 5 else []):
            ctx.s.rw("gpsimd", _mk_emit("collective_compute",
                     ("AllGather", ALU.bypass), dict(
                         replica_groups=ctx.groups, ins=[gi_], outs=[go_])),
                     reads=[ra_], writes=[rb_], cc=True, tag=tg)
        for j2 in (range(4) if B2_SUB >= 5 else []):
            ctx.dma("scalar", gmx[0:16, j2:j2 + 1],
                    gso[j2:j2 + 1, :].rearrange("a p -> p a"),
                    reads=[g2r[5]], writes=[gmx.r], tag=f"ks{j2}")
        ctx.e("vector", "reduce_max", bmax[0:16, 2:3], gmx[0:16, :],
              axis=AXX, reads=[gmx.r], writes=[bmax.r])
        ctx.e("vector", "tensor_mul", bmax[0:16, 3:4], bmax[0:16, 0:1],
              bmax[0:16, 2:3], reads=[], writes=[bmax.r])
        ctx.e("vector", "tensor_scalar_mul", bmax[0:16, 3:4], bmax[0:16, 3:4],
              -1.0 / 8.0, reads=[], writes=[bmax.r])
        ctx.fence("b2proj")
    if B2_STAGE >= 2:
        build_b2_flash(ctx, iv, ir, xn, att, bmax, q_d, q_dr, gko, gvo, g2r)
    if B2_STAGE >= 3:
        build_b23_ffn(ctx, iv, ir, xn, att, xin_d, xin_r, xout_d, xout_r, "b2",
                      attn_proj=True)
    else:
        ctx.dma("sync", xout_d, xin_d, reads=[xin_r], writes=[xout_r],
                tag="copyout")
    return xn, att


def build_b2_flash(ctx, iv, ir, xn, att, bmax, q_d, q_dr, gko, gvo, g2r):
    s, cm = ctx.s, ctx.cm
    SKG = 8
    NG = (4 * T) // (128 * SKG)
    with contextlib.ExitStack() as fs:
        i16f = ctx.sb("b2_i16f", [16, 16], F32, fs)
        onesf = ctx.sb("b2_onesf", [1, 64], F32, fs)
        brow = ctx.sb("b2_brow", [1, 16], F32, fs)
        negB = ctx.sb("b2_negB", [128, 16], F32, fs)
        qbuf = [ctx.sb("b2_qb0", [128, NC, TT], BF16, fs)] * 2
        kslab = [ctx.sb("b2_ks0", [128, NC, SKG * 128], BF16, fs)] * 2
        vslab = [ctx.sb("b2_vsl0", [128, SKG, 16, 65], BF16, fs)] * 2
        pbuf = [ctx.sb(f"b2_pb{i}", [128, TT], BF16, fs) for i in range(2)]
        oacc = ctx.sb("b2_oacc", [65, 16, TT], F32, fs)
        nv = ctx.sb("b2_nv", [1, TT], F32, fs)

        ctx.dma("sync", i16f[:], iv["i16f"], reads=[ir["i16f"]], writes=[i16f.r])
        ctx.e("vector", "memset", onesf[:], 1.0, writes=[onesf.r])
        for vb in vslab:
            ctx.e("vector", "memset", vb[:, :, :, 64:65], 1.0, writes=[vb.r])
        psb = ctx.psum()
        ctx.mm(psb[0:1, 0:16], bmax[0:16, 3:4], i16f[:], start=True, stop=True,
               reads=[bmax.r, i16f.r], writes=[psb.r])
        ctx.e("vector", "tensor_copy", brow[:], psb[0:1, 0:16],
              reads=[psb.r], writes=[brow.r])
        psb2 = ctx.psum()
        ctx.mm(psb2[:, 0:16], cm["ones1f"][:], brow[:], start=True, stop=True,
               reads=[cm["ones1f"].r, brow.r], writes=[psb2.r])
        ctx.e("vector", "tensor_copy", negB[:], psb2[:, 0:16],
              reads=[psb2.r], writes=[negB.r])

        for tt_i in range(NTT):
            ta = ts(tt_i, TT)
            qb = qbuf[tt_i % 2]
            ctx.dma("sync", qb[:], xfm(q_d)[:, :, ta], reads=[q_dr],
                    writes=[qb.r], tag=f"fq{tt_i}")
            ctx.e("vector", "memset", oacc[:], 0.0, writes=[oacc.r])
            for g in range(NG):
                ksl = kslab[g % 2]
                vsl = vslab[g % 2]
                ctx.dma("sync", ksl[:],
                        gko.rearrange("(r cc p) t -> p r cc t", p=128, cc=NC)
                        .rearrange("p r cc t -> p cc (r t)")[:, :, g * SKG * 128:(g + 1) * SKG * 128]
                        if False else
                        gko.rearrange("(r cc p) t -> p (r cc t)", p=128)[:, 0:1],
                        reads=[g2r[1]], writes=[ksl.r], tag="BAD") if False else None
                r0 = (g * SKG * 128) // T
                t0 = (g * SKG * 128) % T
                ctx.dma("sync", ksl[:],
                        gko[r0 * 1024:(r0 + 1) * 1024, t0:t0 + SKG * 128]
                        .rearrange("(cc p) t -> p cc t", p=128),
                        reads=[g2r[1]], writes=[ksl.r], tag=f"fk{tt_i}_{g}")
                for sg in range(SKG):
                    r0v = (g * SKG + sg) * 128
                    ctx.dma("sync", vsl[:, sg, :, 0:64],
                            gvo[r0v:r0v + 128, :]
                            .rearrange("p (hh d) -> p hh d", d=64),
                            reads=[g2r[3]], writes=[vsl.r],
                            tag=f"fv{tt_i}_{g}_{sg}")
                for h in range(16):
                    psO = ctx.psum_at(5)
                    for sk in range(SKG):
                        psA = ctx.psum_at(3 + sk % 2)
                        ctx.mm(psA[:, :], ksl[ts(h % 2, 64), h // 2, ts(sk, 128)],
                               qb[ts(h % 2, 64), h // 2, :], start=True,
                               stop=True, reads=[ksl.r, qb.r], writes=[psA.r])
                        pb = pbuf[sk % 2]
                        ctx.e("scalar", "activation", pb[:, :], psA[:, :],
                              AF.Exp, bias=negB[:, h:h + 1], scale=1.0 / 8.0,
                              reads=[psA.r, negB.r], writes=[pb.r])
                        ctx.mm(psO[0:65, :], vsl[:, sk, h, :], pb[:, :],
                               start=(sk == 0), stop=(sk == SKG - 1),
                               reads=[vsl.r, pb.r], writes=[psO.r])
                    ctx.e("vector", "tensor_add",
                          oacc[0:65, h, :], oacc[0:65, h, :], psO[0:65, :],
                          reads=[psO.r], writes=[oacc.r])
            for h in range(16):
                ctx.e("vector", "reciprocal", nv[:, :], oacc[64:65, h, :],
                      reads=[oacc.r], writes=[nv.r])
                psn = ctx.psum()
                ctx.mm(psn[0:64, :], onesf[:], nv[:, :], start=True, stop=True,
                       reads=[onesf.r, nv.r], writes=[psn.r])
                ctx.e("vector", "tensor_mul", att[ts(h % 2, 64), h // 2, ta],
                      oacc[0:64, h, :], psn[0:64, :],
                      reads=[oacc.r, psn.r], writes=[att.r])
        ctx.fence("b2flash")


def build_b23_ffn(ctx, iv, ir, xn, att, xin_d, xin_r, xout_d, xout_r, pfx,
                  attn_proj=True):
    s, cm = ctx.s, ctx.cm
    e1, e2 = cm["e1"], cm["e2"]
    with contextlib.ExitStack() as fs:
        s1 = ctx.sb(pfx + "_s1s", [128, 32], F32, fs)
        s2 = ctx.sb(pfx + "_s2s", [128, 32], F32, fs)
        sh1 = ctx.sb(pfx + "_sh1s", [128, 32], F32, fs)
        sh2 = ctx.sb(pfx + "_sh2s", [128, 32], F32, fs)
        s3 = ctx.sb(pfx + "_s3s", [128, NC], F32, fs)
        for sb, nm in [(s1, pfx + "_s1"), (s2, pfx + "_s2"), (sh1, pfx + "_sh1"),
                       (sh2, pfx + "_sh2"), (s3, pfx + "_s3")]:
            ctx.dma("sync", sb[:], iv[nm], reads=[ir[nm]], writes=[sb.r], tag=nm)
        if attn_proj:
            bo = ctx.sb(pfx + "_bos", [128, NC], F32, fs)
            aoT = ctx.sb(pfx + "_aoTs", [128, NC, 16], BF16, fs)
            boT = ctx.sb(pfx + "_boTs", [16, 1024], BF16, fs)
            lro = ctx.sb(pfx + "_lro", [16, T], BF16, fs)
            ctx.dma("sync", bo[:], iv[pfx + "_bo"], reads=[ir[pfx + "_bo"]],
                    writes=[bo.r])
            ctx.dma("gpsimd", aoT[:],
                    iv[pfx + "_AoT"].rearrange("(kc p) o -> p kc o", p=128),
                    reads=[ir[pfx + "_AoT"]], writes=[aoT.r])
            ctx.dma("gpsimd", boT[:], iv[pfx + "_BoT"],
                    reads=[ir[pfx + "_BoT"]], writes=[boT.r])
            for tt_i in range(NTT):
                ps = ctx.psum()
                for kc in range(NC):
                    ctx.mm(ps[0:16, :], aoT[:, kc, :], att[:, kc, ts(tt_i, TT)],
                           start=(kc == 0), stop=(kc == NC - 1),
                           reads=[aoT.r, att.r], writes=[ps.r])
                ctx.e("vector", "tensor_scalar_mul",
                      lro[0:16, ts(tt_i, TT)], ps[0:16, :], 1.0 / 16.0,
                      reads=[ps.r], writes=[lro.r])
        h_sb = ctx.sb(pfx + "_h", [128, 32, TT], BF16, fs)
        g_sb = ctx.sb(pfx + "_g", [128, 32, TT], BF16, fs)
        wsl = [ctx.sb(f"{pfx}_wsl{i}", [128, 32, 128], BF16, fs)
               for i in range(2)]
        w14 = [ctx.sb(f"{pfx}_w14{i}", [128, NC, 128], BF16, fs)
               for i in range(2)]
        wos = [ctx.sb(f"{pfx}_wos0", [128, NC, 128], BF16, fs)] * 2

        w1t = iv[pfx + "_w1t"].rearrange("(kc p) o -> p kc o", p=128)
        w2t = iv[pfx + "_w2t"].rearrange("(kc p) o -> p kc o", p=128)
        h1t = iv[pfx + "_h1t"].rearrange("(kc p) o -> p kc o", p=128)
        h2t = iv[pfx + "_h2t"].rearrange("(kc p) o -> p kc o", p=128)
        w3t = iv[pfx + "_w3t"].rearrange("(kc p) o -> p kc o", p=128)

        for tt_i in range(NTT):
            ta = ts(tt_i, TT)
            for hc in range(32):
                wa = w14[hc % 2]
                ctx.dma("gpsimd", wa[:], w1t[:, :, ts(hc, 128)],
                        reads=[ir[pfx + "_w1t"]], writes=[wa.r],
                        tag=f"{pfx}w1{tt_i}{hc}")
                ps = ctx.psum()
                for kc in range(NC):
                    ctx.mm(ps[:, :], wa[:, kc, :], xn[:, kc, ta],
                           start=(kc == 0), stop=(kc == NC - 1),
                           reads=[wa.r, xn.r], writes=[ps.r])
                ctx.e("scalar", "activation", h_sb[:, hc, :], ps[:, :],
                      AF.Silu, scale=s1[:, hc:hc + 1],
                      reads=[ps.r, s1.r], writes=[h_sb.r])
            for hc in range(32):
                wa = w14[hc % 2]
                ctx.dma("gpsimd", wa[:], w2t[:, :, ts(hc, 128)],
                        reads=[ir[pfx + "_w2t"]], writes=[wa.r],
                        tag=f"{pfx}w2{tt_i}{hc}")
                ps = ctx.psum()
                for kc in range(NC):
                    ctx.mm(ps[:, :], wa[:, kc, :], xn[:, kc, ta],
                           start=(kc == 0), stop=(kc == NC - 1),
                           reads=[wa.r, xn.r], writes=[ps.r])
                ctx.e("vector", "tensor_scalar_mul", e1[:, :], ps[:, :],
                      s2[:, hc:hc + 1], reads=[ps.r, s2.r], writes=[e1.r])
                ctx.e("vector", "tensor_mul", h_sb[:, hc, :],
                      h_sb[:, hc, :], e1[:, :], reads=[e1.r],
                      writes=[h_sb.r])
            for hmat, hsrc, hdst, scl in [(h1t, h_sb, g_sb, sh1),
                                          (h2t, g_sb, h_sb, sh2)]:
                nm = pfx + ("_h1t" if hmat is h1t else "_h2t")
                for oc2 in range(32):
                    wsb_ = wsl[oc2 % 2]
                    ctx.dma("gpsimd", wsb_[:],
                            hmat[:, :, ts(oc2, 128)], reads=[ir[nm]],
                            writes=[wsb_.r], tag=f"{nm}{tt_i}{oc2}")
                    ps = ctx.psum()
                    for kc in range(32):
                        ctx.mm(ps[:, :], wsb_[:, kc, :], hsrc[:, kc, :],
                               start=(kc == 0), stop=(kc == 31),
                               reads=[wsb_.r, hsrc.r], writes=[ps.r])
                    ctx.e("scalar", "activation", hdst[:, oc2, :], ps[:, :],
                          AF.Gelu, scale=scl[:, oc2:oc2 + 1],
                          reads=[ps.r, scl.r], writes=[hdst.r])
            for oc in range(NC):
                psF = ctx.psum()
                wsb_ = wsl[oc % 2]
                ctx.dma("gpsimd", wsb_[:], w3t[:, :, ts(oc, 128)],
                        reads=[ir[pfx + "_w3t"]], writes=[wsb_.r],
                        tag=f"{pfx}w3{tt_i}{oc}")
                for kc in range(32):
                    ctx.mm(psF[:, :], wsb_[:, kc, :], h_sb[:, kc, :],
                           start=(kc == 0), stop=(kc == 31),
                           reads=[wsb_.r, h_sb.r], writes=[psF.r])
                xo1 = cm["e3"]
                ctx.dma("sync", xo1[:, :], xfm(xin_d)[:, oc, ta]
                        if xin_d.shape[1] == T else
                        xfm(xin_d)[:, oc, 1 + tt_i * TT:1 + (tt_i + 1) * TT],
                        reads=[xin_r], writes=[xo1.r], tag=f"{pfx}xo{tt_i}{oc}")
                if attn_proj:
                    psA = ctx.psum()
                    wob = wos[oc % 2]
                    ctx.dma("gpsimd", wob[:], wslices_ap(iv[pfx + "_WoT"], oc),
                            reads=[ir[pfx + "_WoT"]], writes=[wob.r],
                            tag=f"{pfx}wo{tt_i}{oc}")
                    for kc in range(NC):
                        ctx.mm(psA[:, :], wob[:, kc, :], att[:, kc, ta],
                               start=(kc == 0), stop=False,
                               reads=[wob.r, att.r], writes=[psA.r])
                    ctx.mm(psA[:, :], boT[:, ts(oc, 128)], lro[0:16, ta],
                           start=False, stop=True, reads=[boT.r, lro.r],
                           writes=[psA.r])
                    ctx.e("vector", "tensor_scalar", e1[:, :], psA[:, :],
                          bo[:, oc:oc + 1], None, op0=ALU.add,
                          reads=[psA.r, bo.r], writes=[e1.r])
                else:
                    ctx.e("vector", "tensor_copy", e1[:, :], att[:, oc, ta],
                          reads=[att.r], writes=[e1.r])
                ctx.e("vector", "scalar_tensor_tensor", e2[:, :], psF[:, :],
                      s3[:, oc:oc + 1], e1[:, :], op0=ALU.mult, op1=ALU.add,
                      reads=[psF.r, s3.r, e1.r], writes=[e2.r])
                ctx.e("vector", "tensor_add", xo1[:, :], e2[:, :], xo1[:, :],
                      reads=[e2.r], writes=[xo1.r])
                ctx.dma("sync", xfm(xout_d)[:, oc, ta], xo1[:, :],
                        reads=[xo1.r], writes=[xout_r], tag=f"{pfx}xw{tt_i}{oc}")
        ctx.fence(pfx + "ffn")


def build_block3(ctx, iv, ir, xin_d, xin_r, xout_d, xout_r,
                 a1i, a1o, a2i, a2o, g3r):
    s, cm = ctx.s, ctx.cm
    e1, e2, e4 = cm["e1"], cm["e2"], cm["e4"]
    eb1 = cm["eb1"]

    xn = ctx.sb("b3_xn", [128, NC, T], BF16)
    att = ctx.sb("b3_att", [128, NC, T], BF16)
    static_d = ctx.nc.dram_tensor("b3_static", [T, 1024], F32).ap()
    static_r = Reg("b3_static")

    with contextlib.ExitStack() as ms:
        thx = ctx.sb("b3_thx", [128, NC, T], BF16, ms)
        idb = ctx.sb("b3_idb", [128, 128], BF16, ms)
        idf32 = ctx.sb("b3_idf32", [128, 128], F32, ms)
        wsl2 = [ctx.sb(f"b3_wsl2_{i}", [128, NC, 128], BF16, ms) for i in range(2)]
        stf = ctx.sb("b3_stf", [128, TT], F32, ms)

        ctx.dma("gpsimd", idb[:], iv["idf"], reads=[ir["idf"]], writes=[idb.r])
        ctx.dma("sync", idf32[:], iv["idf32"], reads=[ir["idf32"]],
                writes=[idf32.r])
        emit_ln(ctx, xin_d, xin_r, xn, T)
        for c in range(NC):
            for tt_i in range(NTT):
                ctx.e("scalar", "activation", thx[:, c, ts(tt_i, TT)],
                      xn[:, c, ts(tt_i, TT)], AF.Tanh, reads=[xn.r],
                      writes=[thx.r])
        for oc in range(NC):
            wsb_ = wsl2[oc % 2]
            ctx.dma("gpsimd", wsb_[:], wslices_ap(iv["b3_WT"], oc),
                    reads=[ir["b3_WT"]], writes=[wsb_.r], tag=f"b3W{oc}")
            for tt_i in range(NTT):
                ps = ctx.psum()
                for kc in range(NC):
                    ctx.mm(ps[:, :], wsb_[:, kc, :], thx[:, kc, ts(tt_i, TT)],
                           start=(kc == 0), stop=(kc == NC - 1),
                           reads=[wsb_.r, thx.r], writes=[ps.r])
                ctx.e("vector", "tensor_copy", stf[:, :], ps[:, :],
                      reads=[ps.r], writes=[stf.r])
                for i4 in range(4):
                    tc = tt_i * 4 + i4
                    pst = ctx.psum()
                    ctx.e("tensor", "transpose", pst[:, 0:128],
                          stf[:, ts(i4, 128)], idf32[:],
                          reads=[stf.r, idf32.r], writes=[pst.r])
                    ctx.e("vector", "tensor_copy", e1[:, 0:128], pst[:, 0:128],
                          reads=[pst.r], writes=[e1.r])
                    ctx.dma("sync", static_d[ts(tc, 128), ts(oc, 128)],
                            e1[:, 0:128], reads=[e1.r], writes=[static_r],
                            tag=f"b3st{oc}{tc}")
        for c in range(NC):
            for tc in range(T // 128):
                pst = ctx.psumb()
                ctx.e("tensor", "transpose", pst[:, 0:128],
                      thx[:, c, ts(tc, 128)], idb[:],
                      reads=[thx.r, idb.r], writes=[pst.r])
                ctx.e("vector", "tensor_copy", eb1[:, 0:128], pst[:, 0:128],
                      reads=[pst.r], writes=[eb1.r])
                ctx.dma("sync", a1i[ts(tc, 128), ts(c, 128)],
                        eb1[:, 0:128], reads=[eb1.r], writes=[g3r[0]],
                        tag=f"b3a{c}{tc}")
        ctx.dma("gpsimd", a1i[:, 1024:2048], static_d[:, :],
                reads=[static_r, g3r[0]], writes=[g3r[0]], tag="b3s")
        ctx.s.rw("gpsimd", _mk_emit("collective_compute",
                 ("AllGather", ALU.bypass), dict(
                     replica_groups=ctx.groups, ins=[a1i], outs=[a1o])),
                 reads=[g3r[0]], writes=[g3r[1]], cc=True, tag="ag3a")
        ctx.fence("b3proj")

    with contextlib.ExitStack() as ms:
        lamt = ctx.sb("b3_lamt", [1, 8], F32, ms)
        lnl = ctx.sb("b3_lnl", [128, 4], F32, ms)
        lamC = ctx.sb("b3_lamC", [64, 1], F32, ms)
        wvec = ctx.sb("b3_wvec", [128, 2], F32, ms)
        gpm = ctx.sb("b3_gpm", [128, 128], F32, ms)
        musU = ctx.sb("b3_musU", [128, 128], F32, ms)
        muI = ctx.sb("b3_muI", [128, 128], F32, ms)
        tv = ctx.sb("b3_tv", [128, 2], F32, ms)
        adI = ctx.sb("b3_adI", [128, 128], F32, ms)
        bstr = ctx.sb("b3_bstr", [128, 128], F32, ms)
        ktv = ctx.sb("b3_ktv", [128, 512], BF16, ms)
        kraw = ctx.sb("b3_kraw", [128, 2048], BF16, ms)
        qsel = ctx.sb("b3_qsel", [128, 4], F32, ms)
        kfm = ctx.sb("b3_kfm", [64, 128], BF16, ms)
        adS = ctx.sb("b3_adS", [128, 128], BF16, ms)
        ltS = ctx.sb("b3_ltS", [128, 128], BF16, ms)
        bS = ctx.sb("b3_bS", [128, 64], BF16, ms)
        x2S = ctx.sb("b3_x2S", [128, 64], BF16, ms)
        uS = ctx.sb("b3_uS", [128, 64], BF16, ms)
        duS = ctx.sb("b3_duS", [128, 64], BF16, ms)
        ksw = ctx.sb("b3_ksw", [128, 64], F32, ms)
        Sst = [ctx.sb(f"b3_S{i}", [64, 4, 64], F32, ms) for i in range(2)]
        Sb = ctx.sb("b3_Sb", [64, 4, 64], BF16, ms)
        ostg = ctx.sb("b3_ostg", [128, 4, 64], F32, ms)

        ctx.dma("sync", gpm[:], iv["g_pm"], reads=[ir["g_pm"]], writes=[gpm.r])
        ctx.dma("sync", musU[:], iv["musU"], reads=[ir["musU"]], writes=[musU.r])
        ctx.dma("sync", muI[:], iv["muI"], reads=[ir["muI"]], writes=[muI.r])
        ctx.dma("sync", tv[:], iv["tv"], reads=[ir["tv"]], writes=[tv.r])
        ctx.dma("sync", lamt[0:1, 0:1], iv["b3_lam"], reads=[ir["b3_lam"]],
                writes=[lamt.r])
        ctx.dma("sync", lamt[0:1, 1:2], iv["b3_eta"], reads=[ir["b3_eta"]],
                writes=[lamt.r])
        ctx.e("scalar", "activation", lamt[0:1, 2:3], lamt[0:1, 0:1],
              AF.Sigmoid, reads=[lamt.r], writes=[lamt.r])
        ctx.e("scalar", "activation", lamt[0:1, 3:4], lamt[0:1, 2:3],
              AF.Ln, reads=[lamt.r], writes=[lamt.r])
        ctx.e("vector", "tensor_scalar", lamt[0:1, 4:5], lamt[0:1, 2:3],
              -1.0, 1.0, op0=ALU.mult, op1=ALU.add, reads=[lamt.r],
              writes=[lamt.r])
        ctx.e("vector", "tensor_mul", lamt[0:1, 5:6], lamt[0:1, 4:5],
              lamt[0:1, 1:2], reads=[lamt.r], writes=[lamt.r])
        ctx.e("vector", "tensor_scalar_mul", lamt[0:1, 5:6], lamt[0:1, 5:6],
              1.0 / 64.0, reads=[lamt.r], writes=[lamt.r])
        ctx.e("vector", "tensor_scalar_mul", lamt[0:1, 6:7], lamt[0:1, 5:6],
              -1.0, reads=[lamt.r], writes=[lamt.r])
        ctx.e("vector", "tensor_scalar_mul", lamt[0:1, 7:8], lamt[0:1, 3:4],
              128.0, reads=[lamt.r], writes=[lamt.r])
        for col, src in [(0, 3), (1, 5), (2, 6)]:
            psb = ctx.psum()
            ctx.mm(psb[:, 0:1], cm["ones1f"][:], lamt[0:1, src:src + 1],
                   start=True, stop=True, reads=[cm["ones1f"].r, lamt.r],
                   writes=[psb.r])
            ctx.e("vector", "tensor_copy", lnl[:, col:col + 1], psb[:, 0:1],
                  reads=[psb.r], writes=[lnl.r])
        psb = ctx.psum()
        ctx.e("scalar", "activation", lamt[0:1, 7:8], lamt[0:1, 7:8], AF.Exp,
              reads=[lamt.r], writes=[lamt.r])
        ctx.mm(psb[0:64, 0:1], cm["ones1f"][0:1, 0:64], lamt[0:1, 7:8],
               start=True, stop=True, reads=[cm["ones1f"].r, lamt.r],
               writes=[psb.r])
        ctx.e("vector", "tensor_copy", lamC[:], psb[0:64, 0:1],
              reads=[psb.r], writes=[lamC.r])
        for col in range(2):
            ctx.e("vector", "tensor_mul", e1[:, 0:1], tv[:, col:col + 1],
                  lnl[:, 0:1], reads=[tv.r, lnl.r], writes=[e1.r])
            ctx.e("scalar", "activation", wvec[:, col:col + 1], e1[:, 0:1],
                  AF.Exp, reads=[e1.r], writes=[wvec.r])
        ctx.e("vector", "tensor_scalar_mul", e1[:, 0:128], gpm[:], -1.0,
              reads=[gpm.r], writes=[e1.r])
        ctx.e("vector", "tensor_mul", e1[:, 0:128], e1[:, 0:128]
              if False else e1[:, 0:128], lnl[:, 0:1], reads=[lnl.r],
              writes=[e1.r], tag="BAD-INPLACE") if False else None
        ctx.e("vector", "tensor_scalar_mul", e2[:, 0:128], e1[:, 0:128],
              lnl[:, 0:1], reads=[e1.r, lnl.r], writes=[e2.r])
        ctx.e("scalar", "activation", e4[:, 0:128], e2[:, 0:128], AF.Exp,
              reads=[e2.r], writes=[e4.r])
        ctx.e("vector", "tensor_mul", adI[:], e4[:, 0:128], muI[:],
              reads=[e4.r, muI.r], writes=[adI.r])
        ctx.e("vector", "tensor_scalar_mul", bstr[:], musU[:], lamt[0:1, 5:6]
              if False else 1.0, reads=[musU.r], writes=[bstr.r],
              tag="FIXBELOW") if False else None
        ctx.e("vector", "tensor_scalar_mul", bstr[:], musU[:], lnl[:, 1:2],
              reads=[musU.r, lnl.r], writes=[bstr.r])

        ctx.dma("sync", qsel[:], iv["qsel"], reads=[ir["qsel"]],
                writes=[qsel.r])
        ctx.e("vector", "memset", Sst[0][:], 0.0, writes=[Sst[0].r])
        ctx.e("vector", "memset", Sb[:], 0.0, writes=[Sb.r])
        for ch in range(T // 128 * 4):
            ctx.dma("sync", kraw[:], a1o[ts(ch, 128), :], reads=[g3r[1]],
                    writes=[kraw.r], tag=f"b3kv{ch}")
            for half, off in [(0, 0), (1, 1024)]:
                ctx.e("vector", "tensor_scalar_mul", ktv[:, ts(half, 256)],
                      kraw[:, off:off + 256], qsel[:, 0:1],
                      reads=[kraw.r, qsel.r], writes=[ktv.r])
                for jr in range(1, 4):
                    ctx.e("vector", "scalar_tensor_tensor", ktv[:, ts(half, 256)],
                          kraw[:, off + jr * 256:off + (jr + 1) * 256],
                          qsel[:, jr:jr + 1], ktv[:, ts(half, 256)],
                          op0=ALU.mult, op1=ALU.add,
                          reads=[kraw.r, qsel.r, ktv.r], writes=[ktv.r])
            Sprev = Sst[ch % 2]
            Snext = Sst[(ch + 1) % 2]
            for mh in range(4):
                km = ktv[:, ts(mh, 64)]
                vm = ktv[:, 256 + mh * 64:256 + (mh + 1) * 64]
                pst = ctx.psumb()
                ctx.e("tensor", "transpose", pst[0:64, 0:128], km, idb[:],
                      reads=[ktv.r, idb.r], writes=[pst.r])
                ctx.e("scalar", "copy", kfm[:], pst[0:64, 0:128],
                      reads=[pst.r], writes=[kfm.r])
                psA = ctx.psum()
                ctx.mm(psA[:, :128], kfm[:], kfm[:], start=True, stop=True,
                       reads=[kfm.r], writes=[psA.r])
                ctx.e("vector", "tensor_mul", adS[:], psA[:, 0:128], adI[:],
                      reads=[psA.r, adI.r], writes=[adS.r])
                ctx.e("vector", "tensor_mul", ltS[:], psA[:, 0:128], bstr[:],
                      reads=[psA.r, bstr.r], writes=[ltS.r])
                psK = ctx.psum()
                ctx.mm(psK[:, 0:64], kfm[:], Sb[:, mh, :], start=True,
                       stop=True, reads=[kfm.r, Sb.r], writes=[psK.r])
                ctx.e("vector", "tensor_scalar_mul", ksw[:], psK[:, 0:64],
                      wvec[:, 0:1], reads=[psK.r, wvec.r], writes=[ksw.r])
                ctx.e("vector", "scalar_tensor_tensor", e1[:, 0:64],
                      psK[:, 0:64], wvec[:, 0:1], vm, op0=ALU.mult,
                      op1=ALU.subtract, reads=[psK.r, wvec.r, ktv.r],
                      writes=[e1.r])
                ctx.e("vector", "tensor_scalar_mul", bS[:], e1[:, 0:64],
                      lnl[:, 2:3], reads=[e1.r, lnl.r], writes=[bS.r])
                psX = ctx.psum()
                ctx.mm(psX[:, 0:64], ltS[:], bS[:], start=True, stop=True,
                       reads=[ltS.r, bS.r], writes=[psX.r])
                ctx.e("vector", "tensor_tensor", x2S[:], bS[:], psX[:, 0:64],
                      op=ALU.subtract, reads=[bS.r, psX.r], writes=[x2S.r])
                psX2 = ctx.psum()
                ctx.mm(psX2[:, 0:64], ltS[:], x2S[:], start=True, stop=True,
                       reads=[ltS.r, x2S.r], writes=[psX2.r])
                ctx.e("vector", "tensor_tensor", uS[:], bS[:], psX2[:, 0:64],
                      op=ALU.subtract, reads=[bS.r, psX2.r], writes=[uS.r])
                psO = ctx.psum()
                ctx.mm(psO[:, 0:64], adS[:], uS[:], start=True, stop=True,
                       reads=[adS.r, uS.r], writes=[psO.r])
                ctx.e("vector", "tensor_add", ostg[:, mh, :], ksw[:],
                      psO[:, 0:64], reads=[ksw.r, psO.r], writes=[ostg.r])
                ctx.e("vector", "tensor_scalar_mul", duS[:], uS[:],
                      wvec[:, 1:2], reads=[uS.r, wvec.r], writes=[duS.r])
                psS = ctx.psum()
                ctx.mm(psS[0:64, 0:64], km, duS[:], start=True, stop=True,
                       reads=[ktv.r, duS.r], writes=[psS.r])
                ctx.e("vector", "scalar_tensor_tensor", Snext[:, mh, :],
                      Sprev[:, mh, :], lamC[:, 0:1], psS[0:64, 0:64],
                      op0=ALU.mult, op1=ALU.add,
                      reads=[Sprev.r, lamC.r, psS.r], writes=[Snext.r])
                ctx.e("scalar", "copy", Sb[:, mh, :], Snext[:, mh, :],
                      reads=[Snext.r], writes=[Sb.r])
            ctx.dma("sync", a2i[ts(ch, 128), :], ostg[:],
                    reads=[ostg.r], writes=[g3r[2]], tag=f"b3o{ch}")
        ctx.s.rw("gpsimd", _mk_emit("collective_compute",
                 ("AllGather", ALU.bypass), dict(
                     replica_groups=ctx.groups, ins=[a2i], outs=[a2o])),
                 reads=[g3r[2]], writes=[g3r[3]], cc=True, tag="ag3b")
        ctx.fence("b3mix")

    with contextlib.ExitStack() as ms:
        idb2 = ctx.sb("b3_idb2", [128, 128], BF16, ms)
        ytile = ctx.sb("b3_yt", [128, 1024], F32, ms)
        y2 = ctx.sb("b3_y2", [128, 1024], F32, ms)
        gb = ctx.sb("b3_gb", [128, 1024], F32, ms)
        bb = ctx.sb("b3_bb", [128, 1024], F32, ms)
        yst = ctx.sb("b3_yst", [128, 4], F32, ms)
        atb = ctx.sb("b3_atb", [128, 1024], BF16, ms)
        ctx.dma("gpsimd", idb2[:], iv["idf"], reads=[ir["idf"]], writes=[idb2.r])
        for dst, nm in [(gb, "b3_lngr"), (bb, "b3_lnbr")]:
            psb = ctx.psum()
            rowt = ctx.sb("b3_row_" + nm, [1, 1024], F32, ms)
            ctx.dma("sync", rowt[:], iv[nm], reads=[ir[nm]], writes=[rowt.r])
            for half in range(2):
                ctx.mm(psb[:, 0:512], cm["ones1f"][:], rowt[0:1, ts(half, 512)],
                       start=True, stop=True,
                       reads=[cm["ones1f"].r, rowt.r], writes=[psb.r])
                ctx.e("vector", "tensor_copy", dst[:, ts(half, 512)],
                      psb[:, 0:512], reads=[psb.r], writes=[dst.r])
        qsel2 = ctx.sb("b3_qsel2", [128, 4], F32, ms)
        ycnd = ctx.sb("b3_ycnd", [128, 4, 256], F32, ms)
        ctx.dma("sync", qsel2[:], iv["qsel"], reads=[ir["qsel"]],
                writes=[qsel2.r])
        for tc in range(T // 128):
            for i in range(4):
                for jr in range(4):
                    ctx.dma("sync", ycnd[:, jr, :],
                            a2o[i * 4 * T + jr * T + tc * 128:
                                i * 4 * T + jr * T + (tc + 1) * 128, :],
                            reads=[g3r[3]], writes=[ycnd.r],
                            tag=f"b3y{tc}_{i}_{jr}")
                ctx.e("vector", "tensor_scalar_mul",
                      ytile[:, i * 256:(i + 1) * 256], ycnd[:, 0, :],
                      qsel2[:, 0:1], reads=[ycnd.r, qsel2.r],
                      writes=[ytile.r])
                for jr in range(1, 4):
                    ctx.e("vector", "scalar_tensor_tensor",
                          ytile[:, i * 256:(i + 1) * 256], ycnd[:, jr, :],
                          qsel2[:, jr:jr + 1],
                          ytile[:, i * 256:(i + 1) * 256],
                          op0=ALU.mult, op1=ALU.add,
                          reads=[ycnd.r, qsel2.r, ytile.r], writes=[ytile.r])
            ctx.dma("sync", y2[:, :], static_d[ts(tc, 128), :],
                    reads=[static_r], writes=[y2.r], tag=f"b3ys{tc}")
            ctx.e("vector", "tensor_add", ytile[:, :], ytile[:, :], y2[:, :],
                  reads=[y2.r], writes=[ytile.r])
            ctx.e("vector", "reduce_sum", yst[:, 0:1], ytile[:, :], axis=AXX,
                  reads=[ytile.r], writes=[yst.r])
            ctx.e("vector", "tensor_scalar_mul", yst[:, 0:1], yst[:, 0:1],
                  1.0 / 1024.0, reads=[], writes=[yst.r])
            ctx.e("vector", "tensor_scalar", y2[:, :], ytile[:, :],
                  yst[:, 0:1], None, op0=ALU.subtract,
                  reads=[ytile.r, yst.r], writes=[y2.r])
            ctx.e("scalar", "activation", ytile[:, :], y2[:, :], AF.Square,
                  reads=[y2.r], writes=[ytile.r])
            ctx.e("vector", "reduce_sum", yst[:, 1:2], ytile[:, :], axis=AXX,
                  reads=[ytile.r], writes=[yst.r])
            ctx.e("vector", "tensor_scalar", yst[:, 2:3], yst[:, 1:2],
                  1.0 / 1024.0, 1e-5, op0=ALU.mult, op1=ALU.add,
                  reads=[yst.r], writes=[yst.r])
            ctx.e("scalar", "activation", yst[:, 2:3], yst[:, 2:3], AF.Sqrt,
                  reads=[], writes=[yst.r])
            ctx.e("vector", "reciprocal", yst[:, 3:4], yst[:, 2:3],
                  reads=[], writes=[yst.r])
            ctx.e("vector", "tensor_scalar_mul", y2[:, :], y2[:, :],
                  yst[:, 3:4], reads=[yst.r], writes=[y2.r])
            ctx.e("vector", "tensor_mul", y2[:, :], y2[:, :], gb[:, :],
                  reads=[gb.r], writes=[y2.r])
            ctx.e("vector", "tensor_add", atb[:, :], y2[:, :], bb[:, :],
                  reads=[y2.r, bb.r], writes=[atb.r])
            for c in range(NC):
                pst = ctx.psumb()
                ctx.e("tensor", "transpose", pst[:, 0:128],
                      atb[:, ts(c, 128)], idb2[:],
                      reads=[atb.r, idb2.r], writes=[pst.r])
                ctx.e("vector", "tensor_copy", att[:, c, ts(tc, 128)],
                      pst[:, 0:128], reads=[pst.r], writes=[att.r])
        ctx.fence("b3y")
    build_b23_ffn(ctx, iv, ir, xn, att, xin_d, xin_r, xout_d, xout_r, "b3",
                  attn_proj=False)
    return xn, att


def build_prog_b0():
    nc = bass.Bass("TRN2", target_bir_lowering=False, debug=False)
    iv, ir = {}, {}

    def inp(name, shape):
        iv[name] = nc.dram_tensor(name, list(shape), F32, kind="ExternalInput").ap()
        ir[name] = Reg(name)

    inp("xin", [1024, T + 1])
    for nm, shp in B0_INPUTS:
        inp(nm, shp)
    xout = nc.dram_tensor("xout", [1024, T], F32, kind="ExternalOutput").ap()
    xout_r = Reg("xout")
    gin = [nc.dram_tensor(f"g0i_{c}", [2, 128], F32).ap() for c in range(NC)]
    gout = [nc.dram_tensor(f"g0o_{c}", [8, 128], F32).ap() for c in range(NC)]
    gr = []
    for c in range(NC):
        gr += [Reg(f"g0i_{c}"), Reg(f"g0o_{c}")]

    s = Sched(nc)
    with contextlib.ExitStack() as stk:
        ctx = Ctx(nc, stk, s)
        ctx.groups = [[0, 1, 2, 3], [4, 5, 6, 7]]
        ctx.psum_init(n_f32=8, n_bf16=0)
        block = stk.enter_context(nc.Block())
        alloc_common(ctx)
        build_block0(ctx, iv, ir, iv["xin"], ir["xin"], xout, xout_r,
                     gin, gout, gr)
        last = [o for o in s.ops if o.dma and o.engine == "sync"][-1]
        s.op("gpsimd", _mk_emit("nop", (), {}), deps=[last, s.ops[-1]])
        s.build(stk, block)
    return nc


B0_INPUTS = [
    ("b0_mixv", [128, 6, NC]), ("b0_w0v", [16, 1]), ("b0_a0v", [16, 1]),
    ("b0_kk", [128, NC]), ("b0_rk", [128, NC]), ("b0_lnw", [128, NC]),
    ("b0_lnb", [128, NC]), ("pmask", [128, NC]),
    ("ebcf", [16, NC * 128]), ("etf", [128, NC * 16]),
    ("b0_w1", [1024, 64]), ("b0_a1", [1024, 64]), ("b0_g1", [1024, 64]),
    ("b0_w2", [64, 16]), ("b0_a2", [64, 16]), ("b0_g2", [64, 1024]),
    ("b0_Wr", [1024, 1024]), ("b0_Wk", [1024, 1024]), ("b0_Wv", [1024, 1024]),
    ("b0_Wo", [1024, 1024]),
    ("b0f_w1t", [1024, 4096]), ("b0f_b1", [128, 32]),
    ("b0f_w2t", [4096, 1024]), ("b0f_b2", [128, NC]),
]


def fmv(v, ncols):
    v = np.asarray(v, np.float32).reshape(-1)
    return np.ascontiguousarray(v.reshape(ncols, 128).T)


def host_consts(q):
    c = {}
    ebcf = np.zeros((16, NC * 128), np.float32)
    etf = np.zeros((128, NC * 16), np.float32)
    for cc in range(NC):
        for p in range(128):
            h = 2 * cc + p // 64
            ebcf[h, cc * 128 + p] = 1.0
            etf[p, cc * 16 + h] = 1.0
    c["ebcf"] = ebcf
    c["etf"] = etf
    pm = np.zeros((128, NC), np.float32)
    for j in range(4):
        pm[:, j] = 1.0 if j < q else 0.0
        pm[:, 4 + j] = 0.0 if j < q else 1.0
    c["pmask"] = pm
    return c


def stage_b0(ins, core):
    b, q = core // 4, core % 4
    x = np.asarray(ins["L0_attn_x"], np.float32)
    seg = x[b, q * T:(q + 1) * T]
    prev = x[b, q * T - 1:q * T] if q > 0 else np.zeros((1, 1024), np.float32)
    xin = np.ascontiguousarray(np.concatenate([prev, seg], 0).T)
    d = {"xin": xin}
    d.update(host_consts(q))
    mixv = np.zeros((128, 6, NC), np.float32)
    for j, nm in enumerate(["x_r", "x_w", "x_k", "x_v", "x_a", "x_g"]):
        mixv[:, j, :] = fmv(ins["L0_attn_" + nm], NC)
    d["b0_mixv"] = mixv
    d["b0_w0v"] = np.asarray(ins["L0_attn_w0"], np.float32).reshape(16, 1)
    d["b0_a0v"] = np.asarray(ins["L0_attn_a0"], np.float32).reshape(16, 1)
    d["b0_kk"] = fmv(ins["L0_attn_k_k"], NC)
    d["b0_rk"] = fmv(ins["L0_attn_r_k"], NC)
    d["b0_lnw"] = fmv(ins["L0_attn_ln_w"], NC)
    d["b0_lnb"] = fmv(ins["L0_attn_ln_b"], NC)
    for nm in ["w1", "a1", "g1", "w2", "a2", "g2", "Wr", "Wk", "Wv", "Wo"]:
        d["b0_" + nm] = np.ascontiguousarray(np.asarray(ins["L0_attn_" + nm], np.float32))
    d["b0f_w1t"] = np.ascontiguousarray(np.asarray(ins["L0_ffn_w1"], np.float32).T)
    d["b0f_w2t"] = np.ascontiguousarray(np.asarray(ins["L0_ffn_w2"], np.float32).T)
    d["b0f_b1"] = fmv(ins["L0_ffn_b1"], 32)
    d["b0f_b2"] = fmv(ins["L0_ffn_b2"], NC)
    return d


def run_block0(ins, dump=False):
    global DUMP
    DUMP = dump
    nc = build_prog_b0()
    in_maps = [stage_b0(ins, c) for c in range(NCORE)]
    res = run_bass_kernel_spmd(nc, in_maps, core_ids=list(range(NCORE)))
    outs = []
    for c in range(NCORE):
        outs.append(np.asarray(res.results[c]["xout"]).T)
    x1 = np.stack([np.concatenate(outs[0:4], 0), np.concatenate(outs[4:8], 0)])
    if dump:
        return x1, res
    return x1


B1_INPUTS = [
    ("ebcf", [16, NC * 128]), ("pmask", [128, NC]), ("idf", [128, 128]),
    ("l1mask", [128, 256]),
    ("b1_qkvw", [1024, 3072]), ("b1_qb", [128, NC]), ("b1_kb", [128, NC]),
    ("b1_vb", [128, NC]), ("b1_outw", [1024, 1024]), ("b1_ob", [128, NC]),
    ("b1f_w1", [1024, 4096]), ("b1f_b1", [128, 32]), ("b1f_w2", [4096, 1024]),
    ("b1f_b2", [128, NC]), ("b1f_th", [128, 32]), ("b1f_wr", [128, 32]),
    ("b1f_wi", [128, 32]),
]


def stage_b1(ins, core, x_cur):
    b, q = core // 4, core % 4
    seg = x_cur[b, q * T:(q + 1) * T] if x_cur.shape[1] >= (q + 1) * T         else np.zeros((T, 1024), np.float32)
    d = {"xin": np.ascontiguousarray(seg.T)}
    d.update(host_consts(q))
    idf = np.zeros((128, 128), np.float32)
    np.fill_diagonal(idf, 1.0)
    d["idf"] = idf.astype(ml_dtypes.bfloat16)
    mask = np.zeros((128, 128), np.float32)
    for sk in range(128):
        mask[sk, sk:] = 1.0
    d["l1mask"] = np.ascontiguousarray(np.concatenate([mask, mask], axis=1))
    qkv_b = np.asarray(ins["L1_attn_qkv_b"], np.float32)
    d["b1_qkvw"] = np.ascontiguousarray(np.asarray(ins["L1_attn_qkv_w"], np.float32))
    d["b1_qb"] = fmv(qkv_b[0:1024], NC)
    d["b1_kb"] = fmv(qkv_b[1024:2048], NC)
    d["b1_vb"] = fmv(qkv_b[2048:3072], NC)
    d["b1_outw"] = np.ascontiguousarray(np.asarray(ins["L1_attn_out_w"], np.float32))
    d["b1_ob"] = fmv(ins["L1_attn_out_b"], NC)
    d["b1f_w1"] = np.ascontiguousarray(np.asarray(ins["L1_ffn_W1"], np.float32))
    d["b1f_b1"] = fmv(ins["L1_ffn_b1"], 32)
    d["b1f_w2"] = np.ascontiguousarray(np.asarray(ins["L1_ffn_W2"], np.float32))
    d["b1f_b2"] = fmv(ins["L1_ffn_b2"], NC)
    d["b1f_th"] = fmv(ins["L1_ffn_quantum_phases"], 32)
    d["b1f_wr"] = fmv(ins["L1_ffn_quantum_weights_real"], 32)
    d["b1f_wi"] = fmv(ins["L1_ffn_quantum_weights_imag"], 32)
    return d


def build_prog_b1():
    nc = bass.Bass("TRN2", target_bir_lowering=False, debug=False)
    iv, ir = {}, {}

    def inp(name, shape, dt=F32):
        iv[name] = nc.dram_tensor(name, list(shape), dt, kind="ExternalInput").ap()
        ir[name] = Reg(name)

    inp("xin", [1024, T])
    for nm, shp in B1_INPUTS:
        inp(nm, shp, BF16 if nm == "idf" else F32)
    xout = nc.dram_tensor("xout", [1024, T], F32, kind="ExternalOutput").ap()
    xout_r = Reg("xout")
    g1i = nc.dram_tensor("g1i", [16, 4160], F32).ap()
    g1o = nc.dram_tensor("g1o", [64, 4160], F32).ap()
    g1r = [Reg("g1i"), Reg("g1o")]

    s = Sched(nc)
    with contextlib.ExitStack() as stk:
        ctx = Ctx(nc, stk, s)
        ctx.groups = [[0, 1, 2, 3], [4, 5, 6, 7]]
        ctx.psum_init(n_f32=6, n_bf16=1)
        block = stk.enter_context(nc.Block())
        alloc_common(ctx)
        build_block1(ctx, iv, ir, iv["xin"], ir["xin"], xout, xout_r,
                     g1i, g1o, g1r)
        last = [o for o in s.ops if o.dma and o.engine == "sync"][-1]
        s.op("gpsimd", _mk_emit("nop", (), {}), deps=[last, s.ops[-1]])
        s.build(stk, block)
    return nc


def run_block1(ins, x_cur):
    nc = build_prog_b1()
    in_maps = [stage_b1(ins, c, x_cur) for c in range(NCORE)]
    res = run_bass_kernel_spmd(nc, in_maps, core_ids=list(range(NCORE)))
    outs = [np.asarray(res.results[c]["xout"]).T for c in range(NCORE)]
    return np.stack([np.concatenate(outs[0:4], 0), np.concatenate(outs[4:8], 0)])


B2_INPUTS = [
    ("idf", [128, 128], "bf"), ("etf", [128, NC * 16], "f"),
    ("i16f", [16, 16], "f"),
    ("b2_WqT", [1024, 1024], "f"), ("b2_WkT", [1024, 1024], "f"),
    ("b2_WvT", [1024, 1024], "f"), ("b2_WoT", [1024, 1024], "f"),
    ("b2_AqT", [1024, 16], "f"), ("b2_AkT", [1024, 16], "f"),
    ("b2_AvT", [1024, 16], "f"), ("b2_AoT", [1024, 16], "f"),
    ("b2_BqT", [16, 1024], "f"), ("b2_BkT", [16, 1024], "f"),
    ("b2_BvT", [16, 1024], "f"), ("b2_BoT", [16, 1024], "f"),
    ("b2_bo", [128, NC], "f"),
    ("b2_w1t", [1024, 4096], "f"), ("b2_w2t", [1024, 4096], "f"),
    ("b2_h1t", [4096, 4096], "f"), ("b2_h2t", [4096, 4096], "f"),
    ("b2_w3t", [4096, 1024], "f"),
    ("b2_s1", [128, 32], "f"), ("b2_s2", [128, 32], "f"),
    ("b2_sh1", [128, 32], "f"), ("b2_sh2", [128, 32], "f"),
    ("b2_s3", [128, NC], "f"),
]


def stage_b2(ins, core, x_cur):
    b, q = core // 4, core % 4
    seg = x_cur[b, q * T:(q + 1) * T] if x_cur.shape[1] >= (q + 1) * T         else np.zeros((T, 1024), np.float32)
    d = {"xin": np.ascontiguousarray(seg.T)}
    idf = np.zeros((128, 128), np.float32)
    np.fill_diagonal(idf, 1.0)
    d["idf"] = idf.astype(ml_dtypes.bfloat16)
    d["i16f"] = np.eye(16, dtype=np.float32)
    etf = np.zeros((128, NC * 16), np.float32)
    for cc in range(NC):
        for p in range(128):
            etf[p, cc * 16 + 2 * cc + p // 64] = 1.0
    d["etf"] = etf
    wT = lambda k: np.ascontiguousarray(np.asarray(ins[k], np.float32).T)
    for nm, key in [("b2_WqT", "L2_attn_Wq"), ("b2_WkT", "L2_attn_Wk"),
                    ("b2_WvT", "L2_attn_Wv"), ("b2_WoT", "L2_attn_Wo"),
                    ("b2_AqT", "L2_attn_Aq"), ("b2_AkT", "L2_attn_Ak"),
                    ("b2_AvT", "L2_attn_Av"), ("b2_AoT", "L2_attn_Ao"),
                    ("b2_BqT", "L2_attn_Bq"), ("b2_BkT", "L2_attn_Bk"),
                    ("b2_BvT", "L2_attn_Bv"), ("b2_BoT", "L2_attn_Bo"),
                    ("b2_w1t", "L2_ffn_w1_w"), ("b2_w2t", "L2_ffn_w2_w"),
                    ("b2_h1t", "L2_ffn_h1_w"), ("b2_h2t", "L2_ffn_h2_w"),
                    ("b2_w3t", "L2_ffn_w3_w")]:
        d[nm] = wT(key)
    d["b2_bo"] = fmv(ins["L2_attn_bo"], NC)
    d["b2_s1"] = fmv(ins["L2_ffn_w1_s"], 32)
    d["b2_s2"] = fmv(ins["L2_ffn_w2_s"], 32)
    d["b2_sh1"] = fmv(ins["L2_ffn_h1_s"], 32)
    d["b2_sh2"] = fmv(ins["L2_ffn_h2_s"], 32)
    d["b2_s3"] = fmv(ins["L2_ffn_w3_s"], NC)
    return d


def build_prog_b2():
    nc = bass.Bass("TRN2", target_bir_lowering=False, debug=False)
    iv, ir = {}, {}

    def inp(name, shape, dt=F32):
        iv[name] = nc.dram_tensor(name, list(shape), dt, kind="ExternalInput").ap()
        ir[name] = Reg(name)

    inp("xin", [1024, T])
    for nm, shp, k in B2_INPUTS:
        inp(nm, shp, BF16 if k == "bf" else F32)
    xout = nc.dram_tensor("xout", [1024, T], F32, kind="ExternalOutput").ap()
    xout_r = Reg("xout")
    gki = nc.dram_tensor("gki", [1024, T], BF16).ap()
    gko = nc.dram_tensor("gko", [4096, T], BF16).ap()
    gvi = nc.dram_tensor("gvi", [T, 1024], BF16).ap()
    gvo = nc.dram_tensor("gvo", [4 * T, 1024], BF16).ap()
    gsi = nc.dram_tensor("gsi", [1, 16], F32).ap()
    gso = nc.dram_tensor("gso", [4, 16], F32).ap()
    g2r = [Reg(x) for x in ["gki", "gko", "gvi", "gvo", "gsi", "gso"]]

    s = Sched(nc)
    with contextlib.ExitStack() as stk:
        ctx = Ctx(nc, stk, s)
        ctx.groups = [[0, 1, 2, 3], [4, 5, 6, 7]]
        ctx.psum_init(n_f32=6, n_bf16=1)
        block = stk.enter_context(nc.Block())
        alloc_common(ctx)
        build_block2(ctx, iv, ir, iv["xin"], ir["xin"], xout, xout_r,
                     gki, gko, gvi, gvo, gsi, gso, g2r)
        last = [o for o in s.ops if o.dma and o.engine == "sync"][-1]
        s.op("gpsimd", _mk_emit("nop", (), {}), deps=[last, s.ops[-1]])
        s.build(stk, block)
    return nc


def run_block2(ins, x_cur):
    nc = build_prog_b2()
    in_maps = [stage_b2(ins, c, x_cur) for c in range(NCORE)]
    res = run_bass_kernel_spmd(nc, in_maps, core_ids=list(range(NCORE)))
    outs = [np.asarray(res.results[c]["xout"]).T for c in range(NCORE)]
    return np.stack([np.concatenate(outs[0:4], 0), np.concatenate(outs[4:8], 0)])


B3_INPUTS = [
    ("idf", [128, 128], "bf"), ("idf32", [128, 128], "f"),
    ("g_pm", [128, 128], "f"), ("musU", [128, 128], "f"),
    ("muI", [128, 128], "f"), ("tv", [128, 2], "f"),
    ("b3_WT", [1024, 1024], "f"), ("b3_lngr", [1, 1024], "f"),
    ("b3_lnbr", [1, 1024], "f"), ("b3_lam", [1, 1], "f"),
    ("b3_eta", [1, 1], "f"),
    ("b3_w1t", [1024, 4096], "f"), ("b3_w2t", [1024, 4096], "f"),
    ("b3_h1t", [4096, 4096], "f"), ("b3_h2t", [4096, 4096], "f"),
    ("b3_w3t", [4096, 1024], "f"),
    ("b3_s1", [128, 32], "f"), ("b3_s2", [128, 32], "f"),
    ("b3_sh1", [128, 32], "f"), ("b3_sh2", [128, 32], "f"),
    ("b3_s3", [128, NC], "f"), ("qsel", [128, 4], "f"),
]


def stage_b3(ins, core, x_cur):
    b, q = core // 4, core % 4
    seg = x_cur[b, q * T:(q + 1) * T] if x_cur.shape[1] >= (q + 1) * T         else np.zeros((T, 1024), np.float32)
    d = {"xin": np.ascontiguousarray(seg.T)}
    idf = np.zeros((128, 128), np.float32)
    np.fill_diagonal(idf, 1.0)
    d["idf"] = idf.astype(ml_dtypes.bfloat16)
    d["idf32"] = idf
    p = np.arange(128, dtype=np.float32)
    d["g_pm"] = np.ascontiguousarray(p[:, None] - p[None, :])
    d["musU"] = (p[:, None] < p[None, :]).astype(np.float32)
    d["muI"] = (p[:, None] <= p[None, :]).astype(np.float32)
    d["tv"] = np.ascontiguousarray(
        np.stack([p + 1.0, 128.0 - (p + 1.0)], axis=1))
    d["b3_WT"] = np.ascontiguousarray(np.asarray(ins["L3_attn_W"], np.float32).T)
    d["b3_lngr"] = np.asarray(ins["L3_attn_ln_gamma"], np.float32).reshape(1, 1024)
    d["b3_lnbr"] = np.asarray(ins["L3_attn_ln_beta"], np.float32).reshape(1, 1024)
    d["b3_lam"] = np.asarray(ins["L3_attn_lam_logit"], np.float32).reshape(1, 1)
    d["b3_eta"] = np.asarray(ins["L3_attn_eta"], np.float32).reshape(1, 1)
    wT = lambda k: np.ascontiguousarray(np.asarray(ins[k], np.float32).T)
    d["b3_w1t"] = wT("L3_ffn_w1_w")
    d["b3_w2t"] = wT("L3_ffn_w2_w")
    d["b3_h1t"] = wT("L3_ffn_h1_w")
    d["b3_h2t"] = wT("L3_ffn_h2_w")
    d["b3_w3t"] = wT("L3_ffn_w3_w")
    d["b3_s1"] = fmv(ins["L3_ffn_w1_s"], 32)
    d["b3_s2"] = fmv(ins["L3_ffn_w2_s"], 32)
    d["b3_sh1"] = fmv(ins["L3_ffn_h1_s"], 32)
    d["b3_sh2"] = fmv(ins["L3_ffn_h2_s"], 32)
    d["b3_s3"] = fmv(ins["L3_ffn_w3_s"], NC)
    qs = np.zeros((128, 4), np.float32)
    qs[:, q] = 1.0
    d["qsel"] = qs
    return d


def build_prog_b3():
    nc = bass.Bass("TRN2", target_bir_lowering=False, debug=False)
    iv, ir = {}, {}

    def inp(name, shape, dt=F32):
        iv[name] = nc.dram_tensor(name, list(shape), dt, kind="ExternalInput").ap()
        ir[name] = Reg(name)

    inp("xin", [1024, T])
    for nm, shp, k in B3_INPUTS:
        inp(nm, shp, BF16 if k == "bf" else F32)
    xout = nc.dram_tensor("xout", [1024, T], F32, kind="ExternalOutput").ap()
    xout_r = Reg("xout")
    a1i = nc.dram_tensor("a1i", [T, 2048], BF16).ap()
    a1o = nc.dram_tensor("a1o", [4 * T, 2048], BF16).ap()
    a2i = nc.dram_tensor("a2i", [4 * T, 256], F32).ap()
    a2o = nc.dram_tensor("a2o", [16 * T, 256], F32).ap()
    g3r = [Reg(x) for x in ["a1i", "a1o", "a2i", "a2o"]]

    s = Sched(nc)
    with contextlib.ExitStack() as stk:
        ctx = Ctx(nc, stk, s)
        ctx.groups = [[0, 1, 2, 3], [4, 5, 6, 7]]
        ctx.psum_init(n_f32=6, n_bf16=1)
        block = stk.enter_context(nc.Block())
        alloc_common(ctx)
        build_block3(ctx, iv, ir, iv["xin"], ir["xin"], xout, xout_r,
                     a1i, a1o, a2i, a2o, g3r)
        last = [o for o in s.ops if o.dma and o.engine == "sync"][-1]
        s.op("gpsimd", _mk_emit("nop", (), {}), deps=[last, s.ops[-1]])
        s.build(stk, block)
    return nc


def run_block3(ins, x_cur):
    nc = build_prog_b3()
    in_maps = [stage_b3(ins, c, x_cur) for c in range(NCORE)]
    res = run_bass_kernel_spmd(nc, in_maps, core_ids=list(range(NCORE)))
    outs = [np.asarray(res.results[c]["xout"]).T for c in range(NCORE)]
    return np.stack([np.concatenate(outs[0:4], 0), np.concatenate(outs[4:8], 0)])


def kernel(**inputs):
    _unused = (inputs["L0_attn_v0"], inputs["L0_attn_v1"], inputs["L0_attn_v2"])
    return kernel_fused(**inputs)


def kernel_unfused(**inputs):
    x1 = run_block0(inputs)
    x2 = run_block1(inputs, x1)
    x3 = run_block2(inputs, x2)
    x4 = run_block3(inputs, x3)
    return np.ascontiguousarray(x4.astype(np.float32))


def build_prog_fused():
    nc = bass.Bass("TRN2", target_bir_lowering=False, debug=False)
    iv, ir = {}, {}

    def inp(name, shape, dt=F32):
        if name in iv:
            return
        iv[name] = nc.dram_tensor(name, list(shape), dt, kind="ExternalInput").ap()
        ir[name] = Reg(name)

    inp("xin", [1024, T + 1])
    for nm, shp in B0_INPUTS:
        inp(nm, shp)
    for nm, shp in B1_INPUTS:
        inp(nm, shp, BF16 if nm == "idf" else F32)
    for nm, shp, k in B2_INPUTS:
        inp(nm, shp, BF16 if k == "bf" else F32)
    for nm, shp, k in B3_INPUTS:
        inp(nm, shp, BF16 if k == "bf" else F32)
    xout = nc.dram_tensor("xout", [1024, T], F32, kind="ExternalOutput").ap()
    xout_r = Reg("xout")
    x1_d = nc.dram_tensor("x1_d", [1024, T], F32).ap()
    x2_d = nc.dram_tensor("x2_d", [1024, T], F32).ap()
    x3_d = nc.dram_tensor("x3_d", [1024, T], F32).ap()
    x1_r, x2_r, x3_r = Reg("x1"), Reg("x2"), Reg("x3")
    gin = [nc.dram_tensor(f"g0i_{c}", [2, 128], F32).ap() for c in range(NC)]
    gout = [nc.dram_tensor(f"g0o_{c}", [8, 128], F32).ap() for c in range(NC)]
    gr = []
    for c in range(NC):
        gr += [Reg(f"g0i_{c}"), Reg(f"g0o_{c}")]
    g1i = nc.dram_tensor("g1i", [16, 4160], F32).ap()
    g1o = nc.dram_tensor("g1o", [64, 4160], F32).ap()
    g1r = [Reg("g1i"), Reg("g1o")]
    gki = nc.dram_tensor("gki", [1024, T], BF16).ap()
    gko = nc.dram_tensor("gko", [4096, T], BF16).ap()
    gvi = nc.dram_tensor("gvi", [T, 1024], BF16).ap()
    gvo = nc.dram_tensor("gvo", [4 * T, 1024], BF16).ap()
    gsi = nc.dram_tensor("gsi", [1, 16], F32).ap()
    gso = nc.dram_tensor("gso", [4, 16], F32).ap()
    g2r = [Reg(x) for x in ["gki", "gko", "gvi", "gvo", "gsi", "gso"]]
    a1i = nc.dram_tensor("a1i", [T, 2048], BF16).ap()
    a1o = nc.dram_tensor("a1o", [4 * T, 2048], BF16).ap()
    a2i = nc.dram_tensor("a2i", [4 * T, 256], F32).ap()
    a2o = nc.dram_tensor("a2o", [16 * T, 256], F32).ap()
    g3r = [Reg(x) for x in ["a1i", "a1o", "a2i", "a2o"]]

    s = Sched(nc)
    with contextlib.ExitStack() as stk:
        ctx = Ctx(nc, stk, s)
        ctx.groups = [[0, 1, 2, 3], [4, 5, 6, 7]]
        ctx.psum_init(n_f32=6, n_bf16=1)
        block = stk.enter_context(nc.Block())
        alloc_common(ctx)
        prog_stk = ctx.stk
        for bi, fn in enumerate([
            lambda: build_block0(ctx, iv, ir, iv["xin"], ir["xin"], x1_d, x1_r,
                                 gin, gout, gr),
            lambda: build_block1(ctx, iv, ir, x1_d, x1_r, x2_d, x2_r,
                                 g1i, g1o, g1r),
            lambda: build_block2(ctx, iv, ir, x2_d, x2_r, x3_d, x3_r,
                                 gki, gko, gvi, gvo, gsi, gso, g2r),
            lambda: build_block3(ctx, iv, ir, x3_d, x3_r, xout, xout_r,
                                 a1i, a1o, a2i, a2o, g3r),
        ]):
            with contextlib.ExitStack() as bs:
                ctx.stk = bs
                fn()
                ctx.fence(f"blk{bi}")
            ctx.stk = prog_stk
        last = [o for o in s.ops if o.dma and o.engine == "sync"][-1]
        s.op("gpsimd", _mk_emit("nop", (), {}), deps=[last, s.ops[-1]])
        s.build(stk, block)
    return nc


def kernel_fused(**inputs):
    nc = build_prog_fused()
    in_maps = []
    for c in range(NCORE):
        d = {}
        d.update(stage_b3(inputs, c, np.zeros((2, 1, 1024), np.float32)))
        d.update(stage_b2(inputs, c, np.zeros((2, 1, 1024), np.float32)))
        d.update(stage_b1(inputs, c, np.zeros((2, 1, 1024), np.float32)))
        d.update(stage_b0(inputs, c))
        d.pop("xin")
        b, q = c // 4, c % 4
        x = np.asarray(inputs["L0_attn_x"], np.float32)
        seg = x[b, q * T:(q + 1) * T]
        prev = x[b, q * T - 1:q * T] if q > 0 else np.zeros((1, 1024), np.float32)
        d["xin"] = np.ascontiguousarray(np.concatenate([prev, seg], 0).T)
        in_maps.append(d)
    res = run_bass_kernel_spmd(nc, in_maps, core_ids=list(range(NCORE)))
    outs = [np.asarray(res.results[c]["xout"]).T for c in range(NCORE)]
    x4 = np.stack([np.concatenate(outs[0:4], 0), np.concatenate(outs[4:8], 0)])
    return np.ascontiguousarray(x4.astype(np.float32))


# version 05
# speedup vs baseline: 1.0865x; 1.0865x over previous
import contextlib
import math
import numpy as np
import ml_dtypes
import concourse.bass as bass
import concourse.mybir as mybir
from concourse.bass_utils import run_bass_kernel_spmd

F32 = mybir.dt.float32
BF16 = mybir.dt.bfloat16
AF = mybir.ActivationFunctionType
ALU = mybir.AluOpType
AXX = mybir.AxisListType.X

NCORE = 8
DUMP = False
B1_STAGE = 5
B1_SUB = 9
B2_STAGE = 5
B2_SUB = 9
T = 2048
NC = 8
H = 16
HD = 64
TT = 512
NTT = T // TT

ENGINES = ("tensor", "vector", "scalar", "gpsimd", "sync")
DMA_POOL = {"sync": 18, "gpsimd": 18, "scalar": 12, "vector": 4, "tensor": 4}
CC_POOL = 6


class Reg:
    __slots__ = ("writer", "readers", "name")

    def __init__(self, name=""):
        self.writer = None
        self.readers = []
        self.name = name


class _Op:
    __slots__ = ("engine", "emit", "deps", "dma", "cc", "seq", "flagged",
                 "tag", "sem_slot", "sem_use", "eidx")

    def __init__(self, engine, emit, deps, dma, cc, seq, tag):
        self.engine = engine
        self.emit = emit
        self.deps = deps
        self.dma = dma
        self.cc = cc
        self.seq = seq
        self.flagged = False
        self.tag = tag
        self.sem_slot = None
        self.sem_use = None
        self.eidx = None


class Sched:
    def __init__(self, nc):
        self.nc = nc
        self.ops = []
        self.streams = {e: [] for e in ENGINES}
        self._dma_count = {e: 0 for e in ENGINES}
        self._cc_count = 0

    def op(self, engine, emit, deps=(), dma=False, cc=False, tag=""):
        assert engine in ENGINES, engine
        deps = [d for d in deps if d is not None]
        o = _Op(engine, emit, deps, dma, cc, len(self.ops), tag)
        if dma:
            n = self._dma_count[engine]
            pool = DMA_POOL[engine]
            o.sem_slot = (engine, n % pool)
            o.sem_use = n // pool + 1
            self._dma_count[engine] = n + 1
        elif cc:
            n = self._cc_count
            o.sem_slot = ("cc", n % CC_POOL)
            o.sem_use = n // CC_POOL + 1
            self._cc_count = n + 1
        self.streams[engine].append(o)
        self.ops.append(o)
        for d in deps:
            if not (d.engine == "tensor" and engine == "tensor"
                    and not d.dma and not d.cc and not dma and not cc):
                d.flagged = True
        return o

    def rw(self, engine, emit, reads=(), writes=(), deps=(), dma=False, cc=False,
           tag=""):
        dd = list(deps)
        for r in reads:
            if r.writer is not None:
                dd.append(r.writer)
        for w in writes:
            if w.writer is not None:
                dd.append(w.writer)
            dd.extend(w.readers)
        h = self.op(engine, emit, deps=dd, dma=dma, cc=cc, tag=tag)
        for r in reads:
            r.readers.append(h)
        for w in writes:
            w.writer = h
            w.readers = []
        return h

    def dma(self, queue, out, in_, reads=(), writes=(), deps=(), tag="", **kw):
        return self.rw(queue, lambda e: e.dma_start(out=out, in_=in_, **kw),
                       reads=reads, writes=writes, deps=deps, dma=True, tag=tag)

    def prepare(self, stack):
        nc = self.nc
        self._esem = {e: stack.enter_context(nc.semaphore(f"s_{e}"))
                      for e in ENGINES}
        self._dsem = {}
        for q, npool in DMA_POOL.items():
            used = self._dma_count[q]
            for slot in range(min(npool, used)):
                self._dsem[(q, slot)] = stack.enter_context(
                    nc.semaphore(f"d_{q}_{slot}"))
        for slot in range(min(CC_POOL, self._cc_count)):
            self._dsem[("cc", slot)] = stack.enter_context(
                nc.semaphore(f"d_cc_{slot}"))
        self._flag_count = {e: {} for e in ENGINES}
        for e in ENGINES:
            c = 0
            for o in self.streams[e]:
                if (not o.dma) and (not o.cc) and o.flagged:
                    c += 1
                o.eidx = c
                self._flag_count[e][o.seq] = c

    def emit_engine(self, engine, handle):
        water = {e: 0 for e in ENGINES}
        dwater = {}
        for o in self.streams[engine]:
            for d in o.deps:
                if d.dma or d.cc:
                    key = d.sem_slot
                    need = (16 if d.dma else 1) * d.sem_use
                    if dwater.get(key, 0) < need:
                        handle.wait_ge(self._dsem[key], need)
                        dwater[key] = need
                elif not (d.engine == "tensor" and engine == "tensor"
                          and not o.dma and not o.cc):
                    need = self._flag_count[d.engine][d.seq]
                    assert need > 0, (d.engine, d.tag, o.tag)
                    if water[d.engine] < need:
                        handle.wait_ge(self._esem[d.engine], need)
                        water[d.engine] = need
            if (o.dma or o.cc) and o.sem_use > 1:
                key = o.sem_slot
                need = (16 if o.dma else 1) * (o.sem_use - 1)
                if dwater.get(key, 0) < need:
                    handle.wait_ge(self._dsem[key], need)
                    dwater[key] = need
            inst = o.emit(handle)
            if o.dma:
                inst.then_inc(self._dsem[o.sem_slot], 16)
            elif o.cc:
                inst.then_inc(self._dsem[o.sem_slot])
            elif o.flagged:
                inst.then_inc(self._esem[engine], 1)

    def build(self, stack, block):
        self.prepare(stack)
        sched = self

        @block.sync
        def _(sync):
            sched.emit_engine("sync", sync)

        @block.vector
        def _(vector):
            sched.emit_engine("vector", vector)

        @block.scalar
        def _(scalar):
            sched.emit_engine("scalar", scalar)

        @block.tensor
        def _(tensor):
            sched.emit_engine("tensor", tensor)

        @block.gpsimd
        def _(gpsimd):
            sched.emit_engine("gpsimd", gpsimd)


class SB:
    def __init__(self, t, reg):
        self.t = t
        self.r = reg

    def __getitem__(self, idx):
        return self.t[idx]


def _mk_emit(fn_name, args, kw):
    def emit(e):
        return getattr(e, fn_name)(*args, **kw)
    return emit


class Ctx:
    def __init__(self, nc, stk, sched):
        self.nc = nc
        self.stk = stk
        self.s = sched
        self.psum_banks = []
        self.psum_rr = 0
        self.cm = {}

    def sb(self, name, shape, dtype, stk=None):
        t = (stk or self.stk).enter_context(
            self.nc.sbuf_tensor("s_" + name, list(shape), dtype))
        r = Reg(name)
        r.writer = getattr(self, "_fence", None)
        return SB(t, r)

    def fence(self, tag="fence"):
        lasts = [st[-1] for st in self.s.streams.values() if st]
        hs = []
        for e in ENGINES:
            hs.append(self.s.op(e, _mk_emit("nop", (), {}), deps=lasts,
                                tag=tag + e))
        f = self.s.op("gpsimd", _mk_emit("nop", (), {}), deps=hs, tag=tag)
        self._fence = f
        return f

    def psum_init(self, n_f32=6, n_bf16=2):
        for i in range(n_f32):
            t = self.stk.enter_context(self.nc.psum_tensor(f"pb{i}", [128, 512], F32))
            self.psum_banks.append(SB(t, Reg(f"pb{i}")))
        self.psum_bf = []
        for i in range(n_bf16):
            t = self.stk.enter_context(self.nc.psum_tensor(f"pbh{i}", [128, 1024], BF16))
            self.psum_bf.append(SB(t, Reg(f"pbh{i}")))
        self.psum_bf_rr = 0

    def psum(self):
        b = self.psum_banks[self.psum_rr % len(self.psum_banks)]
        self.psum_rr += 1
        return b

    def psum_at(self, i):
        return self.psum_banks[i]

    def psumb(self):
        b = self.psum_bf[self.psum_bf_rr % len(self.psum_bf)]
        self.psum_bf_rr += 1
        return b

    def e(self, engine, fn, *args, reads=(), writes=(), deps=(), tag="", **kw):
        return self.s.rw(engine, _mk_emit(fn, args, kw),
                         reads=reads, writes=writes, deps=deps, tag=tag)

    def mm(self, out_ap, lhsT_ap, rhs_ap, start, stop, reads=(), writes=(),
           deps=(), tag=""):
        return self.s.rw("tensor", _mk_emit(
            "matmul", (out_ap, lhsT_ap, rhs_ap),
            dict(start=start, stop=stop, skip_group_check=True)),
            reads=reads, writes=writes, deps=deps, tag=tag)

    def dma(self, queue, out, in_, reads=(), writes=(), deps=(), tag="", **kw):
        return self.s.dma(queue, out, in_, reads=reads, writes=writes,
                          deps=deps, tag=tag, **kw)


def ts(i, n):
    return slice(i * n, (i + 1) * n)


def alloc_common(ctx):
    cm = ctx.cm
    cm["xt"] = ctx.sb("cm_xt", [128, NC, TT], F32)
    cm["xb"] = ctx.sb("cm_xb", [128, NC, TT], BF16)
    cm["lnst"] = ctx.sb("cm_lnst", [1, 4 * TT], F32)
    cm["absb"] = ctx.sb("cm_absb", [128, TT], F32)
    cm["bbsb"] = ctx.sb("cm_bbsb", [128, TT], F32)
    cm["ones_b"] = ctx.sb("cm_onesb", [128, 1], BF16)
    cm["ones1f"] = ctx.sb("cm_ones1f", [1, 128], F32)
    cm["e1"] = ctx.sb("cm_e1", [128, TT], F32)
    cm["ei"] = ctx.sb("cm_ei", [128, TT], mybir.dt.int32)
    cm["eb1"] = ctx.sb("cm_eb1", [128, TT], BF16)
    cm["e2"] = ctx.sb("cm_e2", [128, TT], F32)
    cm["e3"] = ctx.sb("cm_e3", [128, TT], F32)
    cm["e4"] = ctx.sb("cm_e4", [128, TT], F32)
    cm["st16"] = ctx.sb("cm_st16", [16, TT], F32)
    ctx.e("vector", "memset", cm["ones_b"][:], 1.0, writes=[cm["ones_b"].r])
    ctx.e("vector", "memset", cm["ones1f"][:], 1.0, writes=[cm["ones1f"].r])


def emit_ln(ctx, xdram, xdram_reg, xn, Tc):
    cm = ctx.cm
    starts = [i * TT for i in range(Tc // TT)]
    if Tc % TT:
        starts.append(Tc - TT)
    for it, a0 in enumerate(starts):
        w = TT
        xt, xb, st = cm["xt"], cm["xb"], cm["lnst"]
        ctx.dma("sync", xt[:, :, :w],
                xdram.rearrange("(c p) t -> p c t", p=128)[:, :, a0:a0 + w],
                reads=[xdram_reg], writes=[xt.r], tag=f"lnld{it}",
                **({"allow_slow_non_contiguous": True} if w < 16 else {}))
        ctx.e("vector", "tensor_copy", xb[:, :, :w], xt[:, :, :w],
              reads=[xt.r], writes=[xb.r])
        psA = ctx.psum()
        psB = ctx.psum()
        for c in range(NC):
            ctx.mm(psA[0:1, :w], cm["ones_b"][:, 0:1], xb[:, c, :w],
                   start=(c == 0), stop=(c == NC - 1),
                   reads=[cm["ones_b"].r, xb.r], writes=[psA.r])
        ctx.e("scalar", "activation", xb[:, :, :w], xb[:, :, :w], AF.Square,
              reads=[xb.r], writes=[xb.r])
        for c in range(NC):
            ctx.mm(psB[0:1, :w], cm["ones_b"][:, 0:1], xb[:, c, :w],
                   start=(c == 0), stop=(c == NC - 1),
                   reads=[cm["ones_b"].r, xb.r], writes=[psB.r])
        M = slice(0, w)
        V = slice(TT, TT + w)
        R = slice(2 * TT, 2 * TT + w)
        Bb = slice(3 * TT, 3 * TT + w)
        ctx.e("vector", "tensor_scalar_mul", st[0:1, M], psA[0:1, :w], 1.0 / 1024.0,
              reads=[psA.r], writes=[st.r])
        ctx.e("vector", "tensor_scalar_mul", st[0:1, V], psB[0:1, :w], 1.0 / 1024.0,
              reads=[psB.r], writes=[st.r])
        ctx.e("vector", "tensor_mul", st[0:1, R], st[0:1, M], st[0:1, M],
              reads=[st.r], writes=[st.r])
        ctx.e("vector", "tensor_sub", st[0:1, V], st[0:1, V], st[0:1, R],
              reads=[st.r], writes=[st.r])
        ctx.e("vector", "tensor_scalar_add", st[0:1, V], st[0:1, V], 1e-5,
              reads=[st.r], writes=[st.r])
        ctx.e("scalar", "activation", st[0:1, R], st[0:1, V], AF.Sqrt,
              reads=[st.r], writes=[st.r])
        ctx.e("vector", "reciprocal", st[0:1, Bb], st[0:1, R],
              reads=[st.r], writes=[st.r])
        ctx.e("vector", "scalar_tensor_tensor", st[0:1, M], st[0:1, M], -1.0,
              st[0:1, Bb], op0=ALU.mult, op1=ALU.mult,
              reads=[st.r], writes=[st.r])
        psC = ctx.psum()
        psD = ctx.psum()
        ctx.mm(psC[:, :w], cm["ones1f"][:], st[0:1, Bb], start=True, stop=True,
               reads=[cm["ones1f"].r, st.r], writes=[psC.r])
        ctx.mm(psD[:, :w], cm["ones1f"][:], st[0:1, M], start=True, stop=True,
               reads=[cm["ones1f"].r, st.r], writes=[psD.r])
        ab, bb = cm["absb"], cm["bbsb"]
        ctx.e("scalar", "copy", ab[:, :w], psC[:, :w], reads=[psC.r], writes=[ab.r])
        ctx.e("scalar", "copy", bb[:, :w], psD[:, :w], reads=[psD.r], writes=[bb.r])
        for c in range(NC):
            eng = "vector" if c % 2 == 0 else "gpsimd"
            eb = cm["e1"] if c % 2 == 0 else cm["e2"]
            ctx.e(eng, "tensor_mul", eb[:, :w], xt[:, c, :w], ab[:, :w],
                  reads=[xt.r, ab.r], writes=[eb.r])
            ctx.e(eng, "tensor_add", xn[:, c, a0:a0 + w], eb[:, :w], bb[:, :w],
                  reads=[eb.r, bb.r], writes=[xn.r])


def wslices_ap(wdram, c):
    return wdram.rearrange("(kc p) o -> p kc o", p=128)[:, :, ts(c, 128)]


def xfm(xdram, p=128):
    return xdram.rearrange("(c p) t -> p c t", p=p)


def build_block0(ctx, iv, ir, xin_d, xin_r, xout_d, xout_r, gin, gout, gr):
    s, cm = ctx.s, ctx.cm
    TC = T + 1
    e1, e2, e3, e4 = cm["e1"], cm["e2"], cm["e3"], cm["e4"]
    st16 = cm["st16"]

    xn = ctx.sb("b0_xn", [128, NC, TC], BF16)
    att = ctx.sb("b0_att", [128, NC, T], BF16)

    with contextlib.ExitStack() as ms:
        zsh = ctx.sb("b0_zsh", [64, T], BF16, ms)
        zg = ctx.sb("b0_zg", [64, T], BF16, ms)
        wsb = ctx.sb("b0_w", [16, T], F32, ms)
        asb = ctx.sb("b0_a", [16, T], F32, ms)
        mixsb = ctx.sb("b0_mix", [128, 6, NC], F32, ms)
        w0sb = ctx.sb("b0_w0s", [16, 1], F32, ms)
        a0sb = ctx.sb("b0_a0s", [16, 1], F32, ms)
        kkv = ctx.sb("b0_kkvs", [128, NC], F32, ms)
        rkv = ctx.sb("b0_rkvs", [128, NC], F32, ms)
        lnw = ctx.sb("b0_lnws", [128, NC], F32, ms)
        lnb = ctx.sb("b0_lnbs", [128, NC], F32, ms)
        pmk = ctx.sb("b0_pmks", [128, NC], F32, ms)
        ebcf = ctx.sb("b0_ebcfs", [16, NC * 128], F32, ms)
        etf = ctx.sb("b0_etfs", [128, NC * 16], F32, ms)
        w1sb = ctx.sb("b0_w1s", [128, NC, 64], BF16, ms)
        a1sb = ctx.sb("b0_a1s", [128, NC, 64], BF16, ms)
        g1sb = ctx.sb("b0_g1s", [128, NC, 64], BF16, ms)
        w2sb = ctx.sb("b0_w2s", [64, 16], BF16, ms)
        a2sb = ctx.sb("b0_a2s", [64, 16], BF16, ms)
        g2sb = ctx.sb("b0_g2s", [64, 1024], BF16, ms)
        rsb = ctx.sb("b0_r", [128, T], BF16, ms)
        kkn = ctx.sb("b0_kkn", [128, T], BF16, ms)
        vsb = ctx.sb("b0_v", [128, T], BF16, ms)
        kvsb = ctx.sb("b0_kv", [128, T], F32, ms)
        wbc = ctx.sb("b0_wbc", [128, T], F32, ms)
        states = ctx.sb("b0_states", [128, T], F32, ms)
        mxa = ctx.sb("b0_mxa", [128, TT], BF16, ms)
        mxb = ctx.sb("b0_mxb", [128, TT], BF16, ms)
        mxc = ctx.sb("b0_mxc", [128, TT], BF16, ms)
        sxa = ctx.sb("b0_sxa", [128, TT], BF16, ms)
        wk3 = [ctx.sb(f"b0_wk3_{i}", [128, NC, 3 * 128], BF16, ms) for i in range(2)]
        stin = [ctx.sb(f"b0_stin{i}", [128, 1], F32, ms) for i in range(4)]
        jS = [ctx.sb(f"b0_jS{i}", [128, 1], F32, ms) for i in range(3)]
        jA = [ctx.sb(f"b0_jA{i}", [128, 1], F32, ms) for i in range(3)]
        jA2 = [ctx.sb(f"b0_jA2{i}", [128, 1], F32, ms) for i in range(3)]
        jS2 = [ctx.sb(f"b0_jS2{i}", [128, 1], F32, ms) for i in range(3)]
        acol = ctx.sb("b0_acol", [128, 1], F32, ms)
        lsum = ctx.sb("b0_lsum", [16, 8], F32, ms)
        nsb = ctx.sb("b0_nsb", [16, TT], F32, ms)
        bcoef = ctx.sb("b0_bcoef", [16, TT], F32, ms)

        for sb, nm in [(mixsb, "b0_mixv"), (w0sb, "b0_w0v"), (a0sb, "b0_a0v"),
                       (kkv, "b0_kk"), (rkv, "b0_rk"), (lnw, "b0_lnw"),
                       (lnb, "b0_lnb"), (pmk, "pmask"), (ebcf, "ebcf"),
                       (etf, "etf")]:
            ctx.dma("sync", sb[:], iv[nm], reads=[ir[nm]], writes=[sb.r], tag=nm)
        for sb, nm in [(w1sb, "b0_w1"), (a1sb, "b0_a1"), (g1sb, "b0_g1")]:
            ctx.dma("gpsimd", sb[:], iv[nm].rearrange("(kc p) o -> p kc o", p=128),
                    reads=[ir[nm]], writes=[sb.r], tag=nm)
        for sb, nm in [(w2sb, "b0_w2"), (a2sb, "b0_a2"), (g2sb, "b0_g2")]:
            ctx.dma("gpsimd", sb[:], iv[nm], reads=[ir[nm]], writes=[sb.r], tag=nm)

        emit_ln(ctx, xin_d, xin_r, xn, TC)

        def mix_tiles(mjs, dsts, kc, tt_i, w=TT):
            a0 = tt_i * TT
            sxs = sxa
            ctx.e("gpsimd", "tensor_sub", sxs[:, :w], xn[:, kc, 1 + a0:1 + a0 + w],
                  xn[:, kc, a0:a0 + w], reads=[xn.r], writes=[sxs.r])
            for mj, dst in zip(mjs, dsts):
                ctx.e("vector", "scalar_tensor_tensor", dst[:, :w],
                      sxs[:, :w], mixsb[:, mj, kc:kc + 1],
                      xn[:, kc, 1 + a0:1 + a0 + w], op0=ALU.mult, op1=ALU.add,
                      reads=[sxs.r, mixsb.r, xn.r], writes=[dst.r])

        mz = [mxa, mxb, mxc]
        for tt_i in range(NTT):
            psw, psa, psg = ctx.psum(), ctx.psum(), ctx.psum()
            for kc in range(NC):
                mix_tiles([1, 4, 5], mz, kc, tt_i)
                for ps, w1x, mx in [(psw, w1sb, mz[0]), (psa, a1sb, mz[1]),
                                    (psg, g1sb, mz[2])]:
                    ctx.mm(ps[0:64, :], w1x[:, kc, :], mx[:, :],
                           start=(kc == 0), stop=(kc == NC - 1),
                           reads=[w1x.r, mx.r], writes=[ps.r])
            ctx.e("scalar", "activation", zsh[:, ts(tt_i, TT)], psw[0:64, :],
                  AF.Tanh, reads=[psw.r], writes=[zsh.r])
            ctx.e("scalar", "activation", zg[:, ts(tt_i, TT)], psg[0:64, :],
                  AF.Tanh, reads=[psg.r], writes=[zg.r])
            ctx.e("scalar", "activation", mz[1][0:64, 0:TT], psa[0:64, :],
                  AF.Tanh, reads=[psa.r], writes=[mz[1].r], tag="zatile")
            ps2 = ctx.psum()
            ctx.mm(ps2[0:16, :], a2sb[:], mz[1][0:64, 0:TT], start=True,
                   stop=True, reads=[a2sb.r, mz[1].r], writes=[ps2.r])
            ctx.e("scalar", "activation", st16[0:16, :], ps2[0:16, :],
                  AF.Sigmoid, bias=a0sb[:, 0:1], scale=1.0,
                  reads=[ps2.r, a0sb.r], writes=[st16.r])
            ctx.e("vector", "tensor_copy", asb[:, ts(tt_i, TT)], st16[0:16, :],
                  reads=[st16.r], writes=[asb.r])
        for tt_i in range(NTT):
            ps2 = ctx.psum()
            ctx.mm(ps2[0:16, :], w2sb[:], zsh[:, ts(tt_i, TT)], start=True,
                   stop=True, reads=[w2sb.r, zsh.r], writes=[ps2.r])
            ctx.e("scalar", "activation", st16[0:16, :], ps2[0:16, :],
                  AF.Sigmoid, bias=w0sb[:, 0:1], scale=1.0,
                  reads=[ps2.r, w0sb.r], writes=[st16.r])
            ctx.e("vector", "tensor_scalar_mul", wsb[:, ts(tt_i, TT)],
                  st16[0:16, :], math.exp(-0.5), reads=[st16.r], writes=[wsb.r])
        for tt_i in range(NTT):
            ctx.e("scalar", "activation", st16[0:16, :], wsb[:, ts(tt_i, TT)],
                  AF.Ln, reads=[wsb.r], writes=[st16.r])
            ctx.e("vector", "reduce_sum", lsum[0:16, tt_i + 1:tt_i + 2],
                  st16[0:16, :], axis=AXX, reads=[st16.r], writes=[lsum.r])
        ctx.e("vector", "tensor_add", lsum[0:16, 1:2], lsum[0:16, 1:2],
              lsum[0:16, 2:3], reads=[lsum.r], writes=[lsum.r])
        ctx.e("vector", "tensor_add", lsum[0:16, 3:4], lsum[0:16, 3:4],
              lsum[0:16, 4:5], reads=[lsum.r], writes=[lsum.r])
        ctx.e("vector", "tensor_add", lsum[0:16, 1:2], lsum[0:16, 1:2],
              lsum[0:16, 3:4], reads=[lsum.r], writes=[lsum.r])
        ctx.e("scalar", "activation", lsum[0:16, 0:1], lsum[0:16, 1:2],
              AF.Exp, reads=[lsum.r], writes=[lsum.r])

        if DUMP:
            dxn = ctx.nc.dram_tensor("d_xn", [1024, T + 1], F32, kind="ExternalOutput").ap()
            dxnr = Reg("d_xn")
            for cc in range(NC):
                ctx.dma("gpsimd", dxn.rearrange("(c p) t -> p c t", p=128)[:, cc, :],
                        xn[:, cc, :], reads=[xn.r], writes=[dxnr], tag=f"dxn{cc}")
            dw = ctx.nc.dram_tensor("d_w", [16, T], F32, kind="ExternalOutput").ap()
            dwr = Reg("d_w")
            ctx.dma("gpsimd", dw, wsb[:], reads=[wsb.r], writes=[dwr])
            da = ctx.nc.dram_tensor("d_a", [16, T], F32, kind="ExternalOutput").ap()
            dar = Reg("d_a")
            ctx.dma("gpsimd", da, asb[:], reads=[asb.r], writes=[dar])
            dzg = ctx.nc.dram_tensor("d_zg", [64, T], F32, kind="ExternalOutput").ap()
            dzgr = Reg("d_zg")
            ctx.dma("gpsimd", dzg, zg[:], reads=[zg.r], writes=[dzgr])
            dst_d = ctx.nc.dram_tensor("d_states", [1024, T], F32, kind="ExternalOutput").ap()
            dstr = Reg("d_states")
            dkv_d = ctx.nc.dram_tensor("d_kv", [1024, T], F32, kind="ExternalOutput").ap()
            dkvr = Reg("d_kv")
            dr_d = ctx.nc.dram_tensor("d_r", [1024, T], F32, kind="ExternalOutput").ap()
            drr = Reg("d_r")
            dkk_d = ctx.nc.dram_tensor("d_kkn", [1024, T], F32, kind="ExternalOutput").ap()
            dkkr = Reg("d_kkn")
            dv_d = ctx.nc.dram_tensor("d_v", [1024, T], F32, kind="ExternalOutput").ap()
            dvr = Reg("d_v")
            dsi_d = ctx.nc.dram_tensor("d_stin", [1024, NC], F32, kind="ExternalOutput").ap()
            dsir = Reg("d_stin")
            dls_d = ctx.nc.dram_tensor("d_lsum", [16, NC], F32, kind="ExternalOutput").ap()
            dlsr = Reg("d_lsum")
            ctx.dma("gpsimd", dls_d, lsum[0:16, :], reads=[lsum.r], writes=[dlsr])
        for c in range(NC):
            wk = wk3[c % 2]
            for j, nm in enumerate(["b0_Wr", "b0_Wk", "b0_Wv"]):
                ctx.dma("gpsimd", wk[:, :, ts(j, 128)], wslices_ap(iv[nm], c),
                        reads=[ir[nm]], writes=[wk.r], tag=f"{nm}{c}")
            ebc_c = ebcf[:, ts(c, 128)]
            et_c = etf[:, ts(c, 16)]
            for tt_i in range(NTT):
                ta = ts(tt_i, TT)
                psR, psK, psV = ctx.psum(), ctx.psum(), ctx.psum()
                m3 = [mz[0], mz[1], mz[2]]
                for kc in range(NC):
                    mix_tiles([0, 2, 3], m3, kc, tt_i)
                    for ps, j, mx in [(psR, 0, m3[0]), (psK, 1, m3[1]),
                                      (psV, 2, m3[2])]:
                        ctx.mm(ps[:, :], wk[:, kc, ts(j, 128)], mx[:, :],
                               start=(kc == 0), stop=(kc == NC - 1),
                               reads=[wk.r, mx.r], writes=[ps.r])
                ctx.e("scalar", "copy", rsb[:, ta], psR[:, :],
                      reads=[psR.r], writes=[rsb.r])
                ctx.e("scalar", "copy", vsb[:, ta], psV[:, :],
                      reads=[psV.r], writes=[vsb.r])
                ctx.e("vector", "tensor_scalar_mul", kkn[:, ta], psK[:, :],
                      kkv[:, c:c + 1], reads=[psK.r, kkv.r], writes=[kkn.r])
                ctx.e("scalar", "activation", e1[:, :], kkn[:, ta], AF.Square,
                      reads=[kkn.r], writes=[e1.r])
                psN = ctx.psum()
                ctx.mm(psN[0:16, :], et_c, e1[:, :], start=True, stop=True,
                       reads=[etf.r, e1.r], writes=[psN.r])
                ctx.e("scalar", "activation", nsb[0:16, :], psN[0:16, :],
                      AF.Sqrt, reads=[psN.r], writes=[nsb.r])
                ctx.e("vector", "tensor_scalar_max", nsb[0:16, :],
                      nsb[0:16, :], 1e-12, reads=[nsb.r], writes=[nsb.r])
                ctx.e("vector", "reciprocal", nsb[0:16, :], nsb[0:16, :],
                      reads=[nsb.r], writes=[nsb.r])
                psI = ctx.psum()
                ctx.mm(psI[:, :], ebc_c, nsb[0:16, :], start=True, stop=True,
                       reads=[ebcf.r, nsb.r], writes=[psI.r])
                ctx.e("vector", "tensor_mul", kkn[:, ta], kkn[:, ta], psI[:, :],
                      reads=[kkn.r, psI.r], writes=[kkn.r])
                psA = ctx.psum()
                ctx.mm(psA[:, :], ebc_c, asb[:, ta], start=True, stop=True,
                       reads=[ebcf.r, asb.r], writes=[psA.r])
                ctx.e("gpsimd", "tensor_mul", e2[:, :], kkn[:, ta], vsb[:, ta],
                      reads=[kkn.r, vsb.r], writes=[e2.r])
                ctx.e("vector", "tensor_mul", kvsb[:, ta], e2[:, :], psA[:, :],
                      reads=[e2.r, psA.r], writes=[kvsb.r])
                psW = ctx.psum()
                ctx.mm(psW[:, :], ebc_c, wsb[:, ta], start=True, stop=True,
                       reads=[ebcf.r, wsb.r], writes=[psW.r])
                ctx.e("scalar", "copy", wbc[:, ta], psW[:, :],
                      reads=[psW.r], writes=[wbc.r])
            ctx.e("vector", "tensor_tensor_scan", states[:], wbc[:], kvsb[:],
                  0.0, op0=ALU.mult, op1=ALU.add, reads=[wbc.r, kvsb.r],
                  writes=[states.r])
            psq = ctx.psum()
            ctx.mm(psq[:, 0:1], ebc_c, lsum[0:16, 0:1], start=True, stop=True,
                   reads=[ebcf.r, lsum.r], writes=[psq.r])
            ctx.e("vector", "tensor_copy", acol[:], psq[:, 0:1],
                  reads=[psq.r], writes=[acol.r])
            gi, go = gin[c], gout[c]
            gir, gor = gr[2 * c], gr[2 * c + 1]
            ctx.dma("scalar", gi[0:1, :].rearrange("a p -> p a"),
                    states[:, T - 1:T], reads=[states.r], writes=[gir],
                    tag=f"gS{c}")
            ctx.dma("scalar", gi[1:2, :].rearrange("a p -> p a"), acol[:],
                    reads=[acol.r, gir], writes=[gir], tag=f"gA{c}")
            ctx.s.rw("gpsimd", _mk_emit("collective_compute",
                     ("AllGather", ALU.bypass), dict(
                         replica_groups=ctx.groups, ins=[gi], outs=[go])),
                     reads=[gir], writes=[gor], cc=True, tag=f"ag0_{c}")
            ctx.e("vector", "memset", stin[0][:], 0.0, writes=[stin[0].r])
            for j in range(3):
                ctx.dma("scalar", jS[j][:],
                        go[2 * j:2 * j + 1, :].rearrange("a p -> p a"),
                        reads=[gor], writes=[jS[j].r], tag=f"ldS{c}_{j}")
                ctx.dma("scalar", jA[j][:],
                        go[2 * j + 1:2 * j + 2, :].rearrange("a p -> p a"),
                        reads=[gor], writes=[jA[j].r], tag=f"ldA{c}_{j}")
                ctx.e("vector", "tensor_mul", jA2[j][:], stin[j][:], jA[j][:],
                      reads=[stin[j].r, jA[j].r], writes=[jA2[j].r])
                ctx.e("vector", "tensor_add", jS2[j][:], jA2[j][:], jS[j][:],
                      reads=[jA2[j].r, jS[j].r], writes=[jS2[j].r])
                ctx.e("vector", "tensor_mul", jA2[j][:], jS2[j][:],
                      pmk[:, j:j + 1], reads=[jS2[j].r, pmk.r],
                      writes=[jA2[j].r])
                ctx.e("vector", "tensor_mul", jS2[j][:], stin[j][:],
                      pmk[:, 4 + j:5 + j], reads=[stin[j].r, pmk.r],
                      writes=[jS2[j].r])
                ctx.e("vector", "tensor_add", stin[j + 1][:], jA2[j][:],
                      jS2[j][:], reads=[jA2[j].r, jS2[j].r],
                      writes=[stin[j + 1].r])
            if DUMP:
                ctx.dma("gpsimd", dsi_d.rearrange("(cc p) o -> p cc o", p=128)[:, c, c:c + 1],
                        stin[3][:], reads=[stin[3].r], writes=[dsir], tag=f"dsi{c}",
                        allow_slow_non_contiguous=True)
            ctx.e("vector", "tensor_tensor_scan", states[:], wbc[:], kvsb[:],
                  stin[3][:, 0:1], op0=ALU.mult, op1=ALU.add,
                  reads=[wbc.r, kvsb.r, stin[3].r], writes=[states.r])
            if DUMP:
                ctx.dma("gpsimd", dst_d.rearrange("(cc p) t -> p cc t", p=128)[:, c, :],
                        states[:], reads=[states.r], writes=[dstr], tag=f"dst{c}")
                ctx.dma("gpsimd", dkv_d.rearrange("(cc p) t -> p cc t", p=128)[:, c, :],
                        kvsb[:], reads=[kvsb.r], writes=[dkvr], tag=f"dkv{c}")
                ctx.dma("gpsimd", dr_d.rearrange("(cc p) t -> p cc t", p=128)[:, c, :],
                        rsb[:], reads=[rsb.r], writes=[drr], tag=f"dr{c}")
                ctx.dma("gpsimd", dkk_d.rearrange("(cc p) t -> p cc t", p=128)[:, c, :],
                        kkn[:], reads=[kkn.r], writes=[dkkr], tag=f"dkk{c}")
                ctx.dma("gpsimd", dv_d.rearrange("(cc p) t -> p cc t", p=128)[:, c, :],
                        vsb[:], reads=[vsb.r], writes=[dvr], tag=f"dv{c}")
            for tt_i in range(NTT):
                ta = ts(tt_i, TT)
                ctx.e("gpsimd", "tensor_mul", e1[:, :], rsb[:, ta], kkn[:, ta],
                      reads=[rsb.r, kkn.r], writes=[e1.r])
                ctx.e("vector", "tensor_scalar_mul", e1[:, :], e1[:, :],
                      rkv[:, c:c + 1], reads=[rkv.r], writes=[e1.r])
                psB16 = ctx.psum()
                ctx.mm(psB16[0:16, :], et_c, e1[:, :], start=True, stop=True,
                       reads=[etf.r, e1.r], writes=[psB16.r])
                ctx.e("scalar", "copy", bcoef[0:16, :], psB16[0:16, :],
                      reads=[psB16.r], writes=[bcoef.r])
                psBC = ctx.psum()
                ctx.mm(psBC[:, :], ebc_c, bcoef[0:16, :], start=True, stop=True,
                       reads=[ebcf.r, bcoef.r], writes=[psBC.r])
                ctx.e("gpsimd", "tensor_mul", e2[:, :], states[:, ta],
                      rsb[:, ta], reads=[states.r, rsb.r], writes=[e2.r])
                ctx.e("vector", "tensor_mul", e3[:, :], vsb[:, ta], psBC[:, :],
                      reads=[vsb.r, psBC.r], writes=[e3.r])
                ctx.e("vector", "tensor_add", e2[:, :], e2[:, :], e3[:, :],
                      reads=[e3.r], writes=[e2.r])
                ctx.e("scalar", "activation", e3[:, :], e2[:, :], AF.Square,
                      reads=[e2.r], writes=[e3.r])
                psS = ctx.psum()
                ctx.mm(psS[0:16, :], et_c, e2[:, :], start=True, stop=True,
                       reads=[etf.r, e2.r], writes=[psS.r])
                psQ = ctx.psum()
                ctx.mm(psQ[0:16, :], et_c, e3[:, :], start=True, stop=True,
                       reads=[etf.r, e3.r], writes=[psQ.r])
                ctx.e("vector", "tensor_scalar_mul", nsb[0:16, :],
                      psS[0:16, :], 1.0 / 64.0, reads=[psS.r], writes=[nsb.r])
                ctx.e("vector", "tensor_scalar_mul", bcoef[0:16, :],
                      psQ[0:16, :], 1.0 / 64.0, reads=[psQ.r], writes=[bcoef.r])
                ctx.e("vector", "tensor_mul", st16[0:16, :], nsb[0:16, :],
                      nsb[0:16, :], reads=[nsb.r], writes=[st16.r])
                ctx.e("vector", "tensor_sub", bcoef[0:16, :], bcoef[0:16, :],
                      st16[0:16, :], reads=[st16.r], writes=[bcoef.r])
                ctx.e("vector", "tensor_scalar_add", bcoef[0:16, :],
                      bcoef[0:16, :], 1e-5, reads=[bcoef.r], writes=[bcoef.r])
                ctx.e("scalar", "activation", bcoef[0:16, :], bcoef[0:16, :],
                      AF.Sqrt, reads=[bcoef.r], writes=[bcoef.r])
                ctx.e("vector", "reciprocal", bcoef[0:16, :], bcoef[0:16, :],
                      reads=[bcoef.r], writes=[bcoef.r])
                ctx.e("vector", "scalar_tensor_tensor", st16[0:16, :],
                      nsb[0:16, :], -1.0, bcoef[0:16, :], op0=ALU.mult,
                      op1=ALU.mult, reads=[nsb.r, bcoef.r], writes=[st16.r])
                psAA = ctx.psum()
                ctx.mm(psAA[:, :], ebc_c, bcoef[0:16, :], start=True,
                       stop=True, reads=[ebcf.r, bcoef.r], writes=[psAA.r])
                psBB = ctx.psum()
                ctx.mm(psBB[:, :], ebc_c, st16[0:16, :], start=True, stop=True,
                       reads=[ebcf.r, st16.r], writes=[psBB.r])
                ctx.e("vector", "tensor_mul", e4[:, :], e2[:, :], psAA[:, :],
                      reads=[e2.r, psAA.r], writes=[e4.r])
                ctx.e("vector", "tensor_add", e4[:, :], e4[:, :], psBB[:, :],
                      reads=[psBB.r], writes=[e4.r])
                psG = ctx.psum()
                ctx.mm(psG[:, :], g2sb[:, ts(c, 128)], zg[:, ta], start=True,
                       stop=True, reads=[g2sb.r, zg.r], writes=[psG.r])
                ctx.e("scalar", "activation", e3[:, :], psG[:, :], AF.Sigmoid,
                      reads=[psG.r], writes=[e3.r])
                ctx.e("vector", "tensor_scalar", e4[:, :], e4[:, :],
                      lnw[:, c:c + 1], lnb[:, c:c + 1], op0=ALU.mult,
                      op1=ALU.add, reads=[lnw.r, lnb.r], writes=[e4.r])
                ctx.e("vector", "tensor_mul", att[:, c, ta], e4[:, :],
                      e3[:, :], reads=[e4.r, e3.r], writes=[att.r])
        ctx.fence("b0mix")
    build_b0_ffn(ctx, iv, ir, xn, att, xin_d, xin_r, xout_d, xout_r)
    return xn, att


def build_b0_ffn(ctx, iv, ir, xn, att, xin_d, xin_r, xout_d, xout_r):
    s, cm = ctx.s, ctx.cm
    e1 = cm["e1"]
    with contextlib.ExitStack() as fs:
        b1sb = ctx.sb("b0f_b1s", [128, 32], F32, fs)
        b2sb = ctx.sb("b0f_b2s", [128, NC], F32, fs)
        al1 = ctx.sb("b0f_al1", [128, 32], F32, fs)
        al2 = ctx.sb("b0f_al2", [128, NC], F32, fs)
        w1t = iv["b0f_w1t"].rearrange("(kc p) o -> p kc o", p=128)
        w2t = iv["b0f_w2t"].rearrange("(kc p) o -> p kc o", p=128)
        ctx.dma("sync", b1sb[:], iv["b0f_b1"], reads=[ir["b0f_b1"]], writes=[b1sb.r])
        ctx.dma("sync", b2sb[:], iv["b0f_b2"], reads=[ir["b0f_b2"]], writes=[b2sb.r])
        with contextlib.ExitStack() as als:
            aw = ctx.sb("b0f_aw", [128, 4096], BF16, als)
            awa = ctx.sb("b0f_awa", [128, 4096], BF16, als)
            psL = ctx.psum()
            for kc in range(NC):
                ctx.dma("gpsimd", aw[:], w1t[:, kc, :], reads=[ir["b0f_w1t"]],
                        writes=[aw.r], tag=f"alw1_{kc}")
                ctx.e("scalar", "activation", awa[:], aw[:], AF.Abs,
                      reads=[aw.r], writes=[awa.r])
                for hc in range(32):
                    ctx.mm(psL[:, hc:hc + 1], awa[:, ts(hc, 128)],
                           cm["ones_b"][:, 0:1],
                           start=(kc == 0 and hc == 0),
                           stop=(kc == NC - 1 and hc == 31),
                           reads=[awa.r, cm["ones_b"].r], writes=[psL.r])
            ctx.e("vector", "tensor_scalar_mul", al1[:, 0:32], psL[:, 0:32],
                  1.0 / 1024.0, reads=[psL.r], writes=[al1.r])
            psL2 = ctx.psum()
            for kc in range(32):
                ctx.dma("gpsimd", aw[:, 0:1024], w2t[:, kc, :],
                        reads=[ir["b0f_w2t"]], writes=[aw.r], tag=f"alw2_{kc}")
                ctx.e("scalar", "activation", awa[:, 0:1024], aw[:, 0:1024],
                      AF.Abs, reads=[aw.r], writes=[awa.r])
                for oc in range(NC):
                    ctx.mm(psL2[:, oc:oc + 1], awa[:, ts(oc, 128)],
                           cm["ones_b"][:, 0:1],
                           start=(kc == 0 and oc == 0),
                           stop=(kc == 31 and oc == NC - 1),
                           reads=[awa.r, cm["ones_b"].r], writes=[psL2.r])
            ctx.e("vector", "tensor_scalar_mul", al2[:, 0:NC], psL2[:, 0:NC],
                  1.0 / 4096.0, reads=[psL2.r], writes=[al2.r])
            ctx.fence("b0alpha")
        w1q = [ctx.sb(f"b0f_w1q{i}", [128, NC, 1024], BF16, fs) for i in range(2)]
        w2q = [ctx.sb(f"b0f_w2q{i}", [128, NC, 1024], BF16, fs) for i in range(2)]
        wos = [ctx.sb(f"b0_wos{i}", [128, NC, 128], BF16, fs) for i in range(2)]
        hsb = ctx.sb("b0f_h", [128, NC, TT], BF16, fs)
        oacc = ctx.sb("b0f_oacc", [128, NC, TT], F32, fs)
        xo1 = cm["e3"]

        for tt_i in range(NTT):
            ta = ts(tt_i, TT)
            xcols = slice(1 + tt_i * TT, 1 + (tt_i + 1) * TT)
            for quar in range(4):
                w1b = w1q[quar % 2]
                w2b = w2q[quar % 2]
                ctx.dma("gpsimd", w1b[:], w1t[:, :, ts(quar, 1024)],
                        reads=[ir["b0f_w1t"]], writes=[w1b.r], tag=f"w1q{tt_i}{quar}")
                ctx.e("scalar", "activation", w1b[:, :, :], w1b[:, :, :],
                      AF.Sign, reads=[w1b.r], writes=[w1b.r])
                ctx.dma("gpsimd", w2b[:], w2t[:, ts(quar, NC), :],
                        reads=[ir["b0f_w2t"]], writes=[w2b.r], tag=f"w2q{tt_i}{quar}")
                ctx.e("scalar", "activation", w2b[:, :, :], w2b[:, :, :],
                      AF.Sign, reads=[w2b.r], writes=[w2b.r])
                for hcl in range(NC):
                    hc = quar * NC + hcl
                    ps = ctx.psum()
                    for kc in range(NC):
                        ctx.mm(ps[:, :], w1b[:, kc, ts(hcl, 128)],
                               xn[:, kc, xcols], start=(kc == 0),
                               stop=(kc == NC - 1),
                               reads=[w1b.r, xn.r], writes=[ps.r])
                    ctx.e("scalar", "activation", hsb[:, hcl, :], ps[:, :],
                          AF.Gelu, bias=b1sb[:, hc:hc + 1],
                          scale=al1[:, hc:hc + 1],
                          reads=[ps.r, b1sb.r, al1.r], writes=[hsb.r])
                for oc in range(NC):
                    ps = ctx.psum()
                    for hcl in range(NC):
                        ctx.mm(ps[:, :], w2b[:, hcl, ts(oc, 128)],
                               hsb[:, hcl, :], start=(hcl == 0),
                               stop=(hcl == NC - 1),
                               reads=[w2b.r, hsb.r], writes=[ps.r])
                    if quar == 0:
                        ctx.e("vector", "tensor_scalar_mul", oacc[:, oc, :],
                              ps[:, :], al2[:, oc:oc + 1],
                              reads=[ps.r, al2.r], writes=[oacc.r])
                    else:
                        ctx.e("vector", "scalar_tensor_tensor", oacc[:, oc, :],
                              ps[:, :], al2[:, oc:oc + 1], oacc[:, oc, :],
                              op0=ALU.mult, op1=ALU.add,
                              reads=[ps.r, al2.r], writes=[oacc.r])
            for oc in range(NC):
                wob = wos[oc % 2]
                ctx.dma("gpsimd", wob[:], wslices_ap(iv["b0_Wo"], oc),
                        reads=[ir["b0_Wo"]], writes=[wob.r], tag=f"wo{tt_i}{oc}")
                ctx.dma("sync", xo1[:], xfm(xin_d)[:, oc, xcols],
                        reads=[xin_r], writes=[xo1.r], tag=f"xold{tt_i}{oc}")
                ps = ctx.psum()
                for kc in range(NC):
                    ctx.mm(ps[:, :], wob[:, kc, :], att[:, kc, ta],
                           start=(kc == 0), stop=(kc == NC - 1),
                           reads=[wob.r, att.r], writes=[ps.r])
                ctx.e("vector", "tensor_scalar", e1[:, :], ps[:, :],
                      b2sb[:, oc:oc + 1], None, op0=ALU.add,
                      reads=[ps.r, b2sb.r], writes=[e1.r])
                ctx.e("vector", "tensor_add", e1[:, :], e1[:, :],
                      oacc[:, oc, :], reads=[oacc.r], writes=[e1.r])
                ctx.e("vector", "tensor_add", xo1[:, :], e1[:, :],
                      xo1[:, :], reads=[e1.r], writes=[xo1.r])
                ctx.dma("sync", xfm(xout_d)[:, oc, ta], xo1[:],
                        reads=[xo1.r], writes=[xout_r], tag=f"xo{tt_i}{oc}")
        ctx.fence("b0ffn")


def build_block1(ctx, iv, ir, xin_d, xin_r, xout_d, xout_r, g1i, g1o, g1r):
    s, cm = ctx.s, ctx.cm
    e1, e2, e3, e4 = cm["e1"], cm["e2"], cm["e3"], cm["e4"]

    xn = ctx.sb("b1_xn", [128, NC, T], BF16)
    att = ctx.sb("b1_att", [128, NC, T], BF16)
    normnd_d = ctx.nc.dram_tensor("b1_normnd", [16, T], F32).ap()
    normnd_r = Reg("b1_normnd")

    mixstk = contextlib.ExitStack()
    qp = ctx.sb("b1_qp", [128, NC, T], BF16, mixstk)
    ebcf = ctx.sb("b1_ebcf", [16, NC * 128], F32, mixstk)
    pmk = ctx.sb("b1_pmk", [128, NC], F32, mixstk)
    with contextlib.ExitStack() as ms:
        qb = ctx.sb("b1_qb", [128, NC], F32, ms)
        kb = ctx.sb("b1_kb", [128, NC], F32, ms)
        vb = ctx.sb("b1_vb", [128, NC], F32, ms)
        mskd = ctx.sb("b1_mskd", [128, 256], F32, ms)
        kp = ctx.sb("b1_kp", [128, T], BF16, ms)
        vsb = ctx.sb("b1_v", [128, T], BF16, ms)
        ktm = ctx.sb("b1_ktm", [128, 128], BF16, ms)
        vtm = ctx.sb("b1_vtm", [128, 130], BF16, ms)
        p01 = ctx.sb("b1_p01", [128, 256], BF16, ms)
        nf = ctx.sb("b1_nf", [1, 512], F32, ms)
        Sm = [ctx.sb(f"b1_Sm_{i}", [128, 65], F32, ms) for i in range(2)]
        Smb = ctx.sb("b1_Smb", [128, 65], BF16, ms)
        nrow = ctx.sb("b1_nrow", [1, 2, T], F32, ms)
        wk3 = [ctx.sb(f"b1_wk3_{i}", [128, NC, 3 * 128], BF16, ms)
               for i in range(2)]
        idb = ctx.sb("b1_idb", [128, 128], BF16, ms)

        for sb, nm in [(ebcf, "ebcf"), (pmk, "pmask"), (qb, "b1_qb"),
                       (kb, "b1_kb"), (vb, "b1_vb"), (mskd, "l1mask")]:
            ctx.dma("sync", sb[:], iv[nm], reads=[ir[nm]], writes=[sb.r], tag=nm)
        ctx.dma("gpsimd", idb[:], iv["idf"], reads=[ir["idf"]], writes=[idb.r])
        ctx.e("vector", "memset", vtm[:, 64:65], 1.0, writes=[vtm.r])
        ctx.e("vector", "memset", vtm[:, 129:130], 1.0, writes=[vtm.r])

        emit_ln(ctx, xin_d, xin_r, xn, T)

        qkvw = iv["b1_qkvw"]
        for c in (range(NC) if B1_STAGE >= 2 else []):
            wk = wk3[c % 2]
            for j, off in enumerate([c * 128, 1024 + c * 128, 2048 + c * 128]):
                ctx.dma("gpsimd", wk[:, :, ts(j, 128)],
                        qkvw.rearrange("(kc p) o -> p kc o", p=128)[:, :, off:off + 128],
                        reads=[ir["b1_qkvw"]], writes=[wk.r], tag=f"b1w{c}{j}")
            for tt_i in range(NTT):
                ta = ts(tt_i, TT)
                psQ, psK, psV = ctx.psum(), ctx.psum(), ctx.psum()
                for kc in range(NC):
                    for ps, j in [(psQ, 0), (psK, 1), (psV, 2)]:
                        ctx.mm(ps[:, :], wk[:, kc, ts(j, 128)], xn[:, kc, ta],
                               start=(kc == 0), stop=(kc == NC - 1),
                               reads=[wk.r, xn.r], writes=[ps.r])
                for ps, bcol, dst in [(psQ, qb, None), (psK, kb, None)]:
                    ctx.e("vector", "tensor_scalar", e1[:, :], ps[:, :],
                          bcol[:, c:c + 1], None, op0=ALU.add,
                          reads=[ps.r, bcol.r], writes=[e1.r])
                    ctx.e("vector", "tensor_scalar_min", e2[:, :], e1[:, :],
                          0.0, reads=[e1.r], writes=[e2.r])
                    ctx.e("scalar", "activation", e3[:, :], e2[:, :], AF.Exp,
                          reads=[e2.r], writes=[e3.r])
                    ctx.e("vector", "tensor_scalar_max", e4[:, :], e1[:, :],
                          0.0, reads=[e1.r], writes=[e4.r])
                    out_ap = qp[:, c, ta] if ps is psQ else kp[:, ta]
                    out_rg = qp.r if ps is psQ else kp.r
                    ctx.e("vector", "tensor_add", out_ap, e3[:, :], e4[:, :],
                          reads=[e3.r, e4.r], writes=[out_rg])
                ctx.e("vector", "tensor_scalar", vsb[:, ta], psV[:, :],
                      vb[:, c:c + 1], None, op0=ALU.add,
                      reads=[psV.r, vb.r], writes=[vsb.r])
            ctx.e("vector", "memset", Sm[0][:], 0.0, writes=[Sm[0].r])
            ctx.e("vector", "memset", Smb[:], 0.0, writes=[Smb.r])
            for tc in (range(T // 128) if B1_STAGE >= 3 else []):
                tb = ts(tc, 128)
                pst = ctx.psumb()
                ctx.e("tensor", "transpose", pst[:, 0:128], kp[:, tb], idb[:],
                      reads=[kp.r, idb.r], writes=[pst.r])
                ctx.e("tensor", "transpose", pst[:, 128:256], vsb[:, tb], idb[:],
                      reads=[vsb.r, idb.r], writes=[pst.r])
                ctx.e("scalar", "copy", ktm[:, :], pst[:, 0:128],
                      reads=[pst.r], writes=[ktm.r])
                ctx.e("scalar", "copy", vtm[:, 0:64], pst[:, 128:192],
                      reads=[pst.r], writes=[vtm.r])
                ctx.e("scalar", "copy", vtm[:, 65:129], pst[:, 192:256],
                      reads=[pst.r], writes=[vtm.r])
                if B1_SUB < 2:
                    continue
                psA0 = ctx.psum()
                psA1 = ctx.psum()
                for h, psA in [(0, psA0), (1, psA1)]:
                    ctx.mm(psA[:, 0:128], kp[ts(h, 64), tb],
                           qp[ts(h, 64), c, tb], start=True, stop=True,
                           reads=[kp.r, qp.r], writes=[psA.r])
                for h, psA in [(0, psA0), (1, psA1)]:
                    ctx.e("vector", "tensor_mul", p01[:, ts(h, 128)],
                          psA[:, 0:128], mskd[:, ts(h, 128)],
                          reads=[psA.r, mskd.r], writes=[p01.r])
                if B1_SUB < 3:
                    continue
                psC = [ctx.psum(), ctx.psum()]
                for h in range(2):
                    ctx.mm(psC[h][0:65, 0:128], vtm[:, 65 * h:65 * h + 65],
                           p01[:, ts(h, 128)], start=True, stop=False,
                           reads=[vtm.r, p01.r], writes=[psC[h].r])
                    ctx.mm(psC[h][0:65, 0:128], Smb[ts(h, 64), :],
                           qp[ts(h, 64), c, tb], start=False, stop=True,
                           reads=[Smb.r, qp.r], writes=[psC[h].r])
                for h in range(2):
                    ctx.e("vector", "tensor_copy", att[ts(h, 64), c, tb],
                          psC[h][0:64, 0:128], reads=[psC[h].r], writes=[att.r])
                    ctx.e("vector", "tensor_copy", nrow[0:1, h, tb],
                          psC[h][64:65, 0:128], reads=[psC[h].r],
                          writes=[nrow.r])
                if B1_SUB < 4:
                    continue
                psSa = ctx.psum()
                psSb = ctx.psum()
                for h, psS in [(0, psSa), (1, psSb)]:
                    ctx.mm(psS[ts(h, 64), 0:65], ktm[:, ts(h, 64)],
                           vtm[:, 65 * h:65 * h + 65], start=True,
                           stop=True, reads=[ktm.r, vtm.r],
                           writes=[psS.r])
                src = Sm[tc % 2]
                dst = Sm[(tc + 1) % 2]
                for h, psS in [(0, psSa), (1, psSb)]:
                    ctx.e("vector", "tensor_add", dst[ts(h, 64), :],
                          src[ts(h, 64), :], psS[ts(h, 64), 0:65],
                          reads=[src.r, psS.r], writes=[dst.r])
                ctx.e("vector", "tensor_copy", Smb[:], dst[:],
                      reads=[dst.r], writes=[Smb.r])
            fin = T // 128 % 2
            for h in range(2):
                ctx.dma("scalar",
                        g1i[2 * c + h:2 * c + h + 1, :].rearrange(
                            "a (p q) -> (a p) q", p=64),
                        Sm[fin][ts(h, 64), :], reads=[Sm[fin].r],
                        writes=[g1r[0]], tag=f"b1gi{c}{h}")
            ctx.dma("gpsimd", normnd_d[2 * c:2 * c + 2, :], nrow[0:1, :, :],
                    reads=[nrow.r], writes=[normnd_r], tag=f"b1nd{c}")
        if DUMP:
            dqp = ctx.nc.dram_tensor("d_qp", [1024, T], F32, kind="ExternalOutput").ap()
            dkp = ctx.nc.dram_tensor("d_kp", [1024, T], F32, kind="ExternalOutput").ap()
            dvv = ctx.nc.dram_tensor("d_v", [1024, T], F32, kind="ExternalOutput").ap()
            dcx = ctx.nc.dram_tensor("d_ctxnd", [1024, T], F32, kind="ExternalOutput").ap()
            dnn = ctx.nc.dram_tensor("d_nnd", [16, T], F32, kind="ExternalOutput").ap()
            rgs = [Reg(x) for x in "abcde"]
            for cc2 in range(NC):
                ctx.dma("gpsimd", dqp.rearrange("(c p) t -> p c t", p=128)[:, cc2, :],
                        qp[:, cc2, :], reads=[qp.r], writes=[rgs[0]], tag=f"dq{cc2}")
                ctx.dma("gpsimd", dcx.rearrange("(c p) t -> p c t", p=128)[:, cc2, :],
                        att[:, cc2, :], reads=[att.r], writes=[rgs[3]], tag=f"dc{cc2}")
            ctx.dma("gpsimd", dnn, normnd_d, reads=[normnd_r], writes=[rgs[4]], tag="dn")
        if B1_STAGE < 3:
            for h2 in range(16):
                ctx.e("vector", "memset", Sm[0][:], 0.0, writes=[Sm[0].r])
        ctx.s.rw("gpsimd", _mk_emit("collective_compute",
                 ("AllGather", ALU.bypass), dict(
                     replica_groups=ctx.groups, ins=[g1i], outs=[g1o])),
                 reads=[g1r[0]], writes=[g1r[1]], cc=True, tag="ag1")
        ctx.fence("b1tc")
    with contextlib.ExitStack() as ms:
        Sacc = [ctx.sb(f"b1_Sacc{i}", [64, 16, 65], F32, ms) for i in range(2)]
        Sg = ctx.sb("b1_Sg", [64, 16, 65], F32, ms)
        ctx.e("vector", "memset", Sacc[0][:], 0.0, writes=[Sacc[0].r])
        for j in range(3):
            ctx.dma("scalar", Sg[:],
                    g1o[16 * j:16 * (j + 1), :].rearrange(
                        "h (p q) -> p h q", p=64),
                    reads=[g1r[1]], writes=[Sg.r], tag=f"b1ld{j}")
            src, dst = Sacc[j % 2], Sacc[(j + 1) % 2]
            ctx.e("vector", "scalar_tensor_tensor", dst[:], Sg[:],
                  pmk[0:64, j:j + 1], src[:], op0=ALU.mult, op1=ALU.add,
                  reads=[Sg.r, pmk.r, src.r], writes=[dst.r])
        SaccF = Sacc[3 % 2]
        mca = ctx.sb("b1_mca", [128, NC, 16], BF16, ms)
        Scm = ctx.sb("b1_Scm", [128, 65], BF16, ms)
        normsb = ctx.sb("b1_normsb", [16, T], F32, ms)
        ninv = ctx.sb("b1_ninv", [16, TT], F32, ms)
        ctx.e("vector", "memset", mca[:], 0.0, writes=[mca.r])
        for h in range(16):
            ctx.e("vector", "tensor_copy", mca[ts(h % 2, 64), h // 2, h:h + 1],
                  SaccF[:, h, 64:65], reads=[SaccF.r], writes=[mca.r])
        ctx.dma("sync", normsb[:], normnd_d, reads=[normnd_r],
                writes=[normsb.r], tag="b1nl")
        for tt_i in (range(NTT) if B1_STAGE >= 4 else []):
            ta = ts(tt_i, TT)
            psM = ctx.psum()
            for c in range(NC):
                ctx.mm(psM[0:16, :], mca[:, c, :], qp[:, c, ta],
                       start=(c == 0), stop=(c == NC - 1),
                       reads=[mca.r, qp.r], writes=[psM.r])
            ctx.e("vector", "tensor_add", ninv[0:16, :], psM[0:16, :],
                  normsb[0:16, ta], reads=[psM.r, normsb.r], writes=[ninv.r])
            ctx.e("vector", "tensor_scalar_add", ninv[0:16, :], ninv[0:16, :],
                  1e-6, reads=[], writes=[ninv.r])
            ctx.e("vector", "reciprocal", ninv[0:16, :], ninv[0:16, :],
                  reads=[], writes=[ninv.r])
            for c in range(NC):
                for h in range(2):
                    ctx.e("vector", "tensor_copy", Scm[ts(h, 64), :],
                          SaccF[:, 2 * c + h, :], reads=[SaccF.r],
                          writes=[Scm.r])
                psI = ctx.psum()
                ctx.mm(psI[:, :], ebcf[:, ts(c, 128)], ninv[0:16, :],
                       start=True, stop=True, reads=[ebcf.r, ninv.r],
                       writes=[psI.r])
                ctx.e("scalar", "copy", e4[:, :], psI[:, :],
                      reads=[psI.r], writes=[e4.r])
                psK2a = ctx.psum()
                psK2b = ctx.psum()
                for h, psK2 in [(0, psK2a), (1, psK2b)]:
                    ctx.mm(psK2[0:64, :], Scm[ts(h, 64), 0:64],
                           qp[ts(h, 64), c, ta], start=True, stop=True,
                           reads=[Scm.r, qp.r], writes=[psK2.r])
                for h, psK2 in [(0, psK2a), (1, psK2b)]:
                    ctx.e("vector", "tensor_add", e1[ts(h, 64), :],
                          att[ts(h, 64), c, ta], psK2[0:64, :],
                          reads=[att.r, psK2.r], writes=[e1.r])
                ctx.e("vector", "tensor_mul", att[:, c, ta], e1[:, :],
                      e4[:, :], reads=[e1.r, e4.r], writes=[att.r])
        ctx.fence("b1mix")
    mixstk.close()
    build_b1_ffn(ctx, iv, ir, xn, att, xin_d, xin_r, xout_d, xout_r)
    return xn, att


def build_b1_ffn(ctx, iv, ir, xn, att, xin_d, xin_r, xout_d, xout_r):
    s, cm = ctx.s, ctx.cm
    e1, e2, e3 = cm["e1"], cm["e2"], cm["e3"]
    with contextlib.ExitStack() as fs:
        b1sb = ctx.sb("b1f_b1s", [128, 32], F32, fs)
        b2sb = ctx.sb("b1f_b2s", [128, NC], F32, fs)
        obsb = ctx.sb("b1f_obs", [128, NC], F32, fs)
        thsb = ctx.sb("b1f_th", [128, 32], F32, fs)
        wrsb = ctx.sb("b1f_wr", [128, 32], F32, fs)
        wisb = ctx.sb("b1f_wi", [128, 32], F32, fs)
        coef = ctx.sb("b1f_coef", [128, 32], F32, fs)
        w1q = [ctx.sb(f"b1f_w1q{i}", [128, NC, 1024], BF16, fs) for i in range(2)]
        w2q = [ctx.sb(f"b1f_w2q{i}", [128, NC, 1024], BF16, fs) for i in range(2)]
        wos = [ctx.sb(f"b1_wos{i}", [128, NC, 128], BF16, fs) for i in range(2)]
        hsb = ctx.sb("b1f_h", [128, NC, TT], BF16, fs)
        oacc = ctx.sb("b1f_oacc", [128, NC, TT], F32, fs)
        xo1 = cm["e3"]

        for sb, nm in [(b1sb, "b1f_b1"), (b2sb, "b1f_b2"), (obsb, "b1_ob"),
                       (thsb, "b1f_th"), (wrsb, "b1f_wr"), (wisb, "b1f_wi")]:
            ctx.dma("sync", sb[:], iv[nm], reads=[ir[nm]], writes=[sb.r], tag=nm)
        ei = cm["ei"]

        def rr_sin(dst, shift):
            ctx.e("vector", "tensor_scalar", e1[:, 0:32], thsb[:], shift,
                  1.0 / (2.0 * math.pi), op0=ALU.add, op1=ALU.mult,
                  reads=[thsb.r], writes=[e1.r])
            ctx.e("vector", "tensor_copy", ei[:, 0:32], e1[:, 0:32],
                  reads=[e1.r], writes=[ei.r])
            ctx.e("vector", "tensor_copy", e4[:, 0:32], ei[:, 0:32],
                  reads=[ei.r], writes=[e4.r])
            ctx.e("vector", "tensor_sub", e4[:, 0:32], e1[:, 0:32],
                  e4[:, 0:32], reads=[e1.r], writes=[e4.r])
            ctx.e("vector", "tensor_scalar_mul", e4[:, 0:32], e4[:, 0:32],
                  2.0 * math.pi, reads=[], writes=[e4.r])
            ctx.e("scalar", "activation", dst[:, 0:32], e4[:, 0:32], AF.Sin,
                  reads=[e4.r], writes=[dst.r])
        e4 = cm["e4"]
        rr_sin(e2, 0.0)
        rr_sin(e3, math.pi / 2.0)
        ctx.e("vector", "tensor_mul", e3[:, 0:32], e3[:, 0:32], wrsb[:],
              reads=[wrsb.r], writes=[e3.r])
        ctx.e("vector", "tensor_mul", e1[:, 0:32], e2[:, 0:32], wisb[:],
              reads=[e2.r, wisb.r], writes=[e1.r])
        ctx.e("vector", "tensor_add", e3[:, 0:32], e3[:, 0:32], e1[:, 0:32],
              reads=[e1.r], writes=[e3.r])
        ctx.e("vector", "tensor_mul", e3[:, 0:32], e3[:, 0:32], wisb[:],
              reads=[wisb.r], writes=[e3.r])
        ctx.e("vector", "tensor_scalar_mul", coef[:], e3[:, 0:32], 0.1,
              reads=[e3.r], writes=[coef.r])

        w1t = iv["b1f_w1"].rearrange("(kc p) o -> p kc o", p=128)
        w2t = iv["b1f_w2"].rearrange("(kc p) o -> p kc o", p=128)
        for tt_i in range(NTT):
            ta = ts(tt_i, TT)
            for quar in (range(4) if B1_STAGE >= 5 else [0]):
                w1b = w1q[quar % 2]
                w2b = w2q[quar % 2]
                ctx.dma("gpsimd", w1b[:], w1t[:, :, ts(quar, 1024)],
                        reads=[ir["b1f_w1"]], writes=[w1b.r], tag=f"b1w1{tt_i}{quar}")
                ctx.dma("gpsimd", w2b[:], w2t[:, ts(quar, NC), :],
                        reads=[ir["b1f_w2"]], writes=[w2b.r], tag=f"b1w2{tt_i}{quar}")
                for hcl in range(NC):
                    hc = quar * NC + hcl
                    ps = ctx.psum()
                    for kc in range(NC):
                        ctx.mm(ps[:, :], w1b[:, kc, ts(hcl, 128)],
                               xn[:, kc, ta], start=(kc == 0),
                               stop=(kc == NC - 1),
                               reads=[w1b.r, xn.r], writes=[ps.r])
                    ctx.e("vector", "tensor_scalar", e1[:, :], ps[:, :],
                          b1sb[:, hc:hc + 1], None, op0=ALU.add,
                          reads=[ps.r, b1sb.r], writes=[e1.r])
                    ctx.e("vector", "tensor_scalar", e2[:, :], e1[:, :],
                          0.1, thsb[:, hc:hc + 1], op0=ALU.mult, op1=ALU.add,
                          reads=[e1.r, thsb.r], writes=[e2.r])
                    ctx.e("vector", "tensor_scalar_mul", e3[:, :], e2[:, :],
                          1.0 / (2.0 * math.pi), reads=[e2.r], writes=[e3.r])
                    ctx.e("vector", "tensor_copy", cm["ei"][:, :], e3[:, :],
                          reads=[e3.r], writes=[cm["ei"].r])
                    ctx.e("vector", "tensor_copy", e2[:, :], cm["ei"][:, :],
                          reads=[cm["ei"].r], writes=[e2.r])
                    ctx.e("vector", "tensor_sub", e2[:, :], e3[:, :],
                          e2[:, :], reads=[e3.r], writes=[e2.r])
                    ctx.e("vector", "tensor_scalar_mul", e2[:, :], e2[:, :],
                          2.0 * math.pi, reads=[], writes=[e2.r])
                    ctx.e("scalar", "activation", e2[:, :], e2[:, :], AF.Sin,
                          reads=[], writes=[e2.r])
                    ctx.e("vector", "scalar_tensor_tensor", e3[:, :],
                          e2[:, :], coef[:, hc:hc + 1], e1[:, :],
                          op0=ALU.mult, op1=ALU.add,
                          reads=[e2.r, coef.r, e1.r], writes=[e3.r])
                    ctx.e("vector", "tensor_scalar_max", hsb[:, hcl, :],
                          e3[:, :], 0.0, reads=[e3.r], writes=[hsb.r])
                for oc in range(NC):
                    ps = ctx.psum()
                    for hcl in range(NC):
                        ctx.mm(ps[:, :], w2b[:, hcl, ts(oc, 128)],
                               hsb[:, hcl, :], start=(hcl == 0),
                               stop=(hcl == NC - 1),
                               reads=[w2b.r, hsb.r], writes=[ps.r])
                    if quar == 0:
                        ctx.e("vector", "tensor_scalar", oacc[:, oc, :],
                              ps[:, :], b2sb[:, oc:oc + 1], None, op0=ALU.add,
                              reads=[ps.r, b2sb.r], writes=[oacc.r])
                    else:
                        ctx.e("vector", "tensor_add", oacc[:, oc, :],
                              ps[:, :], oacc[:, oc, :],
                              reads=[ps.r], writes=[oacc.r])
            for oc in range(NC):
                wob = wos[oc % 2]
                ctx.dma("gpsimd", wob[:], wslices_ap(iv["b1_outw"], oc),
                        reads=[ir["b1_outw"]], writes=[wob.r], tag=f"b1wo{tt_i}{oc}")
                ctx.dma("sync", xo1[:], xfm(xin_d)[:, oc, ta],
                        reads=[xin_r], writes=[xo1.r], tag=f"b1xo{tt_i}{oc}")
                ps = ctx.psum()
                for kc in range(NC):
                    ctx.mm(ps[:, :], wob[:, kc, :], att[:, kc, ta],
                           start=(kc == 0), stop=(kc == NC - 1),
                           reads=[wob.r, att.r], writes=[ps.r])
                ctx.e("vector", "tensor_scalar", e1[:, :], ps[:, :],
                      obsb[:, oc:oc + 1], None, op0=ALU.add,
                      reads=[ps.r, obsb.r], writes=[e1.r])
                ctx.e("vector", "tensor_add", e1[:, :], e1[:, :],
                      oacc[:, oc, :], reads=[oacc.r], writes=[e1.r])
                ctx.e("vector", "tensor_add", xo1[:, :], e1[:, :],
                      xo1[:, :], reads=[e1.r], writes=[xo1.r])
                ctx.dma("sync", xfm(xout_d)[:, oc, ta], xo1[:],
                        reads=[xo1.r], writes=[xout_r], tag=f"b1xw{tt_i}{oc}")
        ctx.fence("b1ffn")


def build_block2(ctx, iv, ir, xin_d, xin_r, xout_d, xout_r,
                 gki, gko, gvi, gvo, gsi, gso, g2r):
    s, cm = ctx.s, ctx.cm
    e1, e2 = cm["e1"], cm["e2"]
    eb1 = cm["eb1"]

    xn = ctx.sb("b2_xn", [128, NC, T], BF16)
    att = ctx.sb("b2_att", [128, NC, T], BF16)
    bmax = ctx.sb("b2_bmax", [16, 4], F32)
    q_d = ctx.nc.dram_tensor("b2_qd", [1024, T], BF16).ap()
    q_dr = Reg("b2_qd")

    with contextlib.ExitStack() as ms:
        aqT = ctx.sb("b2_aqT", [128, NC, 48], BF16, ms)
        bqT = ctx.sb("b2_bqT", [16, 3 * 1024], BF16, ms)
        lra = ctx.sb("b2_lra", [16, 3 * T], BF16, ms)
        wk3 = [ctx.sb(f"b2_wk3_{i}", [128, NC, 3 * 128], BF16, ms)
               for i in range(2)]
        vsb = ctx.sb("b2_vs", [128, T], BF16, ms)
        idb = ctx.sb("b2_idb", [128, 128], BF16, ms)
        qst = ctx.sb("b2_qst", [16, 2], F32, ms)
        qmx = ctx.sb("b2_qmx", [16, 32], F32, ms)
        kmx = ctx.sb("b2_kmx", [16, 32], F32, ms)
        gmx = ctx.sb("b2_gmx", [16, 4], F32, ms)
        etf = ctx.sb("b2_etf", [128, NC * 16], F32, ms)

        for dst, nm, sl in [(aqT, "b2_AqT", ts(0, 16)), (aqT, "b2_AkT", ts(1, 16)),
                            (aqT, "b2_AvT", ts(2, 16))]:
            ctx.dma("gpsimd", dst[:, :, sl],
                    iv[nm].rearrange("(kc p) o -> p kc o", p=128),
                    reads=[ir[nm]], writes=[dst.r], tag=nm)
        for nm, sl in [("b2_BqT", ts(0, 1024)), ("b2_BkT", ts(1, 1024)),
                       ("b2_BvT", ts(2, 1024))]:
            ctx.dma("gpsimd", bqT[:, sl], iv[nm], reads=[ir[nm]],
                    writes=[bqT.r], tag=nm)
        ctx.dma("gpsimd", idb[:], iv["idf"], reads=[ir["idf"]], writes=[idb.r])
        ctx.dma("sync", etf[:], iv["etf"], reads=[ir["etf"]], writes=[etf.r])
        ctx.e("vector", "memset", bmax[:], 0.0, writes=[bmax.r])

        emit_ln(ctx, xin_d, xin_r, xn, T)

        for j in range(3):
            for tt_i in range(NTT):
                ps = ctx.psum()
                for kc in range(NC):
                    ctx.mm(ps[0:16, :], aqT[:, kc, ts(j, 16)],
                           xn[:, kc, ts(tt_i, TT)],
                           start=(kc == 0), stop=(kc == NC - 1),
                           reads=[aqT.r, xn.r], writes=[ps.r])
                ctx.e("vector", "tensor_scalar_mul",
                      lra[0:16, j * T + tt_i * TT: j * T + (tt_i + 1) * TT],
                      ps[0:16, :], 1.0 / 16.0, reads=[ps.r], writes=[lra.r])

        for c in (range(NC) if B2_SUB >= 2 else []):
            wk = wk3[c % 2]
            for j, nm in enumerate(["b2_WqT", "b2_WkT", "b2_WvT"]):
                ctx.dma("gpsimd", wk[:, :, ts(j, 128)], wslices_ap(iv[nm], c),
                        reads=[ir[nm]], writes=[wk.r], tag=f"{nm}{c}")
            for tt_i in range(NTT):
                ta = ts(tt_i, TT)
                pj = [ctx.psum(), ctx.psum(), ctx.psum()]
                for kc in range(NC):
                    for j in range(3):
                        ctx.mm(pj[j][:, :], wk[:, kc, ts(j, 128)], xn[:, kc, ta],
                               start=(kc == 0), stop=False,
                               reads=[wk.r, xn.r], writes=[pj[j].r])
                for j in range(3):
                    ctx.mm(pj[j][:, :],
                           bqT[:, j * 1024 + c * 128: j * 1024 + (c + 1) * 128],
                           lra[0:16, j * T + tt_i * TT: j * T + (tt_i + 1) * TT],
                           start=False, stop=True, reads=[bqT.r, lra.r],
                           writes=[pj[j].r])
                def stats(j, src, src_r):
                    ctx.e("scalar", "activation", e2[:, :], src, AF.Square,
                          reads=[src_r], writes=[e2.r])
                    psq = ctx.psum()
                    ctx.mm(psq[0:16, :], etf[:, ts(c, 16)], e2[:, :],
                           start=True, stop=True, reads=[etf.r, e2.r],
                           writes=[psq.r])
                    dst_m = qmx if j == 0 else kmx
                    ctx.e("vector", "reduce_max",
                          dst_m[0:16, c * 4 + tt_i:c * 4 + tt_i + 1],
                          psq[0:16, :], axis=AXX, reads=[psq.r],
                          writes=[dst_m.r])

                ctx.e("vector", "tensor_copy", eb1[:, :], pj[0][:, :],
                      reads=[pj[0].r], writes=[eb1.r])
                ctx.dma("sync", xfm(q_d)[:, c, ta], eb1[:, :],
                        reads=[eb1.r], writes=[q_dr], tag=f"q{c}{tt_i}")
                if B2_SUB >= 3:
                    stats(0, eb1[:, :], eb1.r)
                ctx.e("vector", "tensor_copy", eb1[:, :], pj[1][:, :],
                      reads=[pj[1].r], writes=[eb1.r])
                ctx.dma("sync", gki.rearrange("(cc p) t -> p cc t", p=128)[:, c, ta],
                        eb1[:, :], reads=[eb1.r], writes=[g2r[0]],
                        tag=f"k{c}{tt_i}")
                if B2_SUB >= 3:
                    stats(1, eb1[:, :], eb1.r)
                ctx.e("vector", "tensor_copy", vsb[:, ta], pj[2][:, :],
                      reads=[pj[2].r], writes=[vsb.r])
            for tc in (range(T // 128) if B2_SUB >= 4 else []):
                pst = ctx.psumb()
                ctx.e("tensor", "transpose", pst[:, 0:128], vsb[:, ts(tc, 128)],
                      idb[:], reads=[vsb.r, idb.r], writes=[pst.r])
                ctx.e("vector", "tensor_copy", eb1[:, 0:128], pst[:, 0:128],
                      reads=[pst.r], writes=[eb1.r])
                ctx.dma("sync", gvi[ts(tc, 128), ts(c, 128)], eb1[:, 0:128],
                        reads=[eb1.r], writes=[g2r[2]], tag=f"v{c}{tc}")
        ctx.e("vector", "reduce_max", bmax[0:16, 0:1], qmx[0:16, :],
              axis=AXX, reads=[qmx.r], writes=[bmax.r])
        ctx.e("vector", "reduce_max", bmax[0:16, 1:2], kmx[0:16, :],
              axis=AXX, reads=[kmx.r], writes=[bmax.r])
        ctx.e("scalar", "activation", bmax[0:16, 0:2], bmax[0:16, 0:2],
              AF.Sqrt, reads=[], writes=[bmax.r])
        ctx.dma("scalar", gsi[0:1, :].rearrange("a p -> p a"), bmax[0:16, 1:2],
                reads=[bmax.r], writes=[g2r[4]], tag="kstat")
        for gi_, go_, ra_, rb_, tg in ([(gki, gko, g2r[0], g2r[1], "agK"),
                                       (gvi, gvo, g2r[2], g2r[3], "agV"),
                                       (gsi, gso, g2r[4], g2r[5], "agS")]
                                       if B2_SUB >= 5 else []):
            ctx.s.rw("gpsimd", _mk_emit("collective_compute",
                     ("AllGather", ALU.bypass), dict(
                         replica_groups=ctx.groups, ins=[gi_], outs=[go_])),
                     reads=[ra_], writes=[rb_], cc=True, tag=tg)
        for j2 in (range(4) if B2_SUB >= 5 else []):
            ctx.dma("scalar", gmx[0:16, j2:j2 + 1],
                    gso[j2:j2 + 1, :].rearrange("a p -> p a"),
                    reads=[g2r[5]], writes=[gmx.r], tag=f"ks{j2}")
        ctx.e("vector", "reduce_max", bmax[0:16, 2:3], gmx[0:16, :],
              axis=AXX, reads=[gmx.r], writes=[bmax.r])
        ctx.e("vector", "tensor_mul", bmax[0:16, 3:4], bmax[0:16, 0:1],
              bmax[0:16, 2:3], reads=[], writes=[bmax.r])
        ctx.e("vector", "tensor_scalar_mul", bmax[0:16, 3:4], bmax[0:16, 3:4],
              -1.0 / 8.0, reads=[], writes=[bmax.r])
        ctx.fence("b2proj")
    if B2_STAGE >= 2:
        build_b2_flash(ctx, iv, ir, xn, att, bmax, q_d, q_dr, gko, gvo, g2r)
    if B2_STAGE >= 3:
        build_b23_ffn(ctx, iv, ir, xn, att, xin_d, xin_r, xout_d, xout_r, "b2",
                      attn_proj=True)
    else:
        ctx.dma("sync", xout_d, xin_d, reads=[xin_r], writes=[xout_r],
                tag="copyout")
    return xn, att


def build_b2_flash(ctx, iv, ir, xn, att, bmax, q_d, q_dr, gko, gvo, g2r):
    s, cm = ctx.s, ctx.cm
    SKG = 4
    NG = (4 * T) // (128 * SKG)
    with contextlib.ExitStack() as fs:
        i16f = ctx.sb("b2_i16f", [16, 16], F32, fs)
        onesf = ctx.sb("b2_onesf", [1, 64], F32, fs)
        brow = ctx.sb("b2_brow", [1, 16], F32, fs)
        negB = ctx.sb("b2_negB", [128, 16], F32, fs)
        qbuf = [ctx.sb("b2_qb0", [128, NC, TT], BF16, fs)] * 2
        kslab = [ctx.sb(f"b2_ks{i}", [128, NC, SKG * 128], BF16, fs)
                 for i in range(2)]
        vslab = [ctx.sb(f"b2_vsl{i}", [128, SKG, 16, 65], BF16, fs)
                 for i in range(2)]
        pbuf = [ctx.sb(f"b2_pb{i}", [128, TT], BF16, fs) for i in range(2)]
        oacc = ctx.sb("b2_oacc", [65, 16, TT], F32, fs)
        nv = ctx.sb("b2_nv", [1, TT], F32, fs)

        ctx.dma("sync", i16f[:], iv["i16f"], reads=[ir["i16f"]], writes=[i16f.r])
        ctx.e("vector", "memset", onesf[:], 1.0, writes=[onesf.r])
        for vb in vslab:
            ctx.e("vector", "memset", vb[:, :, :, 64:65], 1.0, writes=[vb.r])
        psb = ctx.psum()
        ctx.mm(psb[0:1, 0:16], bmax[0:16, 3:4], i16f[:], start=True, stop=True,
               reads=[bmax.r, i16f.r], writes=[psb.r])
        ctx.e("vector", "tensor_copy", brow[:], psb[0:1, 0:16],
              reads=[psb.r], writes=[brow.r])
        psb2 = ctx.psum()
        ctx.mm(psb2[:, 0:16], cm["ones1f"][:], brow[:], start=True, stop=True,
               reads=[cm["ones1f"].r, brow.r], writes=[psb2.r])
        ctx.e("vector", "tensor_copy", negB[:], psb2[:, 0:16],
              reads=[psb2.r], writes=[negB.r])

        for tt_i in range(NTT):
            ta = ts(tt_i, TT)
            qb = qbuf[tt_i % 2]
            ctx.dma("sync", qb[:], xfm(q_d)[:, :, ta], reads=[q_dr],
                    writes=[qb.r], tag=f"fq{tt_i}")
            ctx.e("vector", "memset", oacc[:], 0.0, writes=[oacc.r])
            for g in range(NG):
                ksl = kslab[g % 2]
                vsl = vslab[g % 2]
                ctx.dma("sync", ksl[:],
                        gko.rearrange("(r cc p) t -> p r cc t", p=128, cc=NC)
                        .rearrange("p r cc t -> p cc (r t)")[:, :, g * SKG * 128:(g + 1) * SKG * 128]
                        if False else
                        gko.rearrange("(r cc p) t -> p (r cc t)", p=128)[:, 0:1],
                        reads=[g2r[1]], writes=[ksl.r], tag="BAD") if False else None
                r0 = (g * SKG * 128) // T
                t0 = (g * SKG * 128) % T
                ctx.dma("sync", ksl[:],
                        gko[r0 * 1024:(r0 + 1) * 1024, t0:t0 + SKG * 128]
                        .rearrange("(cc p) t -> p cc t", p=128),
                        reads=[g2r[1]], writes=[ksl.r], tag=f"fk{tt_i}_{g}")
                for sg in range(SKG):
                    r0v = (g * SKG + sg) * 128
                    ctx.dma("sync", vsl[:, sg, :, 0:64],
                            gvo[r0v:r0v + 128, :]
                            .rearrange("p (hh d) -> p hh d", d=64),
                            reads=[g2r[3]], writes=[vsl.r],
                            tag=f"fv{tt_i}_{g}_{sg}")
                for h in range(16):
                    psO = ctx.psum_at(5)
                    for sk in range(SKG):
                        psA = ctx.psum_at(3 + sk % 2)
                        ctx.mm(psA[:, :], ksl[ts(h % 2, 64), h // 2, ts(sk, 128)],
                               qb[ts(h % 2, 64), h // 2, :], start=True,
                               stop=True, reads=[ksl.r, qb.r], writes=[psA.r])
                        pb = pbuf[sk % 2]
                        ctx.e("scalar", "activation", pb[:, :], psA[:, :],
                              AF.Exp, bias=negB[:, h:h + 1], scale=1.0 / 8.0,
                              reads=[psA.r, negB.r], writes=[pb.r])
                        ctx.mm(psO[0:65, :], vsl[:, sk, h, :], pb[:, :],
                               start=(sk == 0), stop=(sk == SKG - 1),
                               reads=[vsl.r, pb.r], writes=[psO.r])
                    ctx.e("vector", "tensor_add",
                          oacc[0:65, h, :], oacc[0:65, h, :], psO[0:65, :],
                          reads=[psO.r], writes=[oacc.r])
            for h in range(16):
                ctx.e("vector", "reciprocal", nv[:, :], oacc[64:65, h, :],
                      reads=[oacc.r], writes=[nv.r])
                psn = ctx.psum()
                ctx.mm(psn[0:64, :], onesf[:], nv[:, :], start=True, stop=True,
                       reads=[onesf.r, nv.r], writes=[psn.r])
                ctx.e("vector", "tensor_mul", att[ts(h % 2, 64), h // 2, ta],
                      oacc[0:64, h, :], psn[0:64, :],
                      reads=[oacc.r, psn.r], writes=[att.r])
        ctx.fence("b2flash")


def build_b23_ffn(ctx, iv, ir, xn, att, xin_d, xin_r, xout_d, xout_r, pfx,
                  attn_proj=True):
    s, cm = ctx.s, ctx.cm
    e1, e2 = cm["e1"], cm["e2"]
    with contextlib.ExitStack() as fs:
        s1 = ctx.sb(pfx + "_s1s", [128, 32], F32, fs)
        s2 = ctx.sb(pfx + "_s2s", [128, 32], F32, fs)
        sh1 = ctx.sb(pfx + "_sh1s", [128, 32], F32, fs)
        sh2 = ctx.sb(pfx + "_sh2s", [128, 32], F32, fs)
        s3 = ctx.sb(pfx + "_s3s", [128, NC], F32, fs)
        for sb, nm in [(s1, pfx + "_s1"), (s2, pfx + "_s2"), (sh1, pfx + "_sh1"),
                       (sh2, pfx + "_sh2"), (s3, pfx + "_s3")]:
            ctx.dma("sync", sb[:], iv[nm], reads=[ir[nm]], writes=[sb.r], tag=nm)
        if attn_proj:
            bo = ctx.sb(pfx + "_bos", [128, NC], F32, fs)
            aoT = ctx.sb(pfx + "_aoTs", [128, NC, 16], BF16, fs)
            boT = ctx.sb(pfx + "_boTs", [16, 1024], BF16, fs)
            lro = ctx.sb(pfx + "_lro", [16, TT], BF16, fs)
            ctx.dma("sync", bo[:], iv[pfx + "_bo"], reads=[ir[pfx + "_bo"]],
                    writes=[bo.r])
            ctx.dma("gpsimd", aoT[:],
                    iv[pfx + "_AoT"].rearrange("(kc p) o -> p kc o", p=128),
                    reads=[ir[pfx + "_AoT"]], writes=[aoT.r])
            ctx.dma("gpsimd", boT[:], iv[pfx + "_BoT"],
                    reads=[ir[pfx + "_BoT"]], writes=[boT.r])

        h_sb = ctx.sb(pfx + "_h", [128, 32, TT], BF16, fs)
        g_sb = ctx.sb(pfx + "_g", [128, 32, TT], BF16, fs)
        wsl = [ctx.sb(f"{pfx}_wsl{i}", [128, 32, 128], BF16, fs)
               for i in range(3)]
        w14 = [wsl[0], wsl[1], wsl[2]]
        wos = [ctx.sb(f"{pfx}_wos0", [128, NC, 128], BF16, fs)] * 2

        w1t = iv[pfx + "_w1t"].rearrange("(kc p) o -> p kc o", p=128)
        w2t = iv[pfx + "_w2t"].rearrange("(kc p) o -> p kc o", p=128)
        h1t = iv[pfx + "_h1t"].rearrange("(kc p) o -> p kc o", p=128)
        h2t = iv[pfx + "_h2t"].rearrange("(kc p) o -> p kc o", p=128)
        w3t = iv[pfx + "_w3t"].rearrange("(kc p) o -> p kc o", p=128)

        for tt_i in range(NTT):
            ta = ts(tt_i, TT)
            if attn_proj:
                ps = ctx.psum()
                for kc in range(NC):
                    ctx.mm(ps[0:16, :], aoT[:, kc, :], att[:, kc, ta],
                           start=(kc == 0), stop=(kc == NC - 1),
                           reads=[aoT.r, att.r], writes=[ps.r])
                ctx.e("vector", "tensor_scalar_mul",
                      lro[0:16, :], ps[0:16, :], 1.0 / 16.0,
                      reads=[ps.r], writes=[lro.r])
            for hc in range(32):
                wa = w14[hc % 3]
                ctx.dma("gpsimd", wa[:, 0:NC, :], w1t[:, :, ts(hc, 128)],
                        reads=[ir[pfx + "_w1t"]], writes=[wa.r],
                        tag=f"{pfx}w1{tt_i}{hc}")
                ps = ctx.psum()
                for kc in range(NC):
                    ctx.mm(ps[:, :], wa[:, kc, :], xn[:, kc, ta],
                           start=(kc == 0), stop=(kc == NC - 1),
                           reads=[wa.r, xn.r], writes=[ps.r])
                ctx.e("scalar", "activation", h_sb[:, hc, :], ps[:, :],
                      AF.Silu, scale=s1[:, hc:hc + 1],
                      reads=[ps.r, s1.r], writes=[h_sb.r])
            for hc in range(32):
                wa = w14[hc % 3]
                ctx.dma("gpsimd", wa[:, 0:NC, :], w2t[:, :, ts(hc, 128)],
                        reads=[ir[pfx + "_w2t"]], writes=[wa.r],
                        tag=f"{pfx}w2{tt_i}{hc}")
                ps = ctx.psum()
                for kc in range(NC):
                    ctx.mm(ps[:, :], wa[:, kc, :], xn[:, kc, ta],
                           start=(kc == 0), stop=(kc == NC - 1),
                           reads=[wa.r, xn.r], writes=[ps.r])
                ctx.e("vector", "tensor_scalar_mul", e1[:, :], ps[:, :],
                      s2[:, hc:hc + 1], reads=[ps.r, s2.r], writes=[e1.r])
                ctx.e("vector", "tensor_mul", h_sb[:, hc, :],
                      h_sb[:, hc, :], e1[:, :], reads=[e1.r],
                      writes=[h_sb.r])
            for hmat, hsrc, hdst, scl in [(h1t, h_sb, g_sb, sh1),
                                          (h2t, g_sb, h_sb, sh2)]:
                nm = pfx + ("_h1t" if hmat is h1t else "_h2t")
                for oc2 in range(32):
                    wsb_ = wsl[oc2 % 3]
                    ctx.dma("gpsimd", wsb_[:],
                            hmat[:, :, ts(oc2, 128)], reads=[ir[nm]],
                            writes=[wsb_.r], tag=f"{nm}{tt_i}{oc2}")
                    ps = ctx.psum()
                    for kc in range(32):
                        ctx.mm(ps[:, :], wsb_[:, kc, :], hsrc[:, kc, :],
                               start=(kc == 0), stop=(kc == 31),
                               reads=[wsb_.r, hsrc.r], writes=[ps.r])
                    ctx.e("scalar", "activation", hdst[:, oc2, :], ps[:, :],
                          AF.Gelu, scale=scl[:, oc2:oc2 + 1],
                          reads=[ps.r, scl.r], writes=[hdst.r])
            for oc in range(NC):
                psF = ctx.psum()
                wsb_ = wsl[oc % 3]
                ctx.dma("gpsimd", wsb_[:], w3t[:, :, ts(oc, 128)],
                        reads=[ir[pfx + "_w3t"]], writes=[wsb_.r],
                        tag=f"{pfx}w3{tt_i}{oc}")
                for kc in range(32):
                    ctx.mm(psF[:, :], wsb_[:, kc, :], h_sb[:, kc, :],
                           start=(kc == 0), stop=(kc == 31),
                           reads=[wsb_.r, h_sb.r], writes=[psF.r])
                xo1 = cm["e3"]
                ctx.dma("sync", xo1[:, :], xfm(xin_d)[:, oc, ta]
                        if xin_d.shape[1] == T else
                        xfm(xin_d)[:, oc, 1 + tt_i * TT:1 + (tt_i + 1) * TT],
                        reads=[xin_r], writes=[xo1.r], tag=f"{pfx}xo{tt_i}{oc}")
                if attn_proj:
                    psA = ctx.psum()
                    wob = wos[oc % 2]
                    ctx.dma("gpsimd", wob[:], wslices_ap(iv[pfx + "_WoT"], oc),
                            reads=[ir[pfx + "_WoT"]], writes=[wob.r],
                            tag=f"{pfx}wo{tt_i}{oc}")
                    for kc in range(NC):
                        ctx.mm(psA[:, :], wob[:, kc, :], att[:, kc, ta],
                               start=(kc == 0), stop=False,
                               reads=[wob.r, att.r], writes=[psA.r])
                    ctx.mm(psA[:, :], boT[:, ts(oc, 128)], lro[0:16, :],
                           start=False, stop=True, reads=[boT.r, lro.r],
                           writes=[psA.r])
                    ctx.e("vector", "tensor_scalar", e1[:, :], psA[:, :],
                          bo[:, oc:oc + 1], None, op0=ALU.add,
                          reads=[psA.r, bo.r], writes=[e1.r])
                else:
                    ctx.e("vector", "tensor_copy", e1[:, :], att[:, oc, ta],
                          reads=[att.r], writes=[e1.r])
                ctx.e("vector", "scalar_tensor_tensor", e2[:, :], psF[:, :],
                      s3[:, oc:oc + 1], e1[:, :], op0=ALU.mult, op1=ALU.add,
                      reads=[psF.r, s3.r, e1.r], writes=[e2.r])
                ctx.e("vector", "tensor_add", xo1[:, :], e2[:, :], xo1[:, :],
                      reads=[e2.r], writes=[xo1.r])
                ctx.dma("sync", xfm(xout_d)[:, oc, ta], xo1[:, :],
                        reads=[xo1.r], writes=[xout_r], tag=f"{pfx}xw{tt_i}{oc}")
        ctx.fence(pfx + "ffn")


def build_block3(ctx, iv, ir, xin_d, xin_r, xout_d, xout_r,
                 a1i, a1o, a2i, a2o, g3r):
    s, cm = ctx.s, ctx.cm
    e1, e2, e4 = cm["e1"], cm["e2"], cm["e4"]
    eb1 = cm["eb1"]

    xn = ctx.sb("b3_xn", [128, NC, T], BF16)
    att = ctx.sb("b3_att", [128, NC, T], BF16)
    static_d = ctx.nc.dram_tensor("b3_static", [T, 1024], F32).ap()
    static_r = Reg("b3_static")

    with contextlib.ExitStack() as ms:
        thx = ctx.sb("b3_thx", [128, NC, T], BF16, ms)
        idb = ctx.sb("b3_idb", [128, 128], BF16, ms)
        idf32 = ctx.sb("b3_idf32", [128, 128], F32, ms)
        wsl2 = [ctx.sb(f"b3_wsl2_{i}", [128, NC, 128], BF16, ms) for i in range(2)]
        stf = ctx.sb("b3_stf", [128, TT], F32, ms)

        ctx.dma("gpsimd", idb[:], iv["idf"], reads=[ir["idf"]], writes=[idb.r])
        ctx.dma("sync", idf32[:], iv["idf32"], reads=[ir["idf32"]],
                writes=[idf32.r])
        emit_ln(ctx, xin_d, xin_r, xn, T)
        for c in range(NC):
            for tt_i in range(NTT):
                ctx.e("scalar", "activation", thx[:, c, ts(tt_i, TT)],
                      xn[:, c, ts(tt_i, TT)], AF.Tanh, reads=[xn.r],
                      writes=[thx.r])
        for oc in range(NC):
            wsb_ = wsl2[oc % 2]
            ctx.dma("gpsimd", wsb_[:], wslices_ap(iv["b3_WT"], oc),
                    reads=[ir["b3_WT"]], writes=[wsb_.r], tag=f"b3W{oc}")
            for tt_i in range(NTT):
                ps = ctx.psum()
                for kc in range(NC):
                    ctx.mm(ps[:, :], wsb_[:, kc, :], thx[:, kc, ts(tt_i, TT)],
                           start=(kc == 0), stop=(kc == NC - 1),
                           reads=[wsb_.r, thx.r], writes=[ps.r])
                ctx.e("vector", "tensor_copy", stf[:, :], ps[:, :],
                      reads=[ps.r], writes=[stf.r])
                for i4 in range(4):
                    tc = tt_i * 4 + i4
                    pst = ctx.psum()
                    ctx.e("tensor", "transpose", pst[:, 0:128],
                          stf[:, ts(i4, 128)], idf32[:],
                          reads=[stf.r, idf32.r], writes=[pst.r])
                    ctx.e("vector", "tensor_copy", e1[:, 0:128], pst[:, 0:128],
                          reads=[pst.r], writes=[e1.r])
                    ctx.dma("sync", static_d[ts(tc, 128), ts(oc, 128)],
                            e1[:, 0:128], reads=[e1.r], writes=[static_r],
                            tag=f"b3st{oc}{tc}")
        for c in range(NC):
            for tc in range(T // 128):
                pst = ctx.psumb()
                ctx.e("tensor", "transpose", pst[:, 0:128],
                      thx[:, c, ts(tc, 128)], idb[:],
                      reads=[thx.r, idb.r], writes=[pst.r])
                ctx.e("vector", "tensor_copy", eb1[:, 0:128], pst[:, 0:128],
                      reads=[pst.r], writes=[eb1.r])
                ctx.dma("sync", a1i[ts(tc, 128), ts(c, 128)],
                        eb1[:, 0:128], reads=[eb1.r], writes=[g3r[0]],
                        tag=f"b3a{c}{tc}")
        ctx.dma("gpsimd", a1i[:, 1024:2048], static_d[:, :],
                reads=[static_r, g3r[0]], writes=[g3r[0]], tag="b3s")
        ctx.s.rw("gpsimd", _mk_emit("collective_compute",
                 ("AllGather", ALU.bypass), dict(
                     replica_groups=ctx.groups, ins=[a1i], outs=[a1o])),
                 reads=[g3r[0]], writes=[g3r[1]], cc=True, tag="ag3a")
        ctx.fence("b3proj")

    with contextlib.ExitStack() as ms:
        lamt = ctx.sb("b3_lamt", [1, 8], F32, ms)
        lnl = ctx.sb("b3_lnl", [128, 4], F32, ms)
        lamC = ctx.sb("b3_lamC", [64, 1], F32, ms)
        wvec = ctx.sb("b3_wvec", [128, 2], F32, ms)
        gpm = ctx.sb("b3_gpm", [128, 128], F32, ms)
        musU = ctx.sb("b3_musU", [128, 128], F32, ms)
        muI = ctx.sb("b3_muI", [128, 128], F32, ms)
        tv = ctx.sb("b3_tv", [128, 2], F32, ms)
        adI = ctx.sb("b3_adI", [128, 128], F32, ms)
        bstr = ctx.sb("b3_bstr", [128, 128], F32, ms)
        ktv = ctx.sb("b3_ktv", [128, 512], BF16, ms)
        kraw = ctx.sb("b3_kraw", [128, 2048], BF16, ms)
        qsel = ctx.sb("b3_qsel", [128, 4], F32, ms)
        kfm = ctx.sb("b3_kfm", [64, 128], BF16, ms)
        adS = ctx.sb("b3_adS", [128, 128], BF16, ms)
        ltS = ctx.sb("b3_ltS", [128, 128], BF16, ms)
        bS = ctx.sb("b3_bS", [128, 64], BF16, ms)
        x2S = ctx.sb("b3_x2S", [128, 64], BF16, ms)
        uS = ctx.sb("b3_uS", [128, 64], BF16, ms)
        duS = ctx.sb("b3_duS", [128, 64], BF16, ms)
        ksw = ctx.sb("b3_ksw", [128, 64], F32, ms)
        Sst = [ctx.sb(f"b3_S{i}", [64, 4, 64], F32, ms) for i in range(2)]
        Sb = ctx.sb("b3_Sb", [64, 4, 64], BF16, ms)
        ostg = ctx.sb("b3_ostg", [128, 4, 64], F32, ms)

        ctx.dma("sync", gpm[:], iv["g_pm"], reads=[ir["g_pm"]], writes=[gpm.r])
        ctx.dma("sync", musU[:], iv["musU"], reads=[ir["musU"]], writes=[musU.r])
        ctx.dma("sync", muI[:], iv["muI"], reads=[ir["muI"]], writes=[muI.r])
        ctx.dma("sync", tv[:], iv["tv"], reads=[ir["tv"]], writes=[tv.r])
        ctx.dma("sync", lamt[0:1, 0:1], iv["b3_lam"], reads=[ir["b3_lam"]],
                writes=[lamt.r])
        ctx.dma("sync", lamt[0:1, 1:2], iv["b3_eta"], reads=[ir["b3_eta"]],
                writes=[lamt.r])
        ctx.e("scalar", "activation", lamt[0:1, 2:3], lamt[0:1, 0:1],
              AF.Sigmoid, reads=[lamt.r], writes=[lamt.r])
        ctx.e("scalar", "activation", lamt[0:1, 3:4], lamt[0:1, 2:3],
              AF.Ln, reads=[lamt.r], writes=[lamt.r])
        ctx.e("vector", "tensor_scalar", lamt[0:1, 4:5], lamt[0:1, 2:3],
              -1.0, 1.0, op0=ALU.mult, op1=ALU.add, reads=[lamt.r],
              writes=[lamt.r])
        ctx.e("vector", "tensor_mul", lamt[0:1, 5:6], lamt[0:1, 4:5],
              lamt[0:1, 1:2], reads=[lamt.r], writes=[lamt.r])
        ctx.e("vector", "tensor_scalar_mul", lamt[0:1, 5:6], lamt[0:1, 5:6],
              1.0 / 64.0, reads=[lamt.r], writes=[lamt.r])
        ctx.e("vector", "tensor_scalar_mul", lamt[0:1, 6:7], lamt[0:1, 5:6],
              -1.0, reads=[lamt.r], writes=[lamt.r])
        ctx.e("vector", "tensor_scalar_mul", lamt[0:1, 7:8], lamt[0:1, 3:4],
              128.0, reads=[lamt.r], writes=[lamt.r])
        for col, src in [(0, 3), (1, 5), (2, 6)]:
            psb = ctx.psum()
            ctx.mm(psb[:, 0:1], cm["ones1f"][:], lamt[0:1, src:src + 1],
                   start=True, stop=True, reads=[cm["ones1f"].r, lamt.r],
                   writes=[psb.r])
            ctx.e("vector", "tensor_copy", lnl[:, col:col + 1], psb[:, 0:1],
                  reads=[psb.r], writes=[lnl.r])
        psb = ctx.psum()
        ctx.e("scalar", "activation", lamt[0:1, 7:8], lamt[0:1, 7:8], AF.Exp,
              reads=[lamt.r], writes=[lamt.r])
        ctx.mm(psb[0:64, 0:1], cm["ones1f"][0:1, 0:64], lamt[0:1, 7:8],
               start=True, stop=True, reads=[cm["ones1f"].r, lamt.r],
               writes=[psb.r])
        ctx.e("vector", "tensor_copy", lamC[:], psb[0:64, 0:1],
              reads=[psb.r], writes=[lamC.r])
        for col in range(2):
            ctx.e("vector", "tensor_mul", e1[:, 0:1], tv[:, col:col + 1],
                  lnl[:, 0:1], reads=[tv.r, lnl.r], writes=[e1.r])
            ctx.e("scalar", "activation", wvec[:, col:col + 1], e1[:, 0:1],
                  AF.Exp, reads=[e1.r], writes=[wvec.r])
        ctx.e("vector", "tensor_scalar_mul", e1[:, 0:128], gpm[:], -1.0,
              reads=[gpm.r], writes=[e1.r])
        ctx.e("vector", "tensor_mul", e1[:, 0:128], e1[:, 0:128]
              if False else e1[:, 0:128], lnl[:, 0:1], reads=[lnl.r],
              writes=[e1.r], tag="BAD-INPLACE") if False else None
        ctx.e("vector", "tensor_scalar_mul", e2[:, 0:128], e1[:, 0:128],
              lnl[:, 0:1], reads=[e1.r, lnl.r], writes=[e2.r])
        ctx.e("scalar", "activation", e4[:, 0:128], e2[:, 0:128], AF.Exp,
              reads=[e2.r], writes=[e4.r])
        ctx.e("vector", "tensor_mul", adI[:], e4[:, 0:128], muI[:],
              reads=[e4.r, muI.r], writes=[adI.r])
        ctx.e("vector", "tensor_scalar_mul", bstr[:], musU[:], lamt[0:1, 5:6]
              if False else 1.0, reads=[musU.r], writes=[bstr.r],
              tag="FIXBELOW") if False else None
        ctx.e("vector", "tensor_scalar_mul", bstr[:], musU[:], lnl[:, 1:2],
              reads=[musU.r, lnl.r], writes=[bstr.r])

        ctx.dma("sync", qsel[:], iv["qsel"], reads=[ir["qsel"]],
                writes=[qsel.r])
        ctx.e("vector", "memset", Sst[0][:], 0.0, writes=[Sst[0].r])
        ctx.e("vector", "memset", Sb[:], 0.0, writes=[Sb.r])
        for ch in range(T // 128 * 4):
            ctx.dma("sync", kraw[:], a1o[ts(ch, 128), :], reads=[g3r[1]],
                    writes=[kraw.r], tag=f"b3kv{ch}")
            for half, off in [(0, 0), (1, 1024)]:
                ctx.e("vector", "tensor_scalar_mul", ktv[:, ts(half, 256)],
                      kraw[:, off:off + 256], qsel[:, 0:1],
                      reads=[kraw.r, qsel.r], writes=[ktv.r])
                for jr in range(1, 4):
                    ctx.e("vector", "scalar_tensor_tensor", ktv[:, ts(half, 256)],
                          kraw[:, off + jr * 256:off + (jr + 1) * 256],
                          qsel[:, jr:jr + 1], ktv[:, ts(half, 256)],
                          op0=ALU.mult, op1=ALU.add,
                          reads=[kraw.r, qsel.r, ktv.r], writes=[ktv.r])
            Sprev = Sst[ch % 2]
            Snext = Sst[(ch + 1) % 2]
            for mh in range(4):
                km = ktv[:, ts(mh, 64)]
                vm = ktv[:, 256 + mh * 64:256 + (mh + 1) * 64]
                pst = ctx.psumb()
                ctx.e("tensor", "transpose", pst[0:64, 0:128], km, idb[:],
                      reads=[ktv.r, idb.r], writes=[pst.r])
                ctx.e("scalar", "copy", kfm[:], pst[0:64, 0:128],
                      reads=[pst.r], writes=[kfm.r])
                psA = ctx.psum()
                ctx.mm(psA[:, :128], kfm[:], kfm[:], start=True, stop=True,
                       reads=[kfm.r], writes=[psA.r])
                ctx.e("vector", "tensor_mul", adS[:], psA[:, 0:128], adI[:],
                      reads=[psA.r, adI.r], writes=[adS.r])
                ctx.e("vector", "tensor_mul", ltS[:], psA[:, 0:128], bstr[:],
                      reads=[psA.r, bstr.r], writes=[ltS.r])
                psK = ctx.psum()
                ctx.mm(psK[:, 0:64], kfm[:], Sb[:, mh, :], start=True,
                       stop=True, reads=[kfm.r, Sb.r], writes=[psK.r])
                ctx.e("vector", "tensor_scalar_mul", ksw[:], psK[:, 0:64],
                      wvec[:, 0:1], reads=[psK.r, wvec.r], writes=[ksw.r])
                ctx.e("vector", "scalar_tensor_tensor", e1[:, 0:64],
                      psK[:, 0:64], wvec[:, 0:1], vm, op0=ALU.mult,
                      op1=ALU.subtract, reads=[psK.r, wvec.r, ktv.r],
                      writes=[e1.r])
                ctx.e("vector", "tensor_scalar_mul", bS[:], e1[:, 0:64],
                      lnl[:, 2:3], reads=[e1.r, lnl.r], writes=[bS.r])
                psX = ctx.psum()
                ctx.mm(psX[:, 0:64], ltS[:], bS[:], start=True, stop=True,
                       reads=[ltS.r, bS.r], writes=[psX.r])
                ctx.e("vector", "tensor_tensor", x2S[:], bS[:], psX[:, 0:64],
                      op=ALU.subtract, reads=[bS.r, psX.r], writes=[x2S.r])
                psX2 = ctx.psum()
                ctx.mm(psX2[:, 0:64], ltS[:], x2S[:], start=True, stop=True,
                       reads=[ltS.r, x2S.r], writes=[psX2.r])
                ctx.e("vector", "tensor_tensor", uS[:], bS[:], psX2[:, 0:64],
                      op=ALU.subtract, reads=[bS.r, psX2.r], writes=[uS.r])
                psO = ctx.psum()
                ctx.mm(psO[:, 0:64], adS[:], uS[:], start=True, stop=True,
                       reads=[adS.r, uS.r], writes=[psO.r])
                ctx.e("vector", "tensor_add", ostg[:, mh, :], ksw[:],
                      psO[:, 0:64], reads=[ksw.r, psO.r], writes=[ostg.r])
                ctx.e("vector", "tensor_scalar_mul", duS[:], uS[:],
                      wvec[:, 1:2], reads=[uS.r, wvec.r], writes=[duS.r])
                psS = ctx.psum()
                ctx.mm(psS[0:64, 0:64], km, duS[:], start=True, stop=True,
                       reads=[ktv.r, duS.r], writes=[psS.r])
                ctx.e("vector", "scalar_tensor_tensor", Snext[:, mh, :],
                      Sprev[:, mh, :], lamC[:, 0:1], psS[0:64, 0:64],
                      op0=ALU.mult, op1=ALU.add,
                      reads=[Sprev.r, lamC.r, psS.r], writes=[Snext.r])
                ctx.e("scalar", "copy", Sb[:, mh, :], Snext[:, mh, :],
                      reads=[Snext.r], writes=[Sb.r])
            ctx.dma("sync", a2i[ts(ch, 128), :], ostg[:],
                    reads=[ostg.r], writes=[g3r[2]], tag=f"b3o{ch}")
        ctx.s.rw("gpsimd", _mk_emit("collective_compute",
                 ("AllGather", ALU.bypass), dict(
                     replica_groups=ctx.groups, ins=[a2i], outs=[a2o])),
                 reads=[g3r[2]], writes=[g3r[3]], cc=True, tag="ag3b")
        ctx.fence("b3mix")

    with contextlib.ExitStack() as ms:
        idb2 = ctx.sb("b3_idb2", [128, 128], BF16, ms)
        ytile = ctx.sb("b3_yt", [128, 1024], F32, ms)
        y2 = ctx.sb("b3_y2", [128, 1024], F32, ms)
        gb = ctx.sb("b3_gb", [128, 1024], F32, ms)
        bb = ctx.sb("b3_bb", [128, 1024], F32, ms)
        yst = ctx.sb("b3_yst", [128, 4], F32, ms)
        atb = ctx.sb("b3_atb", [128, 1024], BF16, ms)
        ctx.dma("gpsimd", idb2[:], iv["idf"], reads=[ir["idf"]], writes=[idb2.r])
        for dst, nm in [(gb, "b3_lngr"), (bb, "b3_lnbr")]:
            psb = ctx.psum()
            rowt = ctx.sb("b3_row_" + nm, [1, 1024], F32, ms)
            ctx.dma("sync", rowt[:], iv[nm], reads=[ir[nm]], writes=[rowt.r])
            for half in range(2):
                ctx.mm(psb[:, 0:512], cm["ones1f"][:], rowt[0:1, ts(half, 512)],
                       start=True, stop=True,
                       reads=[cm["ones1f"].r, rowt.r], writes=[psb.r])
                ctx.e("vector", "tensor_copy", dst[:, ts(half, 512)],
                      psb[:, 0:512], reads=[psb.r], writes=[dst.r])
        qsel2 = ctx.sb("b3_qsel2", [128, 4], F32, ms)
        ycnd = ctx.sb("b3_ycnd", [128, 4, 256], F32, ms)
        ctx.dma("sync", qsel2[:], iv["qsel"], reads=[ir["qsel"]],
                writes=[qsel2.r])
        for tc in range(T // 128):
            for i in range(4):
                for jr in range(4):
                    ctx.dma("sync", ycnd[:, jr, :],
                            a2o[i * 4 * T + jr * T + tc * 128:
                                i * 4 * T + jr * T + (tc + 1) * 128, :],
                            reads=[g3r[3]], writes=[ycnd.r],
                            tag=f"b3y{tc}_{i}_{jr}")
                ctx.e("vector", "tensor_scalar_mul",
                      ytile[:, i * 256:(i + 1) * 256], ycnd[:, 0, :],
                      qsel2[:, 0:1], reads=[ycnd.r, qsel2.r],
                      writes=[ytile.r])
                for jr in range(1, 4):
                    ctx.e("vector", "scalar_tensor_tensor",
                          ytile[:, i * 256:(i + 1) * 256], ycnd[:, jr, :],
                          qsel2[:, jr:jr + 1],
                          ytile[:, i * 256:(i + 1) * 256],
                          op0=ALU.mult, op1=ALU.add,
                          reads=[ycnd.r, qsel2.r, ytile.r], writes=[ytile.r])
            ctx.dma("sync", y2[:, :], static_d[ts(tc, 128), :],
                    reads=[static_r], writes=[y2.r], tag=f"b3ys{tc}")
            ctx.e("vector", "tensor_add", ytile[:, :], ytile[:, :], y2[:, :],
                  reads=[y2.r], writes=[ytile.r])
            ctx.e("vector", "reduce_sum", yst[:, 0:1], ytile[:, :], axis=AXX,
                  reads=[ytile.r], writes=[yst.r])
            ctx.e("vector", "tensor_scalar_mul", yst[:, 0:1], yst[:, 0:1],
                  1.0 / 1024.0, reads=[], writes=[yst.r])
            ctx.e("vector", "tensor_scalar", y2[:, :], ytile[:, :],
                  yst[:, 0:1], None, op0=ALU.subtract,
                  reads=[ytile.r, yst.r], writes=[y2.r])
            ctx.e("scalar", "activation", ytile[:, :], y2[:, :], AF.Square,
                  reads=[y2.r], writes=[ytile.r])
            ctx.e("vector", "reduce_sum", yst[:, 1:2], ytile[:, :], axis=AXX,
                  reads=[ytile.r], writes=[yst.r])
            ctx.e("vector", "tensor_scalar", yst[:, 2:3], yst[:, 1:2],
                  1.0 / 1024.0, 1e-5, op0=ALU.mult, op1=ALU.add,
                  reads=[yst.r], writes=[yst.r])
            ctx.e("scalar", "activation", yst[:, 2:3], yst[:, 2:3], AF.Sqrt,
                  reads=[], writes=[yst.r])
            ctx.e("vector", "reciprocal", yst[:, 3:4], yst[:, 2:3],
                  reads=[], writes=[yst.r])
            ctx.e("vector", "tensor_scalar_mul", y2[:, :], y2[:, :],
                  yst[:, 3:4], reads=[yst.r], writes=[y2.r])
            ctx.e("vector", "tensor_mul", y2[:, :], y2[:, :], gb[:, :],
                  reads=[gb.r], writes=[y2.r])
            ctx.e("vector", "tensor_add", atb[:, :], y2[:, :], bb[:, :],
                  reads=[y2.r, bb.r], writes=[atb.r])
            for c in range(NC):
                pst = ctx.psumb()
                ctx.e("tensor", "transpose", pst[:, 0:128],
                      atb[:, ts(c, 128)], idb2[:],
                      reads=[atb.r, idb2.r], writes=[pst.r])
                ctx.e("vector", "tensor_copy", att[:, c, ts(tc, 128)],
                      pst[:, 0:128], reads=[pst.r], writes=[att.r])
        ctx.fence("b3y")
    build_b23_ffn(ctx, iv, ir, xn, att, xin_d, xin_r, xout_d, xout_r, "b3",
                  attn_proj=False)
    return xn, att


def build_prog_b0():
    nc = bass.Bass("TRN2", target_bir_lowering=False, debug=False)
    iv, ir = {}, {}

    def inp(name, shape):
        iv[name] = nc.dram_tensor(name, list(shape), F32, kind="ExternalInput").ap()
        ir[name] = Reg(name)

    inp("xin", [1024, T + 1])
    for nm, shp in B0_INPUTS:
        inp(nm, shp)
    xout = nc.dram_tensor("xout", [1024, T], F32, kind="ExternalOutput").ap()
    xout_r = Reg("xout")
    gin = [nc.dram_tensor(f"g0i_{c}", [2, 128], F32).ap() for c in range(NC)]
    gout = [nc.dram_tensor(f"g0o_{c}", [8, 128], F32).ap() for c in range(NC)]
    gr = []
    for c in range(NC):
        gr += [Reg(f"g0i_{c}"), Reg(f"g0o_{c}")]

    s = Sched(nc)
    with contextlib.ExitStack() as stk:
        ctx = Ctx(nc, stk, s)
        ctx.groups = [[0, 1, 2, 3], [4, 5, 6, 7]]
        ctx.psum_init(n_f32=8, n_bf16=0)
        block = stk.enter_context(nc.Block())
        alloc_common(ctx)
        build_block0(ctx, iv, ir, iv["xin"], ir["xin"], xout, xout_r,
                     gin, gout, gr)
        last = [o for o in s.ops if o.dma and o.engine == "sync"][-1]
        s.op("gpsimd", _mk_emit("nop", (), {}), deps=[last, s.ops[-1]])
        s.build(stk, block)
    return nc


B0_INPUTS = [
    ("b0_mixv", [128, 6, NC]), ("b0_w0v", [16, 1]), ("b0_a0v", [16, 1]),
    ("b0_kk", [128, NC]), ("b0_rk", [128, NC]), ("b0_lnw", [128, NC]),
    ("b0_lnb", [128, NC]), ("pmask", [128, NC]),
    ("ebcf", [16, NC * 128]), ("etf", [128, NC * 16]),
    ("b0_w1", [1024, 64]), ("b0_a1", [1024, 64]), ("b0_g1", [1024, 64]),
    ("b0_w2", [64, 16]), ("b0_a2", [64, 16]), ("b0_g2", [64, 1024]),
    ("b0_Wr", [1024, 1024]), ("b0_Wk", [1024, 1024]), ("b0_Wv", [1024, 1024]),
    ("b0_Wo", [1024, 1024]),
    ("b0f_w1t", [1024, 4096]), ("b0f_b1", [128, 32]),
    ("b0f_w2t", [4096, 1024]), ("b0f_b2", [128, NC]),
]


def fmv(v, ncols):
    v = np.asarray(v, np.float32).reshape(-1)
    return np.ascontiguousarray(v.reshape(ncols, 128).T)


def host_consts(q):
    c = {}
    ebcf = np.zeros((16, NC * 128), np.float32)
    etf = np.zeros((128, NC * 16), np.float32)
    for cc in range(NC):
        for p in range(128):
            h = 2 * cc + p // 64
            ebcf[h, cc * 128 + p] = 1.0
            etf[p, cc * 16 + h] = 1.0
    c["ebcf"] = ebcf
    c["etf"] = etf
    pm = np.zeros((128, NC), np.float32)
    for j in range(4):
        pm[:, j] = 1.0 if j < q else 0.0
        pm[:, 4 + j] = 0.0 if j < q else 1.0
    c["pmask"] = pm
    return c


def stage_b0(ins, core):
    b, q = core // 4, core % 4
    x = np.asarray(ins["L0_attn_x"], np.float32)
    seg = x[b, q * T:(q + 1) * T]
    prev = x[b, q * T - 1:q * T] if q > 0 else np.zeros((1, 1024), np.float32)
    xin = np.ascontiguousarray(np.concatenate([prev, seg], 0).T)
    d = {"xin": xin}
    d.update(host_consts(q))
    mixv = np.zeros((128, 6, NC), np.float32)
    for j, nm in enumerate(["x_r", "x_w", "x_k", "x_v", "x_a", "x_g"]):
        mixv[:, j, :] = fmv(ins["L0_attn_" + nm], NC)
    d["b0_mixv"] = mixv
    d["b0_w0v"] = np.asarray(ins["L0_attn_w0"], np.float32).reshape(16, 1)
    d["b0_a0v"] = np.asarray(ins["L0_attn_a0"], np.float32).reshape(16, 1)
    d["b0_kk"] = fmv(ins["L0_attn_k_k"], NC)
    d["b0_rk"] = fmv(ins["L0_attn_r_k"], NC)
    d["b0_lnw"] = fmv(ins["L0_attn_ln_w"], NC)
    d["b0_lnb"] = fmv(ins["L0_attn_ln_b"], NC)
    for nm in ["w1", "a1", "g1", "w2", "a2", "g2", "Wr", "Wk", "Wv", "Wo"]:
        d["b0_" + nm] = np.ascontiguousarray(np.asarray(ins["L0_attn_" + nm], np.float32))
    d["b0f_w1t"] = np.ascontiguousarray(np.asarray(ins["L0_ffn_w1"], np.float32).T)
    d["b0f_w2t"] = np.ascontiguousarray(np.asarray(ins["L0_ffn_w2"], np.float32).T)
    d["b0f_b1"] = fmv(ins["L0_ffn_b1"], 32)
    d["b0f_b2"] = fmv(ins["L0_ffn_b2"], NC)
    return d


def run_block0(ins, dump=False):
    global DUMP
    DUMP = dump
    nc = build_prog_b0()
    in_maps = [stage_b0(ins, c) for c in range(NCORE)]
    res = run_bass_kernel_spmd(nc, in_maps, core_ids=list(range(NCORE)))
    outs = []
    for c in range(NCORE):
        outs.append(np.asarray(res.results[c]["xout"]).T)
    x1 = np.stack([np.concatenate(outs[0:4], 0), np.concatenate(outs[4:8], 0)])
    if dump:
        return x1, res
    return x1


B1_INPUTS = [
    ("ebcf", [16, NC * 128]), ("pmask", [128, NC]), ("idf", [128, 128]),
    ("l1mask", [128, 256]),
    ("b1_qkvw", [1024, 3072]), ("b1_qb", [128, NC]), ("b1_kb", [128, NC]),
    ("b1_vb", [128, NC]), ("b1_outw", [1024, 1024]), ("b1_ob", [128, NC]),
    ("b1f_w1", [1024, 4096]), ("b1f_b1", [128, 32]), ("b1f_w2", [4096, 1024]),
    ("b1f_b2", [128, NC]), ("b1f_th", [128, 32]), ("b1f_wr", [128, 32]),
    ("b1f_wi", [128, 32]),
]


def stage_b1(ins, core, x_cur):
    b, q = core // 4, core % 4
    seg = x_cur[b, q * T:(q + 1) * T] if x_cur.shape[1] >= (q + 1) * T         else np.zeros((T, 1024), np.float32)
    d = {"xin": np.ascontiguousarray(seg.T)}
    d.update(host_consts(q))
    idf = np.zeros((128, 128), np.float32)
    np.fill_diagonal(idf, 1.0)
    d["idf"] = idf.astype(ml_dtypes.bfloat16)
    mask = np.zeros((128, 128), np.float32)
    for sk in range(128):
        mask[sk, sk:] = 1.0
    d["l1mask"] = np.ascontiguousarray(np.concatenate([mask, mask], axis=1))
    qkv_b = np.asarray(ins["L1_attn_qkv_b"], np.float32)
    d["b1_qkvw"] = np.ascontiguousarray(np.asarray(ins["L1_attn_qkv_w"], np.float32))
    d["b1_qb"] = fmv(qkv_b[0:1024], NC)
    d["b1_kb"] = fmv(qkv_b[1024:2048], NC)
    d["b1_vb"] = fmv(qkv_b[2048:3072], NC)
    d["b1_outw"] = np.ascontiguousarray(np.asarray(ins["L1_attn_out_w"], np.float32))
    d["b1_ob"] = fmv(ins["L1_attn_out_b"], NC)
    d["b1f_w1"] = np.ascontiguousarray(np.asarray(ins["L1_ffn_W1"], np.float32))
    d["b1f_b1"] = fmv(ins["L1_ffn_b1"], 32)
    d["b1f_w2"] = np.ascontiguousarray(np.asarray(ins["L1_ffn_W2"], np.float32))
    d["b1f_b2"] = fmv(ins["L1_ffn_b2"], NC)
    d["b1f_th"] = fmv(ins["L1_ffn_quantum_phases"], 32)
    d["b1f_wr"] = fmv(ins["L1_ffn_quantum_weights_real"], 32)
    d["b1f_wi"] = fmv(ins["L1_ffn_quantum_weights_imag"], 32)
    return d


def build_prog_b1():
    nc = bass.Bass("TRN2", target_bir_lowering=False, debug=False)
    iv, ir = {}, {}

    def inp(name, shape, dt=F32):
        iv[name] = nc.dram_tensor(name, list(shape), dt, kind="ExternalInput").ap()
        ir[name] = Reg(name)

    inp("xin", [1024, T])
    for nm, shp in B1_INPUTS:
        inp(nm, shp, BF16 if nm == "idf" else F32)
    xout = nc.dram_tensor("xout", [1024, T], F32, kind="ExternalOutput").ap()
    xout_r = Reg("xout")
    g1i = nc.dram_tensor("g1i", [16, 4160], F32).ap()
    g1o = nc.dram_tensor("g1o", [64, 4160], F32).ap()
    g1r = [Reg("g1i"), Reg("g1o")]

    s = Sched(nc)
    with contextlib.ExitStack() as stk:
        ctx = Ctx(nc, stk, s)
        ctx.groups = [[0, 1, 2, 3], [4, 5, 6, 7]]
        ctx.psum_init(n_f32=6, n_bf16=1)
        block = stk.enter_context(nc.Block())
        alloc_common(ctx)
        build_block1(ctx, iv, ir, iv["xin"], ir["xin"], xout, xout_r,
                     g1i, g1o, g1r)
        last = [o for o in s.ops if o.dma and o.engine == "sync"][-1]
        s.op("gpsimd", _mk_emit("nop", (), {}), deps=[last, s.ops[-1]])
        s.build(stk, block)
    return nc


def run_block1(ins, x_cur):
    nc = build_prog_b1()
    in_maps = [stage_b1(ins, c, x_cur) for c in range(NCORE)]
    res = run_bass_kernel_spmd(nc, in_maps, core_ids=list(range(NCORE)))
    outs = [np.asarray(res.results[c]["xout"]).T for c in range(NCORE)]
    return np.stack([np.concatenate(outs[0:4], 0), np.concatenate(outs[4:8], 0)])


B2_INPUTS = [
    ("idf", [128, 128], "bf"), ("etf", [128, NC * 16], "f"),
    ("i16f", [16, 16], "f"),
    ("b2_WqT", [1024, 1024], "f"), ("b2_WkT", [1024, 1024], "f"),
    ("b2_WvT", [1024, 1024], "f"), ("b2_WoT", [1024, 1024], "f"),
    ("b2_AqT", [1024, 16], "f"), ("b2_AkT", [1024, 16], "f"),
    ("b2_AvT", [1024, 16], "f"), ("b2_AoT", [1024, 16], "f"),
    ("b2_BqT", [16, 1024], "f"), ("b2_BkT", [16, 1024], "f"),
    ("b2_BvT", [16, 1024], "f"), ("b2_BoT", [16, 1024], "f"),
    ("b2_bo", [128, NC], "f"),
    ("b2_w1t", [1024, 4096], "f"), ("b2_w2t", [1024, 4096], "f"),
    ("b2_h1t", [4096, 4096], "f"), ("b2_h2t", [4096, 4096], "f"),
    ("b2_w3t", [4096, 1024], "f"),
    ("b2_s1", [128, 32], "f"), ("b2_s2", [128, 32], "f"),
    ("b2_sh1", [128, 32], "f"), ("b2_sh2", [128, 32], "f"),
    ("b2_s3", [128, NC], "f"),
]


def stage_b2(ins, core, x_cur):
    b, q = core // 4, core % 4
    seg = x_cur[b, q * T:(q + 1) * T] if x_cur.shape[1] >= (q + 1) * T         else np.zeros((T, 1024), np.float32)
    d = {"xin": np.ascontiguousarray(seg.T)}
    idf = np.zeros((128, 128), np.float32)
    np.fill_diagonal(idf, 1.0)
    d["idf"] = idf.astype(ml_dtypes.bfloat16)
    d["i16f"] = np.eye(16, dtype=np.float32)
    etf = np.zeros((128, NC * 16), np.float32)
    for cc in range(NC):
        for p in range(128):
            etf[p, cc * 16 + 2 * cc + p // 64] = 1.0
    d["etf"] = etf
    wT = lambda k: np.ascontiguousarray(np.asarray(ins[k], np.float32).T)
    for nm, key in [("b2_WqT", "L2_attn_Wq"), ("b2_WkT", "L2_attn_Wk"),
                    ("b2_WvT", "L2_attn_Wv"), ("b2_WoT", "L2_attn_Wo"),
                    ("b2_AqT", "L2_attn_Aq"), ("b2_AkT", "L2_attn_Ak"),
                    ("b2_AvT", "L2_attn_Av"), ("b2_AoT", "L2_attn_Ao"),
                    ("b2_BqT", "L2_attn_Bq"), ("b2_BkT", "L2_attn_Bk"),
                    ("b2_BvT", "L2_attn_Bv"), ("b2_BoT", "L2_attn_Bo"),
                    ("b2_w1t", "L2_ffn_w1_w"), ("b2_w2t", "L2_ffn_w2_w"),
                    ("b2_h1t", "L2_ffn_h1_w"), ("b2_h2t", "L2_ffn_h2_w"),
                    ("b2_w3t", "L2_ffn_w3_w")]:
        d[nm] = wT(key)
    d["b2_bo"] = fmv(ins["L2_attn_bo"], NC)
    d["b2_s1"] = fmv(ins["L2_ffn_w1_s"], 32)
    d["b2_s2"] = fmv(ins["L2_ffn_w2_s"], 32)
    d["b2_sh1"] = fmv(ins["L2_ffn_h1_s"], 32)
    d["b2_sh2"] = fmv(ins["L2_ffn_h2_s"], 32)
    d["b2_s3"] = fmv(ins["L2_ffn_w3_s"], NC)
    return d


def build_prog_b2():
    nc = bass.Bass("TRN2", target_bir_lowering=False, debug=False)
    iv, ir = {}, {}

    def inp(name, shape, dt=F32):
        iv[name] = nc.dram_tensor(name, list(shape), dt, kind="ExternalInput").ap()
        ir[name] = Reg(name)

    inp("xin", [1024, T])
    for nm, shp, k in B2_INPUTS:
        inp(nm, shp, BF16 if k == "bf" else F32)
    xout = nc.dram_tensor("xout", [1024, T], F32, kind="ExternalOutput").ap()
    xout_r = Reg("xout")
    gki = nc.dram_tensor("gki", [1024, T], BF16).ap()
    gko = nc.dram_tensor("gko", [4096, T], BF16).ap()
    gvi = nc.dram_tensor("gvi", [T, 1024], BF16).ap()
    gvo = nc.dram_tensor("gvo", [4 * T, 1024], BF16).ap()
    gsi = nc.dram_tensor("gsi", [1, 16], F32).ap()
    gso = nc.dram_tensor("gso", [4, 16], F32).ap()
    g2r = [Reg(x) for x in ["gki", "gko", "gvi", "gvo", "gsi", "gso"]]

    s = Sched(nc)
    with contextlib.ExitStack() as stk:
        ctx = Ctx(nc, stk, s)
        ctx.groups = [[0, 1, 2, 3], [4, 5, 6, 7]]
        ctx.psum_init(n_f32=6, n_bf16=1)
        block = stk.enter_context(nc.Block())
        alloc_common(ctx)
        build_block2(ctx, iv, ir, iv["xin"], ir["xin"], xout, xout_r,
                     gki, gko, gvi, gvo, gsi, gso, g2r)
        last = [o for o in s.ops if o.dma and o.engine == "sync"][-1]
        s.op("gpsimd", _mk_emit("nop", (), {}), deps=[last, s.ops[-1]])
        s.build(stk, block)
    return nc


def run_block2(ins, x_cur):
    nc = build_prog_b2()
    in_maps = [stage_b2(ins, c, x_cur) for c in range(NCORE)]
    res = run_bass_kernel_spmd(nc, in_maps, core_ids=list(range(NCORE)))
    outs = [np.asarray(res.results[c]["xout"]).T for c in range(NCORE)]
    return np.stack([np.concatenate(outs[0:4], 0), np.concatenate(outs[4:8], 0)])


B3_INPUTS = [
    ("idf", [128, 128], "bf"), ("idf32", [128, 128], "f"),
    ("g_pm", [128, 128], "f"), ("musU", [128, 128], "f"),
    ("muI", [128, 128], "f"), ("tv", [128, 2], "f"),
    ("b3_WT", [1024, 1024], "f"), ("b3_lngr", [1, 1024], "f"),
    ("b3_lnbr", [1, 1024], "f"), ("b3_lam", [1, 1], "f"),
    ("b3_eta", [1, 1], "f"),
    ("b3_w1t", [1024, 4096], "f"), ("b3_w2t", [1024, 4096], "f"),
    ("b3_h1t", [4096, 4096], "f"), ("b3_h2t", [4096, 4096], "f"),
    ("b3_w3t", [4096, 1024], "f"),
    ("b3_s1", [128, 32], "f"), ("b3_s2", [128, 32], "f"),
    ("b3_sh1", [128, 32], "f"), ("b3_sh2", [128, 32], "f"),
    ("b3_s3", [128, NC], "f"), ("qsel", [128, 4], "f"),
]


def stage_b3(ins, core, x_cur):
    b, q = core // 4, core % 4
    seg = x_cur[b, q * T:(q + 1) * T] if x_cur.shape[1] >= (q + 1) * T         else np.zeros((T, 1024), np.float32)
    d = {"xin": np.ascontiguousarray(seg.T)}
    idf = np.zeros((128, 128), np.float32)
    np.fill_diagonal(idf, 1.0)
    d["idf"] = idf.astype(ml_dtypes.bfloat16)
    d["idf32"] = idf
    p = np.arange(128, dtype=np.float32)
    d["g_pm"] = np.ascontiguousarray(p[:, None] - p[None, :])
    d["musU"] = (p[:, None] < p[None, :]).astype(np.float32)
    d["muI"] = (p[:, None] <= p[None, :]).astype(np.float32)
    d["tv"] = np.ascontiguousarray(
        np.stack([p + 1.0, 128.0 - (p + 1.0)], axis=1))
    d["b3_WT"] = np.ascontiguousarray(np.asarray(ins["L3_attn_W"], np.float32).T)
    d["b3_lngr"] = np.asarray(ins["L3_attn_ln_gamma"], np.float32).reshape(1, 1024)
    d["b3_lnbr"] = np.asarray(ins["L3_attn_ln_beta"], np.float32).reshape(1, 1024)
    d["b3_lam"] = np.asarray(ins["L3_attn_lam_logit"], np.float32).reshape(1, 1)
    d["b3_eta"] = np.asarray(ins["L3_attn_eta"], np.float32).reshape(1, 1)
    wT = lambda k: np.ascontiguousarray(np.asarray(ins[k], np.float32).T)
    d["b3_w1t"] = wT("L3_ffn_w1_w")
    d["b3_w2t"] = wT("L3_ffn_w2_w")
    d["b3_h1t"] = wT("L3_ffn_h1_w")
    d["b3_h2t"] = wT("L3_ffn_h2_w")
    d["b3_w3t"] = wT("L3_ffn_w3_w")
    d["b3_s1"] = fmv(ins["L3_ffn_w1_s"], 32)
    d["b3_s2"] = fmv(ins["L3_ffn_w2_s"], 32)
    d["b3_sh1"] = fmv(ins["L3_ffn_h1_s"], 32)
    d["b3_sh2"] = fmv(ins["L3_ffn_h2_s"], 32)
    d["b3_s3"] = fmv(ins["L3_ffn_w3_s"], NC)
    qs = np.zeros((128, 4), np.float32)
    qs[:, q] = 1.0
    d["qsel"] = qs
    return d


def build_prog_b3():
    nc = bass.Bass("TRN2", target_bir_lowering=False, debug=False)
    iv, ir = {}, {}

    def inp(name, shape, dt=F32):
        iv[name] = nc.dram_tensor(name, list(shape), dt, kind="ExternalInput").ap()
        ir[name] = Reg(name)

    inp("xin", [1024, T])
    for nm, shp, k in B3_INPUTS:
        inp(nm, shp, BF16 if k == "bf" else F32)
    xout = nc.dram_tensor("xout", [1024, T], F32, kind="ExternalOutput").ap()
    xout_r = Reg("xout")
    a1i = nc.dram_tensor("a1i", [T, 2048], BF16).ap()
    a1o = nc.dram_tensor("a1o", [4 * T, 2048], BF16).ap()
    a2i = nc.dram_tensor("a2i", [4 * T, 256], F32).ap()
    a2o = nc.dram_tensor("a2o", [16 * T, 256], F32).ap()
    g3r = [Reg(x) for x in ["a1i", "a1o", "a2i", "a2o"]]

    s = Sched(nc)
    with contextlib.ExitStack() as stk:
        ctx = Ctx(nc, stk, s)
        ctx.groups = [[0, 1, 2, 3], [4, 5, 6, 7]]
        ctx.psum_init(n_f32=6, n_bf16=1)
        block = stk.enter_context(nc.Block())
        alloc_common(ctx)
        build_block3(ctx, iv, ir, iv["xin"], ir["xin"], xout, xout_r,
                     a1i, a1o, a2i, a2o, g3r)
        last = [o for o in s.ops if o.dma and o.engine == "sync"][-1]
        s.op("gpsimd", _mk_emit("nop", (), {}), deps=[last, s.ops[-1]])
        s.build(stk, block)
    return nc


def run_block3(ins, x_cur):
    nc = build_prog_b3()
    in_maps = [stage_b3(ins, c, x_cur) for c in range(NCORE)]
    res = run_bass_kernel_spmd(nc, in_maps, core_ids=list(range(NCORE)))
    outs = [np.asarray(res.results[c]["xout"]).T for c in range(NCORE)]
    return np.stack([np.concatenate(outs[0:4], 0), np.concatenate(outs[4:8], 0)])


def kernel(**inputs):
    _unused = (inputs["L0_attn_v0"], inputs["L0_attn_v1"], inputs["L0_attn_v2"])
    return kernel_fused(**inputs)


def kernel_unfused(**inputs):
    x1 = run_block0(inputs)
    x2 = run_block1(inputs, x1)
    x3 = run_block2(inputs, x2)
    x4 = run_block3(inputs, x3)
    return np.ascontiguousarray(x4.astype(np.float32))


def build_prog_fused():
    nc = bass.Bass("TRN2", target_bir_lowering=False, debug=False)
    iv, ir = {}, {}

    def inp(name, shape, dt=F32):
        if name in iv:
            return
        iv[name] = nc.dram_tensor(name, list(shape), dt, kind="ExternalInput").ap()
        ir[name] = Reg(name)

    inp("xin", [1024, T + 1])
    for nm, shp in B0_INPUTS:
        inp(nm, shp)
    for nm, shp in B1_INPUTS:
        inp(nm, shp, BF16 if nm == "idf" else F32)
    for nm, shp, k in B2_INPUTS:
        inp(nm, shp, BF16 if k == "bf" else F32)
    for nm, shp, k in B3_INPUTS:
        inp(nm, shp, BF16 if k == "bf" else F32)
    xout = nc.dram_tensor("xout", [1024, T], F32, kind="ExternalOutput").ap()
    xout_r = Reg("xout")
    x1_d = nc.dram_tensor("x1_d", [1024, T], F32).ap()
    x2_d = nc.dram_tensor("x2_d", [1024, T], F32).ap()
    x3_d = nc.dram_tensor("x3_d", [1024, T], F32).ap()
    x1_r, x2_r, x3_r = Reg("x1"), Reg("x2"), Reg("x3")
    gin = [nc.dram_tensor(f"g0i_{c}", [2, 128], F32).ap() for c in range(NC)]
    gout = [nc.dram_tensor(f"g0o_{c}", [8, 128], F32).ap() for c in range(NC)]
    gr = []
    for c in range(NC):
        gr += [Reg(f"g0i_{c}"), Reg(f"g0o_{c}")]
    g1i = nc.dram_tensor("g1i", [16, 4160], F32).ap()
    g1o = nc.dram_tensor("g1o", [64, 4160], F32).ap()
    g1r = [Reg("g1i"), Reg("g1o")]
    gki = nc.dram_tensor("gki", [1024, T], BF16).ap()
    gko = nc.dram_tensor("gko", [4096, T], BF16).ap()
    gvi = nc.dram_tensor("gvi", [T, 1024], BF16).ap()
    gvo = nc.dram_tensor("gvo", [4 * T, 1024], BF16).ap()
    gsi = nc.dram_tensor("gsi", [1, 16], F32).ap()
    gso = nc.dram_tensor("gso", [4, 16], F32).ap()
    g2r = [Reg(x) for x in ["gki", "gko", "gvi", "gvo", "gsi", "gso"]]
    a1i = nc.dram_tensor("a1i", [T, 2048], BF16).ap()
    a1o = nc.dram_tensor("a1o", [4 * T, 2048], BF16).ap()
    a2i = nc.dram_tensor("a2i", [4 * T, 256], F32).ap()
    a2o = nc.dram_tensor("a2o", [16 * T, 256], F32).ap()
    g3r = [Reg(x) for x in ["a1i", "a1o", "a2i", "a2o"]]

    s = Sched(nc)
    with contextlib.ExitStack() as stk:
        ctx = Ctx(nc, stk, s)
        ctx.groups = [[0, 1, 2, 3], [4, 5, 6, 7]]
        ctx.psum_init(n_f32=6, n_bf16=1)
        block = stk.enter_context(nc.Block())
        alloc_common(ctx)
        prog_stk = ctx.stk
        for bi, fn in enumerate([
            lambda: build_block0(ctx, iv, ir, iv["xin"], ir["xin"], x1_d, x1_r,
                                 gin, gout, gr),
            lambda: build_block1(ctx, iv, ir, x1_d, x1_r, x2_d, x2_r,
                                 g1i, g1o, g1r),
            lambda: build_block2(ctx, iv, ir, x2_d, x2_r, x3_d, x3_r,
                                 gki, gko, gvi, gvo, gsi, gso, g2r),
            lambda: build_block3(ctx, iv, ir, x3_d, x3_r, xout, xout_r,
                                 a1i, a1o, a2i, a2o, g3r),
        ]):
            with contextlib.ExitStack() as bs:
                ctx.stk = bs
                fn()
                ctx.fence(f"blk{bi}")
            ctx.stk = prog_stk
        last = [o for o in s.ops if o.dma and o.engine == "sync"][-1]
        s.op("gpsimd", _mk_emit("nop", (), {}), deps=[last, s.ops[-1]])
        s.build(stk, block)
    return nc


def kernel_fused(**inputs):
    nc = build_prog_fused()
    in_maps = []
    for c in range(NCORE):
        d = {}
        d.update(stage_b3(inputs, c, np.zeros((2, 1, 1024), np.float32)))
        d.update(stage_b2(inputs, c, np.zeros((2, 1, 1024), np.float32)))
        d.update(stage_b1(inputs, c, np.zeros((2, 1, 1024), np.float32)))
        d.update(stage_b0(inputs, c))
        d.pop("xin")
        b, q = c // 4, c % 4
        x = np.asarray(inputs["L0_attn_x"], np.float32)
        seg = x[b, q * T:(q + 1) * T]
        prev = x[b, q * T - 1:q * T] if q > 0 else np.zeros((1, 1024), np.float32)
        d["xin"] = np.ascontiguousarray(np.concatenate([prev, seg], 0).T)
        in_maps.append(d)
    res = run_bass_kernel_spmd(nc, in_maps, core_ids=list(range(NCORE)))
    outs = [np.asarray(res.results[c]["xout"]).T for c in range(NCORE)]
    x4 = np.stack([np.concatenate(outs[0:4], 0), np.concatenate(outs[4:8], 0)])
    return np.ascontiguousarray(x4.astype(np.float32))


# version 06 (final)
# speedup vs baseline: 1.1086x; 1.0204x over previous
import contextlib
import math
import numpy as np
import ml_dtypes
import concourse.bass as bass
import concourse.mybir as mybir
from concourse.bass_utils import run_bass_kernel_spmd

F32 = mybir.dt.float32
BF16 = mybir.dt.bfloat16
AF = mybir.ActivationFunctionType
ALU = mybir.AluOpType
AXX = mybir.AxisListType.X

NCORE = 8
DUMP = False
B1_STAGE = 5
B1_SUB = 9
B2_STAGE = 5
B2_SUB = 9
T = 2048
NC = 8
H = 16
HD = 64
TT = 512
NTT = T // TT

ENGINES = ("tensor", "vector", "scalar", "gpsimd", "sync")
DMA_POOL = {"sync": 18, "gpsimd": 18, "scalar": 12, "vector": 4, "tensor": 4}
CC_POOL = 6


class Reg:
    __slots__ = ("writer", "readers", "name")

    def __init__(self, name=""):
        self.writer = None
        self.readers = []
        self.name = name


class _Op:
    __slots__ = ("engine", "emit", "deps", "dma", "cc", "seq", "flagged",
                 "tag", "sem_slot", "sem_use", "eidx")

    def __init__(self, engine, emit, deps, dma, cc, seq, tag):
        self.engine = engine
        self.emit = emit
        self.deps = deps
        self.dma = dma
        self.cc = cc
        self.seq = seq
        self.flagged = False
        self.tag = tag
        self.sem_slot = None
        self.sem_use = None
        self.eidx = None


class Sched:
    def __init__(self, nc):
        self.nc = nc
        self.ops = []
        self.streams = {e: [] for e in ENGINES}
        self._dma_count = {e: 0 for e in ENGINES}
        self._cc_count = 0

    def op(self, engine, emit, deps=(), dma=False, cc=False, tag=""):
        assert engine in ENGINES, engine
        deps = [d for d in deps if d is not None]
        o = _Op(engine, emit, deps, dma, cc, len(self.ops), tag)
        if dma:
            n = self._dma_count[engine]
            pool = DMA_POOL[engine]
            o.sem_slot = (engine, n % pool)
            o.sem_use = n // pool + 1
            self._dma_count[engine] = n + 1
        elif cc:
            n = self._cc_count
            o.sem_slot = ("cc", n % CC_POOL)
            o.sem_use = n // CC_POOL + 1
            self._cc_count = n + 1
        self.streams[engine].append(o)
        self.ops.append(o)
        for d in deps:
            if not (d.engine == "tensor" and engine == "tensor"
                    and not d.dma and not d.cc and not dma and not cc):
                d.flagged = True
        return o

    def rw(self, engine, emit, reads=(), writes=(), deps=(), dma=False, cc=False,
           tag=""):
        dd = list(deps)
        for r in reads:
            if r.writer is not None:
                dd.append(r.writer)
        for w in writes:
            if w.writer is not None:
                dd.append(w.writer)
            dd.extend(w.readers)
        h = self.op(engine, emit, deps=dd, dma=dma, cc=cc, tag=tag)
        for r in reads:
            r.readers.append(h)
        for w in writes:
            w.writer = h
            w.readers = []
        return h

    def dma(self, queue, out, in_, reads=(), writes=(), deps=(), tag="", **kw):
        return self.rw(queue, lambda e: e.dma_start(out=out, in_=in_, **kw),
                       reads=reads, writes=writes, deps=deps, dma=True, tag=tag)

    def prepare(self, stack):
        nc = self.nc
        self._esem = {e: stack.enter_context(nc.semaphore(f"s_{e}"))
                      for e in ENGINES}
        self._dsem = {}
        for q, npool in DMA_POOL.items():
            used = self._dma_count[q]
            for slot in range(min(npool, used)):
                self._dsem[(q, slot)] = stack.enter_context(
                    nc.semaphore(f"d_{q}_{slot}"))
        for slot in range(min(CC_POOL, self._cc_count)):
            self._dsem[("cc", slot)] = stack.enter_context(
                nc.semaphore(f"d_cc_{slot}"))
        self._flag_count = {e: {} for e in ENGINES}
        for e in ENGINES:
            c = 0
            for o in self.streams[e]:
                if (not o.dma) and (not o.cc) and o.flagged:
                    c += 1
                o.eidx = c
                self._flag_count[e][o.seq] = c

    def emit_engine(self, engine, handle):
        water = {e: 0 for e in ENGINES}
        dwater = {}
        for o in self.streams[engine]:
            for d in o.deps:
                if d.dma or d.cc:
                    key = d.sem_slot
                    need = (16 if d.dma else 1) * d.sem_use
                    if dwater.get(key, 0) < need:
                        handle.wait_ge(self._dsem[key], need)
                        dwater[key] = need
                elif not (d.engine == "tensor" and engine == "tensor"
                          and not o.dma and not o.cc):
                    need = self._flag_count[d.engine][d.seq]
                    assert need > 0, (d.engine, d.tag, o.tag)
                    if water[d.engine] < need:
                        handle.wait_ge(self._esem[d.engine], need)
                        water[d.engine] = need
            if (o.dma or o.cc) and o.sem_use > 1:
                key = o.sem_slot
                need = (16 if o.dma else 1) * (o.sem_use - 1)
                if dwater.get(key, 0) < need:
                    handle.wait_ge(self._dsem[key], need)
                    dwater[key] = need
            inst = o.emit(handle)
            if o.dma:
                inst.then_inc(self._dsem[o.sem_slot], 16)
            elif o.cc:
                inst.then_inc(self._dsem[o.sem_slot])
            elif o.flagged:
                inst.then_inc(self._esem[engine], 1)

    def build(self, stack, block):
        self.prepare(stack)
        sched = self

        @block.sync
        def _(sync):
            sched.emit_engine("sync", sync)

        @block.vector
        def _(vector):
            sched.emit_engine("vector", vector)

        @block.scalar
        def _(scalar):
            sched.emit_engine("scalar", scalar)

        @block.tensor
        def _(tensor):
            sched.emit_engine("tensor", tensor)

        @block.gpsimd
        def _(gpsimd):
            sched.emit_engine("gpsimd", gpsimd)


class SB:
    def __init__(self, t, reg):
        self.t = t
        self.r = reg

    def __getitem__(self, idx):
        return self.t[idx]


def _mk_emit(fn_name, args, kw):
    def emit(e):
        return getattr(e, fn_name)(*args, **kw)
    return emit


class Ctx:
    def __init__(self, nc, stk, sched):
        self.nc = nc
        self.stk = stk
        self.s = sched
        self.psum_banks = []
        self.psum_rr = 0
        self.cm = {}

    def sb(self, name, shape, dtype, stk=None):
        t = (stk or self.stk).enter_context(
            self.nc.sbuf_tensor("s_" + name, list(shape), dtype))
        r = Reg(name)
        r.writer = getattr(self, "_fence", None)
        return SB(t, r)

    def fence(self, tag="fence"):
        lasts = [st[-1] for st in self.s.streams.values() if st]
        hs = []
        for e in ENGINES:
            hs.append(self.s.op(e, _mk_emit("nop", (), {}), deps=lasts,
                                tag=tag + e))
        f = self.s.op("gpsimd", _mk_emit("nop", (), {}), deps=hs, tag=tag)
        self._fence = f
        return f

    def psum_init(self, n_f32=6, n_bf16=2):
        for i in range(n_f32):
            t = self.stk.enter_context(self.nc.psum_tensor(f"pb{i}", [128, 512], F32))
            self.psum_banks.append(SB(t, Reg(f"pb{i}")))
        self.psum_bf = []
        for i in range(n_bf16):
            t = self.stk.enter_context(self.nc.psum_tensor(f"pbh{i}", [128, 1024], BF16))
            self.psum_bf.append(SB(t, Reg(f"pbh{i}")))
        self.psum_bf_rr = 0

    def psum(self):
        b = self.psum_banks[self.psum_rr % len(self.psum_banks)]
        self.psum_rr += 1
        return b

    def psum_at(self, i):
        return self.psum_banks[i]

    def psumb(self):
        b = self.psum_bf[self.psum_bf_rr % len(self.psum_bf)]
        self.psum_bf_rr += 1
        return b

    def e(self, engine, fn, *args, reads=(), writes=(), deps=(), tag="", **kw):
        return self.s.rw(engine, _mk_emit(fn, args, kw),
                         reads=reads, writes=writes, deps=deps, tag=tag)

    def mm(self, out_ap, lhsT_ap, rhs_ap, start, stop, reads=(), writes=(),
           deps=(), tag=""):
        return self.s.rw("tensor", _mk_emit(
            "matmul", (out_ap, lhsT_ap, rhs_ap),
            dict(start=start, stop=stop, skip_group_check=True)),
            reads=reads, writes=writes, deps=deps, tag=tag)

    def dma(self, queue, out, in_, reads=(), writes=(), deps=(), tag="", **kw):
        return self.s.dma(queue, out, in_, reads=reads, writes=writes,
                          deps=deps, tag=tag, **kw)


def ts(i, n):
    return slice(i * n, (i + 1) * n)


def alloc_common(ctx):
    cm = ctx.cm
    cm["xt"] = ctx.sb("cm_xt", [128, NC, TT], F32)
    cm["xb"] = ctx.sb("cm_xb", [128, NC, TT], BF16)
    cm["lnst"] = ctx.sb("cm_lnst", [1, 4 * TT], F32)
    cm["absb"] = ctx.sb("cm_absb", [128, TT], F32)
    cm["bbsb"] = ctx.sb("cm_bbsb", [128, TT], F32)
    cm["ones_b"] = ctx.sb("cm_onesb", [128, 1], BF16)
    cm["ones1f"] = ctx.sb("cm_ones1f", [1, 128], F32)
    cm["e1"] = ctx.sb("cm_e1", [128, TT], F32)
    cm["ei"] = ctx.sb("cm_ei", [128, TT], mybir.dt.int32)
    cm["eb1"] = ctx.sb("cm_eb1", [128, TT], BF16)
    cm["e2"] = ctx.sb("cm_e2", [128, TT], F32)
    cm["e3"] = ctx.sb("cm_e3", [128, TT], F32)
    cm["e4"] = ctx.sb("cm_e4", [128, TT], F32)
    cm["st16"] = ctx.sb("cm_st16", [16, TT], F32)
    ctx.e("vector", "memset", cm["ones_b"][:], 1.0, writes=[cm["ones_b"].r])
    ctx.e("vector", "memset", cm["ones1f"][:], 1.0, writes=[cm["ones1f"].r])


def emit_ln(ctx, xdram, xdram_reg, xn, Tc):
    cm = ctx.cm
    starts = [i * TT for i in range(Tc // TT)]
    if Tc % TT:
        starts.append(Tc - TT)
    for it, a0 in enumerate(starts):
        w = TT
        xt, xb, st = cm["xt"], cm["xb"], cm["lnst"]
        ctx.dma("sync", xt[:, :, :w],
                xdram.rearrange("(c p) t -> p c t", p=128)[:, :, a0:a0 + w],
                reads=[xdram_reg], writes=[xt.r], tag=f"lnld{it}",
                **({"allow_slow_non_contiguous": True} if w < 16 else {}))
        ctx.e("vector", "tensor_copy", xb[:, :, :w], xt[:, :, :w],
              reads=[xt.r], writes=[xb.r])
        psA = ctx.psum()
        psB = ctx.psum()
        for c in range(NC):
            ctx.mm(psA[0:1, :w], cm["ones_b"][:, 0:1], xb[:, c, :w],
                   start=(c == 0), stop=(c == NC - 1),
                   reads=[cm["ones_b"].r, xb.r], writes=[psA.r])
        ctx.e("scalar", "activation", xb[:, :, :w], xb[:, :, :w], AF.Square,
              reads=[xb.r], writes=[xb.r])
        for c in range(NC):
            ctx.mm(psB[0:1, :w], cm["ones_b"][:, 0:1], xb[:, c, :w],
                   start=(c == 0), stop=(c == NC - 1),
                   reads=[cm["ones_b"].r, xb.r], writes=[psB.r])
        M = slice(0, w)
        V = slice(TT, TT + w)
        R = slice(2 * TT, 2 * TT + w)
        Bb = slice(3 * TT, 3 * TT + w)
        ctx.e("vector", "tensor_scalar_mul", st[0:1, M], psA[0:1, :w], 1.0 / 1024.0,
              reads=[psA.r], writes=[st.r])
        ctx.e("vector", "tensor_scalar_mul", st[0:1, V], psB[0:1, :w], 1.0 / 1024.0,
              reads=[psB.r], writes=[st.r])
        ctx.e("vector", "tensor_mul", st[0:1, R], st[0:1, M], st[0:1, M],
              reads=[st.r], writes=[st.r])
        ctx.e("vector", "tensor_sub", st[0:1, V], st[0:1, V], st[0:1, R],
              reads=[st.r], writes=[st.r])
        ctx.e("vector", "tensor_scalar_add", st[0:1, V], st[0:1, V], 1e-5,
              reads=[st.r], writes=[st.r])
        ctx.e("scalar", "activation", st[0:1, R], st[0:1, V], AF.Sqrt,
              reads=[st.r], writes=[st.r])
        ctx.e("vector", "reciprocal", st[0:1, Bb], st[0:1, R],
              reads=[st.r], writes=[st.r])
        ctx.e("vector", "scalar_tensor_tensor", st[0:1, M], st[0:1, M], -1.0,
              st[0:1, Bb], op0=ALU.mult, op1=ALU.mult,
              reads=[st.r], writes=[st.r])
        psC = ctx.psum()
        psD = ctx.psum()
        ctx.mm(psC[:, :w], cm["ones1f"][:], st[0:1, Bb], start=True, stop=True,
               reads=[cm["ones1f"].r, st.r], writes=[psC.r])
        ctx.mm(psD[:, :w], cm["ones1f"][:], st[0:1, M], start=True, stop=True,
               reads=[cm["ones1f"].r, st.r], writes=[psD.r])
        ab, bb = cm["absb"], cm["bbsb"]
        ctx.e("scalar", "copy", ab[:, :w], psC[:, :w], reads=[psC.r], writes=[ab.r])
        ctx.e("scalar", "copy", bb[:, :w], psD[:, :w], reads=[psD.r], writes=[bb.r])
        for c in range(NC):
            eng = "vector" if c % 2 == 0 else "gpsimd"
            eb = cm["e1"] if c % 2 == 0 else cm["e2"]
            ctx.e(eng, "tensor_mul", eb[:, :w], xt[:, c, :w], ab[:, :w],
                  reads=[xt.r, ab.r], writes=[eb.r])
            ctx.e(eng, "tensor_add", xn[:, c, a0:a0 + w], eb[:, :w], bb[:, :w],
                  reads=[eb.r, bb.r], writes=[xn.r])


def wslices_ap(wdram, c):
    return wdram.rearrange("(kc p) o -> p kc o", p=128)[:, :, ts(c, 128)]


def xfm(xdram, p=128):
    return xdram.rearrange("(c p) t -> p c t", p=p)


def build_block0(ctx, iv, ir, xin_d, xin_r, xout_d, xout_r, gin, gout, gr):
    s, cm = ctx.s, ctx.cm
    TC = T + 1
    e1, e2, e3, e4 = cm["e1"], cm["e2"], cm["e3"], cm["e4"]
    st16 = cm["st16"]

    xn = ctx.sb("b0_xn", [128, NC, TC], BF16)
    att = ctx.sb("b0_att", [128, NC, T], BF16)

    with contextlib.ExitStack() as ms:
        zsh = ctx.sb("b0_zsh", [64, T], BF16, ms)
        zg = ctx.sb("b0_zg", [64, T], BF16, ms)
        wsb = ctx.sb("b0_w", [16, T], F32, ms)
        asb = ctx.sb("b0_a", [16, T], F32, ms)
        mixsb = ctx.sb("b0_mix", [128, 6, NC], F32, ms)
        w0sb = ctx.sb("b0_w0s", [16, 1], F32, ms)
        a0sb = ctx.sb("b0_a0s", [16, 1], F32, ms)
        kkv = ctx.sb("b0_kkvs", [128, NC], F32, ms)
        rkv = ctx.sb("b0_rkvs", [128, NC], F32, ms)
        lnw = ctx.sb("b0_lnws", [128, NC], F32, ms)
        lnb = ctx.sb("b0_lnbs", [128, NC], F32, ms)
        pmk = ctx.sb("b0_pmks", [128, NC], F32, ms)
        ebcf = ctx.sb("b0_ebcfs", [16, NC * 128], F32, ms)
        etf = ctx.sb("b0_etfs", [128, NC * 16], F32, ms)
        w1sb = ctx.sb("b0_w1s", [128, NC, 64], BF16, ms)
        a1sb = ctx.sb("b0_a1s", [128, NC, 64], BF16, ms)
        g1sb = ctx.sb("b0_g1s", [128, NC, 64], BF16, ms)
        w2sb = ctx.sb("b0_w2s", [64, 16], BF16, ms)
        a2sb = ctx.sb("b0_a2s", [64, 16], BF16, ms)
        g2sb = ctx.sb("b0_g2s", [64, 1024], BF16, ms)
        rsb = ctx.sb("b0_r", [128, T], BF16, ms)
        kkn = ctx.sb("b0_kkn", [128, T], BF16, ms)
        vsb = ctx.sb("b0_v", [128, T], BF16, ms)
        kvsb = ctx.sb("b0_kv", [128, T], F32, ms)
        wbc = ctx.sb("b0_wbc", [128, T], F32, ms)
        states = ctx.sb("b0_states", [128, T], F32, ms)
        mxa = ctx.sb("b0_mxa", [128, TT], BF16, ms)
        mxb = ctx.sb("b0_mxb", [128, TT], BF16, ms)
        mxc = ctx.sb("b0_mxc", [128, TT], BF16, ms)
        sxa = ctx.sb("b0_sxa", [128, TT], BF16, ms)
        wk3 = [ctx.sb(f"b0_wk3_{i}", [128, NC, 3 * 128], BF16, ms) for i in range(2)]
        stin = [ctx.sb(f"b0_stin{i}", [128, 1], F32, ms) for i in range(4)]
        jS = [ctx.sb(f"b0_jS{i}", [128, 1], F32, ms) for i in range(3)]
        jA = [ctx.sb(f"b0_jA{i}", [128, 1], F32, ms) for i in range(3)]
        jA2 = [ctx.sb(f"b0_jA2{i}", [128, 1], F32, ms) for i in range(3)]
        jS2 = [ctx.sb(f"b0_jS2{i}", [128, 1], F32, ms) for i in range(3)]
        acol = ctx.sb("b0_acol", [128, 1], F32, ms)
        lsum = ctx.sb("b0_lsum", [16, 8], F32, ms)
        nsb = ctx.sb("b0_nsb", [16, TT], F32, ms)
        bcoef = ctx.sb("b0_bcoef", [16, TT], F32, ms)

        for sb, nm in [(mixsb, "b0_mixv"), (w0sb, "b0_w0v"), (a0sb, "b0_a0v"),
                       (kkv, "b0_kk"), (rkv, "b0_rk"), (lnw, "b0_lnw"),
                       (lnb, "b0_lnb"), (pmk, "pmask"), (ebcf, "ebcf"),
                       (etf, "etf")]:
            ctx.dma("sync", sb[:], iv[nm], reads=[ir[nm]], writes=[sb.r], tag=nm)
        for sb, nm in [(w1sb, "b0_w1"), (a1sb, "b0_a1"), (g1sb, "b0_g1")]:
            ctx.dma("gpsimd", sb[:], iv[nm].rearrange("(kc p) o -> p kc o", p=128),
                    reads=[ir[nm]], writes=[sb.r], tag=nm)
        for sb, nm in [(w2sb, "b0_w2"), (a2sb, "b0_a2"), (g2sb, "b0_g2")]:
            ctx.dma("gpsimd", sb[:], iv[nm], reads=[ir[nm]], writes=[sb.r], tag=nm)

        emit_ln(ctx, xin_d, xin_r, xn, TC)

        def mix_tiles(mjs, dsts, kc, tt_i, w=TT):
            a0 = tt_i * TT
            sxs = sxa
            ctx.e("gpsimd", "tensor_sub", sxs[:, :w], xn[:, kc, 1 + a0:1 + a0 + w],
                  xn[:, kc, a0:a0 + w], reads=[xn.r], writes=[sxs.r])
            for mj, dst in zip(mjs, dsts):
                ctx.e("vector", "scalar_tensor_tensor", dst[:, :w],
                      sxs[:, :w], mixsb[:, mj, kc:kc + 1],
                      xn[:, kc, 1 + a0:1 + a0 + w], op0=ALU.mult, op1=ALU.add,
                      reads=[sxs.r, mixsb.r, xn.r], writes=[dst.r])

        mz = [mxa, mxb, mxc]
        for tt_i in range(NTT):
            psw, psa, psg = ctx.psum(), ctx.psum(), ctx.psum()
            for kc in range(NC):
                mix_tiles([1, 4, 5], mz, kc, tt_i)
                for ps, w1x, mx in [(psw, w1sb, mz[0]), (psa, a1sb, mz[1]),
                                    (psg, g1sb, mz[2])]:
                    ctx.mm(ps[0:64, :], w1x[:, kc, :], mx[:, :],
                           start=(kc == 0), stop=(kc == NC - 1),
                           reads=[w1x.r, mx.r], writes=[ps.r])
            ctx.e("scalar", "activation", zsh[:, ts(tt_i, TT)], psw[0:64, :],
                  AF.Tanh, reads=[psw.r], writes=[zsh.r])
            ctx.e("scalar", "activation", zg[:, ts(tt_i, TT)], psg[0:64, :],
                  AF.Tanh, reads=[psg.r], writes=[zg.r])
            ctx.e("scalar", "activation", mz[1][0:64, 0:TT], psa[0:64, :],
                  AF.Tanh, reads=[psa.r], writes=[mz[1].r], tag="zatile")
            ps2 = ctx.psum()
            ctx.mm(ps2[0:16, :], a2sb[:], mz[1][0:64, 0:TT], start=True,
                   stop=True, reads=[a2sb.r, mz[1].r], writes=[ps2.r])
            ctx.e("scalar", "activation", st16[0:16, :], ps2[0:16, :],
                  AF.Sigmoid, bias=a0sb[:, 0:1], scale=1.0,
                  reads=[ps2.r, a0sb.r], writes=[st16.r])
            ctx.e("vector", "tensor_copy", asb[:, ts(tt_i, TT)], st16[0:16, :],
                  reads=[st16.r], writes=[asb.r])
        for tt_i in range(NTT):
            ps2 = ctx.psum()
            ctx.mm(ps2[0:16, :], w2sb[:], zsh[:, ts(tt_i, TT)], start=True,
                   stop=True, reads=[w2sb.r, zsh.r], writes=[ps2.r])
            ctx.e("scalar", "activation", st16[0:16, :], ps2[0:16, :],
                  AF.Sigmoid, bias=w0sb[:, 0:1], scale=1.0,
                  reads=[ps2.r, w0sb.r], writes=[st16.r])
            ctx.e("vector", "tensor_scalar_mul", wsb[:, ts(tt_i, TT)],
                  st16[0:16, :], math.exp(-0.5), reads=[st16.r], writes=[wsb.r])
        for tt_i in range(NTT):
            ctx.e("scalar", "activation", st16[0:16, :], wsb[:, ts(tt_i, TT)],
                  AF.Ln, reads=[wsb.r], writes=[st16.r])
            ctx.e("vector", "reduce_sum", lsum[0:16, tt_i + 1:tt_i + 2],
                  st16[0:16, :], axis=AXX, reads=[st16.r], writes=[lsum.r])
        ctx.e("vector", "tensor_add", lsum[0:16, 1:2], lsum[0:16, 1:2],
              lsum[0:16, 2:3], reads=[lsum.r], writes=[lsum.r])
        ctx.e("vector", "tensor_add", lsum[0:16, 3:4], lsum[0:16, 3:4],
              lsum[0:16, 4:5], reads=[lsum.r], writes=[lsum.r])
        ctx.e("vector", "tensor_add", lsum[0:16, 1:2], lsum[0:16, 1:2],
              lsum[0:16, 3:4], reads=[lsum.r], writes=[lsum.r])
        ctx.e("scalar", "activation", lsum[0:16, 0:1], lsum[0:16, 1:2],
              AF.Exp, reads=[lsum.r], writes=[lsum.r])

        if DUMP:
            dxn = ctx.nc.dram_tensor("d_xn", [1024, T + 1], F32, kind="ExternalOutput").ap()
            dxnr = Reg("d_xn")
            for cc in range(NC):
                ctx.dma("gpsimd", dxn.rearrange("(c p) t -> p c t", p=128)[:, cc, :],
                        xn[:, cc, :], reads=[xn.r], writes=[dxnr], tag=f"dxn{cc}")
            dw = ctx.nc.dram_tensor("d_w", [16, T], F32, kind="ExternalOutput").ap()
            dwr = Reg("d_w")
            ctx.dma("gpsimd", dw, wsb[:], reads=[wsb.r], writes=[dwr])
            da = ctx.nc.dram_tensor("d_a", [16, T], F32, kind="ExternalOutput").ap()
            dar = Reg("d_a")
            ctx.dma("gpsimd", da, asb[:], reads=[asb.r], writes=[dar])
            dzg = ctx.nc.dram_tensor("d_zg", [64, T], F32, kind="ExternalOutput").ap()
            dzgr = Reg("d_zg")
            ctx.dma("gpsimd", dzg, zg[:], reads=[zg.r], writes=[dzgr])
            dst_d = ctx.nc.dram_tensor("d_states", [1024, T], F32, kind="ExternalOutput").ap()
            dstr = Reg("d_states")
            dkv_d = ctx.nc.dram_tensor("d_kv", [1024, T], F32, kind="ExternalOutput").ap()
            dkvr = Reg("d_kv")
            dr_d = ctx.nc.dram_tensor("d_r", [1024, T], F32, kind="ExternalOutput").ap()
            drr = Reg("d_r")
            dkk_d = ctx.nc.dram_tensor("d_kkn", [1024, T], F32, kind="ExternalOutput").ap()
            dkkr = Reg("d_kkn")
            dv_d = ctx.nc.dram_tensor("d_v", [1024, T], F32, kind="ExternalOutput").ap()
            dvr = Reg("d_v")
            dsi_d = ctx.nc.dram_tensor("d_stin", [1024, NC], F32, kind="ExternalOutput").ap()
            dsir = Reg("d_stin")
            dls_d = ctx.nc.dram_tensor("d_lsum", [16, NC], F32, kind="ExternalOutput").ap()
            dlsr = Reg("d_lsum")
            ctx.dma("gpsimd", dls_d, lsum[0:16, :], reads=[lsum.r], writes=[dlsr])
        for c in range(NC):
            wk = wk3[c % 2]
            for j, nm in enumerate(["b0_Wr", "b0_Wk", "b0_Wv"]):
                ctx.dma("gpsimd", wk[:, :, ts(j, 128)], wslices_ap(iv[nm], c),
                        reads=[ir[nm]], writes=[wk.r], tag=f"{nm}{c}")
            ebc_c = ebcf[:, ts(c, 128)]
            et_c = etf[:, ts(c, 16)]
            for tt_i in range(NTT):
                ta = ts(tt_i, TT)
                psR, psK, psV = ctx.psum(), ctx.psum(), ctx.psum()
                m3 = [mz[0], mz[1], mz[2]]
                for kc in range(NC):
                    mix_tiles([0, 2, 3], m3, kc, tt_i)
                    for ps, j, mx in [(psR, 0, m3[0]), (psK, 1, m3[1]),
                                      (psV, 2, m3[2])]:
                        ctx.mm(ps[:, :], wk[:, kc, ts(j, 128)], mx[:, :],
                               start=(kc == 0), stop=(kc == NC - 1),
                               reads=[wk.r, mx.r], writes=[ps.r])
                ctx.e("scalar", "copy", rsb[:, ta], psR[:, :],
                      reads=[psR.r], writes=[rsb.r])
                ctx.e("scalar", "copy", vsb[:, ta], psV[:, :],
                      reads=[psV.r], writes=[vsb.r])
                ctx.e("vector", "tensor_scalar_mul", kkn[:, ta], psK[:, :],
                      kkv[:, c:c + 1], reads=[psK.r, kkv.r], writes=[kkn.r])
                ctx.e("scalar", "activation", e1[:, :], kkn[:, ta], AF.Square,
                      reads=[kkn.r], writes=[e1.r])
                psN = ctx.psum()
                ctx.mm(psN[0:16, :], et_c, e1[:, :], start=True, stop=True,
                       reads=[etf.r, e1.r], writes=[psN.r])
                ctx.e("scalar", "activation", nsb[0:16, :], psN[0:16, :],
                      AF.Sqrt, reads=[psN.r], writes=[nsb.r])
                ctx.e("vector", "tensor_scalar_max", nsb[0:16, :],
                      nsb[0:16, :], 1e-12, reads=[nsb.r], writes=[nsb.r])
                ctx.e("vector", "reciprocal", nsb[0:16, :], nsb[0:16, :],
                      reads=[nsb.r], writes=[nsb.r])
                psI = ctx.psum()
                ctx.mm(psI[:, :], ebc_c, nsb[0:16, :], start=True, stop=True,
                       reads=[ebcf.r, nsb.r], writes=[psI.r])
                ctx.e("vector", "tensor_mul", kkn[:, ta], kkn[:, ta], psI[:, :],
                      reads=[kkn.r, psI.r], writes=[kkn.r])
                psA = ctx.psum()
                ctx.mm(psA[:, :], ebc_c, asb[:, ta], start=True, stop=True,
                       reads=[ebcf.r, asb.r], writes=[psA.r])
                ctx.e("gpsimd", "tensor_mul", e2[:, :], kkn[:, ta], vsb[:, ta],
                      reads=[kkn.r, vsb.r], writes=[e2.r])
                ctx.e("vector", "tensor_mul", kvsb[:, ta], e2[:, :], psA[:, :],
                      reads=[e2.r, psA.r], writes=[kvsb.r])
                psW = ctx.psum()
                ctx.mm(psW[:, :], ebc_c, wsb[:, ta], start=True, stop=True,
                       reads=[ebcf.r, wsb.r], writes=[psW.r])
                ctx.e("scalar", "copy", wbc[:, ta], psW[:, :],
                      reads=[psW.r], writes=[wbc.r])
            ctx.e("vector", "tensor_tensor_scan", states[:], wbc[:], kvsb[:],
                  0.0, op0=ALU.mult, op1=ALU.add, reads=[wbc.r, kvsb.r],
                  writes=[states.r])
            psq = ctx.psum()
            ctx.mm(psq[:, 0:1], ebc_c, lsum[0:16, 0:1], start=True, stop=True,
                   reads=[ebcf.r, lsum.r], writes=[psq.r])
            ctx.e("vector", "tensor_copy", acol[:], psq[:, 0:1],
                  reads=[psq.r], writes=[acol.r])
            gi, go = gin[c], gout[c]
            gir, gor = gr[2 * c], gr[2 * c + 1]
            ctx.dma("scalar", gi[0:1, :].rearrange("a p -> p a"),
                    states[:, T - 1:T], reads=[states.r], writes=[gir],
                    tag=f"gS{c}")
            ctx.dma("scalar", gi[1:2, :].rearrange("a p -> p a"), acol[:],
                    reads=[acol.r, gir], writes=[gir], tag=f"gA{c}")
            ctx.s.rw("gpsimd", _mk_emit("collective_compute",
                     ("AllGather", ALU.bypass), dict(
                         replica_groups=ctx.groups, ins=[gi], outs=[go])),
                     reads=[gir], writes=[gor], cc=True, tag=f"ag0_{c}")
            ctx.e("vector", "memset", stin[0][:], 0.0, writes=[stin[0].r])
            for j in range(3):
                ctx.dma("scalar", jS[j][:],
                        go[2 * j:2 * j + 1, :].rearrange("a p -> p a"),
                        reads=[gor], writes=[jS[j].r], tag=f"ldS{c}_{j}")
                ctx.dma("scalar", jA[j][:],
                        go[2 * j + 1:2 * j + 2, :].rearrange("a p -> p a"),
                        reads=[gor], writes=[jA[j].r], tag=f"ldA{c}_{j}")
                ctx.e("vector", "tensor_mul", jA2[j][:], stin[j][:], jA[j][:],
                      reads=[stin[j].r, jA[j].r], writes=[jA2[j].r])
                ctx.e("vector", "tensor_add", jS2[j][:], jA2[j][:], jS[j][:],
                      reads=[jA2[j].r, jS[j].r], writes=[jS2[j].r])
                ctx.e("vector", "tensor_mul", jA2[j][:], jS2[j][:],
                      pmk[:, j:j + 1], reads=[jS2[j].r, pmk.r],
                      writes=[jA2[j].r])
                ctx.e("vector", "tensor_mul", jS2[j][:], stin[j][:],
                      pmk[:, 4 + j:5 + j], reads=[stin[j].r, pmk.r],
                      writes=[jS2[j].r])
                ctx.e("vector", "tensor_add", stin[j + 1][:], jA2[j][:],
                      jS2[j][:], reads=[jA2[j].r, jS2[j].r],
                      writes=[stin[j + 1].r])
            if DUMP:
                ctx.dma("gpsimd", dsi_d.rearrange("(cc p) o -> p cc o", p=128)[:, c, c:c + 1],
                        stin[3][:], reads=[stin[3].r], writes=[dsir], tag=f"dsi{c}",
                        allow_slow_non_contiguous=True)
            ctx.e("vector", "tensor_tensor_scan", states[:], wbc[:], kvsb[:],
                  stin[3][:, 0:1], op0=ALU.mult, op1=ALU.add,
                  reads=[wbc.r, kvsb.r, stin[3].r], writes=[states.r])
            if DUMP:
                ctx.dma("gpsimd", dst_d.rearrange("(cc p) t -> p cc t", p=128)[:, c, :],
                        states[:], reads=[states.r], writes=[dstr], tag=f"dst{c}")
                ctx.dma("gpsimd", dkv_d.rearrange("(cc p) t -> p cc t", p=128)[:, c, :],
                        kvsb[:], reads=[kvsb.r], writes=[dkvr], tag=f"dkv{c}")
                ctx.dma("gpsimd", dr_d.rearrange("(cc p) t -> p cc t", p=128)[:, c, :],
                        rsb[:], reads=[rsb.r], writes=[drr], tag=f"dr{c}")
                ctx.dma("gpsimd", dkk_d.rearrange("(cc p) t -> p cc t", p=128)[:, c, :],
                        kkn[:], reads=[kkn.r], writes=[dkkr], tag=f"dkk{c}")
                ctx.dma("gpsimd", dv_d.rearrange("(cc p) t -> p cc t", p=128)[:, c, :],
                        vsb[:], reads=[vsb.r], writes=[dvr], tag=f"dv{c}")
            for tt_i in range(NTT):
                ta = ts(tt_i, TT)
                ctx.e("gpsimd", "tensor_mul", e1[:, :], rsb[:, ta], kkn[:, ta],
                      reads=[rsb.r, kkn.r], writes=[e1.r])
                ctx.e("vector", "tensor_scalar_mul", e1[:, :], e1[:, :],
                      rkv[:, c:c + 1], reads=[rkv.r], writes=[e1.r])
                psB16 = ctx.psum()
                ctx.mm(psB16[0:16, :], et_c, e1[:, :], start=True, stop=True,
                       reads=[etf.r, e1.r], writes=[psB16.r])
                ctx.e("scalar", "copy", bcoef[0:16, :], psB16[0:16, :],
                      reads=[psB16.r], writes=[bcoef.r])
                psBC = ctx.psum()
                ctx.mm(psBC[:, :], ebc_c, bcoef[0:16, :], start=True, stop=True,
                       reads=[ebcf.r, bcoef.r], writes=[psBC.r])
                ctx.e("gpsimd", "tensor_mul", e2[:, :], states[:, ta],
                      rsb[:, ta], reads=[states.r, rsb.r], writes=[e2.r])
                ctx.e("vector", "tensor_mul", e3[:, :], vsb[:, ta], psBC[:, :],
                      reads=[vsb.r, psBC.r], writes=[e3.r])
                ctx.e("vector", "tensor_add", e2[:, :], e2[:, :], e3[:, :],
                      reads=[e3.r], writes=[e2.r])
                ctx.e("scalar", "activation", e3[:, :], e2[:, :], AF.Square,
                      reads=[e2.r], writes=[e3.r])
                psS = ctx.psum()
                ctx.mm(psS[0:16, :], et_c, e2[:, :], start=True, stop=True,
                       reads=[etf.r, e2.r], writes=[psS.r])
                psQ = ctx.psum()
                ctx.mm(psQ[0:16, :], et_c, e3[:, :], start=True, stop=True,
                       reads=[etf.r, e3.r], writes=[psQ.r])
                ctx.e("vector", "tensor_scalar_mul", nsb[0:16, :],
                      psS[0:16, :], 1.0 / 64.0, reads=[psS.r], writes=[nsb.r])
                ctx.e("vector", "tensor_scalar_mul", bcoef[0:16, :],
                      psQ[0:16, :], 1.0 / 64.0, reads=[psQ.r], writes=[bcoef.r])
                ctx.e("vector", "tensor_mul", st16[0:16, :], nsb[0:16, :],
                      nsb[0:16, :], reads=[nsb.r], writes=[st16.r])
                ctx.e("vector", "tensor_sub", bcoef[0:16, :], bcoef[0:16, :],
                      st16[0:16, :], reads=[st16.r], writes=[bcoef.r])
                ctx.e("vector", "tensor_scalar_add", bcoef[0:16, :],
                      bcoef[0:16, :], 1e-5, reads=[bcoef.r], writes=[bcoef.r])
                ctx.e("scalar", "activation", bcoef[0:16, :], bcoef[0:16, :],
                      AF.Sqrt, reads=[bcoef.r], writes=[bcoef.r])
                ctx.e("vector", "reciprocal", bcoef[0:16, :], bcoef[0:16, :],
                      reads=[bcoef.r], writes=[bcoef.r])
                ctx.e("vector", "scalar_tensor_tensor", st16[0:16, :],
                      nsb[0:16, :], -1.0, bcoef[0:16, :], op0=ALU.mult,
                      op1=ALU.mult, reads=[nsb.r, bcoef.r], writes=[st16.r])
                psAA = ctx.psum()
                ctx.mm(psAA[:, :], ebc_c, bcoef[0:16, :], start=True,
                       stop=True, reads=[ebcf.r, bcoef.r], writes=[psAA.r])
                psBB = ctx.psum()
                ctx.mm(psBB[:, :], ebc_c, st16[0:16, :], start=True, stop=True,
                       reads=[ebcf.r, st16.r], writes=[psBB.r])
                ctx.e("vector", "tensor_mul", e4[:, :], e2[:, :], psAA[:, :],
                      reads=[e2.r, psAA.r], writes=[e4.r])
                ctx.e("vector", "tensor_add", e4[:, :], e4[:, :], psBB[:, :],
                      reads=[psBB.r], writes=[e4.r])
                psG = ctx.psum()
                ctx.mm(psG[:, :], g2sb[:, ts(c, 128)], zg[:, ta], start=True,
                       stop=True, reads=[g2sb.r, zg.r], writes=[psG.r])
                ctx.e("scalar", "activation", e3[:, :], psG[:, :], AF.Sigmoid,
                      reads=[psG.r], writes=[e3.r])
                ctx.e("vector", "tensor_scalar", e4[:, :], e4[:, :],
                      lnw[:, c:c + 1], lnb[:, c:c + 1], op0=ALU.mult,
                      op1=ALU.add, reads=[lnw.r, lnb.r], writes=[e4.r])
                ctx.e("vector", "tensor_mul", att[:, c, ta], e4[:, :],
                      e3[:, :], reads=[e4.r, e3.r], writes=[att.r])
        ctx.fence("b0mix")
    build_b0_ffn(ctx, iv, ir, xn, att, xin_d, xin_r, xout_d, xout_r)
    return xn, att


def build_b0_ffn(ctx, iv, ir, xn, att, xin_d, xin_r, xout_d, xout_r):
    s, cm = ctx.s, ctx.cm
    e1 = cm["e1"]
    with contextlib.ExitStack() as fs:
        b1sb = ctx.sb("b0f_b1s", [128, 32], F32, fs)
        b2sb = ctx.sb("b0f_b2s", [128, NC], F32, fs)
        al1 = ctx.sb("b0f_al1", [128, 32], F32, fs)
        al2 = ctx.sb("b0f_al2", [128, NC], F32, fs)
        w1t = iv["b0f_w1t"].rearrange("(kc p) o -> p kc o", p=128)
        w2t = iv["b0f_w2t"].rearrange("(kc p) o -> p kc o", p=128)
        ctx.dma("sync", b1sb[:], iv["b0f_b1"], reads=[ir["b0f_b1"]], writes=[b1sb.r])
        ctx.dma("sync", b2sb[:], iv["b0f_b2"], reads=[ir["b0f_b2"]], writes=[b2sb.r])
        with contextlib.ExitStack() as als:
            aw = ctx.sb("b0f_aw", [128, 4096], BF16, als)
            awa = ctx.sb("b0f_awa", [128, 4096], BF16, als)
            psL = ctx.psum()
            for kc in range(NC):
                ctx.dma("gpsimd", aw[:], w1t[:, kc, :], reads=[ir["b0f_w1t"]],
                        writes=[aw.r], tag=f"alw1_{kc}")
                ctx.e("scalar", "activation", awa[:], aw[:], AF.Abs,
                      reads=[aw.r], writes=[awa.r])
                for hc in range(32):
                    ctx.mm(psL[:, hc:hc + 1], awa[:, ts(hc, 128)],
                           cm["ones_b"][:, 0:1],
                           start=(kc == 0 and hc == 0),
                           stop=(kc == NC - 1 and hc == 31),
                           reads=[awa.r, cm["ones_b"].r], writes=[psL.r])
            ctx.e("vector", "tensor_scalar_mul", al1[:, 0:32], psL[:, 0:32],
                  1.0 / 1024.0, reads=[psL.r], writes=[al1.r])
            psL2 = ctx.psum()
            for kc in range(32):
                ctx.dma("gpsimd", aw[:, 0:1024], w2t[:, kc, :],
                        reads=[ir["b0f_w2t"]], writes=[aw.r], tag=f"alw2_{kc}")
                ctx.e("scalar", "activation", awa[:, 0:1024], aw[:, 0:1024],
                      AF.Abs, reads=[aw.r], writes=[awa.r])
                for oc in range(NC):
                    ctx.mm(psL2[:, oc:oc + 1], awa[:, ts(oc, 128)],
                           cm["ones_b"][:, 0:1],
                           start=(kc == 0 and oc == 0),
                           stop=(kc == 31 and oc == NC - 1),
                           reads=[awa.r, cm["ones_b"].r], writes=[psL2.r])
            ctx.e("vector", "tensor_scalar_mul", al2[:, 0:NC], psL2[:, 0:NC],
                  1.0 / 4096.0, reads=[psL2.r], writes=[al2.r])
            ctx.fence("b0alpha")
        w1q = [ctx.sb(f"b0f_w1q{i}", [128, NC, 1024], BF16, fs) for i in range(2)]
        w2q = [ctx.sb(f"b0f_w2q{i}", [128, NC, 1024], BF16, fs) for i in range(2)]
        wos = [ctx.sb(f"b0_wos{i}", [128, NC, 128], BF16, fs) for i in range(2)]
        hsb = ctx.sb("b0f_h", [128, NC, TT], BF16, fs)
        oacc = ctx.sb("b0f_oacc", [128, NC, TT], F32, fs)
        xo1 = cm["e3"]

        for tt_i in range(NTT):
            ta = ts(tt_i, TT)
            xcols = slice(1 + tt_i * TT, 1 + (tt_i + 1) * TT)
            for quar in range(4):
                w1b = w1q[quar % 2]
                w2b = w2q[quar % 2]
                ctx.dma("gpsimd", w1b[:], w1t[:, :, ts(quar, 1024)],
                        reads=[ir["b0f_w1t"]], writes=[w1b.r], tag=f"w1q{tt_i}{quar}")
                ctx.e("scalar", "activation", w1b[:, :, :], w1b[:, :, :],
                      AF.Sign, reads=[w1b.r], writes=[w1b.r])
                ctx.dma("gpsimd", w2b[:], w2t[:, ts(quar, NC), :],
                        reads=[ir["b0f_w2t"]], writes=[w2b.r], tag=f"w2q{tt_i}{quar}")
                ctx.e("scalar", "activation", w2b[:, :, :], w2b[:, :, :],
                      AF.Sign, reads=[w2b.r], writes=[w2b.r])
                for hcl in range(NC):
                    hc = quar * NC + hcl
                    ps = ctx.psum()
                    for kc in range(NC):
                        ctx.mm(ps[:, :], w1b[:, kc, ts(hcl, 128)],
                               xn[:, kc, xcols], start=(kc == 0),
                               stop=(kc == NC - 1),
                               reads=[w1b.r, xn.r], writes=[ps.r])
                    ctx.e("scalar", "activation", hsb[:, hcl, :], ps[:, :],
                          AF.Gelu, bias=b1sb[:, hc:hc + 1],
                          scale=al1[:, hc:hc + 1],
                          reads=[ps.r, b1sb.r, al1.r], writes=[hsb.r])
                for oc in range(NC):
                    ps = ctx.psum()
                    for hcl in range(NC):
                        ctx.mm(ps[:, :], w2b[:, hcl, ts(oc, 128)],
                               hsb[:, hcl, :], start=(hcl == 0),
                               stop=(hcl == NC - 1),
                               reads=[w2b.r, hsb.r], writes=[ps.r])
                    if quar == 0:
                        ctx.e("vector", "tensor_scalar_mul", oacc[:, oc, :],
                              ps[:, :], al2[:, oc:oc + 1],
                              reads=[ps.r, al2.r], writes=[oacc.r])
                    else:
                        ctx.e("vector", "scalar_tensor_tensor", oacc[:, oc, :],
                              ps[:, :], al2[:, oc:oc + 1], oacc[:, oc, :],
                              op0=ALU.mult, op1=ALU.add,
                              reads=[ps.r, al2.r], writes=[oacc.r])
            for oc in range(NC):
                wob = wos[oc % 2]
                ctx.dma("gpsimd", wob[:], wslices_ap(iv["b0_Wo"], oc),
                        reads=[ir["b0_Wo"]], writes=[wob.r], tag=f"wo{tt_i}{oc}")
                ctx.dma("sync", xo1[:], xfm(xin_d)[:, oc, xcols],
                        reads=[xin_r], writes=[xo1.r], tag=f"xold{tt_i}{oc}")
                ps = ctx.psum()
                for kc in range(NC):
                    ctx.mm(ps[:, :], wob[:, kc, :], att[:, kc, ta],
                           start=(kc == 0), stop=(kc == NC - 1),
                           reads=[wob.r, att.r], writes=[ps.r])
                ctx.e("vector", "tensor_scalar", e1[:, :], ps[:, :],
                      b2sb[:, oc:oc + 1], None, op0=ALU.add,
                      reads=[ps.r, b2sb.r], writes=[e1.r])
                ctx.e("vector", "tensor_add", e1[:, :], e1[:, :],
                      oacc[:, oc, :], reads=[oacc.r], writes=[e1.r])
                ctx.e("vector", "tensor_add", xo1[:, :], e1[:, :],
                      xo1[:, :], reads=[e1.r], writes=[xo1.r])
                ctx.dma("sync", xfm(xout_d)[:, oc, ta], xo1[:],
                        reads=[xo1.r], writes=[xout_r], tag=f"xo{tt_i}{oc}")
        ctx.fence("b0ffn")


def build_block1(ctx, iv, ir, xin_d, xin_r, xout_d, xout_r, g1i, g1o, g1r):
    s, cm = ctx.s, ctx.cm
    e1, e2, e3, e4 = cm["e1"], cm["e2"], cm["e3"], cm["e4"]

    xn = ctx.sb("b1_xn", [128, NC, T], BF16)
    att = ctx.sb("b1_att", [128, NC, T], BF16)
    normnd_d = ctx.nc.dram_tensor("b1_normnd", [16, T], F32).ap()
    normnd_r = Reg("b1_normnd")

    mixstk = contextlib.ExitStack()
    qp = ctx.sb("b1_qp", [128, NC, T], BF16, mixstk)
    ebcf = ctx.sb("b1_ebcf", [16, NC * 128], F32, mixstk)
    pmk = ctx.sb("b1_pmk", [128, NC], F32, mixstk)
    with contextlib.ExitStack() as ms:
        qb = ctx.sb("b1_qb", [128, NC], F32, ms)
        kb = ctx.sb("b1_kb", [128, NC], F32, ms)
        vb = ctx.sb("b1_vb", [128, NC], F32, ms)
        mskd = ctx.sb("b1_mskd", [128, 256], F32, ms)
        kp = ctx.sb("b1_kp", [128, T], BF16, ms)
        vsb = ctx.sb("b1_v", [128, T], BF16, ms)
        ktm = ctx.sb("b1_ktm", [128, 128], BF16, ms)
        vtm = ctx.sb("b1_vtm", [128, 130], BF16, ms)
        p01 = ctx.sb("b1_p01", [128, 256], BF16, ms)
        nf = ctx.sb("b1_nf", [1, 512], F32, ms)
        Sm = [ctx.sb(f"b1_Sm_{i}", [128, 65], F32, ms) for i in range(2)]
        Smb = ctx.sb("b1_Smb", [128, 65], BF16, ms)
        nrow = ctx.sb("b1_nrow", [1, 2, T], F32, ms)
        wk3 = [ctx.sb(f"b1_wk3_{i}", [128, NC, 3 * 128], BF16, ms)
               for i in range(2)]
        idb = ctx.sb("b1_idb", [128, 128], BF16, ms)

        for sb, nm in [(ebcf, "ebcf"), (pmk, "pmask"), (qb, "b1_qb"),
                       (kb, "b1_kb"), (vb, "b1_vb"), (mskd, "l1mask")]:
            ctx.dma("sync", sb[:], iv[nm], reads=[ir[nm]], writes=[sb.r], tag=nm)
        ctx.dma("gpsimd", idb[:], iv["idf"], reads=[ir["idf"]], writes=[idb.r])
        ctx.e("vector", "memset", vtm[:, 64:65], 1.0, writes=[vtm.r])
        ctx.e("vector", "memset", vtm[:, 129:130], 1.0, writes=[vtm.r])

        emit_ln(ctx, xin_d, xin_r, xn, T)

        qkvw = iv["b1_qkvw"]
        for c in (range(NC) if B1_STAGE >= 2 else []):
            wk = wk3[c % 2]
            for j, off in enumerate([c * 128, 1024 + c * 128, 2048 + c * 128]):
                ctx.dma("gpsimd", wk[:, :, ts(j, 128)],
                        qkvw.rearrange("(kc p) o -> p kc o", p=128)[:, :, off:off + 128],
                        reads=[ir["b1_qkvw"]], writes=[wk.r], tag=f"b1w{c}{j}")
            for tt_i in range(NTT):
                ta = ts(tt_i, TT)
                psQ, psK, psV = ctx.psum(), ctx.psum(), ctx.psum()
                for kc in range(NC):
                    for ps, j in [(psQ, 0), (psK, 1), (psV, 2)]:
                        ctx.mm(ps[:, :], wk[:, kc, ts(j, 128)], xn[:, kc, ta],
                               start=(kc == 0), stop=(kc == NC - 1),
                               reads=[wk.r, xn.r], writes=[ps.r])
                for ps, bcol, dst in [(psQ, qb, None), (psK, kb, None)]:
                    ctx.e("vector", "tensor_scalar", e1[:, :], ps[:, :],
                          bcol[:, c:c + 1], None, op0=ALU.add,
                          reads=[ps.r, bcol.r], writes=[e1.r])
                    ctx.e("vector", "tensor_scalar_min", e2[:, :], e1[:, :],
                          0.0, reads=[e1.r], writes=[e2.r])
                    ctx.e("scalar", "activation", e3[:, :], e2[:, :], AF.Exp,
                          reads=[e2.r], writes=[e3.r])
                    ctx.e("vector", "tensor_scalar_max", e4[:, :], e1[:, :],
                          0.0, reads=[e1.r], writes=[e4.r])
                    out_ap = qp[:, c, ta] if ps is psQ else kp[:, ta]
                    out_rg = qp.r if ps is psQ else kp.r
                    ctx.e("vector", "tensor_add", out_ap, e3[:, :], e4[:, :],
                          reads=[e3.r, e4.r], writes=[out_rg])
                ctx.e("vector", "tensor_scalar", vsb[:, ta], psV[:, :],
                      vb[:, c:c + 1], None, op0=ALU.add,
                      reads=[psV.r, vb.r], writes=[vsb.r])
            ctx.e("vector", "memset", Sm[0][:], 0.0, writes=[Sm[0].r])
            ctx.e("vector", "memset", Smb[:], 0.0, writes=[Smb.r])
            for tc in (range(T // 128) if B1_STAGE >= 3 else []):
                tb = ts(tc, 128)
                pst = ctx.psumb()
                ctx.e("tensor", "transpose", pst[:, 0:128], kp[:, tb], idb[:],
                      reads=[kp.r, idb.r], writes=[pst.r])
                ctx.e("tensor", "transpose", pst[:, 128:256], vsb[:, tb], idb[:],
                      reads=[vsb.r, idb.r], writes=[pst.r])
                ctx.e("scalar", "copy", ktm[:, :], pst[:, 0:128],
                      reads=[pst.r], writes=[ktm.r])
                ctx.e("scalar", "copy", vtm[:, 0:64], pst[:, 128:192],
                      reads=[pst.r], writes=[vtm.r])
                ctx.e("scalar", "copy", vtm[:, 65:129], pst[:, 192:256],
                      reads=[pst.r], writes=[vtm.r])
                if B1_SUB < 2:
                    continue
                psA0 = ctx.psum()
                psA1 = ctx.psum()
                for h, psA in [(0, psA0), (1, psA1)]:
                    ctx.mm(psA[:, 0:128], kp[ts(h, 64), tb],
                           qp[ts(h, 64), c, tb], start=True, stop=True,
                           reads=[kp.r, qp.r], writes=[psA.r])
                for h, psA in [(0, psA0), (1, psA1)]:
                    ctx.e("vector", "tensor_mul", p01[:, ts(h, 128)],
                          psA[:, 0:128], mskd[:, ts(h, 128)],
                          reads=[psA.r, mskd.r], writes=[p01.r])
                if B1_SUB < 3:
                    continue
                psC = [ctx.psum(), ctx.psum()]
                for h in range(2):
                    ctx.mm(psC[h][0:65, 0:128], vtm[:, 65 * h:65 * h + 65],
                           p01[:, ts(h, 128)], start=True, stop=False,
                           reads=[vtm.r, p01.r], writes=[psC[h].r])
                    ctx.mm(psC[h][0:65, 0:128], Smb[ts(h, 64), :],
                           qp[ts(h, 64), c, tb], start=False, stop=True,
                           reads=[Smb.r, qp.r], writes=[psC[h].r])
                for h in range(2):
                    ctx.e("vector", "tensor_copy", att[ts(h, 64), c, tb],
                          psC[h][0:64, 0:128], reads=[psC[h].r], writes=[att.r])
                    ctx.e("vector", "tensor_copy", nrow[0:1, h, tb],
                          psC[h][64:65, 0:128], reads=[psC[h].r],
                          writes=[nrow.r])
                if B1_SUB < 4:
                    continue
                psSa = ctx.psum()
                psSb = ctx.psum()
                for h, psS in [(0, psSa), (1, psSb)]:
                    ctx.mm(psS[ts(h, 64), 0:65], ktm[:, ts(h, 64)],
                           vtm[:, 65 * h:65 * h + 65], start=True,
                           stop=True, reads=[ktm.r, vtm.r],
                           writes=[psS.r])
                src = Sm[tc % 2]
                dst = Sm[(tc + 1) % 2]
                for h, psS in [(0, psSa), (1, psSb)]:
                    ctx.e("vector", "tensor_add", dst[ts(h, 64), :],
                          src[ts(h, 64), :], psS[ts(h, 64), 0:65],
                          reads=[src.r, psS.r], writes=[dst.r])
                ctx.e("vector", "tensor_copy", Smb[:], dst[:],
                      reads=[dst.r], writes=[Smb.r])
            fin = T // 128 % 2
            for h in range(2):
                ctx.dma("scalar",
                        g1i[2 * c + h:2 * c + h + 1, :].rearrange(
                            "a (p q) -> (a p) q", p=64),
                        Sm[fin][ts(h, 64), :], reads=[Sm[fin].r],
                        writes=[g1r[0]], tag=f"b1gi{c}{h}")
            ctx.dma("gpsimd", normnd_d[2 * c:2 * c + 2, :], nrow[0:1, :, :],
                    reads=[nrow.r], writes=[normnd_r], tag=f"b1nd{c}")
        if DUMP:
            dqp = ctx.nc.dram_tensor("d_qp", [1024, T], F32, kind="ExternalOutput").ap()
            dkp = ctx.nc.dram_tensor("d_kp", [1024, T], F32, kind="ExternalOutput").ap()
            dvv = ctx.nc.dram_tensor("d_v", [1024, T], F32, kind="ExternalOutput").ap()
            dcx = ctx.nc.dram_tensor("d_ctxnd", [1024, T], F32, kind="ExternalOutput").ap()
            dnn = ctx.nc.dram_tensor("d_nnd", [16, T], F32, kind="ExternalOutput").ap()
            rgs = [Reg(x) for x in "abcde"]
            for cc2 in range(NC):
                ctx.dma("gpsimd", dqp.rearrange("(c p) t -> p c t", p=128)[:, cc2, :],
                        qp[:, cc2, :], reads=[qp.r], writes=[rgs[0]], tag=f"dq{cc2}")
                ctx.dma("gpsimd", dcx.rearrange("(c p) t -> p c t", p=128)[:, cc2, :],
                        att[:, cc2, :], reads=[att.r], writes=[rgs[3]], tag=f"dc{cc2}")
            ctx.dma("gpsimd", dnn, normnd_d, reads=[normnd_r], writes=[rgs[4]], tag="dn")
        if B1_STAGE < 3:
            for h2 in range(16):
                ctx.e("vector", "memset", Sm[0][:], 0.0, writes=[Sm[0].r])
        ctx.s.rw("gpsimd", _mk_emit("collective_compute",
                 ("AllGather", ALU.bypass), dict(
                     replica_groups=ctx.groups, ins=[g1i], outs=[g1o])),
                 reads=[g1r[0]], writes=[g1r[1]], cc=True, tag="ag1")
        ctx.fence("b1tc")
    with contextlib.ExitStack() as ms:
        Sacc = [ctx.sb(f"b1_Sacc{i}", [64, 16, 65], F32, ms) for i in range(2)]
        Sg = ctx.sb("b1_Sg", [64, 16, 65], F32, ms)
        ctx.e("vector", "memset", Sacc[0][:], 0.0, writes=[Sacc[0].r])
        for j in range(3):
            ctx.dma("scalar", Sg[:],
                    g1o[16 * j:16 * (j + 1), :].rearrange(
                        "h (p q) -> p h q", p=64),
                    reads=[g1r[1]], writes=[Sg.r], tag=f"b1ld{j}")
            src, dst = Sacc[j % 2], Sacc[(j + 1) % 2]
            ctx.e("vector", "scalar_tensor_tensor", dst[:], Sg[:],
                  pmk[0:64, j:j + 1], src[:], op0=ALU.mult, op1=ALU.add,
                  reads=[Sg.r, pmk.r, src.r], writes=[dst.r])
        SaccF = Sacc[3 % 2]
        mca = ctx.sb("b1_mca", [128, NC, 16], BF16, ms)
        Scm = ctx.sb("b1_Scm", [128, 65], BF16, ms)
        normsb = ctx.sb("b1_normsb", [16, T], F32, ms)
        ninv = ctx.sb("b1_ninv", [16, TT], F32, ms)
        ctx.e("vector", "memset", mca[:], 0.0, writes=[mca.r])
        for h in range(16):
            ctx.e("vector", "tensor_copy", mca[ts(h % 2, 64), h // 2, h:h + 1],
                  SaccF[:, h, 64:65], reads=[SaccF.r], writes=[mca.r])
        ctx.dma("sync", normsb[:], normnd_d, reads=[normnd_r],
                writes=[normsb.r], tag="b1nl")
        for tt_i in (range(NTT) if B1_STAGE >= 4 else []):
            ta = ts(tt_i, TT)
            psM = ctx.psum()
            for c in range(NC):
                ctx.mm(psM[0:16, :], mca[:, c, :], qp[:, c, ta],
                       start=(c == 0), stop=(c == NC - 1),
                       reads=[mca.r, qp.r], writes=[psM.r])
            ctx.e("vector", "tensor_add", ninv[0:16, :], psM[0:16, :],
                  normsb[0:16, ta], reads=[psM.r, normsb.r], writes=[ninv.r])
            ctx.e("vector", "tensor_scalar_add", ninv[0:16, :], ninv[0:16, :],
                  1e-6, reads=[], writes=[ninv.r])
            ctx.e("vector", "reciprocal", ninv[0:16, :], ninv[0:16, :],
                  reads=[], writes=[ninv.r])
            for c in range(NC):
                for h in range(2):
                    ctx.e("vector", "tensor_copy", Scm[ts(h, 64), :],
                          SaccF[:, 2 * c + h, :], reads=[SaccF.r],
                          writes=[Scm.r])
                psI = ctx.psum()
                ctx.mm(psI[:, :], ebcf[:, ts(c, 128)], ninv[0:16, :],
                       start=True, stop=True, reads=[ebcf.r, ninv.r],
                       writes=[psI.r])
                ctx.e("scalar", "copy", e4[:, :], psI[:, :],
                      reads=[psI.r], writes=[e4.r])
                psK2a = ctx.psum()
                psK2b = ctx.psum()
                for h, psK2 in [(0, psK2a), (1, psK2b)]:
                    ctx.mm(psK2[0:64, :], Scm[ts(h, 64), 0:64],
                           qp[ts(h, 64), c, ta], start=True, stop=True,
                           reads=[Scm.r, qp.r], writes=[psK2.r])
                for h, psK2 in [(0, psK2a), (1, psK2b)]:
                    ctx.e("vector", "tensor_add", e1[ts(h, 64), :],
                          att[ts(h, 64), c, ta], psK2[0:64, :],
                          reads=[att.r, psK2.r], writes=[e1.r])
                ctx.e("vector", "tensor_mul", att[:, c, ta], e1[:, :],
                      e4[:, :], reads=[e1.r, e4.r], writes=[att.r])
        ctx.fence("b1mix")
    mixstk.close()
    build_b1_ffn(ctx, iv, ir, xn, att, xin_d, xin_r, xout_d, xout_r)
    return xn, att


def build_b1_ffn(ctx, iv, ir, xn, att, xin_d, xin_r, xout_d, xout_r):
    s, cm = ctx.s, ctx.cm
    e1, e2, e3 = cm["e1"], cm["e2"], cm["e3"]
    with contextlib.ExitStack() as fs:
        b1sb = ctx.sb("b1f_b1s", [128, 32], F32, fs)
        b2sb = ctx.sb("b1f_b2s", [128, NC], F32, fs)
        obsb = ctx.sb("b1f_obs", [128, NC], F32, fs)
        thsb = ctx.sb("b1f_th", [128, 32], F32, fs)
        wrsb = ctx.sb("b1f_wr", [128, 32], F32, fs)
        wisb = ctx.sb("b1f_wi", [128, 32], F32, fs)
        coef = ctx.sb("b1f_coef", [128, 32], F32, fs)
        w1q = [ctx.sb(f"b1f_w1q{i}", [128, NC, 1024], BF16, fs) for i in range(2)]
        w2q = [ctx.sb(f"b1f_w2q{i}", [128, NC, 1024], BF16, fs) for i in range(2)]
        wos = [ctx.sb(f"b1_wos{i}", [128, NC, 128], BF16, fs) for i in range(2)]
        hsb = ctx.sb("b1f_h", [128, NC, TT], BF16, fs)
        oacc = ctx.sb("b1f_oacc", [128, NC, TT], F32, fs)
        xo1 = cm["e3"]

        for sb, nm in [(b1sb, "b1f_b1"), (b2sb, "b1f_b2"), (obsb, "b1_ob"),
                       (thsb, "b1f_th"), (wrsb, "b1f_wr"), (wisb, "b1f_wi")]:
            ctx.dma("sync", sb[:], iv[nm], reads=[ir[nm]], writes=[sb.r], tag=nm)
        ei = cm["ei"]

        def rr_sin(dst, shift):
            ctx.e("vector", "tensor_scalar", e1[:, 0:32], thsb[:], shift,
                  1.0 / (2.0 * math.pi), op0=ALU.add, op1=ALU.mult,
                  reads=[thsb.r], writes=[e1.r])
            ctx.e("vector", "tensor_copy", ei[:, 0:32], e1[:, 0:32],
                  reads=[e1.r], writes=[ei.r])
            ctx.e("vector", "tensor_copy", e4[:, 0:32], ei[:, 0:32],
                  reads=[ei.r], writes=[e4.r])
            ctx.e("vector", "tensor_sub", e4[:, 0:32], e1[:, 0:32],
                  e4[:, 0:32], reads=[e1.r], writes=[e4.r])
            ctx.e("vector", "tensor_scalar_mul", e4[:, 0:32], e4[:, 0:32],
                  2.0 * math.pi, reads=[], writes=[e4.r])
            ctx.e("scalar", "activation", dst[:, 0:32], e4[:, 0:32], AF.Sin,
                  reads=[e4.r], writes=[dst.r])
        e4 = cm["e4"]
        rr_sin(e2, 0.0)
        rr_sin(e3, math.pi / 2.0)
        ctx.e("vector", "tensor_mul", e3[:, 0:32], e3[:, 0:32], wrsb[:],
              reads=[wrsb.r], writes=[e3.r])
        ctx.e("vector", "tensor_mul", e1[:, 0:32], e2[:, 0:32], wisb[:],
              reads=[e2.r, wisb.r], writes=[e1.r])
        ctx.e("vector", "tensor_add", e3[:, 0:32], e3[:, 0:32], e1[:, 0:32],
              reads=[e1.r], writes=[e3.r])
        ctx.e("vector", "tensor_mul", e3[:, 0:32], e3[:, 0:32], wisb[:],
              reads=[wisb.r], writes=[e3.r])
        ctx.e("vector", "tensor_scalar_mul", coef[:], e3[:, 0:32], 0.1,
              reads=[e3.r], writes=[coef.r])

        w1t = iv["b1f_w1"].rearrange("(kc p) o -> p kc o", p=128)
        w2t = iv["b1f_w2"].rearrange("(kc p) o -> p kc o", p=128)
        for tt_i in range(NTT):
            ta = ts(tt_i, TT)
            for quar in (range(4) if B1_STAGE >= 5 else [0]):
                w1b = w1q[quar % 2]
                w2b = w2q[quar % 2]
                ctx.dma("gpsimd", w1b[:], w1t[:, :, ts(quar, 1024)],
                        reads=[ir["b1f_w1"]], writes=[w1b.r], tag=f"b1w1{tt_i}{quar}")
                ctx.dma("gpsimd", w2b[:], w2t[:, ts(quar, NC), :],
                        reads=[ir["b1f_w2"]], writes=[w2b.r], tag=f"b1w2{tt_i}{quar}")
                for hcl in range(NC):
                    hc = quar * NC + hcl
                    ps = ctx.psum()
                    for kc in range(NC):
                        ctx.mm(ps[:, :], w1b[:, kc, ts(hcl, 128)],
                               xn[:, kc, ta], start=(kc == 0),
                               stop=(kc == NC - 1),
                               reads=[w1b.r, xn.r], writes=[ps.r])
                    ctx.e("vector", "tensor_scalar", e1[:, :], ps[:, :],
                          b1sb[:, hc:hc + 1], None, op0=ALU.add,
                          reads=[ps.r, b1sb.r], writes=[e1.r])
                    ctx.e("vector", "tensor_scalar", e2[:, :], e1[:, :],
                          0.1, thsb[:, hc:hc + 1], op0=ALU.mult, op1=ALU.add,
                          reads=[e1.r, thsb.r], writes=[e2.r])
                    ctx.e("vector", "tensor_scalar_mul", e3[:, :], e2[:, :],
                          1.0 / (2.0 * math.pi), reads=[e2.r], writes=[e3.r])
                    ctx.e("vector", "tensor_copy", cm["ei"][:, :], e3[:, :],
                          reads=[e3.r], writes=[cm["ei"].r])
                    ctx.e("vector", "tensor_copy", e2[:, :], cm["ei"][:, :],
                          reads=[cm["ei"].r], writes=[e2.r])
                    ctx.e("vector", "tensor_sub", e2[:, :], e3[:, :],
                          e2[:, :], reads=[e3.r], writes=[e2.r])
                    ctx.e("vector", "tensor_scalar_mul", e2[:, :], e2[:, :],
                          2.0 * math.pi, reads=[], writes=[e2.r])
                    ctx.e("scalar", "activation", e2[:, :], e2[:, :], AF.Sin,
                          reads=[], writes=[e2.r])
                    ctx.e("vector", "scalar_tensor_tensor", e3[:, :],
                          e2[:, :], coef[:, hc:hc + 1], e1[:, :],
                          op0=ALU.mult, op1=ALU.add,
                          reads=[e2.r, coef.r, e1.r], writes=[e3.r])
                    ctx.e("vector", "tensor_scalar_max", hsb[:, hcl, :],
                          e3[:, :], 0.0, reads=[e3.r], writes=[hsb.r])
                for oc in range(NC):
                    ps = ctx.psum()
                    for hcl in range(NC):
                        ctx.mm(ps[:, :], w2b[:, hcl, ts(oc, 128)],
                               hsb[:, hcl, :], start=(hcl == 0),
                               stop=(hcl == NC - 1),
                               reads=[w2b.r, hsb.r], writes=[ps.r])
                    if quar == 0:
                        ctx.e("vector", "tensor_scalar", oacc[:, oc, :],
                              ps[:, :], b2sb[:, oc:oc + 1], None, op0=ALU.add,
                              reads=[ps.r, b2sb.r], writes=[oacc.r])
                    else:
                        ctx.e("vector", "tensor_add", oacc[:, oc, :],
                              ps[:, :], oacc[:, oc, :],
                              reads=[ps.r], writes=[oacc.r])
            for oc in range(NC):
                wob = wos[oc % 2]
                ctx.dma("gpsimd", wob[:], wslices_ap(iv["b1_outw"], oc),
                        reads=[ir["b1_outw"]], writes=[wob.r], tag=f"b1wo{tt_i}{oc}")
                ctx.dma("sync", xo1[:], xfm(xin_d)[:, oc, ta],
                        reads=[xin_r], writes=[xo1.r], tag=f"b1xo{tt_i}{oc}")
                ps = ctx.psum()
                for kc in range(NC):
                    ctx.mm(ps[:, :], wob[:, kc, :], att[:, kc, ta],
                           start=(kc == 0), stop=(kc == NC - 1),
                           reads=[wob.r, att.r], writes=[ps.r])
                ctx.e("vector", "tensor_scalar", e1[:, :], ps[:, :],
                      obsb[:, oc:oc + 1], None, op0=ALU.add,
                      reads=[ps.r, obsb.r], writes=[e1.r])
                ctx.e("vector", "tensor_add", e1[:, :], e1[:, :],
                      oacc[:, oc, :], reads=[oacc.r], writes=[e1.r])
                ctx.e("vector", "tensor_add", xo1[:, :], e1[:, :],
                      xo1[:, :], reads=[e1.r], writes=[xo1.r])
                ctx.dma("sync", xfm(xout_d)[:, oc, ta], xo1[:],
                        reads=[xo1.r], writes=[xout_r], tag=f"b1xw{tt_i}{oc}")
        ctx.fence("b1ffn")


def build_block2(ctx, iv, ir, xin_d, xin_r, xout_d, xout_r,
                 gki, gko, gvi, gvo, gsi, gso, g2r):
    s, cm = ctx.s, ctx.cm
    e1, e2 = cm["e1"], cm["e2"]
    eb1 = cm["eb1"]

    xn = ctx.sb("b2_xn", [128, NC, T], BF16)
    att = ctx.sb("b2_att", [128, NC, T], BF16)
    bmax = ctx.sb("b2_bmax", [16, 4], F32)
    q_d = ctx.nc.dram_tensor("b2_qd", [1024, T], BF16).ap()
    q_dr = Reg("b2_qd")

    with contextlib.ExitStack() as ms:
        aqT = ctx.sb("b2_aqT", [128, NC, 48], BF16, ms)
        bqT = ctx.sb("b2_bqT", [16, 3 * 1024], BF16, ms)
        lra = ctx.sb("b2_lra", [16, 3 * T], BF16, ms)
        wk3 = [ctx.sb(f"b2_wk3_{i}", [128, NC, 3 * 128], BF16, ms)
               for i in range(2)]
        vsb = ctx.sb("b2_vs", [128, T], BF16, ms)
        idb = ctx.sb("b2_idb", [128, 128], BF16, ms)
        qst = ctx.sb("b2_qst", [16, 2], F32, ms)
        qmx = ctx.sb("b2_qmx", [16, 32], F32, ms)
        kmx = ctx.sb("b2_kmx", [16, 32], F32, ms)
        gmx = ctx.sb("b2_gmx", [16, 4], F32, ms)
        etf = ctx.sb("b2_etf", [128, NC * 16], F32, ms)

        for dst, nm, sl in [(aqT, "b2_AqT", ts(0, 16)), (aqT, "b2_AkT", ts(1, 16)),
                            (aqT, "b2_AvT", ts(2, 16))]:
            ctx.dma("gpsimd", dst[:, :, sl],
                    iv[nm].rearrange("(kc p) o -> p kc o", p=128),
                    reads=[ir[nm]], writes=[dst.r], tag=nm)
        for nm, sl in [("b2_BqT", ts(0, 1024)), ("b2_BkT", ts(1, 1024)),
                       ("b2_BvT", ts(2, 1024))]:
            ctx.dma("gpsimd", bqT[:, sl], iv[nm], reads=[ir[nm]],
                    writes=[bqT.r], tag=nm)
        ctx.dma("gpsimd", idb[:], iv["idf"], reads=[ir["idf"]], writes=[idb.r])
        ctx.dma("sync", etf[:], iv["etf"], reads=[ir["etf"]], writes=[etf.r])
        ctx.e("vector", "memset", bmax[:], 0.0, writes=[bmax.r])

        emit_ln(ctx, xin_d, xin_r, xn, T)

        for j in range(3):
            for tt_i in range(NTT):
                ps = ctx.psum()
                for kc in range(NC):
                    ctx.mm(ps[0:16, :], aqT[:, kc, ts(j, 16)],
                           xn[:, kc, ts(tt_i, TT)],
                           start=(kc == 0), stop=(kc == NC - 1),
                           reads=[aqT.r, xn.r], writes=[ps.r])
                ctx.e("vector", "tensor_scalar_mul",
                      lra[0:16, j * T + tt_i * TT: j * T + (tt_i + 1) * TT],
                      ps[0:16, :], 1.0 / 16.0, reads=[ps.r], writes=[lra.r])

        for c in (range(NC) if B2_SUB >= 2 else []):
            wk = wk3[c % 2]
            for j, nm in enumerate(["b2_WqT", "b2_WkT", "b2_WvT"]):
                ctx.dma("gpsimd", wk[:, :, ts(j, 128)], wslices_ap(iv[nm], c),
                        reads=[ir[nm]], writes=[wk.r], tag=f"{nm}{c}")
            for tt_i in range(NTT):
                ta = ts(tt_i, TT)
                pj = [ctx.psum(), ctx.psum(), ctx.psum()]
                for kc in range(NC):
                    for j in range(3):
                        ctx.mm(pj[j][:, :], wk[:, kc, ts(j, 128)], xn[:, kc, ta],
                               start=(kc == 0), stop=False,
                               reads=[wk.r, xn.r], writes=[pj[j].r])
                for j in range(3):
                    ctx.mm(pj[j][:, :],
                           bqT[:, j * 1024 + c * 128: j * 1024 + (c + 1) * 128],
                           lra[0:16, j * T + tt_i * TT: j * T + (tt_i + 1) * TT],
                           start=False, stop=True, reads=[bqT.r, lra.r],
                           writes=[pj[j].r])
                def stats(j, src, src_r):
                    ctx.e("scalar", "activation", e2[:, :], src, AF.Square,
                          reads=[src_r], writes=[e2.r])
                    psq = ctx.psum()
                    ctx.mm(psq[0:16, :], etf[:, ts(c, 16)], e2[:, :],
                           start=True, stop=True, reads=[etf.r, e2.r],
                           writes=[psq.r])
                    dst_m = qmx if j == 0 else kmx
                    ctx.e("vector", "reduce_max",
                          dst_m[0:16, c * 4 + tt_i:c * 4 + tt_i + 1],
                          psq[0:16, :], axis=AXX, reads=[psq.r],
                          writes=[dst_m.r])

                ctx.e("vector", "tensor_copy", eb1[:, :], pj[0][:, :],
                      reads=[pj[0].r], writes=[eb1.r])
                ctx.dma("sync", xfm(q_d)[:, c, ta], eb1[:, :],
                        reads=[eb1.r], writes=[q_dr], tag=f"q{c}{tt_i}")
                if B2_SUB >= 3:
                    stats(0, eb1[:, :], eb1.r)
                ctx.e("vector", "tensor_copy", eb1[:, :], pj[1][:, :],
                      reads=[pj[1].r], writes=[eb1.r])
                ctx.dma("sync", gki.rearrange("(cc p) t -> p cc t", p=128)[:, c, ta],
                        eb1[:, :], reads=[eb1.r], writes=[g2r[0]],
                        tag=f"k{c}{tt_i}")
                if B2_SUB >= 3:
                    stats(1, eb1[:, :], eb1.r)
                ctx.e("vector", "tensor_copy", vsb[:, ta], pj[2][:, :],
                      reads=[pj[2].r], writes=[vsb.r])
            for tc in (range(T // 128) if B2_SUB >= 4 else []):
                pst = ctx.psumb()
                ctx.e("tensor", "transpose", pst[:, 0:128], vsb[:, ts(tc, 128)],
                      idb[:], reads=[vsb.r, idb.r], writes=[pst.r])
                ctx.e("vector", "tensor_copy", eb1[:, 0:128], pst[:, 0:128],
                      reads=[pst.r], writes=[eb1.r])
                ctx.dma("sync", gvi[ts(tc, 128), ts(c, 128)], eb1[:, 0:128],
                        reads=[eb1.r], writes=[g2r[2]], tag=f"v{c}{tc}")
        ctx.e("vector", "reduce_max", bmax[0:16, 0:1], qmx[0:16, :],
              axis=AXX, reads=[qmx.r], writes=[bmax.r])
        ctx.e("vector", "reduce_max", bmax[0:16, 1:2], kmx[0:16, :],
              axis=AXX, reads=[kmx.r], writes=[bmax.r])
        ctx.e("scalar", "activation", bmax[0:16, 0:2], bmax[0:16, 0:2],
              AF.Sqrt, reads=[], writes=[bmax.r])
        ctx.dma("scalar", gsi[0:1, :].rearrange("a p -> p a"), bmax[0:16, 1:2],
                reads=[bmax.r], writes=[g2r[4]], tag="kstat")
        for gi_, go_, ra_, rb_, tg in ([(gki, gko, g2r[0], g2r[1], "agK"),
                                       (gvi, gvo, g2r[2], g2r[3], "agV"),
                                       (gsi, gso, g2r[4], g2r[5], "agS")]
                                       if B2_SUB >= 5 else []):
            ctx.s.rw("gpsimd", _mk_emit("collective_compute",
                     ("AllGather", ALU.bypass), dict(
                         replica_groups=ctx.groups, ins=[gi_], outs=[go_])),
                     reads=[ra_], writes=[rb_], cc=True, tag=tg)
        for j2 in (range(4) if B2_SUB >= 5 else []):
            ctx.dma("scalar", gmx[0:16, j2:j2 + 1],
                    gso[j2:j2 + 1, :].rearrange("a p -> p a"),
                    reads=[g2r[5]], writes=[gmx.r], tag=f"ks{j2}")
        ctx.e("vector", "reduce_max", bmax[0:16, 2:3], gmx[0:16, :],
              axis=AXX, reads=[gmx.r], writes=[bmax.r])
        ctx.e("vector", "tensor_mul", bmax[0:16, 3:4], bmax[0:16, 0:1],
              bmax[0:16, 2:3], reads=[], writes=[bmax.r])
        ctx.e("vector", "tensor_scalar_mul", bmax[0:16, 3:4], bmax[0:16, 3:4],
              -1.0 / 8.0, reads=[], writes=[bmax.r])
        ctx.fence("b2proj")
    if B2_STAGE >= 2:
        build_b2_flash(ctx, iv, ir, xn, att, bmax, q_d, q_dr, gko, gvo, g2r)
    if B2_STAGE >= 3:
        build_b23_ffn(ctx, iv, ir, xn, att, xin_d, xin_r, xout_d, xout_r, "b2",
                      attn_proj=True)
    else:
        ctx.dma("sync", xout_d, xin_d, reads=[xin_r], writes=[xout_r],
                tag="copyout")
    return xn, att


def build_b2_flash(ctx, iv, ir, xn, att, bmax, q_d, q_dr, gko, gvo, g2r):
    s, cm = ctx.s, ctx.cm
    SKG = 4
    NG = (4 * T) // (128 * SKG)
    with contextlib.ExitStack() as fs:
        i16f = ctx.sb("b2_i16f", [16, 16], F32, fs)
        onesf = ctx.sb("b2_onesf", [1, 64], F32, fs)
        brow = ctx.sb("b2_brow", [1, 16], F32, fs)
        negB = ctx.sb("b2_negB", [128, 16], F32, fs)
        qbuf = [ctx.sb("b2_qb0", [128, NC, TT], BF16, fs)] * 2
        kslab = [ctx.sb(f"b2_ks{i}", [128, NC, SKG * 128], BF16, fs)
                 for i in range(2)]
        vslab = [ctx.sb(f"b2_vsl{i}", [128, SKG, 16, 65], BF16, fs)
                 for i in range(2)]
        pbuf = [ctx.sb(f"b2_pb{i}", [128, TT], BF16, fs) for i in range(2)]
        oacc = ctx.sb("b2_oacc", [65, 16, TT], F32, fs)
        nv = ctx.sb("b2_nv", [1, TT], F32, fs)

        ctx.dma("sync", i16f[:], iv["i16f"], reads=[ir["i16f"]], writes=[i16f.r])
        ctx.e("vector", "memset", onesf[:], 1.0, writes=[onesf.r])
        for vb in vslab:
            ctx.e("vector", "memset", vb[:, :, :, 64:65], 1.0, writes=[vb.r])
        psb = ctx.psum()
        ctx.mm(psb[0:1, 0:16], bmax[0:16, 3:4], i16f[:], start=True, stop=True,
               reads=[bmax.r, i16f.r], writes=[psb.r])
        ctx.e("vector", "tensor_copy", brow[:], psb[0:1, 0:16],
              reads=[psb.r], writes=[brow.r])
        psb2 = ctx.psum()
        ctx.mm(psb2[:, 0:16], cm["ones1f"][:], brow[:], start=True, stop=True,
               reads=[cm["ones1f"].r, brow.r], writes=[psb2.r])
        ctx.e("vector", "tensor_copy", negB[:], psb2[:, 0:16],
              reads=[psb2.r], writes=[negB.r])

        for tt_i in range(NTT):
            ta = ts(tt_i, TT)
            qb = qbuf[tt_i % 2]
            ctx.dma("sync", qb[:], xfm(q_d)[:, :, ta], reads=[q_dr],
                    writes=[qb.r], tag=f"fq{tt_i}")
            ctx.e("vector", "memset", oacc[:], 0.0, writes=[oacc.r])
            for g in range(NG):
                ksl = kslab[g % 2]
                vsl = vslab[g % 2]
                ctx.dma("sync", ksl[:],
                        gko.rearrange("(r cc p) t -> p r cc t", p=128, cc=NC)
                        .rearrange("p r cc t -> p cc (r t)")[:, :, g * SKG * 128:(g + 1) * SKG * 128]
                        if False else
                        gko.rearrange("(r cc p) t -> p (r cc t)", p=128)[:, 0:1],
                        reads=[g2r[1]], writes=[ksl.r], tag="BAD") if False else None
                r0 = (g * SKG * 128) // T
                t0 = (g * SKG * 128) % T
                ctx.dma("sync", ksl[:],
                        gko[r0 * 1024:(r0 + 1) * 1024, t0:t0 + SKG * 128]
                        .rearrange("(cc p) t -> p cc t", p=128),
                        reads=[g2r[1]], writes=[ksl.r], tag=f"fk{tt_i}_{g}")
                for sg in range(SKG):
                    r0v = (g * SKG + sg) * 128
                    ctx.dma("sync", vsl[:, sg, :, 0:64],
                            gvo[r0v:r0v + 128, :]
                            .rearrange("p (hh d) -> p hh d", d=64),
                            reads=[g2r[3]], writes=[vsl.r],
                            tag=f"fv{tt_i}_{g}_{sg}")
                for h in range(16):
                    psO = ctx.psum_at(5)
                    for sk in range(SKG):
                        psA = ctx.psum_at(3 + sk % 2)
                        ctx.mm(psA[:, :], ksl[ts(h % 2, 64), h // 2, ts(sk, 128)],
                               qb[ts(h % 2, 64), h // 2, :], start=True,
                               stop=True, reads=[ksl.r, qb.r], writes=[psA.r])
                        pb = pbuf[sk % 2]
                        ctx.e("scalar", "activation", pb[:, :], psA[:, :],
                              AF.Exp, bias=negB[:, h:h + 1], scale=1.0 / 8.0,
                              reads=[psA.r, negB.r], writes=[pb.r])
                        ctx.mm(psO[0:65, :], vsl[:, sk, h, :], pb[:, :],
                               start=(sk == 0), stop=(sk == SKG - 1),
                               reads=[vsl.r, pb.r], writes=[psO.r])
                    ctx.e("vector", "tensor_add",
                          oacc[0:65, h, :], oacc[0:65, h, :], psO[0:65, :],
                          reads=[psO.r], writes=[oacc.r])
            for h in range(16):
                ctx.e("vector", "reciprocal", nv[:, :], oacc[64:65, h, :],
                      reads=[oacc.r], writes=[nv.r])
                psn = ctx.psum()
                ctx.mm(psn[0:64, :], onesf[:], nv[:, :], start=True, stop=True,
                       reads=[onesf.r, nv.r], writes=[psn.r])
                ctx.e("vector", "tensor_mul", att[ts(h % 2, 64), h // 2, ta],
                      oacc[0:64, h, :], psn[0:64, :],
                      reads=[oacc.r, psn.r], writes=[att.r])
        ctx.fence("b2flash")


def build_b23_ffn(ctx, iv, ir, xn, att, xin_d, xin_r, xout_d, xout_r, pfx,
                  attn_proj=True):
    s, cm = ctx.s, ctx.cm
    e1, e2 = cm["e1"], cm["e2"]
    with contextlib.ExitStack() as fs:
        s1 = ctx.sb(pfx + "_s1s", [128, 32], F32, fs)
        s2 = ctx.sb(pfx + "_s2s", [128, 32], F32, fs)
        sh1 = ctx.sb(pfx + "_sh1s", [128, 32], F32, fs)
        sh2 = ctx.sb(pfx + "_sh2s", [128, 32], F32, fs)
        s3 = ctx.sb(pfx + "_s3s", [128, NC], F32, fs)
        for sb, nm in [(s1, pfx + "_s1"), (s2, pfx + "_s2"), (sh1, pfx + "_sh1"),
                       (sh2, pfx + "_sh2"), (s3, pfx + "_s3")]:
            ctx.dma("sync", sb[:], iv[nm], reads=[ir[nm]], writes=[sb.r], tag=nm)
        if attn_proj:
            bo = ctx.sb(pfx + "_bos", [128, NC], F32, fs)
            aoT = ctx.sb(pfx + "_aoTs", [128, NC, 16], BF16, fs)
            boT = ctx.sb(pfx + "_boTs", [16, 1024], BF16, fs)
            lro = ctx.sb(pfx + "_lro", [16, TT], BF16, fs)
            ctx.dma("sync", bo[:], iv[pfx + "_bo"], reads=[ir[pfx + "_bo"]],
                    writes=[bo.r])
            ctx.dma("gpsimd", aoT[:],
                    iv[pfx + "_AoT"].rearrange("(kc p) o -> p kc o", p=128),
                    reads=[ir[pfx + "_AoT"]], writes=[aoT.r])
            ctx.dma("gpsimd", boT[:], iv[pfx + "_BoT"],
                    reads=[ir[pfx + "_BoT"]], writes=[boT.r])

        h_sb = ctx.sb(pfx + "_h", [128, 32, TT], BF16, fs)
        g_sb = ctx.sb(pfx + "_g", [128, 32, TT], BF16, fs)
        wsl = [ctx.sb(f"{pfx}_wsl{i}", [128, 32, 128], BF16, fs)
               for i in range(3)]
        w14 = [wsl[0], wsl[1], wsl[2]]
        wos = [ctx.sb(f"{pfx}_wos0", [128, NC, 128], BF16, fs)] * 2

        w1t = iv[pfx + "_w1t"].rearrange("(kc p) o -> p kc o", p=128)
        w2t = iv[pfx + "_w2t"].rearrange("(kc p) o -> p kc o", p=128)
        h1t = iv[pfx + "_h1t"].rearrange("(kc p) o -> p kc o", p=128)
        h2t = iv[pfx + "_h2t"].rearrange("(kc p) o -> p kc o", p=128)
        w3t = iv[pfx + "_w3t"].rearrange("(kc p) o -> p kc o", p=128)

        for tt_i in range(NTT):
            ta = ts(tt_i, TT)
            if attn_proj:
                ps = ctx.psum()
                for kc in range(NC):
                    ctx.mm(ps[0:16, :], aoT[:, kc, :], att[:, kc, ta],
                           start=(kc == 0), stop=(kc == NC - 1),
                           reads=[aoT.r, att.r], writes=[ps.r])
                ctx.e("vector", "tensor_scalar_mul",
                      lro[0:16, :], ps[0:16, :], 1.0 / 16.0,
                      reads=[ps.r], writes=[lro.r])
            for hc in range(32):
                wa = w14[hc % 3]
                ctx.dma("gpsimd", wa[:, 0:NC, :], w1t[:, :, ts(hc, 128)],
                        reads=[ir[pfx + "_w1t"]], writes=[wa.r],
                        tag=f"{pfx}w1{tt_i}{hc}")
                ps = ctx.psum()
                for kc in range(NC):
                    ctx.mm(ps[:, :], wa[:, kc, :], xn[:, kc, ta],
                           start=(kc == 0), stop=(kc == NC - 1),
                           reads=[wa.r, xn.r], writes=[ps.r])
                ctx.e("scalar", "activation", h_sb[:, hc, :], ps[:, :],
                      AF.Silu, scale=s1[:, hc:hc + 1],
                      reads=[ps.r, s1.r], writes=[h_sb.r])
            for hc in range(32):
                wa = w14[hc % 3]
                ctx.dma("gpsimd", wa[:, 0:NC, :], w2t[:, :, ts(hc, 128)],
                        reads=[ir[pfx + "_w2t"]], writes=[wa.r],
                        tag=f"{pfx}w2{tt_i}{hc}")
                ps = ctx.psum()
                for kc in range(NC):
                    ctx.mm(ps[:, :], wa[:, kc, :], xn[:, kc, ta],
                           start=(kc == 0), stop=(kc == NC - 1),
                           reads=[wa.r, xn.r], writes=[ps.r])
                ctx.e("vector", "tensor_scalar_mul", e1[:, :], ps[:, :],
                      s2[:, hc:hc + 1], reads=[ps.r, s2.r], writes=[e1.r])
                ctx.e("vector", "tensor_mul", h_sb[:, hc, :],
                      h_sb[:, hc, :], e1[:, :], reads=[e1.r],
                      writes=[h_sb.r])
            for hmat, hsrc, hdst, scl in [(h1t, h_sb, g_sb, sh1),
                                          (h2t, g_sb, h_sb, sh2)]:
                nm = pfx + ("_h1t" if hmat is h1t else "_h2t")
                for oc2 in range(32):
                    wsb_ = wsl[oc2 % 3]
                    ctx.dma("gpsimd", wsb_[:],
                            hmat[:, :, ts(oc2, 128)], reads=[ir[nm]],
                            writes=[wsb_.r], tag=f"{nm}{tt_i}{oc2}")
                    ps = ctx.psum()
                    for kc in range(32):
                        ctx.mm(ps[:, :], wsb_[:, kc, :], hsrc[:, kc, :],
                               start=(kc == 0), stop=(kc == 31),
                               reads=[wsb_.r, hsrc.r], writes=[ps.r])
                    ctx.e("scalar", "activation", hdst[:, oc2, :], ps[:, :],
                          AF.Gelu, scale=scl[:, oc2:oc2 + 1],
                          reads=[ps.r, scl.r], writes=[hdst.r])
            for oc in range(NC):
                psF = ctx.psum()
                wsb_ = wsl[oc % 3]
                ctx.dma("gpsimd", wsb_[:], w3t[:, :, ts(oc, 128)],
                        reads=[ir[pfx + "_w3t"]], writes=[wsb_.r],
                        tag=f"{pfx}w3{tt_i}{oc}")
                for kc in range(32):
                    ctx.mm(psF[:, :], wsb_[:, kc, :], h_sb[:, kc, :],
                           start=(kc == 0), stop=(kc == 31),
                           reads=[wsb_.r, h_sb.r], writes=[psF.r])
                xo1 = cm["e3"]
                ctx.dma("sync", xo1[:, :], xfm(xin_d)[:, oc, ta]
                        if xin_d.shape[1] == T else
                        xfm(xin_d)[:, oc, 1 + tt_i * TT:1 + (tt_i + 1) * TT],
                        reads=[xin_r], writes=[xo1.r], tag=f"{pfx}xo{tt_i}{oc}")
                if attn_proj:
                    psA = ctx.psum()
                    wob = wos[oc % 2]
                    ctx.dma("gpsimd", wob[:], wslices_ap(iv[pfx + "_WoT"], oc),
                            reads=[ir[pfx + "_WoT"]], writes=[wob.r],
                            tag=f"{pfx}wo{tt_i}{oc}")
                    for kc in range(NC):
                        ctx.mm(psA[:, :], wob[:, kc, :], att[:, kc, ta],
                               start=(kc == 0), stop=False,
                               reads=[wob.r, att.r], writes=[psA.r])
                    ctx.mm(psA[:, :], boT[:, ts(oc, 128)], lro[0:16, :],
                           start=False, stop=True, reads=[boT.r, lro.r],
                           writes=[psA.r])
                    ctx.e("vector", "tensor_scalar", e1[:, :], psA[:, :],
                          bo[:, oc:oc + 1], None, op0=ALU.add,
                          reads=[psA.r, bo.r], writes=[e1.r])
                else:
                    ctx.e("vector", "tensor_copy", e1[:, :], att[:, oc, ta],
                          reads=[att.r], writes=[e1.r])
                ctx.e("vector", "scalar_tensor_tensor", e2[:, :], psF[:, :],
                      s3[:, oc:oc + 1], e1[:, :], op0=ALU.mult, op1=ALU.add,
                      reads=[psF.r, s3.r, e1.r], writes=[e2.r])
                ctx.e("vector", "tensor_add", xo1[:, :], e2[:, :], xo1[:, :],
                      reads=[e2.r], writes=[xo1.r])
                ctx.dma("sync", xfm(xout_d)[:, oc, ta], xo1[:, :],
                        reads=[xo1.r], writes=[xout_r], tag=f"{pfx}xw{tt_i}{oc}")
        ctx.fence(pfx + "ffn")


def build_block3(ctx, iv, ir, xin_d, xin_r, xout_d, xout_r,
                 a1i, a1o, a2i, a2o, g3r):
    s, cm = ctx.s, ctx.cm
    e1, e2, e4 = cm["e1"], cm["e2"], cm["e4"]
    eb1 = cm["eb1"]

    xn = ctx.sb("b3_xn", [128, NC, T], BF16)
    att = ctx.sb("b3_att", [128, NC, T], BF16)
    static_d = ctx.nc.dram_tensor("b3_static", [T, 1024], F32).ap()
    static_r = Reg("b3_static")

    with contextlib.ExitStack() as ms:
        thx = ctx.sb("b3_thx", [128, NC, T], BF16, ms)
        idb = ctx.sb("b3_idb", [128, 128], BF16, ms)
        idf32 = ctx.sb("b3_idf32", [128, 128], F32, ms)
        wsl2 = [ctx.sb(f"b3_wsl2_{i}", [128, NC, 128], BF16, ms) for i in range(2)]
        stf = ctx.sb("b3_stf", [128, TT], F32, ms)

        ctx.dma("gpsimd", idb[:], iv["idf"], reads=[ir["idf"]], writes=[idb.r])
        ctx.dma("sync", idf32[:], iv["idf32"], reads=[ir["idf32"]],
                writes=[idf32.r])
        emit_ln(ctx, xin_d, xin_r, xn, T)
        for c in range(NC):
            for tt_i in range(NTT):
                ctx.e("scalar", "activation", thx[:, c, ts(tt_i, TT)],
                      xn[:, c, ts(tt_i, TT)], AF.Tanh, reads=[xn.r],
                      writes=[thx.r])
        for oc in range(NC):
            wsb_ = wsl2[oc % 2]
            ctx.dma("gpsimd", wsb_[:], wslices_ap(iv["b3_WT"], oc),
                    reads=[ir["b3_WT"]], writes=[wsb_.r], tag=f"b3W{oc}")
            for tt_i in range(NTT):
                ps = ctx.psum()
                for kc in range(NC):
                    ctx.mm(ps[:, :], wsb_[:, kc, :], thx[:, kc, ts(tt_i, TT)],
                           start=(kc == 0), stop=(kc == NC - 1),
                           reads=[wsb_.r, thx.r], writes=[ps.r])
                ctx.e("vector", "tensor_copy", stf[:, :], ps[:, :],
                      reads=[ps.r], writes=[stf.r])
                for i4 in range(4):
                    tc = tt_i * 4 + i4
                    pst = ctx.psum()
                    ctx.e("tensor", "transpose", pst[:, 0:128],
                          stf[:, ts(i4, 128)], idf32[:],
                          reads=[stf.r, idf32.r], writes=[pst.r])
                    ctx.e("vector", "tensor_copy", e1[:, 0:128], pst[:, 0:128],
                          reads=[pst.r], writes=[e1.r])
                    ctx.dma("sync", static_d[ts(tc, 128), ts(oc, 128)],
                            e1[:, 0:128], reads=[e1.r], writes=[static_r],
                            tag=f"b3st{oc}{tc}")
        for c in range(NC):
            for tc in range(T // 128):
                pst = ctx.psumb()
                ctx.e("tensor", "transpose", pst[:, 0:128],
                      thx[:, c, ts(tc, 128)], idb[:],
                      reads=[thx.r, idb.r], writes=[pst.r])
                ctx.e("vector", "tensor_copy", eb1[:, 0:128], pst[:, 0:128],
                      reads=[pst.r], writes=[eb1.r])
                ctx.dma("sync", a1i[ts(tc, 128), ts(c, 128)],
                        eb1[:, 0:128], reads=[eb1.r], writes=[g3r[0]],
                        tag=f"b3a{c}{tc}")
        ctx.dma("gpsimd", a1i[:, 1024:2048], static_d[:, :],
                reads=[static_r, g3r[0]], writes=[g3r[0]], tag="b3s")
        ctx.s.rw("gpsimd", _mk_emit("collective_compute",
                 ("AllGather", ALU.bypass), dict(
                     replica_groups=ctx.groups, ins=[a1i], outs=[a1o])),
                 reads=[g3r[0]], writes=[g3r[1]], cc=True, tag="ag3a")
        ctx.fence("b3proj")

    with contextlib.ExitStack() as ms:
        lamt = ctx.sb("b3_lamt", [1, 8], F32, ms)
        lnl = ctx.sb("b3_lnl", [128, 4], F32, ms)
        lamC = ctx.sb("b3_lamC", [64, 1], F32, ms)
        wvec = ctx.sb("b3_wvec", [128, 2], F32, ms)
        gpm = ctx.sb("b3_gpm", [128, 128], F32, ms)
        musU = ctx.sb("b3_musU", [128, 128], F32, ms)
        muI = ctx.sb("b3_muI", [128, 128], F32, ms)
        tv = ctx.sb("b3_tv", [128, 2], F32, ms)
        adI = ctx.sb("b3_adI", [128, 128], F32, ms)
        bstr = ctx.sb("b3_bstr", [128, 128], F32, ms)
        ktv = ctx.sb("b3_ktv", [128, 512], BF16, ms)
        kraw = ctx.sb("b3_kraw", [128, 2048], BF16, ms)
        qsel = ctx.sb("b3_qsel", [128, 4], F32, ms)
        kfm4 = [ctx.sb(f"b3_kfm{i}", [64, 128], BF16, ms) for i in range(4)]
        adS4 = [ctx.sb(f"b3_adS{i}", [128, 128], BF16, ms) for i in range(4)]
        ltS4 = [ctx.sb(f"b3_ltS{i}", [128, 128], BF16, ms) for i in range(4)]
        bS4 = [ctx.sb(f"b3_bS{i}", [128, 64], BF16, ms) for i in range(4)]
        x2S4 = [ctx.sb(f"b3_x2S{i}", [128, 64], BF16, ms) for i in range(4)]
        uS4 = [ctx.sb(f"b3_uS{i}", [128, 64], BF16, ms) for i in range(4)]
        duS4 = [ctx.sb(f"b3_duS{i}", [128, 64], BF16, ms) for i in range(4)]
        ksw4 = [ctx.sb(f"b3_ksw{i}", [128, 64], F32, ms) for i in range(4)]
        Sst = [ctx.sb(f"b3_S{i}", [64, 4, 64], F32, ms) for i in range(2)]
        Sb = ctx.sb("b3_Sb", [64, 4, 64], BF16, ms)
        ostg = ctx.sb("b3_ostg", [128, 4, 64], F32, ms)

        ctx.dma("sync", gpm[:], iv["g_pm"], reads=[ir["g_pm"]], writes=[gpm.r])
        ctx.dma("sync", musU[:], iv["musU"], reads=[ir["musU"]], writes=[musU.r])
        ctx.dma("sync", muI[:], iv["muI"], reads=[ir["muI"]], writes=[muI.r])
        ctx.dma("sync", tv[:], iv["tv"], reads=[ir["tv"]], writes=[tv.r])
        ctx.dma("sync", lamt[0:1, 0:1], iv["b3_lam"], reads=[ir["b3_lam"]],
                writes=[lamt.r])
        ctx.dma("sync", lamt[0:1, 1:2], iv["b3_eta"], reads=[ir["b3_eta"]],
                writes=[lamt.r])
        ctx.e("scalar", "activation", lamt[0:1, 2:3], lamt[0:1, 0:1],
              AF.Sigmoid, reads=[lamt.r], writes=[lamt.r])
        ctx.e("scalar", "activation", lamt[0:1, 3:4], lamt[0:1, 2:3],
              AF.Ln, reads=[lamt.r], writes=[lamt.r])
        ctx.e("vector", "tensor_scalar", lamt[0:1, 4:5], lamt[0:1, 2:3],
              -1.0, 1.0, op0=ALU.mult, op1=ALU.add, reads=[lamt.r],
              writes=[lamt.r])
        ctx.e("vector", "tensor_mul", lamt[0:1, 5:6], lamt[0:1, 4:5],
              lamt[0:1, 1:2], reads=[lamt.r], writes=[lamt.r])
        ctx.e("vector", "tensor_scalar_mul", lamt[0:1, 5:6], lamt[0:1, 5:6],
              1.0 / 64.0, reads=[lamt.r], writes=[lamt.r])
        ctx.e("vector", "tensor_scalar_mul", lamt[0:1, 6:7], lamt[0:1, 5:6],
              -1.0, reads=[lamt.r], writes=[lamt.r])
        ctx.e("vector", "tensor_scalar_mul", lamt[0:1, 7:8], lamt[0:1, 3:4],
              128.0, reads=[lamt.r], writes=[lamt.r])
        for col, src in [(0, 3), (1, 5), (2, 6)]:
            psb = ctx.psum()
            ctx.mm(psb[:, 0:1], cm["ones1f"][:], lamt[0:1, src:src + 1],
                   start=True, stop=True, reads=[cm["ones1f"].r, lamt.r],
                   writes=[psb.r])
            ctx.e("vector", "tensor_copy", lnl[:, col:col + 1], psb[:, 0:1],
                  reads=[psb.r], writes=[lnl.r])
        psb = ctx.psum()
        ctx.e("scalar", "activation", lamt[0:1, 7:8], lamt[0:1, 7:8], AF.Exp,
              reads=[lamt.r], writes=[lamt.r])
        ctx.mm(psb[0:64, 0:1], cm["ones1f"][0:1, 0:64], lamt[0:1, 7:8],
               start=True, stop=True, reads=[cm["ones1f"].r, lamt.r],
               writes=[psb.r])
        ctx.e("vector", "tensor_copy", lamC[:], psb[0:64, 0:1],
              reads=[psb.r], writes=[lamC.r])
        for col in range(2):
            ctx.e("vector", "tensor_mul", e1[:, 0:1], tv[:, col:col + 1],
                  lnl[:, 0:1], reads=[tv.r, lnl.r], writes=[e1.r])
            ctx.e("scalar", "activation", wvec[:, col:col + 1], e1[:, 0:1],
                  AF.Exp, reads=[e1.r], writes=[wvec.r])
        ctx.e("vector", "tensor_scalar_mul", e1[:, 0:128], gpm[:], -1.0,
              reads=[gpm.r], writes=[e1.r])
        ctx.e("vector", "tensor_mul", e1[:, 0:128], e1[:, 0:128]
              if False else e1[:, 0:128], lnl[:, 0:1], reads=[lnl.r],
              writes=[e1.r], tag="BAD-INPLACE") if False else None
        ctx.e("vector", "tensor_scalar_mul", e2[:, 0:128], e1[:, 0:128],
              lnl[:, 0:1], reads=[e1.r, lnl.r], writes=[e2.r])
        ctx.e("scalar", "activation", e4[:, 0:128], e2[:, 0:128], AF.Exp,
              reads=[e2.r], writes=[e4.r])
        ctx.e("vector", "tensor_mul", adI[:], e4[:, 0:128], muI[:],
              reads=[e4.r, muI.r], writes=[adI.r])
        ctx.e("vector", "tensor_scalar_mul", bstr[:], musU[:], lamt[0:1, 5:6]
              if False else 1.0, reads=[musU.r], writes=[bstr.r],
              tag="FIXBELOW") if False else None
        ctx.e("vector", "tensor_scalar_mul", bstr[:], musU[:], lnl[:, 1:2],
              reads=[musU.r, lnl.r], writes=[bstr.r])

        ctx.dma("sync", qsel[:], iv["qsel"], reads=[ir["qsel"]],
                writes=[qsel.r])
        idbm = ctx.sb("b3_idbm", [128, 128], BF16, ms)
        ctx.dma("gpsimd", idbm[:], iv["idf"], reads=[ir["idf"]],
                writes=[idbm.r])
        ctx.e("vector", "memset", Sst[0][:], 0.0, writes=[Sst[0].r])
        ctx.e("vector", "memset", Sb[:], 0.0, writes=[Sb.r])
        for ch in range(T // 128 * 4):
            ctx.dma("sync", kraw[:], a1o[ts(ch, 128), :], reads=[g3r[1]],
                    writes=[kraw.r], tag=f"b3kv{ch}")
            for half, off in [(0, 0), (1, 1024)]:
                ctx.e("vector", "tensor_scalar_mul", ktv[:, ts(half, 256)],
                      kraw[:, off:off + 256], qsel[:, 0:1],
                      reads=[kraw.r, qsel.r], writes=[ktv.r])
                for jr in range(1, 4):
                    ctx.e("vector", "scalar_tensor_tensor", ktv[:, ts(half, 256)],
                          kraw[:, off + jr * 256:off + (jr + 1) * 256],
                          qsel[:, jr:jr + 1], ktv[:, ts(half, 256)],
                          op0=ALU.mult, op1=ALU.add,
                          reads=[kraw.r, qsel.r, ktv.r], writes=[ktv.r])
            Sprev = Sst[ch % 2]
            Snext = Sst[(ch + 1) % 2]
            kms = [ktv[:, ts(mh, 64)] for mh in range(4)]
            vms = [ktv[:, 256 + mh * 64:256 + (mh + 1) * 64] for mh in range(4)]
            for mh in range(4):
                pst = ctx.psumb()
                ctx.e("tensor", "transpose", pst[0:64, 0:128], kms[mh],
                      idbm[:], reads=[ktv.r, idbm.r], writes=[pst.r])
                ctx.e("scalar", "copy", kfm4[mh][:], pst[0:64, 0:128],
                      reads=[pst.r], writes=[kfm4[mh].r])
            psA4 = [ctx.psum_at(mh) for mh in range(4)]
            for mh in range(4):
                ctx.mm(psA4[mh][:, :128], kfm4[mh][:], kfm4[mh][:],
                       start=True, stop=True, reads=[kfm4[mh].r],
                       writes=[psA4[mh].r])
            for mh in range(4):
                ctx.e("vector", "tensor_mul", adS4[mh][:], psA4[mh][:, 0:128],
                      adI[:], reads=[psA4[mh].r, adI.r], writes=[adS4[mh].r])
                ctx.e("vector", "tensor_mul", ltS4[mh][:], psA4[mh][:, 0:128],
                      bstr[:], reads=[psA4[mh].r, bstr.r], writes=[ltS4[mh].r])
            psK4 = [ctx.psum_at(mh) for mh in range(4)]
            for mh in range(4):
                ctx.mm(psK4[mh][:, 0:64], kfm4[mh][:], Sb[:, mh, :],
                       start=True, stop=True, reads=[kfm4[mh].r, Sb.r],
                       writes=[psK4[mh].r])
            for mh in range(4):
                ctx.e("vector", "tensor_scalar_mul", ksw4[mh][:],
                      psK4[mh][:, 0:64], wvec[:, 0:1],
                      reads=[psK4[mh].r, wvec.r], writes=[ksw4[mh].r])
                ctx.e("vector", "scalar_tensor_tensor", e1[:, 0:64],
                      psK4[mh][:, 0:64], wvec[:, 0:1], vms[mh],
                      op0=ALU.mult, op1=ALU.subtract,
                      reads=[psK4[mh].r, wvec.r, ktv.r], writes=[e1.r])
                ctx.e("vector", "tensor_scalar_mul", bS4[mh][:], e1[:, 0:64],
                      lnl[:, 2:3], reads=[e1.r, lnl.r], writes=[bS4[mh].r])
            psX4 = [ctx.psum_at(mh) for mh in range(4)]
            for mh in range(4):
                ctx.mm(psX4[mh][:, 0:64], ltS4[mh][:], bS4[mh][:],
                       start=True, stop=True, reads=[ltS4[mh].r, bS4[mh].r],
                       writes=[psX4[mh].r])
            for mh in range(4):
                ctx.e("vector", "tensor_tensor", x2S4[mh][:], bS4[mh][:],
                      psX4[mh][:, 0:64], op=ALU.subtract,
                      reads=[bS4[mh].r, psX4[mh].r], writes=[x2S4[mh].r])
            psX24 = [ctx.psum_at(mh) for mh in range(4)]
            for mh in range(4):
                ctx.mm(psX24[mh][:, 0:64], ltS4[mh][:], x2S4[mh][:],
                       start=True, stop=True, reads=[ltS4[mh].r, x2S4[mh].r],
                       writes=[psX24[mh].r])
            for mh in range(4):
                ctx.e("vector", "tensor_tensor", uS4[mh][:], bS4[mh][:],
                      psX24[mh][:, 0:64], op=ALU.subtract,
                      reads=[bS4[mh].r, psX24[mh].r], writes=[uS4[mh].r])
            psO4 = [ctx.psum_at(mh) for mh in range(4)]
            for mh in range(4):
                ctx.mm(psO4[mh][:, 0:64], adS4[mh][:], uS4[mh][:],
                       start=True, stop=True, reads=[adS4[mh].r, uS4[mh].r],
                       writes=[psO4[mh].r])
            for mh in range(4):
                ctx.e("vector", "tensor_add", ostg[:, mh, :], ksw4[mh][:],
                      psO4[mh][:, 0:64], reads=[ksw4[mh].r, psO4[mh].r],
                      writes=[ostg.r])
                ctx.e("vector", "tensor_scalar_mul", duS4[mh][:], uS4[mh][:],
                      wvec[:, 1:2], reads=[uS4[mh].r, wvec.r],
                      writes=[duS4[mh].r])
            psS4 = [ctx.psum_at(mh) for mh in range(4)]
            for mh in range(4):
                ctx.mm(psS4[mh][0:64, 0:64], kms[mh], duS4[mh][:],
                       start=True, stop=True, reads=[ktv.r, duS4[mh].r],
                       writes=[psS4[mh].r])
            for mh in range(4):
                ctx.e("vector", "scalar_tensor_tensor", Snext[:, mh, :],
                      Sprev[:, mh, :], lamC[:, 0:1], psS4[mh][0:64, 0:64],
                      op0=ALU.mult, op1=ALU.add,
                      reads=[Sprev.r, lamC.r, psS4[mh].r], writes=[Snext.r])
                ctx.e("scalar", "copy", Sb[:, mh, :], Snext[:, mh, :],
                      reads=[Snext.r], writes=[Sb.r])
            ctx.dma("sync", a2i[ts(ch, 128), :], ostg[:],
                    reads=[ostg.r], writes=[g3r[2]], tag=f"b3o{ch}")
        ctx.s.rw("gpsimd", _mk_emit("collective_compute",
                 ("AllGather", ALU.bypass), dict(
                     replica_groups=ctx.groups, ins=[a2i], outs=[a2o])),
                 reads=[g3r[2]], writes=[g3r[3]], cc=True, tag="ag3b")
        ctx.fence("b3mix")

    with contextlib.ExitStack() as ms:
        idb2 = ctx.sb("b3_idb2", [128, 128], BF16, ms)
        ytile = ctx.sb("b3_yt", [128, 1024], F32, ms)
        y2 = ctx.sb("b3_y2", [128, 1024], F32, ms)
        gb = ctx.sb("b3_gb", [128, 1024], F32, ms)
        bb = ctx.sb("b3_bb", [128, 1024], F32, ms)
        yst = ctx.sb("b3_yst", [128, 4], F32, ms)
        atb = ctx.sb("b3_atb", [128, 1024], BF16, ms)
        ctx.dma("gpsimd", idb2[:], iv["idf"], reads=[ir["idf"]], writes=[idb2.r])
        for dst, nm in [(gb, "b3_lngr"), (bb, "b3_lnbr")]:
            psb = ctx.psum()
            rowt = ctx.sb("b3_row_" + nm, [1, 1024], F32, ms)
            ctx.dma("sync", rowt[:], iv[nm], reads=[ir[nm]], writes=[rowt.r])
            for half in range(2):
                ctx.mm(psb[:, 0:512], cm["ones1f"][:], rowt[0:1, ts(half, 512)],
                       start=True, stop=True,
                       reads=[cm["ones1f"].r, rowt.r], writes=[psb.r])
                ctx.e("vector", "tensor_copy", dst[:, ts(half, 512)],
                      psb[:, 0:512], reads=[psb.r], writes=[dst.r])
        qsel2 = ctx.sb("b3_qsel2", [128, 4], F32, ms)
        ycnd = ctx.sb("b3_ycnd", [128, 4, 256], F32, ms)
        ctx.dma("sync", qsel2[:], iv["qsel"], reads=[ir["qsel"]],
                writes=[qsel2.r])
        for tc in range(T // 128):
            for i in range(4):
                for jr in range(4):
                    ctx.dma("sync", ycnd[:, jr, :],
                            a2o[i * 4 * T + jr * T + tc * 128:
                                i * 4 * T + jr * T + (tc + 1) * 128, :],
                            reads=[g3r[3]], writes=[ycnd.r],
                            tag=f"b3y{tc}_{i}_{jr}")
                ctx.e("vector", "tensor_scalar_mul",
                      ytile[:, i * 256:(i + 1) * 256], ycnd[:, 0, :],
                      qsel2[:, 0:1], reads=[ycnd.r, qsel2.r],
                      writes=[ytile.r])
                for jr in range(1, 4):
                    ctx.e("vector", "scalar_tensor_tensor",
                          ytile[:, i * 256:(i + 1) * 256], ycnd[:, jr, :],
                          qsel2[:, jr:jr + 1],
                          ytile[:, i * 256:(i + 1) * 256],
                          op0=ALU.mult, op1=ALU.add,
                          reads=[ycnd.r, qsel2.r, ytile.r], writes=[ytile.r])
            ctx.dma("sync", y2[:, :], static_d[ts(tc, 128), :],
                    reads=[static_r], writes=[y2.r], tag=f"b3ys{tc}")
            ctx.e("vector", "tensor_add", ytile[:, :], ytile[:, :], y2[:, :],
                  reads=[y2.r], writes=[ytile.r])
            ctx.e("vector", "reduce_sum", yst[:, 0:1], ytile[:, :], axis=AXX,
                  reads=[ytile.r], writes=[yst.r])
            ctx.e("vector", "tensor_scalar_mul", yst[:, 0:1], yst[:, 0:1],
                  1.0 / 1024.0, reads=[], writes=[yst.r])
            ctx.e("vector", "tensor_scalar", y2[:, :], ytile[:, :],
                  yst[:, 0:1], None, op0=ALU.subtract,
                  reads=[ytile.r, yst.r], writes=[y2.r])
            ctx.e("scalar", "activation", ytile[:, :], y2[:, :], AF.Square,
                  reads=[y2.r], writes=[ytile.r])
            ctx.e("vector", "reduce_sum", yst[:, 1:2], ytile[:, :], axis=AXX,
                  reads=[ytile.r], writes=[yst.r])
            ctx.e("vector", "tensor_scalar", yst[:, 2:3], yst[:, 1:2],
                  1.0 / 1024.0, 1e-5, op0=ALU.mult, op1=ALU.add,
                  reads=[yst.r], writes=[yst.r])
            ctx.e("scalar", "activation", yst[:, 2:3], yst[:, 2:3], AF.Sqrt,
                  reads=[], writes=[yst.r])
            ctx.e("vector", "reciprocal", yst[:, 3:4], yst[:, 2:3],
                  reads=[], writes=[yst.r])
            ctx.e("vector", "tensor_scalar_mul", y2[:, :], y2[:, :],
                  yst[:, 3:4], reads=[yst.r], writes=[y2.r])
            ctx.e("vector", "tensor_mul", y2[:, :], y2[:, :], gb[:, :],
                  reads=[gb.r], writes=[y2.r])
            ctx.e("vector", "tensor_add", atb[:, :], y2[:, :], bb[:, :],
                  reads=[y2.r, bb.r], writes=[atb.r])
            for c in range(NC):
                pst = ctx.psumb()
                ctx.e("tensor", "transpose", pst[:, 0:128],
                      atb[:, ts(c, 128)], idb2[:],
                      reads=[atb.r, idb2.r], writes=[pst.r])
                ctx.e("vector", "tensor_copy", att[:, c, ts(tc, 128)],
                      pst[:, 0:128], reads=[pst.r], writes=[att.r])
        ctx.fence("b3y")
    build_b23_ffn(ctx, iv, ir, xn, att, xin_d, xin_r, xout_d, xout_r, "b3",
                  attn_proj=False)
    return xn, att


def build_prog_b0():
    nc = bass.Bass("TRN2", target_bir_lowering=False, debug=False)
    iv, ir = {}, {}

    def inp(name, shape):
        iv[name] = nc.dram_tensor(name, list(shape), F32, kind="ExternalInput").ap()
        ir[name] = Reg(name)

    inp("xin", [1024, T + 1])
    for nm, shp in B0_INPUTS:
        inp(nm, shp)
    xout = nc.dram_tensor("xout", [1024, T], F32, kind="ExternalOutput").ap()
    xout_r = Reg("xout")
    gin = [nc.dram_tensor(f"g0i_{c}", [2, 128], F32).ap() for c in range(NC)]
    gout = [nc.dram_tensor(f"g0o_{c}", [8, 128], F32).ap() for c in range(NC)]
    gr = []
    for c in range(NC):
        gr += [Reg(f"g0i_{c}"), Reg(f"g0o_{c}")]

    s = Sched(nc)
    with contextlib.ExitStack() as stk:
        ctx = Ctx(nc, stk, s)
        ctx.groups = [[0, 1, 2, 3], [4, 5, 6, 7]]
        ctx.psum_init(n_f32=8, n_bf16=0)
        block = stk.enter_context(nc.Block())
        alloc_common(ctx)
        build_block0(ctx, iv, ir, iv["xin"], ir["xin"], xout, xout_r,
                     gin, gout, gr)
        last = [o for o in s.ops if o.dma and o.engine == "sync"][-1]
        s.op("gpsimd", _mk_emit("nop", (), {}), deps=[last, s.ops[-1]])
        s.build(stk, block)
    return nc


B0_INPUTS = [
    ("b0_mixv", [128, 6, NC]), ("b0_w0v", [16, 1]), ("b0_a0v", [16, 1]),
    ("b0_kk", [128, NC]), ("b0_rk", [128, NC]), ("b0_lnw", [128, NC]),
    ("b0_lnb", [128, NC]), ("pmask", [128, NC]),
    ("ebcf", [16, NC * 128]), ("etf", [128, NC * 16]),
    ("b0_w1", [1024, 64]), ("b0_a1", [1024, 64]), ("b0_g1", [1024, 64]),
    ("b0_w2", [64, 16]), ("b0_a2", [64, 16]), ("b0_g2", [64, 1024]),
    ("b0_Wr", [1024, 1024]), ("b0_Wk", [1024, 1024]), ("b0_Wv", [1024, 1024]),
    ("b0_Wo", [1024, 1024]),
    ("b0f_w1t", [1024, 4096]), ("b0f_b1", [128, 32]),
    ("b0f_w2t", [4096, 1024]), ("b0f_b2", [128, NC]),
]


def fmv(v, ncols):
    v = np.asarray(v, np.float32).reshape(-1)
    return np.ascontiguousarray(v.reshape(ncols, 128).T)


def host_consts(q):
    c = {}
    ebcf = np.zeros((16, NC * 128), np.float32)
    etf = np.zeros((128, NC * 16), np.float32)
    for cc in range(NC):
        for p in range(128):
            h = 2 * cc + p // 64
            ebcf[h, cc * 128 + p] = 1.0
            etf[p, cc * 16 + h] = 1.0
    c["ebcf"] = ebcf
    c["etf"] = etf
    pm = np.zeros((128, NC), np.float32)
    for j in range(4):
        pm[:, j] = 1.0 if j < q else 0.0
        pm[:, 4 + j] = 0.0 if j < q else 1.0
    c["pmask"] = pm
    return c


def stage_b0(ins, core):
    b, q = core // 4, core % 4
    x = np.asarray(ins["L0_attn_x"], np.float32)
    seg = x[b, q * T:(q + 1) * T]
    prev = x[b, q * T - 1:q * T] if q > 0 else np.zeros((1, 1024), np.float32)
    xin = np.ascontiguousarray(np.concatenate([prev, seg], 0).T)
    d = {"xin": xin}
    d.update(host_consts(q))
    mixv = np.zeros((128, 6, NC), np.float32)
    for j, nm in enumerate(["x_r", "x_w", "x_k", "x_v", "x_a", "x_g"]):
        mixv[:, j, :] = fmv(ins["L0_attn_" + nm], NC)
    d["b0_mixv"] = mixv
    d["b0_w0v"] = np.asarray(ins["L0_attn_w0"], np.float32).reshape(16, 1)
    d["b0_a0v"] = np.asarray(ins["L0_attn_a0"], np.float32).reshape(16, 1)
    d["b0_kk"] = fmv(ins["L0_attn_k_k"], NC)
    d["b0_rk"] = fmv(ins["L0_attn_r_k"], NC)
    d["b0_lnw"] = fmv(ins["L0_attn_ln_w"], NC)
    d["b0_lnb"] = fmv(ins["L0_attn_ln_b"], NC)
    for nm in ["w1", "a1", "g1", "w2", "a2", "g2", "Wr", "Wk", "Wv", "Wo"]:
        d["b0_" + nm] = np.ascontiguousarray(np.asarray(ins["L0_attn_" + nm], np.float32))
    d["b0f_w1t"] = np.ascontiguousarray(np.asarray(ins["L0_ffn_w1"], np.float32).T)
    d["b0f_w2t"] = np.ascontiguousarray(np.asarray(ins["L0_ffn_w2"], np.float32).T)
    d["b0f_b1"] = fmv(ins["L0_ffn_b1"], 32)
    d["b0f_b2"] = fmv(ins["L0_ffn_b2"], NC)
    return d


def run_block0(ins, dump=False):
    global DUMP
    DUMP = dump
    nc = build_prog_b0()
    in_maps = [stage_b0(ins, c) for c in range(NCORE)]
    res = run_bass_kernel_spmd(nc, in_maps, core_ids=list(range(NCORE)))
    outs = []
    for c in range(NCORE):
        outs.append(np.asarray(res.results[c]["xout"]).T)
    x1 = np.stack([np.concatenate(outs[0:4], 0), np.concatenate(outs[4:8], 0)])
    if dump:
        return x1, res
    return x1


B1_INPUTS = [
    ("ebcf", [16, NC * 128]), ("pmask", [128, NC]), ("idf", [128, 128]),
    ("l1mask", [128, 256]),
    ("b1_qkvw", [1024, 3072]), ("b1_qb", [128, NC]), ("b1_kb", [128, NC]),
    ("b1_vb", [128, NC]), ("b1_outw", [1024, 1024]), ("b1_ob", [128, NC]),
    ("b1f_w1", [1024, 4096]), ("b1f_b1", [128, 32]), ("b1f_w2", [4096, 1024]),
    ("b1f_b2", [128, NC]), ("b1f_th", [128, 32]), ("b1f_wr", [128, 32]),
    ("b1f_wi", [128, 32]),
]


def stage_b1(ins, core, x_cur):
    b, q = core // 4, core % 4
    seg = x_cur[b, q * T:(q + 1) * T] if x_cur.shape[1] >= (q + 1) * T         else np.zeros((T, 1024), np.float32)
    d = {"xin": np.ascontiguousarray(seg.T)}
    d.update(host_consts(q))
    idf = np.zeros((128, 128), np.float32)
    np.fill_diagonal(idf, 1.0)
    d["idf"] = idf.astype(ml_dtypes.bfloat16)
    mask = np.zeros((128, 128), np.float32)
    for sk in range(128):
        mask[sk, sk:] = 1.0
    d["l1mask"] = np.ascontiguousarray(np.concatenate([mask, mask], axis=1))
    qkv_b = np.asarray(ins["L1_attn_qkv_b"], np.float32)
    d["b1_qkvw"] = np.ascontiguousarray(np.asarray(ins["L1_attn_qkv_w"], np.float32))
    d["b1_qb"] = fmv(qkv_b[0:1024], NC)
    d["b1_kb"] = fmv(qkv_b[1024:2048], NC)
    d["b1_vb"] = fmv(qkv_b[2048:3072], NC)
    d["b1_outw"] = np.ascontiguousarray(np.asarray(ins["L1_attn_out_w"], np.float32))
    d["b1_ob"] = fmv(ins["L1_attn_out_b"], NC)
    d["b1f_w1"] = np.ascontiguousarray(np.asarray(ins["L1_ffn_W1"], np.float32))
    d["b1f_b1"] = fmv(ins["L1_ffn_b1"], 32)
    d["b1f_w2"] = np.ascontiguousarray(np.asarray(ins["L1_ffn_W2"], np.float32))
    d["b1f_b2"] = fmv(ins["L1_ffn_b2"], NC)
    d["b1f_th"] = fmv(ins["L1_ffn_quantum_phases"], 32)
    d["b1f_wr"] = fmv(ins["L1_ffn_quantum_weights_real"], 32)
    d["b1f_wi"] = fmv(ins["L1_ffn_quantum_weights_imag"], 32)
    return d


def build_prog_b1():
    nc = bass.Bass("TRN2", target_bir_lowering=False, debug=False)
    iv, ir = {}, {}

    def inp(name, shape, dt=F32):
        iv[name] = nc.dram_tensor(name, list(shape), dt, kind="ExternalInput").ap()
        ir[name] = Reg(name)

    inp("xin", [1024, T])
    for nm, shp in B1_INPUTS:
        inp(nm, shp, BF16 if nm == "idf" else F32)
    xout = nc.dram_tensor("xout", [1024, T], F32, kind="ExternalOutput").ap()
    xout_r = Reg("xout")
    g1i = nc.dram_tensor("g1i", [16, 4160], F32).ap()
    g1o = nc.dram_tensor("g1o", [64, 4160], F32).ap()
    g1r = [Reg("g1i"), Reg("g1o")]

    s = Sched(nc)
    with contextlib.ExitStack() as stk:
        ctx = Ctx(nc, stk, s)
        ctx.groups = [[0, 1, 2, 3], [4, 5, 6, 7]]
        ctx.psum_init(n_f32=6, n_bf16=1)
        block = stk.enter_context(nc.Block())
        alloc_common(ctx)
        build_block1(ctx, iv, ir, iv["xin"], ir["xin"], xout, xout_r,
                     g1i, g1o, g1r)
        last = [o for o in s.ops if o.dma and o.engine == "sync"][-1]
        s.op("gpsimd", _mk_emit("nop", (), {}), deps=[last, s.ops[-1]])
        s.build(stk, block)
    return nc


def run_block1(ins, x_cur):
    nc = build_prog_b1()
    in_maps = [stage_b1(ins, c, x_cur) for c in range(NCORE)]
    res = run_bass_kernel_spmd(nc, in_maps, core_ids=list(range(NCORE)))
    outs = [np.asarray(res.results[c]["xout"]).T for c in range(NCORE)]
    return np.stack([np.concatenate(outs[0:4], 0), np.concatenate(outs[4:8], 0)])


B2_INPUTS = [
    ("idf", [128, 128], "bf"), ("etf", [128, NC * 16], "f"),
    ("i16f", [16, 16], "f"),
    ("b2_WqT", [1024, 1024], "f"), ("b2_WkT", [1024, 1024], "f"),
    ("b2_WvT", [1024, 1024], "f"), ("b2_WoT", [1024, 1024], "f"),
    ("b2_AqT", [1024, 16], "f"), ("b2_AkT", [1024, 16], "f"),
    ("b2_AvT", [1024, 16], "f"), ("b2_AoT", [1024, 16], "f"),
    ("b2_BqT", [16, 1024], "f"), ("b2_BkT", [16, 1024], "f"),
    ("b2_BvT", [16, 1024], "f"), ("b2_BoT", [16, 1024], "f"),
    ("b2_bo", [128, NC], "f"),
    ("b2_w1t", [1024, 4096], "f"), ("b2_w2t", [1024, 4096], "f"),
    ("b2_h1t", [4096, 4096], "f"), ("b2_h2t", [4096, 4096], "f"),
    ("b2_w3t", [4096, 1024], "f"),
    ("b2_s1", [128, 32], "f"), ("b2_s2", [128, 32], "f"),
    ("b2_sh1", [128, 32], "f"), ("b2_sh2", [128, 32], "f"),
    ("b2_s3", [128, NC], "f"),
]


def stage_b2(ins, core, x_cur):
    b, q = core // 4, core % 4
    seg = x_cur[b, q * T:(q + 1) * T] if x_cur.shape[1] >= (q + 1) * T         else np.zeros((T, 1024), np.float32)
    d = {"xin": np.ascontiguousarray(seg.T)}
    idf = np.zeros((128, 128), np.float32)
    np.fill_diagonal(idf, 1.0)
    d["idf"] = idf.astype(ml_dtypes.bfloat16)
    d["i16f"] = np.eye(16, dtype=np.float32)
    etf = np.zeros((128, NC * 16), np.float32)
    for cc in range(NC):
        for p in range(128):
            etf[p, cc * 16 + 2 * cc + p // 64] = 1.0
    d["etf"] = etf
    wT = lambda k: np.ascontiguousarray(np.asarray(ins[k], np.float32).T)
    for nm, key in [("b2_WqT", "L2_attn_Wq"), ("b2_WkT", "L2_attn_Wk"),
                    ("b2_WvT", "L2_attn_Wv"), ("b2_WoT", "L2_attn_Wo"),
                    ("b2_AqT", "L2_attn_Aq"), ("b2_AkT", "L2_attn_Ak"),
                    ("b2_AvT", "L2_attn_Av"), ("b2_AoT", "L2_attn_Ao"),
                    ("b2_BqT", "L2_attn_Bq"), ("b2_BkT", "L2_attn_Bk"),
                    ("b2_BvT", "L2_attn_Bv"), ("b2_BoT", "L2_attn_Bo"),
                    ("b2_w1t", "L2_ffn_w1_w"), ("b2_w2t", "L2_ffn_w2_w"),
                    ("b2_h1t", "L2_ffn_h1_w"), ("b2_h2t", "L2_ffn_h2_w"),
                    ("b2_w3t", "L2_ffn_w3_w")]:
        d[nm] = wT(key)
    d["b2_bo"] = fmv(ins["L2_attn_bo"], NC)
    d["b2_s1"] = fmv(ins["L2_ffn_w1_s"], 32)
    d["b2_s2"] = fmv(ins["L2_ffn_w2_s"], 32)
    d["b2_sh1"] = fmv(ins["L2_ffn_h1_s"], 32)
    d["b2_sh2"] = fmv(ins["L2_ffn_h2_s"], 32)
    d["b2_s3"] = fmv(ins["L2_ffn_w3_s"], NC)
    return d


def build_prog_b2():
    nc = bass.Bass("TRN2", target_bir_lowering=False, debug=False)
    iv, ir = {}, {}

    def inp(name, shape, dt=F32):
        iv[name] = nc.dram_tensor(name, list(shape), dt, kind="ExternalInput").ap()
        ir[name] = Reg(name)

    inp("xin", [1024, T])
    for nm, shp, k in B2_INPUTS:
        inp(nm, shp, BF16 if k == "bf" else F32)
    xout = nc.dram_tensor("xout", [1024, T], F32, kind="ExternalOutput").ap()
    xout_r = Reg("xout")
    gki = nc.dram_tensor("gki", [1024, T], BF16).ap()
    gko = nc.dram_tensor("gko", [4096, T], BF16).ap()
    gvi = nc.dram_tensor("gvi", [T, 1024], BF16).ap()
    gvo = nc.dram_tensor("gvo", [4 * T, 1024], BF16).ap()
    gsi = nc.dram_tensor("gsi", [1, 16], F32).ap()
    gso = nc.dram_tensor("gso", [4, 16], F32).ap()
    g2r = [Reg(x) for x in ["gki", "gko", "gvi", "gvo", "gsi", "gso"]]

    s = Sched(nc)
    with contextlib.ExitStack() as stk:
        ctx = Ctx(nc, stk, s)
        ctx.groups = [[0, 1, 2, 3], [4, 5, 6, 7]]
        ctx.psum_init(n_f32=6, n_bf16=1)
        block = stk.enter_context(nc.Block())
        alloc_common(ctx)
        build_block2(ctx, iv, ir, iv["xin"], ir["xin"], xout, xout_r,
                     gki, gko, gvi, gvo, gsi, gso, g2r)
        last = [o for o in s.ops if o.dma and o.engine == "sync"][-1]
        s.op("gpsimd", _mk_emit("nop", (), {}), deps=[last, s.ops[-1]])
        s.build(stk, block)
    return nc


def run_block2(ins, x_cur):
    nc = build_prog_b2()
    in_maps = [stage_b2(ins, c, x_cur) for c in range(NCORE)]
    res = run_bass_kernel_spmd(nc, in_maps, core_ids=list(range(NCORE)))
    outs = [np.asarray(res.results[c]["xout"]).T for c in range(NCORE)]
    return np.stack([np.concatenate(outs[0:4], 0), np.concatenate(outs[4:8], 0)])


B3_INPUTS = [
    ("idf", [128, 128], "bf"), ("idf32", [128, 128], "f"),
    ("g_pm", [128, 128], "f"), ("musU", [128, 128], "f"),
    ("muI", [128, 128], "f"), ("tv", [128, 2], "f"),
    ("b3_WT", [1024, 1024], "f"), ("b3_lngr", [1, 1024], "f"),
    ("b3_lnbr", [1, 1024], "f"), ("b3_lam", [1, 1], "f"),
    ("b3_eta", [1, 1], "f"),
    ("b3_w1t", [1024, 4096], "f"), ("b3_w2t", [1024, 4096], "f"),
    ("b3_h1t", [4096, 4096], "f"), ("b3_h2t", [4096, 4096], "f"),
    ("b3_w3t", [4096, 1024], "f"),
    ("b3_s1", [128, 32], "f"), ("b3_s2", [128, 32], "f"),
    ("b3_sh1", [128, 32], "f"), ("b3_sh2", [128, 32], "f"),
    ("b3_s3", [128, NC], "f"), ("qsel", [128, 4], "f"),
]


def stage_b3(ins, core, x_cur):
    b, q = core // 4, core % 4
    seg = x_cur[b, q * T:(q + 1) * T] if x_cur.shape[1] >= (q + 1) * T         else np.zeros((T, 1024), np.float32)
    d = {"xin": np.ascontiguousarray(seg.T)}
    idf = np.zeros((128, 128), np.float32)
    np.fill_diagonal(idf, 1.0)
    d["idf"] = idf.astype(ml_dtypes.bfloat16)
    d["idf32"] = idf
    p = np.arange(128, dtype=np.float32)
    d["g_pm"] = np.ascontiguousarray(p[:, None] - p[None, :])
    d["musU"] = (p[:, None] < p[None, :]).astype(np.float32)
    d["muI"] = (p[:, None] <= p[None, :]).astype(np.float32)
    d["tv"] = np.ascontiguousarray(
        np.stack([p + 1.0, 128.0 - (p + 1.0)], axis=1))
    d["b3_WT"] = np.ascontiguousarray(np.asarray(ins["L3_attn_W"], np.float32).T)
    d["b3_lngr"] = np.asarray(ins["L3_attn_ln_gamma"], np.float32).reshape(1, 1024)
    d["b3_lnbr"] = np.asarray(ins["L3_attn_ln_beta"], np.float32).reshape(1, 1024)
    d["b3_lam"] = np.asarray(ins["L3_attn_lam_logit"], np.float32).reshape(1, 1)
    d["b3_eta"] = np.asarray(ins["L3_attn_eta"], np.float32).reshape(1, 1)
    wT = lambda k: np.ascontiguousarray(np.asarray(ins[k], np.float32).T)
    d["b3_w1t"] = wT("L3_ffn_w1_w")
    d["b3_w2t"] = wT("L3_ffn_w2_w")
    d["b3_h1t"] = wT("L3_ffn_h1_w")
    d["b3_h2t"] = wT("L3_ffn_h2_w")
    d["b3_w3t"] = wT("L3_ffn_w3_w")
    d["b3_s1"] = fmv(ins["L3_ffn_w1_s"], 32)
    d["b3_s2"] = fmv(ins["L3_ffn_w2_s"], 32)
    d["b3_sh1"] = fmv(ins["L3_ffn_h1_s"], 32)
    d["b3_sh2"] = fmv(ins["L3_ffn_h2_s"], 32)
    d["b3_s3"] = fmv(ins["L3_ffn_w3_s"], NC)
    qs = np.zeros((128, 4), np.float32)
    qs[:, q] = 1.0
    d["qsel"] = qs
    return d


def build_prog_b3():
    nc = bass.Bass("TRN2", target_bir_lowering=False, debug=False)
    iv, ir = {}, {}

    def inp(name, shape, dt=F32):
        iv[name] = nc.dram_tensor(name, list(shape), dt, kind="ExternalInput").ap()
        ir[name] = Reg(name)

    inp("xin", [1024, T])
    for nm, shp, k in B3_INPUTS:
        inp(nm, shp, BF16 if k == "bf" else F32)
    xout = nc.dram_tensor("xout", [1024, T], F32, kind="ExternalOutput").ap()
    xout_r = Reg("xout")
    a1i = nc.dram_tensor("a1i", [T, 2048], BF16).ap()
    a1o = nc.dram_tensor("a1o", [4 * T, 2048], BF16).ap()
    a2i = nc.dram_tensor("a2i", [4 * T, 256], F32).ap()
    a2o = nc.dram_tensor("a2o", [16 * T, 256], F32).ap()
    g3r = [Reg(x) for x in ["a1i", "a1o", "a2i", "a2o"]]

    s = Sched(nc)
    with contextlib.ExitStack() as stk:
        ctx = Ctx(nc, stk, s)
        ctx.groups = [[0, 1, 2, 3], [4, 5, 6, 7]]
        ctx.psum_init(n_f32=6, n_bf16=1)
        block = stk.enter_context(nc.Block())
        alloc_common(ctx)
        build_block3(ctx, iv, ir, iv["xin"], ir["xin"], xout, xout_r,
                     a1i, a1o, a2i, a2o, g3r)
        last = [o for o in s.ops if o.dma and o.engine == "sync"][-1]
        s.op("gpsimd", _mk_emit("nop", (), {}), deps=[last, s.ops[-1]])
        s.build(stk, block)
    return nc


def run_block3(ins, x_cur):
    nc = build_prog_b3()
    in_maps = [stage_b3(ins, c, x_cur) for c in range(NCORE)]
    res = run_bass_kernel_spmd(nc, in_maps, core_ids=list(range(NCORE)))
    outs = [np.asarray(res.results[c]["xout"]).T for c in range(NCORE)]
    return np.stack([np.concatenate(outs[0:4], 0), np.concatenate(outs[4:8], 0)])


def kernel(**inputs):
    _unused = (inputs["L0_attn_v0"], inputs["L0_attn_v1"], inputs["L0_attn_v2"])
    return kernel_fused(**inputs)


def kernel_unfused(**inputs):
    x1 = run_block0(inputs)
    x2 = run_block1(inputs, x1)
    x3 = run_block2(inputs, x2)
    x4 = run_block3(inputs, x3)
    return np.ascontiguousarray(x4.astype(np.float32))


def build_prog_fused():
    nc = bass.Bass("TRN2", target_bir_lowering=False, debug=False)
    iv, ir = {}, {}

    def inp(name, shape, dt=F32):
        if name in iv:
            return
        iv[name] = nc.dram_tensor(name, list(shape), dt, kind="ExternalInput").ap()
        ir[name] = Reg(name)

    inp("xin", [1024, T + 1])
    for nm, shp in B0_INPUTS:
        inp(nm, shp)
    for nm, shp in B1_INPUTS:
        inp(nm, shp, BF16 if nm == "idf" else F32)
    for nm, shp, k in B2_INPUTS:
        inp(nm, shp, BF16 if k == "bf" else F32)
    for nm, shp, k in B3_INPUTS:
        inp(nm, shp, BF16 if k == "bf" else F32)
    xout = nc.dram_tensor("xout", [1024, T], F32, kind="ExternalOutput").ap()
    xout_r = Reg("xout")
    x1_d = nc.dram_tensor("x1_d", [1024, T], F32).ap()
    x2_d = nc.dram_tensor("x2_d", [1024, T], F32).ap()
    x3_d = nc.dram_tensor("x3_d", [1024, T], F32).ap()
    x1_r, x2_r, x3_r = Reg("x1"), Reg("x2"), Reg("x3")
    gin = [nc.dram_tensor(f"g0i_{c}", [2, 128], F32).ap() for c in range(NC)]
    gout = [nc.dram_tensor(f"g0o_{c}", [8, 128], F32).ap() for c in range(NC)]
    gr = []
    for c in range(NC):
        gr += [Reg(f"g0i_{c}"), Reg(f"g0o_{c}")]
    g1i = nc.dram_tensor("g1i", [16, 4160], F32).ap()
    g1o = nc.dram_tensor("g1o", [64, 4160], F32).ap()
    g1r = [Reg("g1i"), Reg("g1o")]
    gki = nc.dram_tensor("gki", [1024, T], BF16).ap()
    gko = nc.dram_tensor("gko", [4096, T], BF16).ap()
    gvi = nc.dram_tensor("gvi", [T, 1024], BF16).ap()
    gvo = nc.dram_tensor("gvo", [4 * T, 1024], BF16).ap()
    gsi = nc.dram_tensor("gsi", [1, 16], F32).ap()
    gso = nc.dram_tensor("gso", [4, 16], F32).ap()
    g2r = [Reg(x) for x in ["gki", "gko", "gvi", "gvo", "gsi", "gso"]]
    a1i = nc.dram_tensor("a1i", [T, 2048], BF16).ap()
    a1o = nc.dram_tensor("a1o", [4 * T, 2048], BF16).ap()
    a2i = nc.dram_tensor("a2i", [4 * T, 256], F32).ap()
    a2o = nc.dram_tensor("a2o", [16 * T, 256], F32).ap()
    g3r = [Reg(x) for x in ["a1i", "a1o", "a2i", "a2o"]]

    s = Sched(nc)
    with contextlib.ExitStack() as stk:
        ctx = Ctx(nc, stk, s)
        ctx.groups = [[0, 1, 2, 3], [4, 5, 6, 7]]
        ctx.psum_init(n_f32=6, n_bf16=1)
        block = stk.enter_context(nc.Block())
        alloc_common(ctx)
        prog_stk = ctx.stk
        for bi, fn in enumerate([
            lambda: build_block0(ctx, iv, ir, iv["xin"], ir["xin"], x1_d, x1_r,
                                 gin, gout, gr),
            lambda: build_block1(ctx, iv, ir, x1_d, x1_r, x2_d, x2_r,
                                 g1i, g1o, g1r),
            lambda: build_block2(ctx, iv, ir, x2_d, x2_r, x3_d, x3_r,
                                 gki, gko, gvi, gvo, gsi, gso, g2r),
            lambda: build_block3(ctx, iv, ir, x3_d, x3_r, xout, xout_r,
                                 a1i, a1o, a2i, a2o, g3r),
        ]):
            with contextlib.ExitStack() as bs:
                ctx.stk = bs
                fn()
                ctx.fence(f"blk{bi}")
            ctx.stk = prog_stk
        last = [o for o in s.ops if o.dma and o.engine == "sync"][-1]
        s.op("gpsimd", _mk_emit("nop", (), {}), deps=[last, s.ops[-1]])
        s.build(stk, block)
    return nc


def kernel_fused(**inputs):
    nc = build_prog_fused()
    in_maps = []
    for c in range(NCORE):
        d = {}
        d.update(stage_b3(inputs, c, np.zeros((2, 1, 1024), np.float32)))
        d.update(stage_b2(inputs, c, np.zeros((2, 1, 1024), np.float32)))
        d.update(stage_b1(inputs, c, np.zeros((2, 1, 1024), np.float32)))
        d.update(stage_b0(inputs, c))
        d.pop("xin")
        b, q = c // 4, c % 4
        x = np.asarray(inputs["L0_attn_x"], np.float32)
        seg = x[b, q * T:(q + 1) * T]
        prev = x[b, q * T - 1:q * T] if q > 0 else np.zeros((1, 1024), np.float32)
        d["xin"] = np.ascontiguousarray(np.concatenate([prev, seg], 0).T)
        in_maps.append(d)
    res = run_bass_kernel_spmd(nc, in_maps, core_ids=list(range(NCORE)))
    outs = [np.asarray(res.results[c]["xout"]).T for c in range(NCORE)]
    x4 = np.stack([np.concatenate(outs[0:4], 0), np.concatenate(outs[4:8], 0)])
    return np.ascontiguousarray(x4.astype(np.float32))
